# version 02
# baseline (speedup 1.0000x reference)
import numpy as np
import ml_dtypes
from contextlib import ExitStack

import concourse.bass as bass
import concourse.mybir as mybir
from concourse.bass_utils import run_bass_kernel_spmd

F32 = mybir.dt.float32
BF16 = mybir.dt.bfloat16
AF = mybir.ActivationFunctionType
ALU = mybir.AluOpType
AX = mybir.AxisListType

T = 2048
TC = 16
D = 1024
DC = 8
NH = 16
HD = 64
NCORES = 8
SEQ_PER_CORE = 4

ARENA_WORDS = 52992



class PB:
    ENGINES = ("sync", "gpsimd", "tensor", "scalar", "vector")

    def __init__(self):
        self.nc = bass.Bass("TRN2", target_bir_lowering=False)
        self.ctx = ExitStack()
        self.q = {e: [] for e in self.ENGINES}
        self.sems = {}
        self.cnt = {}
        self._wdb = {}
        self._rdb = {}
        self._watermark = {}

    def sem(self, name):
        self.sems[name] = self.ctx.enter_context(self.nc.semaphore(name))
        self.cnt[name] = 0

    def dram(self, name, shape, dt=F32, kind="ExternalInput"):
        return self.nc.dram_tensor(name, list(shape), dt, kind=kind).ap()

    def arena_init(self):
        self._arena = self.ctx.enter_context(
            self.nc.sbuf_tensor("arena", [128, ARENA_WORDS], F32))
        self._psum = self.ctx.enter_context(
            self.nc.psum_tensor("ps", [128, 4096], F32))

    def R(self, off, words, dt=F32):
        assert off >= 0 and off + words <= ARENA_WORDS, (off, words)
        ap = self._arena.ap()[:, off:off + words]
        if dt == BF16:
            ap = ap.bitcast(BF16)
        return ap

    def PS(self, bank, nbanks=1):
        return self._psum.ap()[:, bank * 512:(bank + nbanks) * 512]

    _WRITE_KEYS = ("out", "dest", "accum_out")
    _READ_KEYS = ("in_", "in0", "in1", "scalar", "scalar1", "scalar2", "bias",
                  "scale", "data0", "data1", "mask", "on_true", "on_false",
                  "data", "initial")
    _AUTO_ENGS = ("vector", "scalar")

    @staticmethod
    def _interval(ap):
        import concourse.mybir as _mb
        esz = _mb.dt.size(ap.dtype)
        pairs = list(ap.ap)
        pstride = pairs[0][0]
        pcount = pairs[0][1]
        off = ap.offset
        plo = off // pstride if pstride else 0
        off_in = off - plo * pstride
        span = 1
        lo = 0
        hi = 0
        for step, count in pairs[1:]:
            if step >= 0:
                hi += step * (count - 1)
            else:
                lo += step * (count - 1)
        return (ap.tensor.name, (off_in + lo) * esz, (off_in + hi + 1) * esz,
                plo, plo + pcount)

    def _deps(self, eng, args, kwargs):
        from concourse.ap import AP as _AP
        reads, writes = [], []
        for k, v in kwargs.items():
            if isinstance(v, _AP):
                (writes if k in self._WRITE_KEYS else reads).append(v)
        for v in args:
            if isinstance(v, _AP):
                writes.append(v)
        need = 0
        rr = [self._interval(a) for a in reads]
        ww = [self._interval(a) for a in writes]

        def overlap(a, b):
            return (a[0] == b[0] and a[1] < b[2] and b[1] < a[2]
                    and a[3] < b[4] and b[3] < a[4])

        wdb = self._wdb.setdefault(eng, {})
        rdb = self._rdb.setdefault(eng, {})
        for iv in rr:
            for rec in wdb.get(iv[0], ()):
                if overlap(iv, rec[0]):
                    need = max(need, rec[1])
        for iv in ww:
            for rec in wdb.get(iv[0], ()):
                if overlap(iv, rec[0]):
                    need = max(need, rec[1])
            for rec in rdb.get(iv[0], ()):
                if overlap(iv, rec[0]):
                    need = max(need, rec[1])
        ordinal = self.cnt[self._eng_sem[eng]] + 1
        for iv in ww:
            lst = wdb.setdefault(iv[0], [])
            lst[:] = [r for r in lst if not (r[0] == iv)]
            lst.append((iv, ordinal))
            if len(lst) > 256:
                del lst[0]
        for iv in rr:
            lst = rdb.setdefault(iv[0], [])
            lst[:] = [r for r in lst if not (r[0] == iv)]
            lst.append((iv, ordinal))
            if len(lst) > 256:
                del lst[0]
        return need

    _eng_sem = {"vector": "DVE", "scalar": "ACT"}

    def op(self, eng, method, *args, w=(), inc=None, **kwargs):
        auto = eng in self._AUTO_ENGS
        wl = []
        for item in w:
            if item is None:
                continue
            sname, val = item
            if val and val > 0:
                wl.append((sname, val))
        if auto:
            sem = self._eng_sem[eng]
            need = self._deps(eng, args, kwargs)
            wm = self._watermark.get(eng, 0)
            if need > wm:
                wl.append((sem, need))
            for sname, val in wl:
                if sname == sem:
                    wm = max(wm, val)
            self._watermark[eng] = wm
            assert inc is None or inc == sem or inc == (sem, 1), inc
            self.cnt[sem] += 1
            ret = self.cnt[sem]
            inc = (sem, 1)
        elif inc is not None:
            if isinstance(inc, str):
                inc = (inc, 1)
            self.cnt[inc[0]] += inc[1]
            ret = self.cnt[inc[0]]
        else:
            ret = None
        self.q[eng].append((method, args, kwargs, tuple(wl), inc))
        return ret

    def t(self, m, *a, **k):
        return self.op("tensor", m, *a, **k)

    def v(self, m, *a, **k):
        return self.op("vector", m, *a, **k)

    def s(self, m, *a, **k):
        return self.op("scalar", m, *a, **k)

    def dma(self, out, in_, incsem, w=(), cast=False, **kw):
        eng = "gpsimd" if cast else "sync"
        w = list(w) + [(incsem, self.cnt[incsem])]
        v = self.op(eng, "dma_start", w=w, inc=(incsem, 16), out=out, in_=in_, **kw)
        return (incsem, v)

    def wait(self, eng, w):
        self.q[eng].append(("__wait__", (), {}, tuple(x for x in w if x and x[1]), None))

    def finalize(self):
        block = self.ctx.enter_context(self.nc.Block())
        sems = self.sems

        def mk(ops):
            def body(eng):
                for method, args, kwargs, waits, inc in ops:
                    for sname, val in waits:
                        eng.wait_ge(sems[sname], val)
                    if method == "__wait__":
                        continue
                    r = getattr(eng, method)(*args, **kwargs)
                    if inc is not None:
                        r.then_inc(sems[inc[0]], inc[1])
            return body

        for eng in self.ENGINES:
            if self.q[eng]:
                getattr(block, eng)(mk(self.q[eng]))
        self.ctx.close()
        return self.nc


def chunks(ap, n):
    return ap.rearrange("p (c t) -> p c t", c=n)



class Alloc:
    def __init__(self, pb, base=0):
        self.pb = pb
        self.off = base
        self.map = {}

    def r(self, name, words, dt=F32):
        ap = self.pb.R(self.off, words, dt)
        self.map[name] = (self.off, words)
        self.off += words
        return ap

    def cols(self, names):
        return {n: self.r("col_" + n, 4)[:, 0:1] for n in names}


def emit_consts(pb, io, al, need=()):
    c = {}
    c["scr"] = al.r("scr", 16)
    c["ident_bf"] = al.r("ident_bf", 64, BF16)
    c["ident_f"] = al.r("ident_f", 128)
    c["ones_row"] = al.r("ones_row", 128)
    c["ones_bf"] = al.r("ones_bf", 64, BF16)
    pb.dma(c["ident_bf"], io["ident_bf"], "LD")
    pb.dma(c["ident_f"], io["ident_f"], "LD")
    pb.dma(c["ones_row"][0:1, :], io["ones_row"], "LD")
    pb.dma(c["ones_bf"], io["ones_bf"], "LD")
    if "mask" in need:
        c["mask"] = al.r("mask", 128)
        c["maskT"] = al.r("maskT", 128)
        pb.dma(c["mask"], io["mask"], "LD")
        pb.dma(c["maskT"], io["maskT"], "LD")
    if "rope" in need:
        c["cos"] = al.r("cos", 1024)
        c["sin"] = al.r("sin", 1024)
        pb.dma(chunks(c["cos"], TC), io["cos"].rearrange("(tc p) d -> p tc d", p=128), "LD")
        pb.dma(chunks(c["sin"], TC), io["sin"].rearrange("(tc p) d -> p tc d", p=128), "LD")
    c["_ld"] = ("LD", pb.cnt["LD"])
    return c



def emit_ln_to_ft(pb, cols, consts, out_ft, scratch, kind="ln", xin_r=None,
                  src_sbuf=None, gamma_rep=None, gamma_wait=None, eps=1e-5,
                  pbank=6, extra_first_wait=None):
    evac_prev = [0, 0]
    xc_free = [None, None]
    hb_free = [0, 0]
    last_evac = 0
    for tc in range(TC):
        par = tc % 2
        if src_sbuf is not None:
            xc = chunks(src_sbuf, TC)[:, tc]
            v_w = [extra_first_wait] if (tc == 0 and extra_first_wait) else []
        else:
            xc = scratch["XC"][par]
            ldv = pb.dma(xc, xin_r[tc], "LD", w=[xc_free[par]] if xc_free[par] else [])
            v_w = [ldv]
            if tc == 0 and extra_first_wait:
                v_w.append(extra_first_wait)
        pb.v("scalar_tensor_tensor", out=scratch["SQ"], in0=xc, scalar=1.0,
             in1=xc, op0=ALU.mult, op1=ALU.mult, accum_out=cols["SQC"], w=v_w)
        if kind == "ln":
            pb.v("reduce_sum", out=cols["SMC"], in_=xc, axis=AX.X)
            pb.v("tensor_scalar_mul", out=cols["NEGMN"], in0=cols["SMC"], scalar1=-1.0 / D)
            pb.v("tensor_scalar_mul", out=cols["EX2"], in0=cols["SQC"], scalar1=1.0 / D)
            pb.v("tensor_tensor", out=cols["M2"], in0=cols["NEGMN"], in1=cols["NEGMN"], op=ALU.mult)
            vv = pb.v("tensor_tensor", out=cols["VARC"], in0=cols["EX2"], in1=cols["M2"],
                      op=ALU.subtract, inc="DVE")
        else:
            vv = pb.v("tensor_scalar_mul", out=cols["VARC"], in0=cols["SQC"],
                      scalar1=1.0 / D, inc="DVE")
        vv = pb.v("tensor_scalar_add", out=cols["VARC"], in0=cols["VARC"],
                  scalar1=float(eps), w=[("DVE", vv)], inc="DVE")
        av = pb.s("activation", out=cols["SDC"], in_=cols["VARC"], func=AF.Sqrt,
                  w=[("DVE", vv)], inc="ACT")
        rv = pb.v("reciprocal", out=cols["RSTD"], in_=cols["SDC"], w=[("ACT", av)], inc="DVE")
        hb = scratch["HB"][par]
        wl = [("DVE", rv), ("PE", hb_free[par])]
        if kind == "ln":
            nv = pb.v("tensor_scalar", out=hb, in0=xc, scalar1=cols["NEGMN"],
                      scalar2=cols["RSTD"], op0=ALU.add, op1=ALU.mult, w=wl, inc="DVE")
        else:
            gw = [gamma_wait] if (tc == 0 and gamma_wait) else []
            nv = pb.v("scalar_tensor_tensor", out=hb, in0=xc, scalar=cols["RSTD"],
                      in1=gamma_rep, op0=ALU.mult, op1=ALU.mult, w=wl + gw, inc="DVE")
        xc_free[par] = ("DVE", nv)
        pbk = pbank if par == 0 else pbank + 1
        ps_bf = pb.PS(pbk, 1).bitcast(BF16)
        pe_last = None
        for ci in range(DC):
            wlist = []
            if ci == 0:
                wlist = [("DVE", nv), ("ACT", evac_prev[par]), consts["_ld"] if tc == 0 else None]
            pe_last = pb.t("transpose", ps_bf[:, ci * 128:(ci + 1) * 128],
                           hb[:, ci * 128:(ci + 1) * 128], consts["ident_bf"],
                           w=wlist, inc="PE" if ci == DC - 1 else None)
        hb_free[par] = pe_last
        out_view = chunks(out_ft, DC)[:, :, tc * 128:(tc + 1) * 128]
        last_evac = pb.s("copy", out=out_view, in_=chunks(ps_bf, DC),
                         w=[("PE", pe_last)], inc="ACT")
        evac_prev[par] = last_evac
    return ("ACT", last_evac)



def emit_tok_to_ft(pb, consts, tok_bf, ft, tc, pbank, first_wait=(), evac_prev=0):
    ps_bf = pb.PS(pbank, 1).bitcast(BF16)
    pe_last = None
    for ci in range(DC):
        wl = list(first_wait) + [("ACT", evac_prev)] if ci == 0 else []
        pe_last = pb.t("transpose", ps_bf[:, ci * 128:(ci + 1) * 128],
                       tok_bf[:, ci * 128:(ci + 1) * 128], consts["ident_bf"],
                       w=wl, inc="PE" if ci == DC - 1 else None)
    ev = pb.s("copy", out=chunks(ft, DC)[:, :, tc * 128:(tc + 1) * 128],
              in_=chunks(ps_bf, DC), w=[("PE", pe_last)], inc="ACT")
    return pe_last, ev


def emit_attention(pb, io, consts, cols, M4, NEGMH, QT, KT, VEXT, OT, ATB, ovl,
                   scale, causal, entry_free):
    LRW, RHIB, RLOB, RBC, OSCR = (ovl[k] for k in ("LRW", "RHIB", "RLOB", "RBC", "OSCR"))

    def head_sl(ft, h):
        base = 64 * (h % 2)
        return chunks(ft, DC)[base:base + 64, h // 2]

    s_free = {0: entry_free, 4: entry_free}
    o_free = None
    atb_free = [0, 0]
    for h in range(NH):
        qh = head_sl(QT, h)
        kh = head_sl(KT, h)
        for qt in range(TC):
            bank = 0 if qt % 2 == 0 else 4
            ps = pb.PS(bank, 4)
            kmax = (qt + 1) * 128 if causal else T
            nblk = (kmax + 511) // 512
            mm = None
            for n in range(nblk):
                w0 = n * 512
                w1 = min(w0 + 512, kmax)
                wl = [s_free[bank]] if n == 0 else []
                mm = pb.t("matmul", ps[:, w0:w1],
                          qh[:, qt * 128:(qt + 1) * 128], kh[:, w0:w1],
                          start=True, stop=True, w=wl,
                          inc="PE" if n == nblk - 1 else None)
            if causal:
                madd = pb.v("tensor_tensor", out=ps[:, qt * 128:(qt + 1) * 128],
                            in0=ps[:, qt * 128:(qt + 1) * 128], in1=consts["mask"],
                            op=ALU.add, w=[("PE", mm)], inc="DVE")
            mx = pb.v("tensor_reduce", out=M4[:, 4 * qt:4 * qt + 1],
                      in_=ps[:, 0:kmax].rearrange("p (n s) -> p n s", s=4)[:, :, 0],
                      axis=AX.X, op=ALU.max, w=[("PE", mm)], inc="DVE")
            s_free[bank] = ("DVE", mx)
        vv = pb.v("tensor_reduce", out=cols["VARC"],
                  in_=M4.rearrange("p (n s) -> p n s", s=4)[:, :, 0],
                  axis=AX.X, op=ALU.max, inc="DVE")
        pt = pb.t("transpose", pb.PS(0, 1)[0:1, 0:128], cols["VARC"], consts["ident_f"],
                  w=[("DVE", vv), s_free[0]], inc="PE")
        mv = pb.v("tensor_reduce", out=cols["MHS"][0:1, :], in_=pb.PS(0, 1)[0:1, 0:128],
                  axis=AX.X, op=ALU.max, w=[("PE", pt)], inc="DVE")
        bc = pb.t("matmul", pb.PS(1, 1)[:, 0:1], consts["ones_row"][0:1, :],
                  cols["MHS"][0:1, 0:1], start=True, stop=True, w=[("DVE", mv)], inc="PE")
        ng = pb.v("tensor_scalar_mul", out=NEGMH, in0=pb.PS(1, 1)[:, 0:1],
                  scalar1=-scale, w=[("PE", bc)], inc="DVE")
        s_free[0] = ("DVE", ng)
        av_last = None
        for kt in range(TC):
            par = kt % 2
            ps = pb.PS(0, 4)
            q0 = kt * 128 if causal else 0
            blks = []
            pos = q0
            while pos < T:
                w1 = min((pos // 512 + 1) * 512, T)
                blks.append((pos, w1))
                pos = w1
            mm = None
            for i, (b0, b1) in enumerate(blks):
                wl = [s_free[0]] if i == 0 else []
                mm = pb.t("matmul", ps[:, b0:b1],
                          kh[:, kt * 128:(kt + 1) * 128], qh[:, b0:b1],
                          start=True, stop=True, w=wl,
                          inc="PE" if i == len(blks) - 1 else None)
            exw = [("PE", mm), ("PE", atb_free[par])]
            if causal:
                madd = pb.v("tensor_tensor", out=ps[:, kt * 128:(kt + 1) * 128],
                            in0=ps[:, kt * 128:(kt + 1) * 128], in1=consts["maskT"],
                            op=ALU.add, w=[("PE", mm)], inc="DVE")
                exw.append(("DVE", madd))
            ex = pb.s("activation", out=ATB[par][:, q0:T], in_=ps[:, q0:T],
                      func=AF.Exp, bias=NEGMH, scale=scale, w=exw, inc="ACT")
            s_free[0] = ("ACT", ex)
            po = pb.PS(4, 4)
            vsl = chunks(VEXT, TC)[:, kt][:, h * 65:(h + 1) * 65]
            for i, (b0, b1) in enumerate(blks):
                wl = [("ACT", ex)] if i == 0 else []
                if kt == 0 and i == 0 and o_free:
                    wl.append(o_free)
                bank = b0 // 512
                stop = (kt == 4 * bank + 3) if causal else (kt == TC - 1)
                av_last = pb.t("matmul", po[0:65, b0:b1], vsl, ATB[par][:, b0:b1],
                               start=(kt == 0), stop=stop, w=wl,
                               inc="PE" if (kt == TC - 1 and i == len(blks) - 1) else None)
        atb_free[0] = av_last
        atb_free[1] = av_last
        po = pb.PS(4, 4)
        r1 = pb.v("reciprocal", out=LRW[64:65, 0:T], in_=po[64:65, :],
                  w=[("PE", av_last)], inc="DVE")
        r2 = pb.v("tensor_copy", out=RHIB[64:65, 0:T], in_=LRW[64:65, 0:T],
                  w=[("DVE", r1)], inc="DVE")
        r3 = pb.v("tensor_tensor", out=RLOB[64:65, 0:T], in0=LRW[64:65, 0:T],
                  in1=RHIB[64:65, 0:T], op=ALU.subtract, w=[("DVE", r2)], inc="DVE")
        bmm = None
        for n in range(4):
            wl = [("DVE", r3), s_free[0], consts["_ld"] if h == 0 else None] if n == 0 else []
            pb.t("matmul", pb.PS(0, 4)[0:64, n * 512:(n + 1) * 512],
                 consts["ones_bf"][64:65, 0:64], RHIB[64:65, n * 512:(n + 1) * 512],
                 start=True, stop=False, w=wl)
            bmm = pb.t("matmul", pb.PS(0, 4)[0:64, n * 512:(n + 1) * 512],
                       consts["ones_bf"][64:65, 0:64], RLOB[64:65, n * 512:(n + 1) * 512],
                       start=False, stop=True, inc="PE" if n == 3 else None)
        rc = pb.s("copy", out=RBC[0:64, :], in_=pb.PS(0, 4)[0:64, :],
                  w=[("PE", bmm)], inc="ACT")
        s_free[0] = ("ACT", rc)
        if h % 2 == 0:
            dest = chunks(OT, DC)[0:64, h // 2]
            ov = pb.v("tensor_tensor", out=dest, in0=po[0:64, :], in1=RBC[0:64, :],
                      op=ALU.mult, w=[("ACT", rc)], inc="DVE")
            o_free = ("DVE", ov)
        else:
            ov = pb.v("tensor_tensor", out=OSCR[0:64, 0:T], in0=po[0:64, :],
                      in1=RBC[0:64, :], op=ALU.mult, w=[("ACT", rc)], inc="DVE")
            dst = chunks(OT, DC)[64:128, h // 2]
            o_free = pb.dma(dst, OSCR[0:64, 0:T], "LD", w=[("DVE", ov)])
        s_free[4] = o_free
    return o_free



def emit_phase_a(pb, io, consts, al):
    cols = al.cols(["SQC", "SMC", "NEGMN", "EX2", "M2", "VARC", "SDC", "RSTD", "MHS"])
    NEGMH = al.r("NEGMH", 4)[:, 0:1]
    M4 = al.r("M4", 64)
    SQ = al.r("SQ", D)
    XC = [al.r("XC0", D), al.r("XC1", D)]
    HB = [al.r("HB0", 512, BF16), al.r("HB1", 512, BF16)]
    H0T = al.r("H0T", 8192, BF16)
    WQKV = al.r("WQKV", 12288, BF16)
    QT = al.r("QT", 8192, BF16)
    KT = al.r("KT", 8192, BF16)
    VEXT = al.r("VEXT", 8320, BF16)
    ATB = [al.r("ATB0", 1024, BF16), al.r("ATB1", 1024, BF16)]
    assert al.off <= ARENA_WORDS, al.off
    h0t_off = al.map["H0T"][0]
    LRW = pb.R(h0t_off, 2048)
    RHIB = pb.R(h0t_off + 2048, 1024, BF16)
    RLOB = pb.R(h0t_off + 3072, 1024, BF16)
    RBC = pb.R(h0t_off + 4096, 2048)
    OSCR = pb.R(h0t_off + 6144, 1024, BF16)
    wq_off = al.map["WQKV"][0]
    OT = pb.R(wq_off, 8192, BF16)
    WOUT = pb.R(wq_off + 8192, 4096, BF16)

    wq1 = pb.dma(chunks(WQKV, DC), io["w_qkv_a"].rearrange("(c p) f -> p c f", p=128), "LW", cast=True)
    const_v = pb.v("memset",
                   VEXT.rearrange("p (tc h e) -> p tc h e", tc=TC, h=NH)[:, :, :, 64:65],
                   1.0, inc="DVE")
    xin_r = io["x_in"].rearrange("(tc p) d -> tc p d", p=128)

    scratch = {"XC": XC, "SQ": SQ, "HB": HB}
    h0t_done = emit_ln_to_ft(pb, cols, consts, H0T, scratch, kind="ln", xin_r=xin_r, pbank=6)

    wqc = chunks(WQKV, DC)
    h0c = chunks(H0T, DC)
    evac_val = {0: None, 4: None}
    last_pe_v = None
    for which, dest in ((0, QT), (1, KT)):
        for oc in range(DC):
            bank = 0 if oc % 2 == 0 else 4
            ps = pb.PS(bank, 4)
            mm = None
            for k in range(DC):
                lhsT = wqc[:, k, which * 1024 + oc * 128: which * 1024 + (oc + 1) * 128]
                for n in range(4):
                    wl = []
                    if k == 0 and n == 0:
                        wl = [wq1, h0t_done,
                              ("ACT", evac_val[bank]) if evac_val[bank] else None]
                    mm = pb.t("matmul", ps[:, n * 512:(n + 1) * 512], lhsT,
                              h0c[:, k, n * 512:(n + 1) * 512],
                              start=(k == 0), stop=(k == DC - 1), w=wl,
                              inc="PE" if (k == DC - 1 and n == 3) else None)
            evac_val[bank] = pb.s("copy", out=chunks(dest, DC)[:, oc], in_=ps,
                                  w=[("PE", mm)], inc="ACT")
    last_qk_evac = max(evac_val[0], evac_val[4])
    vv_evac = {0: None, 2: None}
    for tc in range(TC):
        bank = 0 if tc % 2 == 0 else 2
        ps = pb.PS(bank, 2)
        mm = None
        for k in range(DC):
            lhsT = h0c[:, k, tc * 128:(tc + 1) * 128]
            for n in range(2):
                wl = []
                if k == 0 and n == 0:
                    if tc < 2:
                        wl = [("ACT", last_qk_evac), ("DVE", const_v)]
                    else:
                        wl = [("ACT", vv_evac[bank])]
                mm = pb.t("matmul", ps[:, n * 512:(n + 1) * 512], lhsT,
                          wqc[:, k, 2048 + n * 512:2048 + (n + 1) * 512],
                          start=(k == 0), stop=(k == DC - 1), w=wl,
                          inc="PE" if (k == DC - 1 and n == 1) else None)
        out_view = chunks(VEXT, TC)[:, tc].rearrange("p (h e) -> p h e", h=NH)[:, :, 0:64]
        vv_evac[bank] = pb.s("copy", out=out_view,
                             in_=ps.rearrange("p (h e) -> p h e", h=NH),
                             w=[("PE", mm)], inc="ACT")
    qkv_done_act = pb.cnt["ACT"]
    pe_after_qkv = pb.cnt["PE"]
    wo1 = pb.dma(chunks(WOUT, DC), io["w_out_a"].rearrange("(c p) f -> p c f", p=128), "LW",
                 cast=True, w=[("PE", pe_after_qkv)])

    ovl = {"LRW": LRW, "RHIB": RHIB, "RLOB": RLOB, "RBC": RBC, "OSCR": OSCR}
    o_free = emit_attention(pb, io, consts, cols, M4, NEGMH, QT, KT, VEXT, OT,
                            ATB, ovl, scale=0.125, causal=False,
                            entry_free=("ACT", qkv_done_act))

    xout_r = io["x_out"].rearrange("(tc p) d -> tc p d", p=128)
    ev_prev = {0: None, 2: None}
    xo_free = [None, None]
    otc = chunks(OT, DC)
    woc = chunks(WOUT, DC)
    for tc in range(TC):
        bank = 0 if tc % 2 == 0 else 2
        ps = pb.PS(bank, 2)
        mm = None
        for k in range(DC):
            for n in range(2):
                wl = []
                if k == 0 and n == 0:
                    wl = [o_free, wo1,
                          ("DVE", ev_prev[bank]) if ev_prev[bank] else None]
                mm = pb.t("matmul", ps[:, n * 512:(n + 1) * 512],
                          otc[:, k, tc * 128:(tc + 1) * 128], woc[:, k, n * 512:(n + 1) * 512],
                          start=(k == 0), stop=(k == DC - 1), w=wl,
                          inc="PE" if (k == DC - 1 and n == 1) else None)
        par = tc % 2
        xc = XC[par]
        ldw = [xo_free[par]] if xo_free[par] else [("ACT", qkv_done_act)]
        ldv = pb.dma(xc, xin_r[tc], "LD", w=ldw)
        addv = pb.v("tensor_tensor", out=xc, in0=xc, in1=ps, op=ALU.add,
                    w=[ldv, ("PE", mm)], inc="DVE")
        ev_prev[bank] = addv
        xo_free[par] = pb.dma(xout_r[tc], xc, "ST", w=[("DVE", addv)])
    pb.wait("sync", [("ST", pb.cnt["ST"])])



EPS_RMS = 1.1920929e-07


def emit_rep(pb, consts, row_io, width, dest, tmprow, pbank=6, w=(), prev=None):
    ld = pb.dma(tmprow[0:1, 0:width], row_io, "LD", w=list(w) + [prev])
    mm = None
    for n0 in range(0, width, 512):
        n1 = min(n0 + 512, width)
        wl = [ld, consts["_ld"], prev] if n0 == 0 else []
        mm = pb.t("matmul", pb.PS(pbank, 2)[:, n0:n1], consts["ones_row"][0:1, :],
                  tmprow[0:1, n0:n1], start=True, stop=True, w=wl,
                  inc="PE" if n1 == width else None)
    cp = pb.v("tensor_copy", out=dest, in_=pb.PS(pbank, 2)[:, 0:width],
              w=[("PE", mm)], inc="DVE")
    return ("DVE", cp)


def emit_up(pb, consts, w_hbm_T, in_ft, nin, nout, dst_ft, th0, tw, evac,
            wbufs_raw, wfree, extra_w=(), evac_w=None, entry=None):
    evs = []
    bank_free = {0: entry, 2: entry}
    wbufs = [b[:, 0:nin * 128].rearrange("p (c f) -> p c f", c=nin) for b in wbufs_raw]
    wr = w_hbm_T.rearrange("(c p) f -> p c f", p=128)
    for oc in range(nout):
        par = oc % 2
        wld = pb.dma(wbufs[par], wr[:, :, oc * 128:(oc + 1) * 128], "LW",
                     cast=True, w=[wfree[par]] + list(extra_w if oc < 2 else ()))
        wfree[par] = None
        bank = 0 if oc % 2 == 0 else 2
        ps = pb.PS(bank, 2)
        mm = None
        for k in range(nin):
            for n0 in range(0, tw, 512):
                n1 = min(n0 + 512, tw)
                wl = []
                if k == 0 and n0 == 0:
                    wl = [wld, bank_free[bank]]
                mm = pb.t("matmul", ps[:, n0:n1], wbufs[par][:, k, :],
                          in_ft[:, k, th0 + n0:th0 + n1],
                          start=(k == 0), stop=(k == nin - 1), w=wl,
                          inc="PE" if (k == nin - 1 and n1 == tw) else None)
        wfree[par] = ("PE", mm)
        dst = dst_ft[:, oc, 0:tw]
        ew = [("PE", mm)] + ([evac_w[oc]] if evac_w else [])
        if evac == "gelu":
            ev = pb.s("activation", out=dst, in_=ps[:, 0:tw], func=AF.Gelu, w=ew, inc="ACT")
            evs.append(("ACT", ev))
        elif evac == "silu":
            ev = pb.s("activation", out=dst, in_=ps[:, 0:tw], func=AF.Silu, w=ew, inc="ACT")
            evs.append(("ACT", ev))
        elif evac == "copy":
            ev = pb.s("copy", out=dst, in_=ps[:, 0:tw], w=ew, inc="ACT")
            evs.append(("ACT", ev))
        elif evac == "mult_into":
            ev = pb.v("tensor_tensor", out=dst, in0=ps[:, 0:tw], in1=dst,
                      op=ALU.mult, w=ew, inc="DVE")
            evs.append(("DVE", ev))
        bank_free[bank] = evs[-1]
    return evs


def emit_down(pb, consts, w_hbm_T, hsrc_ft, nhc, e_idx, COEF, XINT, th0, tw,
              wd_buf, wd_w=(), src_w=None, entry=None):
    wld = pb.dma(wd_buf, w_hbm_T.rearrange("(c p) f -> p c f", p=128)[:, :, 256:1024],
                 "LW", cast=True, w=wd_w)
    bank_free = {4: entry, 6: entry}
    last = None
    for tci in range(tw // 128):
        tc = th0 // 128 + tci
        bank = 4 if tci % 2 == 0 else 6
        ps = pb.PS(bank, 2)
        mm = None
        for hc in range(nhc):
            for (n0, n1) in ((0, 512), (512, 768)):
                wl = []
                if hc == 0 and n0 == 0:
                    wl = [wld, bank_free[bank]]
                    if src_w is not None:
                        wl.append(src_w)
                mm = pb.t("matmul", ps[:, n0:n1],
                          hsrc_ft[:, hc, tci * 128:(tci + 1) * 128],
                          wd_buf[:, hc, n0:n1],
                          start=(hc == 0), stop=(hc == nhc - 1), w=wl,
                          inc="PE" if (hc == nhc - 1 and n0 == 512) else None)
        coef = COEF[:, tc * 4 + e_idx:tc * 4 + e_idx + 1]
        xslice = chunks(XINT, TC)[:, tc][:, 256:1024]
        ev = pb.v("scalar_tensor_tensor", out=xslice, in0=ps[:, 0:768],
                  scalar=coef, in1=xslice, op0=ALU.mult, op1=ALU.add,
                  w=[("PE", mm)], inc="DVE")
        bank_free[bank] = ("DVE", ev)
        last = ("DVE", ev)
    return last


def emit_phase_b(pb, io, consts, al):
    cols = al.cols(["SQC", "SMC", "NEGMN", "EX2", "M2", "VARC", "SDC", "RSTD",
                    "MHS", "RMSV", "RMSD", "RMSR", "NEGL", "SUME", "RS",
                    "M1C", "M2C", "DEN", "RDEN", "W1C", "W2C"])
    NEGMH = al.r("NEGMH", 4)[:, 0:1]
    M4 = al.r("M4", 64)
    SQ = al.r("SQ", D)
    XC = [al.r("XC0", D), al.r("XC1", D)]
    HB = [al.r("HB0", 512, BF16), al.r("HB1", 512, BF16)]
    N1REP = al.r("N1REP", D)
    N2REP = al.r("N2REP", D)
    QNREP = al.r("QNREP", 64)
    KNREP = al.r("KNREP", 64)
    GATEB = al.r("GATEB", 16, BF16)
    COEF = al.r("COEF", 64)
    PRB = al.r("PRB", 4)
    PRN = al.r("PRN", 4)
    MK1 = al.r("MK1", 4)
    MK2 = al.r("MK2", 4)
    PRM = al.r("PRM", 4)
    T2S = al.r("T2S", 4)
    QSC = al.r("QSC", 16)
    QSD = al.r("QSD", 16)
    QRS = al.r("QRS", 16)
    SCR2 = al.r("SCR2", 4096)
    FT1 = al.r("FT1", 8192, BF16)
    QK = al.r("QK", 16384)
    VX = al.r("VX", 8320)
    POOL = al.map["VX"][0] + 8320
    pool_words = ARENA_WORDS - POOL
    assert pool_words >= 8300, pool_words
    s2 = al.map["SCR2"][0]

    QNT = pb.R(al.map["QK"][0], 8192, BF16)
    KNT = pb.R(al.map["QK"][0] + 8192, 8192, BF16)
    XINT = pb.R(al.map["QK"][0], 16384)
    VEXT = pb.R(al.map["VX"][0], 8320, BF16)
    XFT = pb.R(al.map["VX"][0], 8192, BF16)
    XIC = [pb.R(s2, D), pb.R(s2 + 1024, D)]
    QRAW = XIC
    QTOK = [pb.R(s2 + 2048, 512, BF16), pb.R(s2 + 2560, 512, BF16)]
    ROWSC = pb.R(s2 + 3072, D)
    WS1 = [pb.R(s2, 1024, BF16), pb.R(s2 + 1024, 1024, BF16)]
    WS2 = [pb.R(s2 + 2048, 1024, BF16), pb.R(s2 + 3072, 1024, BF16)]
    WB2 = [pb.R(POOL, 4096, BF16), pb.R(POOL + 4096, 4096, BF16)]
    ATB = [pb.R(POOL, 1024, BF16), pb.R(POOL + 1024, 1024, BF16)]
    LRW = pb.R(POOL + 2048, 2048)
    RHIB = pb.R(POOL + 4096, 1024, BF16)
    RLOB = pb.R(POOL + 5120, 1024, BF16)
    OSCR = pb.R(POOL + 2048, 1024, BF16)
    RBC = pb.R(POOL + 3072, 2048)
    WOB = pb.R(POOL, 3072, BF16)
    WD = pb.R(POOL, 6144, BF16)
    D1BQ = pb.R(POOL + 6144, 2048, BF16)
    assert POOL + 6144 + 2048 <= ARENA_WORDS

    xin_r = io["x_in"].rearrange("(tc p) d -> tc p d", p=128)
    xout_r = io["x_out"].rearrange("(tc p) d -> tc p d", p=128)
    xint_r = io["xint_scr"].rearrange("(tc p) d -> tc p d", p=128)

    r1 = emit_rep(pb, consts, io["n1_w"], D, N1REP, ROWSC, pbank=4)
    r2 = emit_rep(pb, consts, io["n2_w"], D, N2REP, ROWSC, pbank=4, prev=r1)
    r3 = emit_rep(pb, consts, io["qn_w"], HD, QNREP, ROWSC, pbank=4, prev=r2)
    rep_done = emit_rep(pb, consts, io["kn_w"], HD, KNREP, ROWSC, pbank=4, prev=r3)
    gld = pb.dma(GATEB.rearrange("p (c f) -> p c f", c=DC),
                 io["gate_b"].rearrange("(c p) f -> p c f", p=128), "LW", cast=True)
    const_v = pb.v("memset",
                   VEXT.rearrange("p (tc h e) -> p tc h e", tc=TC, h=NH)[:, :, :, 64:65],
                   1.0, inc="DVE")
    wq_ld = pb.dma(chunks(WB2[0], DC), io["wq_b"].rearrange("(c p) f -> p c f", p=128), "LW", cast=True)
    wk_ld = pb.dma(chunks(WB2[1], DC), io["wk_b"].rearrange("(c p) f -> p c f", p=128), "LW", cast=True)

    evac_prev = [0, 0]
    xc_free = [None, None]
    xic_free = [None, None]
    hb_free = [0, 0]
    for tc in range(TC):
        par = tc % 2
        xc = XC[par]
        ldv = pb.dma(xc, xin_r[tc], "LD", w=[xc_free[par]])
        pb.v("scalar_tensor_tensor", out=SQ, in0=xc, scalar=1.0, in1=xc,
             op0=ALU.mult, op1=ALU.mult, accum_out=cols["SQC"], w=[ldv])
        pb.v("reduce_sum", out=cols["SMC"], in_=xc, axis=AX.X)
        pb.v("tensor_scalar_mul", out=cols["NEGMN"], in0=cols["SMC"], scalar1=-1.0 / D)
        pb.v("tensor_scalar_mul", out=cols["EX2"], in0=cols["SQC"], scalar1=1.0 / D)
        pb.v("tensor_tensor", out=cols["M2"], in0=cols["NEGMN"], in1=cols["NEGMN"], op=ALU.mult)
        pb.v("tensor_tensor", out=cols["VARC"], in0=cols["EX2"], in1=cols["M2"], op=ALU.subtract)
        vv = pb.v("tensor_scalar_add", out=cols["VARC"], in0=cols["VARC"], scalar1=1e-5, inc="DVE")
        av = pb.s("activation", out=cols["SDC"], in_=cols["VARC"], func=AF.Sqrt,
                  w=[("DVE", vv)], inc="ACT")
        pb.v("reciprocal", out=cols["RSTD"], in_=cols["SDC"], w=[("ACT", av)])
        xic = XIC[par]
        xv = pb.v("tensor_scalar", out=xic, in0=xc, scalar1=cols["NEGMN"],
                  scalar2=cols["RSTD"], op0=ALU.add, op1=ALU.mult,
                  w=[xic_free[par]], inc="DVE")
        xc_free[par] = ("DVE", xv)
        sp = pb.dma(xint_r[tc], xic, "SP", w=[("DVE", xv)])
        pb.v("scalar_tensor_tensor", out=SQ, in0=xic, scalar=1.0, in1=xic,
             op0=ALU.mult, op1=ALU.mult, accum_out=cols["RMSV"])
        rv = pb.v("tensor_scalar", out=cols["RMSV"], in0=cols["RMSV"],
                  scalar1=1.0 / D, scalar2=EPS_RMS, op0=ALU.mult, op1=ALU.add, inc="DVE")
        a2 = pb.s("activation", out=cols["RMSD"], in_=cols["RMSV"], func=AF.Sqrt,
                  w=[("DVE", rv)], inc="ACT")
        pb.v("reciprocal", out=cols["RMSR"], in_=cols["RMSD"], w=[("ACT", a2)])
        hb = HB[par]
        nv = pb.v("scalar_tensor_tensor", out=hb, in0=xic, scalar=cols["RMSR"],
                  in1=N1REP, op0=ALU.mult, op1=ALU.mult,
                  w=[("PE", hb_free[par])] + ([rep_done] if tc == 0 else []), inc="DVE")
        xic_free[par] = (("DVE", nv) if True else None)
        xic_free[par] = sp
        pe_last, ev = emit_tok_to_ft(pb, consts, hb, FT1, tc, 6 + par,
                                     first_wait=[("DVE", nv), ("ACT", evac_prev[par])],
                                     evac_prev=0)
        hb_free[par] = pe_last
        evac_prev[par] = ev
    hrt_done = ("ACT", pb.cnt["ACT"])
    b1_sp = ("SP", pb.cnt["SP"])

    ft1c = chunks(FT1, DC)
    evac_bank = {0: None, 2: None}
    wlds = [wq_ld, wk_ld, None]
    for which in range(3):
        if which == 2:
            wlds[2] = pb.dma(chunks(WB2[0], DC),
                             io["wv_b"].rearrange("(c p) f -> p c f", p=128),
                             "LW", cast=True, w=[("PE", pe_after_q)])
        wb, wld = WB2[which % 2], wlds[which]
        wbc = chunks(wb, DC)
        for tc in range(TC):
            par = tc % 2
            bank = 0 if par == 0 else 2
            ps = pb.PS(bank, 2)
            mm = None
            for k in range(DC):
                for n in range(2):
                    wl = []
                    if k == 0 and n == 0:
                        wl = [wld, hrt_done, evac_bank[bank]]
                    mm = pb.t("matmul", ps[:, n * 512:(n + 1) * 512],
                              ft1c[:, k, tc * 128:(tc + 1) * 128],
                              wbc[:, k, n * 512:(n + 1) * 512],
                              start=(k == 0), stop=(k == DC - 1), w=wl,
                              inc="PE" if (k == DC - 1 and n == 1) else None)
            if which < 2:
                qraw = QRAW[par]
                qv = pb.v("tensor_copy", out=qraw, in_=ps,
                          w=[("PE", mm)] + ([b1_sp] if tc < 2 else []), inc="DVE")
                evac_bank[bank] = ("DVE", qv)
                pb.v("scalar_tensor_tensor", out=SQ, in0=qraw, scalar=1.0, in1=qraw,
                     op0=ALU.mult, op1=ALU.mult)
                pb.v("tensor_reduce", out=QSC, in_=chunks(SQ, NH), axis=AX.X, op=ALU.add)
                qs = pb.v("tensor_scalar", out=QSC, in0=QSC, scalar1=1.0 / HD,
                          scalar2=EPS_RMS, op0=ALU.mult, op1=ALU.add, inc="DVE")
                qa = pb.s("activation", out=QSD, in_=QSC, func=AF.Sqrt,
                          w=[("DVE", qs)], inc="ACT")
                pb.v("reciprocal", out=QRS, in_=QSD, w=[("ACT", qa)])
                qtok = QTOK[par]
                rep = QNREP if which == 0 else KNREP
                nv = None
                for hh in range(NH):
                    nv = pb.v("scalar_tensor_tensor",
                              out=qtok[:, hh * 64:(hh + 1) * 64],
                              in0=qraw[:, hh * 64:(hh + 1) * 64],
                              scalar=QRS[:, hh:hh + 1], in1=rep,
                              op0=ALU.mult, op1=ALU.mult,
                              w=[("PE", hb_free[par])] if hh == 0 else [], inc="DVE")
                dest = QNT if which == 0 else KNT
                pe_last, ev = emit_tok_to_ft(pb, consts, qtok, dest, tc, 6 + par,
                                             first_wait=[("DVE", nv)],
                                             evac_prev=evac_prev[par])
                hb_free[par] = pe_last
                evac_prev[par] = ev
            else:
                out_view = chunks(VEXT, TC)[:, tc].rearrange("p (h e) -> p h e", h=NH)[:, :, 0:64]
                ev = pb.s("copy", out=out_view, in_=ps.rearrange("p (h e) -> p h e", h=NH),
                          w=[("PE", mm), ("DVE", const_v) if tc == 0 else None], inc="ACT")
                evac_bank[bank] = ("ACT", ev)
        if which == 0:
            pe_after_q = pb.cnt["PE"]
    qkv_done = ("ACT", pb.cnt["ACT"])

    ovl = {"LRW": LRW, "RHIB": RHIB, "RLOB": RLOB, "RBC": RBC, "OSCR": OSCR}
    OT = FT1
    o_free = emit_attention(pb, io, consts, cols, M4, NEGMH, QNT, KNT, VEXT, OT,
                            ATB, ovl, scale=0.125, causal=True, entry_free=qkv_done)
    attn_pe = ("PE", pb.cnt["PE"])
    attn_dve = ("DVE", pb.cnt["DVE"])
    attn_act = ("ACT", pb.cnt["ACT"])

    xl = pb.dma(chunks(XINT, TC),
                io["xint_scr"].rearrange("(tc p) d -> p tc d", p=128),
                "LD", w=[attn_pe, attn_dve])
    wo_ld = pb.dma(chunks(WOB, DC),
                   io["wo_b"].rearrange("(c p) f -> p c f", p=128)[:, :, 256:1024],
                   "LW", cast=True, w=[attn_pe, attn_act, o_free])
    otc = chunks(OT, DC)
    wobc = chunks(WOB, DC)
    bank_free = {0: None, 2: None}
    for tc in range(TC):
        bank = 0 if tc % 2 == 0 else 2
        ps = pb.PS(bank, 2)
        mm = None
        for k in range(DC):
            for (n0, n1) in ((0, 512), (512, 768)):
                wl = []
                if k == 0 and n0 == 0:
                    wl = [wo_ld, o_free, bank_free[bank]]
                mm = pb.t("matmul", ps[:, n0:n1], otc[:, k, tc * 128:(tc + 1) * 128],
                          wobc[:, k, n0:n1], start=(k == 0), stop=(k == DC - 1),
                          w=wl, inc="PE" if (k == DC - 1 and n0 == 512) else None)
        xslice = chunks(XINT, TC)[:, tc][:, 256:1024]
        ev = pb.v("tensor_tensor", out=xslice, in0=xslice, in1=ps[:, 0:768],
                  op=ALU.add, w=[("PE", mm), xl], inc="DVE")
        bank_free[bank] = ("DVE", ev)

    scratch = {"XC": XC, "SQ": SQ, "HB": HB}
    xft_done = emit_ln_to_ft(pb, cols, consts, XFT, scratch, kind="rms",
                             src_sbuf=XINT, gamma_rep=N2REP, eps=EPS_RMS,
                             pbank=6, extra_first_wait=attn_pe)
    xftc = chunks(XFT, DC)
    gate_free = None
    for tc in range(TC):
        ps = pb.PS(4, 1)
        mm = None
        for k in range(DC):
            wl = []
            if k == 0:
                wl = [gld, xft_done, gate_free if gate_free else attn_dve]
            mm = pb.t("matmul", ps[:, 0:4], xftc[:, k, tc * 128:(tc + 1) * 128],
                      GATEB.rearrange("p (c f) -> p c f", c=DC)[:, k, :],
                      start=(k == 0), stop=(k == DC - 1), w=wl,
                      inc="PE" if k == DC - 1 else None)
        nm = pb.v("tensor_reduce", out=cols["NEGL"], in_=ps[:, 0:4], axis=AX.X,
                  op=ALU.max, negate=True, w=[("PE", mm)], inc="DVE")
        ea = pb.s("activation", out=PRB, in_=ps[:, 0:4], func=AF.Exp,
                  bias=cols["NEGL"], accum_out=cols["SUME"],
                  w=[("DVE", nm)], inc="ACT")
        pb.v("reciprocal", out=cols["RS"], in_=cols["SUME"], w=[("ACT", ea)])
        pb.v("tensor_single_scalar", out=PRN, in_=PRB, scalar=cols["RS"], op=ALU.mult)
        pb.v("tensor_reduce", out=cols["M1C"], in_=PRN, axis=AX.X, op=ALU.max)
        pb.v("tensor_single_scalar", out=MK1, in_=PRN, scalar=cols["M1C"], op=ALU.is_equal)
        pb.v("scalar_tensor_tensor", out=PRM, in0=MK1, scalar=-1e30, in1=PRN,
             op0=ALU.mult, op1=ALU.add)
        pb.v("tensor_reduce", out=cols["M2C"], in_=PRM, axis=AX.X, op=ALU.max)
        pb.v("tensor_single_scalar", out=MK2, in_=PRM, scalar=cols["M2C"], op=ALU.is_equal)
        pb.v("tensor_tensor", out=cols["DEN"], in0=cols["M1C"], in1=cols["M2C"], op=ALU.add)
        pb.v("tensor_scalar_add", out=cols["DEN"], in0=cols["DEN"], scalar1=1e-8)
        pb.v("reciprocal", out=cols["RDEN"], in_=cols["DEN"])
        pb.v("tensor_tensor", out=cols["W1C"], in0=cols["M1C"], in1=cols["RDEN"], op=ALU.mult)
        pb.v("tensor_tensor", out=cols["W2C"], in0=cols["M2C"], in1=cols["RDEN"], op=ALU.mult)
        pb.v("tensor_single_scalar", out=T2S, in_=MK2, scalar=cols["W2C"], op=ALU.mult)
        gf = pb.v("scalar_tensor_tensor", out=COEF[:, tc * 4:(tc + 1) * 4],
                  in0=MK1, scalar=cols["W1C"], in1=T2S,
                  op0=ALU.mult, op1=ALU.add, inc="DVE")
        gate_free = ("DVE", gf)

    h1t_half = FT1.rearrange("p (c t) -> p c t", c=NH)
    h1t_q = pb.R(al.map["FT1"][0], 4096, BF16).rearrange("p (c t) -> p c t", c=NH)
    h2t_q = pb.R(al.map["FT1"][0] + 4096, 4096, BF16).rearrange("p (c t) -> p c t", c=NH)
    wdv = WD.rearrange("p (c f) -> p c f", c=NH)
    wfree = [attn_pe, attn_pe]
    tail = gate_free
    wd_w = [gate_free, attn_dve, attn_act]
    for half in range(2):
        th0 = half * 1024
        ups = emit_up(pb, consts, io["e0_up_t"], xftc, DC, NH, h1t_half, th0, 1024,
                      "gelu", WS1, wfree, extra_w=[tail, xft_done], entry=tail)
        tail = emit_down(pb, consts, io["e0_down_t"], h1t_half, NH, 0, COEF, XINT,
                         th0, 1024, wdv, wd_w=wd_w, src_w=ups[-1], entry=tail)
        wd_w = [tail]
    for half in range(2):
        th0 = half * 1024
        u1 = emit_up(pb, consts, io["e1_w1_t"], xftc, DC, NH, h1t_half, th0, 1024,
                     "silu", WS1, wfree, extra_w=[tail], entry=tail)
        u2 = emit_up(pb, consts, io["e1_w2_t"], xftc, DC, NH, h1t_half, th0, 1024,
                     "mult_into", WS2, wfree, evac_w=u1, entry=u1[-1])
        tail = emit_down(pb, consts, io["e1_down_t"], h1t_half, NH, 1, COEF, XINT,
                         th0, 1024, wdv, wd_w=wd_w, src_w=u2[-1], entry=tail)
        wd_w = [tail]
    for quar in range(4):
        th0 = quar * 512
        a1 = emit_up(pb, consts, io["e2_l1_t"], xftc, DC, NH, h1t_q, th0, 512,
                     "gelu", WS1, wfree, extra_w=[tail], entry=tail)
        a2 = emit_up(pb, consts, io["e2_l2_t"], h1t_q, NH, NH, h2t_q, 0, 512,
                     "gelu", WS2, wfree, extra_w=[a1[-1]], entry=a1[-1])
        a3 = emit_up(pb, consts, io["e2_l3_t"], h2t_q, NH, NH, h1t_q, 0, 512,
                     "gelu", WS1, wfree, extra_w=[a2[-1]], entry=a2[-1])
        tail = emit_down(pb, consts, io["e2_l4_t"], h1t_q, NH, 2, COEF, XINT,
                         th0, 512, wdv, wd_w=wd_w, src_w=a3[-1], entry=tail)
        wd_w = [tail]
    d1bq = D1BQ.rearrange("p (c t) -> p c t", c=DC)
    for quar in range(4):
        th0 = quar * 512
        b1 = emit_up(pb, consts, io["e3_down_t"], xftc, DC, DC, d1bq, th0, 512,
                     "gelu", WS1, wfree, extra_w=[tail], entry=tail)
        b2 = emit_up(pb, consts, io["e3_up_t"], d1bq, DC, NH, h1t_q, 0, 512,
                     "gelu", WS2, wfree, extra_w=[b1[-1]], entry=b1[-1])
        tail = emit_down(pb, consts, io["e3_out_t"], h1t_q, NH, 3, COEF, XINT,
                         th0, 512, wdv, wd_w=wd_w, src_w=b2[-1], entry=tail)
        wd_w = [tail]

    xo_free = [None, None]
    for tc in range(TC):
        par = tc % 2
        xc = XC[par]
        ldv = pb.dma(xc, xin_r[tc], "LD", w=[xo_free[par] or wd_w[0]])
        addv = pb.v("tensor_tensor", out=xc, in0=xc, in1=chunks(XINT, TC)[:, tc],
                    op=ALU.add, w=[ldv] + ([wd_w[0]] if tc == 0 else []), inc="DVE")
        xo_free[par] = pb.dma(xout_r[tc], xc, "ST", w=[("DVE", addv)])
    pb.wait("sync", [("ST", pb.cnt["ST"])])


def prep_phase_b(inputs):
    tr = lambda k: np.ascontiguousarray(np.asarray(inputs[k]).T)
    d = {
        "n1_w": np.asarray(inputs["L0_ffn_n1_w"]).reshape(1, D),
        "n2_w": np.asarray(inputs["L0_ffn_n2_w"]).reshape(1, D),
        "qn_w": np.asarray(inputs["L0_ffn_qn_w"]).reshape(1, HD),
        "kn_w": np.asarray(inputs["L0_ffn_kn_w"]).reshape(1, HD),
        "wq_b": tr("L0_ffn_wq"), "wk_b": tr("L0_ffn_wk"),
        "wv_b": tr("L0_ffn_wv"), "wo_b": tr("L0_ffn_wo"),
        "gate_b": tr("L0_ffn_gate_w"),
        "e0_up_t": tr("L0_ffn_e0_up"), "e0_down_t": tr("L0_ffn_e0_down"),
        "e1_w1_t": tr("L0_ffn_e1_w1"), "e1_w2_t": tr("L0_ffn_e1_w2"),
        "e1_down_t": tr("L0_ffn_e1_down"),
        "e2_l1_t": tr("L0_ffn_e2_l1"), "e2_l2_t": tr("L0_ffn_e2_l2"),
        "e2_l3_t": tr("L0_ffn_e2_l3"), "e2_l4_t": tr("L0_ffn_e2_l4"),
        "e3_down_t": tr("L0_ffn_e3_down"), "e3_up_t": tr("L0_ffn_e3_up"),
        "e3_out_t": tr("L0_ffn_e3_out"),
    }
    d.update(make_consts("B"))
    return d



def _c_finish(pb, XC, xin_r, xout_r, w=()):
    xo_free = [None, None]
    for tc in range(TC):
        par = tc % 2
        ldv = pb.dma(XC[par], xin_r[tc], "LD", w=[xo_free[par]] + list(w))
        xo_free[par] = pb.dma(xout_r[tc], XC[par], "ST", w=[ldv])
    pb.wait("sync", [("ST", pb.cnt["ST"])])


C_UPTO = 9
C3SUB = 3

def emit_phase_c(pb, io, consts, al):
    cols = al.cols(["SQC", "SMC", "NEGMN", "EX2", "M2", "VARC", "SDC", "RSTD",
                    "DCOL", "RCC"])
    XC = [al.r("XC0", D), al.r("XC1", D)]
    HB = [al.r("HB0", 512, BF16), al.r("HB1", 512, BF16)]
    SQ = al.r("SQ", D)
    TMP1 = SQ
    KVH = al.r("KVH", 512, BF16)
    KVL = al.r("KVL", 512, BF16)
    H2T = al.r("H2T", 8192, BF16)
    WQKVC = al.r("WQKVC", 12288, BF16)
    QFT = al.r("QFT", 8192, BF16)
    KF = al.r("KF", 8192, BF16)
    VEXT1 = al.r("VEXT1", 8320, BF16)
    assert al.off <= ARENA_WORDS, al.off
    OT1 = pb.R(al.map["H2T"][0], 8192, BF16)
    WOUTC = pb.R(al.map["WQKVC"][0], 4096, BF16)
    OTOK = pb.R(al.map["WQKVC"][0] + 4096, 8192, BF16)

    xin_r = io["x_in"].rearrange("(tc p) d -> tc p d", p=128)
    xout_r = io["x_out"].rearrange("(tc p) d -> tc p d", p=128)

    wq_ld = pb.dma(chunks(WQKVC, DC), io["w_qkv_c"].rearrange("(c p) f -> p c f", p=128),
                   "LW", cast=True)
    const_v = pb.v("memset",
                   VEXT1.rearrange("p (tc h e) -> p tc h e", tc=TC, h=NH)[:, :, :, 64:65],
                   1.0, inc="DVE")

    scratch = {"XC": XC, "SQ": SQ, "HB": HB}
    h2t_done = emit_ln_to_ft(pb, cols, consts, H2T, scratch, kind="ln",
                             xin_r=xin_r, pbank=6)

    if C_UPTO < 2:
        _c_finish(pb, XC, xin_r, xout_r)
        return
    h2c = chunks(H2T, DC)
    wqc = chunks(WQKVC, DC)
    QR = XC[0]
    RT = XC[1]
    cosv = chunks(consts["cos"], TC)
    sinv = chunks(consts["sin"], TC)
    evac_bank = {0: ("DVE", 1), 2: ("DVE", 1)}
    tr_evac = [0, 0]
    hb_free2 = [0, 0]
    ae_prev = None
    for which in range(3):
        for tc in range(TC):
            par = tc % 2
            bank = 0 if par == 0 else 2
            ps = pb.PS(bank, 2)
            mm = None
            for k in range(DC):
                for n in range(2):
                    wl = []
                    if k == 0 and n == 0:
                        wl = [wq_ld, h2t_done, evac_bank[bank]]
                    mm = pb.t("matmul", ps[:, n * 512:(n + 1) * 512],
                              h2c[:, k, tc * 128:(tc + 1) * 128],
                              wqc[:, k, which * 1024 + n * 512:which * 1024 + (n + 1) * 512],
                              start=(k == 0), stop=(k == DC - 1), w=wl,
                              inc="PE" if (k == DC - 1 and n == 1) else None)
            if which < 2:
                qv = pb.v("tensor_copy", out=QR, in_=ps,
                          w=[("PE", mm), ae_prev], inc="DVE")
                evac_bank[bank] = ("DVE", qv)
                qrv = QR.rearrange("p (h two x) -> p h two x", h=NH, two=2)
                rtv = RT.rearrange("p (h two x) -> p h two x", h=NH, two=2)
                pb.v("tensor_copy", out=rtv[:, :, 1], in_=qrv[:, :, 0])
                pb.v("tensor_scalar_mul", out=rtv[:, :, 0], in0=qrv[:, :, 1], scalar1=-1.0)
                cosb = cosv[:, tc][:, None, :].broadcast_to([128, NH, HD])
                sinb = sinv[:, tc][:, None, :].broadcast_to([128, NH, HD])
                pb.v("tensor_tensor", out=TMP1.rearrange("p (h x) -> p h x", h=NH),
                     in0=QR.rearrange("p (h x) -> p h x", h=NH),
                     in1=cosb, op=ALU.mult)
                pb.v("tensor_tensor", out=RT.rearrange("p (h x) -> p h x", h=NH),
                     in0=RT.rearrange("p (h x) -> p h x", h=NH),
                     in1=sinb, op=ALU.mult)
                rope_done = pb.v("tensor_tensor", out=QR, in0=TMP1, in1=RT,
                                 op=ALU.add, inc="DVE")
                sc = 0.125 if which == 0 else 1.0
                ae = pb.s("activation", out=TMP1, in_=QR, func=AF.Exp, scale=sc,
                          w=[("DVE", rope_done)], inc="ACT")
                ae_prev = ("ACT", ae)
                pb.v("tensor_scalar_min", out=TMP1, in0=TMP1, scalar1=1.0, w=[("ACT", ae)])
                pb.v("tensor_scalar", out=RT, in0=QR, scalar1=sc, scalar2=1.0,
                     op0=ALU.mult, op1=ALU.add)
                if which == 0:
                    fv = pb.v("tensor_tensor", out=HB[par], in0=RT, in1=TMP1,
                              op=ALU.max, w=[("PE", hb_free2[par])], inc="DVE")
                    pe_last, ev = emit_tok_to_ft(pb, consts, HB[par], QFT, tc, 6 + par,
                                                 first_wait=[("DVE", fv)],
                                                 evac_prev=tr_evac[par])
                    hb_free2[par] = pe_last
                    tr_evac[par] = ev
                else:
                    fv = pb.v("tensor_tensor", out=chunks(KF, TC)[:, tc], in0=RT,
                              in1=TMP1, op=ALU.max, inc="DVE")
            else:
                out_view = chunks(VEXT1, TC)[:, tc].rearrange("p (h e) -> p h e", h=NH)[:, :, 0:64]
                ev = pb.s("copy", out=out_view, in_=ps.rearrange("p (h e) -> p h e", h=NH),
                          w=[("PE", mm), ("DVE", const_v) if tc == 0 else None], inc="ACT")
                evac_bank[bank] = ("ACT", ev)
    qkv_done = ("ACT", pb.cnt["ACT"])
    qkv_dve = ("DVE", pb.cnt["DVE"])
    pe_after_qkv = ("PE", pb.cnt["PE"])
    wo_ld = pb.dma(chunks(WOUTC, DC), io["w_out_c"].rearrange("(c p) f -> p c f", p=128),
                   "LW", cast=True, w=[pe_after_qkv])

    if C_UPTO < 3:
        _c_finish(pb, XC, xin_r, xout_r, w=[qkv_done, qkv_dve])
        return
    kfv = chunks(KF, TC)
    vev = VEXT1.rearrange("p (tc h e) -> p tc h e", tc=TC, h=NH)
    kv_ps = pb.PS(0, 4)
    kvpsoff = lambda h: (h % 4) * 512 + (h // 4) * 66
    kvoff = lambda h: ((h // 2) // 7) * 512 + ((h // 2) % 7) * 65
    mm = None
    for h in range(NH):
        base = 64 * (h % 2)
        out = kv_ps[base:base + 64, kvpsoff(h):kvpsoff(h) + 65]
        for kt in range(TC):
            wl = []
            if h == 0 and kt == 0:
                wl = [qkv_done, qkv_dve]
            mm = pb.t("matmul", out, kfv[:, kt, h * 64:(h + 1) * 64],
                      vev[:, kt, h, :], start=(kt == 0), stop=(kt == TC - 1),
                      w=wl, inc="PE" if (h == NH - 1 and kt == TC - 1) else None)
    kh1 = kl1 = None
    for h in range(NH):
        base = 64 * (h % 2)
        src_ = kv_ps[base:base + 64, kvpsoff(h):kvpsoff(h) + 65]
        dsth = KVH[base:base + 64, kvoff(h):kvoff(h) + 65]
        dstl = KVL[base:base + 64, kvoff(h):kvoff(h) + 65]
        kh1 = pb.s("copy", out=dsth, in_=src_,
                   w=[("PE", mm)] if h == 0 else [], inc="ACT")
        kl1 = pb.v("tensor_tensor", out=dstl, in0=src_, in1=dsth,
                   op=ALU.subtract, w=[("ACT", kh1)], inc="DVE")

    if C_UPTO < 4:
        _c_finish(pb, XC, xin_r, xout_r, w=[("DVE", kl1)])
        return
    zoff = lambda h: (h % 4) * 512 + (h // 4) * 66
    z_free = {0: None}
    tr_evac2 = [0, 0]
    hb_free3 = [0, 0]
    for tc in range(TC):
        par = tc % 2
        zb = 0
        zp = pb.PS(zb, 4)
        mm = {}
        for h in range(NH):
            base = 64 * (h % 2)
            qsl = chunks(QFT, DC)[base:base + 64, h // 2, tc * 128:(tc + 1) * 128]
            out = zp[:, zoff(h):zoff(h) + 65]
            wl = []
            if h == 0:
                wl = [("DVE", kl1), z_free[zb]]
            pb.t("matmul", out, qsl, KVH[base:base + 64, kvoff(h):kvoff(h) + 65],
                 start=True, stop=False, w=wl)
            mm[h] = pb.t("matmul", out, qsl,
                         KVL[base:base + 64, kvoff(h):kvoff(h) + 65],
                         start=False, stop=True, inc="PE" if h == NH - 1 else None)
        last = None
        if C3SUB >= 3:
            for h in range(NH):
                dn = pb.v("tensor_scalar_max", out=cols["DCOL"],
                          in0=zp[:, zoff(h) + 64:zoff(h) + 65], scalar1=1e-6,
                          w=([("PE", mm[NH - 1])] if h == 0 else []) +
                            ([("ACT", last)] if last else []), inc="DVE")
                rc = pb.v("reciprocal", out=cols["RCC"], in_=cols["DCOL"],
                          w=[("DVE", dn)], inc="DVE")
                sc_ap = cols["RCC"] if C_UPTO >= 5 else 1.0
                last = pb.s("activation", out=chunks(OTOK, TC)[:, tc][:, h * 64:(h + 1) * 64],
                            in_=zp[:, zoff(h):zoff(h) + 64], func=AF.Copy,
                            scale=sc_ap, w=[("DVE", rc)] + ([wo_ld] if tc == 0 and h == 0 else []),
                            inc="ACT")
        else:
            for h in range(NH):
                last = pb.s("copy", out=chunks(OTOK, TC)[:, tc][:, h * 64:(h + 1) * 64],
                            in_=zp[:, zoff(h):zoff(h) + 64],
                            w=[("PE", mm[NH - 1])] if h == 0 else [], inc="ACT")
        z_free[zb] = ("ACT", last)
        if C3SUB >= 2:
            fv = pb.v("tensor_copy", out=HB[par], in_=chunks(OTOK, TC)[:, tc],
                      w=[("ACT", last), ("PE", hb_free3[par])], inc="DVE")
            pe_last, ev = emit_tok_to_ft(pb, consts, HB[par], OT1, tc, 6 + par,
                                         first_wait=[("DVE", fv)], evac_prev=tr_evac2[par])
            hb_free3[par] = pe_last
            tr_evac2[par] = ev
    ot_done = ("ACT", pb.cnt["ACT"])

    if C_UPTO < 5:
        _c_finish(pb, XC, xin_r, xout_r, w=[ot_done])
        return
    ev_prev = {0: None, 2: None}
    xo_free = [None, None]
    otc = chunks(OT1, DC)
    woc = chunks(WOUTC, DC)
    for tc in range(TC):
        bank = 0 if tc % 2 == 0 else 2
        ps = pb.PS(bank, 2)
        mm = None
        for k in range(DC):
            for n in range(2):
                wl = []
                if k == 0 and n == 0:
                    wl = [ot_done, wo_ld,
                          ("DVE", ev_prev[bank]) if ev_prev[bank] else None]
                mm = pb.t("matmul", ps[:, n * 512:(n + 1) * 512],
                          otc[:, k, tc * 128:(tc + 1) * 128], woc[:, k, n * 512:(n + 1) * 512],
                          start=(k == 0), stop=(k == DC - 1), w=wl,
                          inc="PE" if (k == DC - 1 and n == 1) else None)
        par = tc % 2
        xc = XC[par]
        ldw = [xo_free[par] or qkv_dve, ot_done]
        ldv = pb.dma(xc, xin_r[tc], "LD", w=ldw)
        addv = pb.v("tensor_tensor", out=xc, in0=xc, in1=ps, op=ALU.add,
                    w=[ldv, ("PE", mm)], inc="DVE")
        ev_prev[bank] = addv
        xo_free[par] = pb.dma(xout_r[tc], xc, "ST", w=[("DVE", addv)])
    pb.wait("sync", [("ST", pb.cnt["ST"])])


def prep_phase_c(inputs):
    d = {"w_qkv_c": np.ascontiguousarray(np.asarray(inputs["L1_attn_W_qkv"])),
         "w_out_c": np.ascontiguousarray(np.asarray(inputs["L1_attn_W_out"]))}
    d.update(make_consts("C"))
    return d



def emit_phase_d(pb, io, consts, al):
    cols = al.cols(["SQC", "SMC", "NEGMN", "EX2", "M2", "VARC", "SDC", "RSTD",
                    "LMX", "NEG", "SUME", "RS"])
    SQ = al.r("SQ", D)
    HB = [al.r("HB0", 512, BF16), al.r("HB1", 512, BF16)]
    MK = al.r("MK", 8)
    COEFD = al.r("COEFD", 128)
    B1S = al.r("B1S", 128)
    B1C = al.r("B1C", 32)
    B2ROW = al.r("B2ROW", 1024)
    B2HI = al.r("B2HI", 512, BF16)
    B2LO = al.r("B2LO", 512, BF16)
    XD = al.r("XD", 16384)
    XFT = al.r("XFT", 8192, BF16)
    H1T = al.r("H1T", 16384, BF16)
    WS1 = [al.r("WS1a", 512, BF16), al.r("WS1b", 512, BF16)]
    WS2 = [al.r("WS2a", 512, BF16), al.r("WS2b", 512, BF16)]
    assert al.off <= ARENA_WORDS, al.off

    xin_r = io["x_in"].rearrange("(tc p) d -> tc p d", p=128)
    xout_r = io["x_out"].rearrange("(tc p) d -> tc p d", p=128)
    GB = al.r("GB", 32, BF16)
    g_ld = pb.dma(GB.rearrange("p (c f) -> p c f", c=DC),
                  io["gate_d"].rearrange("(c p) f -> p c f", p=128), "LW", cast=True)

    xd_ld = pb.dma(chunks(XD, TC), io["x_in"].rearrange("(tc p) d -> p tc d", p=128), "LD")
    scratch = {"XC": None, "SQ": SQ, "HB": HB}
    xft_done = emit_ln_to_ft(pb, cols, consts, XFT, scratch, kind="ln",
                             src_sbuf=XD, pbank=6, extra_first_wait=xd_ld)
    xftc = chunks(XFT, DC)

    gate_free = None
    for tc in range(TC):
        ps = pb.PS(4, 1)
        mm = None
        for k in range(DC):
            wl = []
            if k == 0:
                wl = [g_ld, xft_done, gate_free]
            mm = pb.t("matmul", ps[:, 0:8], xftc[:, k, tc * 128:(tc + 1) * 128],
                      GB.rearrange("p (c f) -> p c f", c=DC)[:, k, :],
                      start=(k == 0), stop=(k == DC - 1), w=wl,
                      inc="PE" if k == DC - 1 else None)
        lm = pb.v("tensor_reduce", out=cols["LMX"], in_=ps[:, 0:8], axis=AX.X,
                  op=ALU.max, w=[("PE", mm)], inc="DVE")
        ng = pb.v("tensor_scalar_mul", out=cols["NEG"], in0=cols["LMX"], scalar1=-1.0,
                  inc="DVE")
        ea = pb.s("activation", out=MK, in_=ps[:, 0:8], func=AF.Exp,
                  bias=cols["NEG"], accum_out=cols["SUME"], w=[("DVE", ng)], inc="ACT")
        pb.v("reciprocal", out=cols["RS"], in_=cols["SUME"], w=[("ACT", ea)])
        pb.v("tensor_single_scalar", out=MK, in_=ps[:, 0:8], scalar=cols["LMX"],
             op=ALU.is_equal, w=[("ACT", ea)])
        gf = pb.v("tensor_single_scalar", out=COEFD[:, tc * 8:(tc + 1) * 8],
                  in_=MK, scalar=cols["RS"], op=ALU.mult, inc="DVE")
        gate_free = ("DVE", gf)

    h1t = H1T.rearrange("p (c t) -> p c t", c=32)
    tail = gate_free
    wfree = [gate_free, gate_free]
    for e in range(8):
        b1l = pb.dma(B1S.rearrange("p f -> p f")[0:32, :], io["b1_d"][e].rearrange("(c f) -> c f", f=128), "LD", w=[tail])
        bt = pb.t("transpose", pb.PS(6, 1)[0:128, 0:32], B1S[0:32, 0:128],
                  consts["ident_f"][0:32, 0:32], w=[b1l, tail], inc="PE")
        bc = pb.v("tensor_copy", out=B1C, in_=pb.PS(6, 1)[:, 0:32], w=[("PE", bt)], inc="DVE")
        b2l = pb.dma(B2ROW[0:1, :], io["b2_d"][e][None, :], "LD", w=[tail])
        bh = pb.v("tensor_copy", out=B2HI[0:1, 0:1024], in_=B2ROW[0:1, :], w=[b2l], inc="DVE")
        bl = pb.v("tensor_tensor", out=B2LO[0:1, 0:1024], in0=B2ROW[0:1, :],
                  in1=B2HI[0:1, 0:1024], op=ALU.subtract, w=[("DVE", bh)], inc="DVE")
        w1r = io["w1_d"][e].rearrange("(c p) f -> p c f", p=128)
        w2r = io["w2_d"][e].rearrange("(c p) f -> p c f", p=128)
        for half in range(2):
            th0 = half * 1024
            bank_free = {0: tail, 2: tail}
            up_last = None
            for oc in range(32):
                par = oc % 2
                wld = pb.dma(WS1[par][:, 0:1024].rearrange("p (c f) -> p c f", c=DC),
                             w1r[:, :, oc * 128:(oc + 1) * 128], "LW",
                             cast=True, w=[wfree[par]] + ([("DVE", bl)] if oc < 2 else []))
                wfree[par] = None
                bank = 0 if par == 0 else 2
                ps = pb.PS(bank, 2)
                mm = None
                for k in range(DC):
                    for n in range(2):
                        wl = []
                        if k == 0 and n == 0:
                            wl = [wld, bank_free[bank], xft_done if oc < 2 else None]
                        mm = pb.t("matmul", ps[:, n * 512:(n + 1) * 512],
                                  WS1[par][:, 0:1024].rearrange("p (c f) -> p c f", c=DC)[:, k, :],
                                  xftc[:, k, th0 + n * 512:th0 + (n + 1) * 512],
                                  start=(k == 0), stop=(k == DC - 1), w=wl,
                                  inc="PE" if (k == DC - 1 and n == 1) else None)
                wfree[par] = ("PE", mm)
                ev = pb.s("activation", out=h1t[:, oc, :], in_=ps, func=AF.Gelu,
                          bias=B1C[:, oc:oc + 1], w=[("PE", mm), ("DVE", bc) if oc < 2 else None],
                          inc="ACT")
                up_last = ("ACT", ev)
                bank_free[bank] = up_last
            for q in range(2):
                qt0 = th0 + q * 512
                bank_of = lambda tci: 2 * tci
                mmq = {}
                for hc in range(32):
                    par = hc % 2
                    wld = pb.dma(WS2[par][:, 0:1024].rearrange("p (a) -> p a") if False else WS2[par][:, 0:1024],
                                 w2r[:, hc, :], "LW", cast=True,
                                 w=[wfree[par]] + ([up_last, tail] if hc < 2 else []))
                    wfree[par] = None
                    for tci in range(4):
                        tc = qt0 // 128 + tci
                        ps = pb.PS(bank_of(tci), 2)
                        mm = None
                        for n in range(2):
                            wl = []
                            if n == 0:
                                wl = [wld]
                                if hc == 0:
                                    wl.append(tail)
                            mm = pb.t("matmul", ps[:, n * 512:(n + 1) * 512],
                                      h1t[:, hc, (qt0 - th0) + tci * 128:(qt0 - th0) + (tci + 1) * 128],
                                      WS2[par][:, n * 512:(n + 1) * 512],
                                      start=(hc == 0), stop=False, w=wl,
                                      inc="PE" if n == 1 else None)
                        mmq[tci] = mm
                    wfree[par] = ("PE", mmq[3])
                last_ev = None
                for tci in range(4):
                    tc = qt0 // 128 + tci
                    ps = pb.PS(bank_of(tci), 2)
                    mm = None
                    for n in range(2):
                        pb.t("matmul", ps[:, n * 512:(n + 1) * 512],
                             consts["ones_bf"][0:1, 0:128],
                             B2HI[0:1, n * 512:(n + 1) * 512],
                             start=False, stop=False,
                             w=[("DVE", bl)] if (tci == 0 and n == 0) else [])
                        mm = pb.t("matmul", ps[:, n * 512:(n + 1) * 512],
                                  consts["ones_bf"][0:1, 0:128],
                                  B2LO[0:1, n * 512:(n + 1) * 512],
                                  start=False, stop=True,
                                  inc="PE" if n == 1 else None)
                    coef = COEFD[:, tc * 8 + e:tc * 8 + e + 1]
                    xslice = chunks(XD, TC)[:, tc]
                    ev = pb.v("scalar_tensor_tensor", out=xslice, in0=ps,
                              scalar=coef, in1=xslice, op0=ALU.mult, op1=ALU.add,
                              w=[("PE", mm)], inc="DVE")
                    last_ev = ("DVE", ev)
                tail = last_ev

    st = None
    for tc in range(TC):
        st = pb.dma(xout_r[tc], chunks(XD, TC)[:, tc], "ST", w=[tail])
    pb.wait("sync", [("ST", pb.cnt["ST"])])


def prep_phase_d(inputs):
    d = {"gate_d": np.ascontiguousarray(np.asarray(inputs["L1_ffn_gate_w"])),
         "w1_d": np.ascontiguousarray(np.asarray(inputs["L1_ffn_w1"])),
         "b1_d": np.ascontiguousarray(np.asarray(inputs["L1_ffn_b1"])),
         "w2_d": np.ascontiguousarray(np.asarray(inputs["L1_ffn_w2"])),
         "b2_d": np.ascontiguousarray(np.asarray(inputs["L1_ffn_b2"]))}
    d.update(make_consts("D"))
    return d



def make_consts(phase):
    c = {}
    c["ident_bf"] = np.eye(128).astype(ml_dtypes.bfloat16)
    c["ident_f"] = np.eye(128, dtype=np.float32)
    c["ones_row"] = np.ones((1, 128), np.float32)
    c["ones_bf"] = np.ones((128, 128), np.float32).astype(ml_dtypes.bfloat16)
    if phase == "B":
        i = np.arange(128)
        c["mask"] = np.where(i[:, None] >= i[None, :], 0.0, -1e30).astype(np.float32)
        c["maskT"] = np.where(i[None, :] >= i[:, None], 0.0, -1e30).astype(np.float32)
    if phase == "C":
        hd = HD
        inv_freq = 1.0 / (500000.0 ** (np.arange(0, hd, 2, dtype=np.float32) / hd))
        t = np.arange(T, dtype=np.float32)
        freqs = t[:, None] * inv_freq[None, :]
        emb = np.concatenate([freqs, freqs], axis=-1)
        c["cos"] = np.cos(emb).astype(np.float32)
        c["sin"] = np.sin(emb).astype(np.float32)
    return c


def prep_phase_a(inputs):
    W = np.asarray(inputs["L0_attn_W_qkv"])
    Wr = np.ascontiguousarray(W.reshape(1024, 16, 64, 3).transpose(0, 3, 1, 2).reshape(1024, 3072))
    d = {"w_qkv_a": Wr,
         "w_out_a": np.ascontiguousarray(np.asarray(inputs["L0_attn_W_out"]))}
    d.update(make_consts("A"))
    return d


_PROG_CACHE = {}
_ALLOC_MAP = {}


def build_phase(phase):
    if phase in _PROG_CACHE:
        return _PROG_CACHE[phase]
    pb = PB()
    for s in ("LD", "LW", "PE", "ACT", "DVE", "ST", "SP"):
        pb.sem(s)
    pb.arena_init()
    io = {}
    io["x_in"] = pb.dram("x_in", [T, D])
    io["x_out"] = pb.dram("x_out", [T, D], kind="ExternalOutput")
    io["ident_bf"] = pb.dram("ident_bf", [128, 128], BF16)
    io["ident_f"] = pb.dram("ident_f", [128, 128])
    io["ones_row"] = pb.dram("ones_row", [1, 128])
    io["ones_bf"] = pb.dram("ones_bf", [128, 128], BF16)
    al = Alloc(pb, 0)
    if phase == "A":
        io["w_qkv_a"] = pb.dram("w_qkv_a", [1024, 3072])
        io["w_out_a"] = pb.dram("w_out_a", [1024, 1024])
        consts = emit_consts(pb, io, al)
        emit_phase_a(pb, io, consts, al)
    elif phase == "B":
        io["mask"] = pb.dram("mask", [128, 128])
        io["maskT"] = pb.dram("maskT", [128, 128])
        io["n1_w"] = pb.dram("n1_w", [1, D])
        io["n2_w"] = pb.dram("n2_w", [1, D])
        io["qn_w"] = pb.dram("qn_w", [1, HD])
        io["kn_w"] = pb.dram("kn_w", [1, HD])
        for nm in ("wq_b", "wk_b", "wv_b", "wo_b", "e3_down_t"):
            io[nm] = pb.dram(nm, [1024, 1024])
        io["gate_b"] = pb.dram("gate_b", [1024, 4])
        for nm in ("e0_up_t", "e1_w1_t", "e1_w2_t", "e2_l1_t", "e3_up_t"):
            io[nm] = pb.dram(nm, [1024, 2048])
        for nm in ("e0_down_t", "e1_down_t", "e2_l4_t", "e3_out_t"):
            io[nm] = pb.dram(nm, [2048, 1024])
        for nm in ("e2_l2_t", "e2_l3_t"):
            io[nm] = pb.dram(nm, [2048, 2048])
        io["xint_scr"] = pb.dram("xint_scr", [T, D], kind="Internal")
        consts = emit_consts(pb, io, al, need=("mask",))
        emit_phase_b(pb, io, consts, al)
    elif phase == "C":
        io["w_qkv_c"] = pb.dram("w_qkv_c", [1024, 3072])
        io["w_out_c"] = pb.dram("w_out_c", [1024, 1024])
        io["cos"] = pb.dram("cos", [T, HD])
        io["sin"] = pb.dram("sin", [T, HD])
        consts = emit_consts(pb, io, al, need=("rope",))
        emit_phase_c(pb, io, consts, al)
    elif phase == "D":
        io["gate_d"] = pb.dram("gate_d", [1024, 8])
        io["w1_d"] = pb.dram("w1_d", [8, 1024, 4096])
        io["b1_d"] = pb.dram("b1_d", [8, 4096])
        io["w2_d"] = pb.dram("w2_d", [8, 4096, 1024])
        io["b2_d"] = pb.dram("b2_d", [8, 1024])
        consts = emit_consts(pb, io, al)
        emit_phase_d(pb, io, consts, al)
    else:
        raise NotImplementedError(phase)
    _ALLOC_MAP.clear()
    _ALLOC_MAP.update(al.map)
    nc = pb.finalize()
    _PROG_CACHE[phase] = nc
    return nc


def kernel(**inputs):
    X = np.asarray(inputs["L0_attn_X"])
    preps = {"A": prep_phase_a(inputs), "B": prep_phase_b(inputs),
             "C": prep_phase_c(inputs), "D": prep_phase_d(inputs)}
    cur = X.reshape(NCORES, SEQ_PER_CORE, T, D)
    for phase in "ABCD":
        nc = build_phase(phase)
        outs = np.empty_like(cur)
        for s in range(SEQ_PER_CORE):
            in_maps = [dict(preps[phase], x_in=np.ascontiguousarray(cur[c, s]))
                       for c in range(NCORES)]
            res = run_bass_kernel_spmd(nc, in_maps, core_ids=list(range(NCORES)))
            for c in range(NCORES):
                outs[c, s] = res.results[c]["x_out"]
        cur = outs
    return np.ascontiguousarray(cur.reshape(32, T, D))


# version 04
# speedup vs baseline: 1.1852x; 1.1852x over previous
import numpy as np
import ml_dtypes
from contextlib import ExitStack

import concourse.bass as bass
import concourse.mybir as mybir
from concourse.bass_utils import run_bass_kernel_spmd

F32 = mybir.dt.float32
BF16 = mybir.dt.bfloat16
AF = mybir.ActivationFunctionType
ALU = mybir.AluOpType
AX = mybir.AxisListType

T = 2048
TC = 16
D = 1024
DC = 8
NH = 16
HD = 64
NCORES = 8
SEQ_PER_CORE = 4

ARENA_WORDS = 52992



class PB:
    ENGINES = ("sync", "gpsimd", "tensor", "scalar", "vector")

    def __init__(self):
        self.nc = bass.Bass("TRN2", target_bir_lowering=False)
        self.ctx = ExitStack()
        self.q = {e: [] for e in self.ENGINES}
        self.sems = {}
        self.cnt = {}
        self._wdb = {}
        self._rdb = {}
        self._watermark = {}
        self._wm_all = {}

    def sem(self, name):
        self.sems[name] = self.ctx.enter_context(self.nc.semaphore(name))
        self.cnt[name] = 0

    def dram(self, name, shape, dt=F32, kind="ExternalInput"):
        return self.nc.dram_tensor(name, list(shape), dt, kind=kind).ap()

    def arena_init(self):
        self._arena = self.ctx.enter_context(
            self.nc.sbuf_tensor("arena", [128, ARENA_WORDS], F32))
        self._psum = self.ctx.enter_context(
            self.nc.psum_tensor("ps", [128, 4096], F32))

    def R(self, off, words, dt=F32):
        assert off >= 0 and off + words <= ARENA_WORDS, (off, words)
        ap = self._arena.ap()[:, off:off + words]
        if dt == BF16:
            ap = ap.bitcast(BF16)
        return ap

    def PS(self, bank, nbanks=1):
        return self._psum.ap()[:, bank * 512:(bank + nbanks) * 512]

    _WRITE_KEYS = ("out", "dest", "accum_out")
    _READ_KEYS = ("in_", "in0", "in1", "scalar", "scalar1", "scalar2", "bias",
                  "scale", "data0", "data1", "mask", "on_true", "on_false",
                  "data", "initial")
    _AUTO_ENGS = ("vector", "scalar")

    @staticmethod
    def _interval(ap):
        import concourse.mybir as _mb
        esz = _mb.dt.size(ap.dtype)
        pairs = list(ap.ap)
        pstride = pairs[0][0]
        pcount = pairs[0][1]
        off = ap.offset
        plo = off // pstride if pstride else 0
        off_in = off - plo * pstride
        span = 1
        lo = 0
        hi = 0
        for step, count in pairs[1:]:
            if step >= 0:
                hi += step * (count - 1)
            else:
                lo += step * (count - 1)
        return (ap.tensor.name, (off_in + lo) * esz, (off_in + hi + 1) * esz,
                plo, plo + pcount)

    def _deps(self, eng, args, kwargs):
        from concourse.ap import AP as _AP
        reads, writes = [], []
        for k, v in kwargs.items():
            if isinstance(v, _AP):
                (writes if k in self._WRITE_KEYS else reads).append(v)
        for v in args:
            if isinstance(v, _AP):
                writes.append(v)
        need = 0
        rr = [self._interval(a) for a in reads]
        ww = [self._interval(a) for a in writes]

        def overlap(a, b):
            return (a[0] == b[0] and a[1] < b[2] and b[1] < a[2]
                    and a[3] < b[4] and b[3] < a[4])

        wdb = self._wdb.setdefault(eng, {})
        rdb = self._rdb.setdefault(eng, {})
        for iv in rr:
            for rec in wdb.get(iv[0], ()):
                if overlap(iv, rec[0]):
                    need = max(need, rec[1])
        for iv in ww:
            for rec in wdb.get(iv[0], ()):
                if overlap(iv, rec[0]):
                    need = max(need, rec[1])
            for rec in rdb.get(iv[0], ()):
                if overlap(iv, rec[0]):
                    need = max(need, rec[1])
        ordinal = self.cnt[self._eng_sem[eng]] + 1
        for iv in ww:
            lst = wdb.setdefault(iv[0], [])
            lst[:] = [r for r in lst if not (r[0] == iv)]
            lst.append((iv, ordinal))
            if len(lst) > 256:
                del lst[0]
        for iv in rr:
            lst = rdb.setdefault(iv[0], [])
            lst[:] = [r for r in lst if not (r[0] == iv)]
            lst.append((iv, ordinal))
            if len(lst) > 256:
                del lst[0]
        return need

    _eng_sem = {"vector": "DVE", "scalar": "ACT"}

    def op(self, eng, method, *args, w=(), inc=None, **kwargs):
        auto = eng in self._AUTO_ENGS
        wm_all = self._wm_all.setdefault(eng, {})
        wl = []
        best = {}
        for item in w:
            if item is None:
                continue
            sname, val = item
            if val and val > 0 and val > best.get(sname, 0):
                best[sname] = val
        for sname, val in best.items():
            if val > wm_all.get(sname, 0):
                wl.append((sname, val))
                wm_all[sname] = val
        if auto:
            sem = self._eng_sem[eng]
            need = self._deps(eng, args, kwargs)
            wm = max(self._watermark.get(eng, 0), wm_all.get(sem, 0))
            if need > wm:
                wl.append((sem, need))
                wm = need
                wm_all[sem] = need
            self._watermark[eng] = wm
            assert inc is None or inc == sem or inc == (sem, 1), inc
            self.cnt[sem] += 1
            ret = self.cnt[sem]
            inc = (sem, 1)
        elif inc is not None:
            if isinstance(inc, str):
                inc = (inc, 1)
            self.cnt[inc[0]] += inc[1]
            ret = self.cnt[inc[0]]
        else:
            ret = None
        self.q[eng].append((method, args, kwargs, tuple(wl), inc))
        return ret

    def t(self, m, *a, **k):
        return self.op("tensor", m, *a, **k)

    def v(self, m, *a, **k):
        return self.op("vector", m, *a, **k)

    def s(self, m, *a, **k):
        return self.op("scalar", m, *a, **k)

    def dma(self, out, in_, incsem, w=(), cast=False, **kw):
        eng = "gpsimd" if cast else "sync"
        if incsem == "LW":
            self._lw_rr = (getattr(self, "_lw_rr", -1) + 1) % 4
            incsem = "LW" + "abcd"[self._lw_rr]
        w = list(w) + [(incsem, self.cnt[incsem])]
        v = self.op(eng, "dma_start", w=w, inc=(incsem, 16), out=out, in_=in_, **kw)
        return (incsem, v)

    def wait(self, eng, w):
        wm_all = self._wm_all.setdefault(eng, {})
        wl = []
        for item in w:
            if not item or not item[1]:
                continue
            sname, val = item
            if val > wm_all.get(sname, 0):
                wl.append((sname, val))
                wm_all[sname] = val
        if wl:
            self.q[eng].append(("__wait__", (), {}, tuple(wl), None))

    def section_barrier(self):
        cur = [(s, self.cnt[s]) for s in self.cnt]
        for eng in self.ENGINES:
            self.wait(eng, cur)
        self._wdb.clear()
        self._rdb.clear()
        for eng in self._AUTO_ENGS:
            self._watermark[eng] = self.cnt[self._eng_sem[eng]]

    def finalize(self):
        block = self.ctx.enter_context(self.nc.Block())
        sems = self.sems

        def mk(ops):
            def body(eng):
                for method, args, kwargs, waits, inc in ops:
                    for sname, val in waits:
                        eng.wait_ge(sems[sname], val)
                    if method == "__wait__":
                        continue
                    r = getattr(eng, method)(*args, **kwargs)
                    if inc is not None:
                        r.then_inc(sems[inc[0]], inc[1])
            return body

        for eng in self.ENGINES:
            if self.q[eng]:
                getattr(block, eng)(mk(self.q[eng]))
        self.ctx.close()
        return self.nc


def chunks(ap, n):
    return ap.rearrange("p (c t) -> p c t", c=n)



class Alloc:
    def __init__(self, pb, base=0):
        self.pb = pb
        self.off = base
        self.map = {}

    def r(self, name, words, dt=F32):
        ap = self.pb.R(self.off, words, dt)
        self.map[name] = (self.off, words)
        self.off += words
        return ap

    def cols(self, names):
        return {n: self.r("col_" + n, 4)[:, 0:1] for n in names}


def emit_consts(pb, io, al, need=()):
    c = {}
    c["scr"] = al.r("scr", 16)
    c["ident_bf"] = al.r("ident_bf", 64, BF16)
    c["ident_f"] = al.r("ident_f", 128)
    c["ones_row"] = al.r("ones_row", 128)
    c["ones_bf"] = al.r("ones_bf", 64, BF16)
    pb.dma(c["ident_bf"], io["ident_bf"], "LD")
    pb.dma(c["ident_f"], io["ident_f"], "LD")
    pb.dma(c["ones_row"][0:1, :], io["ones_row"], "LD")
    pb.dma(c["ones_bf"], io["ones_bf"], "LD")
    if "mask" in need:
        c["mask"] = al.r("mask", 128)
        c["maskT"] = al.r("maskT", 128)
        pb.dma(c["mask"], io["mask"], "LD")
        pb.dma(c["maskT"], io["maskT"], "LD")
    if "rope" in need:
        c["cos"] = al.r("cos", 1024)
        c["sin"] = al.r("sin", 1024)
        pb.dma(chunks(c["cos"], TC), io["cos"].rearrange("(tc p) d -> p tc d", p=128), "LD")
        pb.dma(chunks(c["sin"], TC), io["sin"].rearrange("(tc p) d -> p tc d", p=128), "LD")
    c["_ld"] = ("LD", pb.cnt["LD"])
    return c



def emit_ln_to_ft(pb, cols, consts, out_ft, scratch, kind="ln", xin_r=None,
                  src_sbuf=None, gamma_rep=None, gamma_wait=None, eps=1e-5,
                  pbank=6, extra_first_wait=None):
    evac_prev = [0, 0]
    xc_free = [None, None]
    hb_free = [0, 0]
    last_evac = 0
    for tc in range(TC):
        par = tc % 2
        if src_sbuf is not None:
            xc = chunks(src_sbuf, TC)[:, tc]
            v_w = [extra_first_wait] if (tc == 0 and extra_first_wait) else []
        else:
            xc = scratch["XC"][par]
            ldv = pb.dma(xc, xin_r[tc], "LD", w=[xc_free[par]] if xc_free[par] else [])
            v_w = [ldv]
            if tc == 0 and extra_first_wait:
                v_w.append(extra_first_wait)
        pb.v("scalar_tensor_tensor", out=scratch["SQ"], in0=xc, scalar=1.0,
             in1=xc, op0=ALU.mult, op1=ALU.mult, accum_out=cols["SQC"], w=v_w)
        if kind == "ln":
            pb.v("reduce_sum", out=cols["SMC"], in_=xc, axis=AX.X)
            pb.v("tensor_scalar_mul", out=cols["NEGMN"], in0=cols["SMC"], scalar1=-1.0 / D)
            pb.v("tensor_scalar_mul", out=cols["EX2"], in0=cols["SQC"], scalar1=1.0 / D)
            pb.v("tensor_tensor", out=cols["M2"], in0=cols["NEGMN"], in1=cols["NEGMN"], op=ALU.mult)
            vv = pb.v("tensor_tensor", out=cols["VARC"], in0=cols["EX2"], in1=cols["M2"],
                      op=ALU.subtract, inc="DVE")
        else:
            vv = pb.v("tensor_scalar_mul", out=cols["VARC"], in0=cols["SQC"],
                      scalar1=1.0 / D, inc="DVE")
        vv = pb.v("tensor_scalar_add", out=cols["VARC"], in0=cols["VARC"],
                  scalar1=float(eps), w=[("DVE", vv)], inc="DVE")
        av = pb.s("activation", out=cols["SDC"], in_=cols["VARC"], func=AF.Sqrt,
                  w=[("DVE", vv)], inc="ACT")
        rv = pb.v("reciprocal", out=cols["RSTD"], in_=cols["SDC"], w=[("ACT", av)], inc="DVE")
        hb = scratch["HB"][par]
        wl = [("DVE", rv), ("PE", hb_free[par])]
        if kind == "ln":
            nv = pb.v("tensor_scalar", out=hb, in0=xc, scalar1=cols["NEGMN"],
                      scalar2=cols["RSTD"], op0=ALU.add, op1=ALU.mult, w=wl, inc="DVE")
        else:
            gw = [gamma_wait] if (tc == 0 and gamma_wait) else []
            nv = pb.v("scalar_tensor_tensor", out=hb, in0=xc, scalar=cols["RSTD"],
                      in1=gamma_rep, op0=ALU.mult, op1=ALU.mult, w=wl + gw, inc="DVE")
        xc_free[par] = ("DVE", nv)
        pbk = pbank if par == 0 else pbank + 1
        ps_bf = pb.PS(pbk, 1).bitcast(BF16)
        pe_last = None
        for ci in range(DC):
            wlist = []
            if ci == 0:
                wlist = [("DVE", nv), ("ACT", evac_prev[par]), consts["_ld"] if tc == 0 else None]
            pe_last = pb.t("transpose", ps_bf[:, ci * 128:(ci + 1) * 128],
                           hb[:, ci * 128:(ci + 1) * 128], consts["ident_bf"],
                           w=wlist, inc="PE" if ci == DC - 1 else None)
        hb_free[par] = pe_last
        out_view = chunks(out_ft, DC)[:, :, tc * 128:(tc + 1) * 128]
        last_evac = pb.s("copy", out=out_view, in_=chunks(ps_bf, DC),
                         w=[("PE", pe_last)], inc="ACT")
        evac_prev[par] = last_evac
    return ("ACT", last_evac)



def emit_tok_to_ft(pb, consts, tok_bf, ft, tc, pbank, first_wait=(), evac_prev=0):
    ps_bf = pb.PS(pbank, 1).bitcast(BF16)
    pe_last = None
    for ci in range(DC):
        wl = list(first_wait) + [("ACT", evac_prev)] if ci == 0 else []
        pe_last = pb.t("transpose", ps_bf[:, ci * 128:(ci + 1) * 128],
                       tok_bf[:, ci * 128:(ci + 1) * 128], consts["ident_bf"],
                       w=wl, inc="PE" if ci == DC - 1 else None)
    ev = pb.s("copy", out=chunks(ft, DC)[:, :, tc * 128:(tc + 1) * 128],
              in_=chunks(ps_bf, DC), w=[("PE", pe_last)], inc="ACT")
    return pe_last, ev


def emit_attention(pb, io, consts, cols, M4, NEGMH, QT, KT, VEXT, OT, ATB, ovl,
                   scale, causal, entry_free):
    LRW, RHIB, RLOB, RBC, OSCR = (ovl[k] for k in ("LRW", "RHIB", "RLOB", "RBC", "OSCR"))

    def head_sl(ft, h):
        base = 64 * (h % 2)
        return chunks(ft, DC)[base:base + 64, h // 2]

    s_free = {0: entry_free, 4: entry_free}
    o_free = None
    atb_free = [0, 0]
    for h in range(NH):
        qh = head_sl(QT, h)
        kh = head_sl(KT, h)
        for qt in range(TC):
            bank = 0 if qt % 2 == 0 else 4
            ps = pb.PS(bank, 4)
            kmax = (qt + 1) * 128 if causal else T
            nblk = (kmax + 511) // 512
            mm = None
            for n in range(nblk):
                w0 = n * 512
                w1 = min(w0 + 512, kmax)
                wl = [s_free[bank]] if n == 0 else []
                mm = pb.t("matmul", ps[:, w0:w1],
                          qh[:, qt * 128:(qt + 1) * 128], kh[:, w0:w1],
                          start=True, stop=True, w=wl,
                          inc="PE" if n == nblk - 1 else None)
            if causal:
                madd = pb.v("tensor_tensor", out=ps[:, qt * 128:(qt + 1) * 128],
                            in0=ps[:, qt * 128:(qt + 1) * 128], in1=consts["mask"],
                            op=ALU.add, w=[("PE", mm)], inc="DVE")
            mx = pb.v("tensor_reduce", out=M4[:, 4 * qt:4 * qt + 1],
                      in_=ps[:, 0:kmax].rearrange("p (n s) -> p n s", s=4)[:, :, 0],
                      axis=AX.X, op=ALU.max, w=[("PE", mm)], inc="DVE")
            s_free[bank] = ("DVE", mx)
        vv = pb.v("tensor_reduce", out=cols["VARC"],
                  in_=M4.rearrange("p (n s) -> p n s", s=4)[:, :, 0],
                  axis=AX.X, op=ALU.max, inc="DVE")
        pt = pb.t("transpose", pb.PS(0, 1)[0:1, 0:128], cols["VARC"], consts["ident_f"],
                  w=[("DVE", vv), s_free[0]], inc="PE")
        mv = pb.v("tensor_reduce", out=cols["MHS"][0:1, :], in_=pb.PS(0, 1)[0:1, 0:128],
                  axis=AX.X, op=ALU.max, w=[("PE", pt)], inc="DVE")
        bc = pb.t("matmul", pb.PS(1, 1)[:, 0:1], consts["ones_row"][0:1, :],
                  cols["MHS"][0:1, 0:1], start=True, stop=True, w=[("DVE", mv)], inc="PE")
        ng = pb.v("tensor_scalar_mul", out=NEGMH, in0=pb.PS(1, 1)[:, 0:1],
                  scalar1=-scale, w=[("PE", bc)], inc="DVE")
        s_free[0] = ("DVE", ng)
        av_last = None
        g_free = {0: s_free[0], 1024: s_free[0]}
        atb_free2 = {}
        for kt in range(TC):
            par = kt % 2
            ps = pb.PS(0, 4)
            q0 = kt * 128 if causal else 0
            po = pb.PS(4, 4)
            vsl = chunks(VEXT, TC)[:, kt][:, h * 65:(h + 1) * 65]
            for g0 in (0, 1024):
                ga = max(q0, g0)
                gb = g0 + 1024
                if ga >= gb:
                    continue
                blks = []
                pos = ga
                while pos < gb:
                    w1 = min((pos // 512 + 1) * 512, gb)
                    blks.append((pos, w1))
                    pos = w1
                mm = None
                for i, (b0, b1) in enumerate(blks):
                    wl = [g_free[g0]] if i == 0 else []
                    mm = pb.t("matmul", ps[:, b0:b1],
                              kh[:, kt * 128:(kt + 1) * 128], qh[:, b0:b1],
                              start=True, stop=True, w=wl,
                              inc="PE" if i == len(blks) - 1 else None)
                exw = [("PE", mm), atb_free2.get((par, g0))]
                if causal and g0 <= kt * 128 < gb:
                    madd = pb.v("tensor_tensor", out=ps[:, kt * 128:(kt + 1) * 128],
                                in0=ps[:, kt * 128:(kt + 1) * 128], in1=consts["maskT"],
                                op=ALU.add, w=[("PE", mm)], inc="DVE")
                    exw.append(("DVE", madd))
                ex = pb.s("activation", out=ATB[par][:, ga:gb], in_=ps[:, ga:gb],
                          func=AF.Exp, bias=NEGMH, scale=scale, w=exw, inc="ACT")
                g_free[g0] = ("ACT", ex)
                for i, (b0, b1) in enumerate(blks):
                    wl = [("ACT", ex)] if i == 0 else []
                    if kt == 0 and i == 0 and g0 == 0 and o_free:
                        wl.append(o_free)
                    bank = b0 // 512
                    stop = (kt == 4 * bank + 3) if causal else (kt == TC - 1)
                    av_last = pb.t("matmul", po[0:65, b0:b1], vsl, ATB[par][:, b0:b1],
                                   start=(kt == 0), stop=stop, w=wl,
                                   inc="PE" if (kt == TC - 1 and i == len(blks) - 1) else None)
                    atb_free2[(par, g0)] = ("PE", av_last)
        s_free[0] = g_free[0]
        pb.wait("tensor", [g_free[1024]])
        po = pb.PS(4, 4)
        r1 = pb.v("reciprocal", out=LRW[64:65, 0:T], in_=po[64:65, :],
                  w=[("PE", av_last)], inc="DVE")
        r2 = pb.v("tensor_copy", out=RHIB[64:65, 0:T], in_=LRW[64:65, 0:T],
                  w=[("DVE", r1)], inc="DVE")
        r3 = pb.v("tensor_tensor", out=RLOB[64:65, 0:T], in0=LRW[64:65, 0:T],
                  in1=RHIB[64:65, 0:T], op=ALU.subtract, w=[("DVE", r2)], inc="DVE")
        bmm = None
        for n in range(4):
            wl = [("DVE", r3), s_free[0], consts["_ld"] if h == 0 else None] if n == 0 else []
            pb.t("matmul", pb.PS(0, 4)[0:64, n * 512:(n + 1) * 512],
                 consts["ones_bf"][64:65, 0:64], RHIB[64:65, n * 512:(n + 1) * 512],
                 start=True, stop=False, w=wl)
            bmm = pb.t("matmul", pb.PS(0, 4)[0:64, n * 512:(n + 1) * 512],
                       consts["ones_bf"][64:65, 0:64], RLOB[64:65, n * 512:(n + 1) * 512],
                       start=False, stop=True, inc="PE" if n == 3 else None)
        rc = pb.s("copy", out=RBC[0:64, :], in_=pb.PS(0, 4)[0:64, :],
                  w=[("PE", bmm)], inc="ACT")
        s_free[0] = ("ACT", rc)
        if h % 2 == 0:
            dest = chunks(OT, DC)[0:64, h // 2]
            ov = pb.v("tensor_tensor", out=dest, in0=po[0:64, :], in1=RBC[0:64, :],
                      op=ALU.mult, w=[("ACT", rc)], inc="DVE")
            o_free = ("DVE", ov)
        else:
            ov = pb.v("tensor_tensor", out=OSCR[0:64, 0:T], in0=po[0:64, :],
                      in1=RBC[0:64, :], op=ALU.mult, w=[("ACT", rc)], inc="DVE")
            dst = chunks(OT, DC)[64:128, h // 2]
            o_free = pb.dma(dst, OSCR[0:64, 0:T], "LD", w=[("DVE", ov)])
        s_free[4] = o_free
    return o_free



def emit_phase_a(pb, io, consts, al):
    cols = al.cols(["SQC", "SMC", "NEGMN", "EX2", "M2", "VARC", "SDC", "RSTD", "MHS"])
    NEGMH = al.r("NEGMH", 4)[:, 0:1]
    M4 = al.r("M4", 64)
    SQ = al.r("SQ", D)
    XC = [al.r("XC0", D), al.r("XC1", D)]
    HB = [al.r("HB0", 512, BF16), al.r("HB1", 512, BF16)]
    H0T = al.r("H0T", 8192, BF16)
    WQKV = al.r("WQKV", 12288, BF16)
    QT = al.r("QT", 8192, BF16)
    KT = al.r("KT", 8192, BF16)
    VEXT = al.r("VEXT", 8320, BF16)
    ATB = [al.r("ATB0", 1024, BF16), al.r("ATB1", 1024, BF16)]
    assert al.off <= ARENA_WORDS, al.off
    h0t_off = al.map["H0T"][0]
    LRW = pb.R(h0t_off, 2048)
    RHIB = pb.R(h0t_off + 2048, 1024, BF16)
    RLOB = pb.R(h0t_off + 3072, 1024, BF16)
    RBC = pb.R(h0t_off + 4096, 2048)
    OSCR = pb.R(h0t_off + 6144, 1024, BF16)
    wq_off = al.map["WQKV"][0]
    OT = pb.R(wq_off, 8192, BF16)
    WOUT = pb.R(wq_off + 8192, 4096, BF16)

    wq1 = pb.dma(chunks(WQKV, DC), io["w_qkv_a"].rearrange("(c p) f -> p c f", p=128), "LW", cast=True)
    const_v = pb.v("memset",
                   VEXT.rearrange("p (tc h e) -> p tc h e", tc=TC, h=NH)[:, :, :, 64:65],
                   1.0, inc="DVE")
    xin_r = io["x_in"].rearrange("(tc p) d -> tc p d", p=128)

    scratch = {"XC": XC, "SQ": SQ, "HB": HB}
    h0t_done = emit_ln_to_ft(pb, cols, consts, H0T, scratch, kind="ln", xin_r=xin_r, pbank=6)

    wqc = chunks(WQKV, DC)
    h0c = chunks(H0T, DC)
    evac_val = {0: None, 4: None}
    last_pe_v = None
    for which, dest in ((0, QT), (1, KT)):
        for oc in range(DC):
            bank = 0 if oc % 2 == 0 else 4
            ps = pb.PS(bank, 4)
            mm = None
            for k in range(DC):
                lhsT = wqc[:, k, which * 1024 + oc * 128: which * 1024 + (oc + 1) * 128]
                for n in range(4):
                    wl = []
                    if k == 0 and n == 0:
                        wl = [wq1, h0t_done,
                              ("ACT", evac_val[bank]) if evac_val[bank] else None]
                    mm = pb.t("matmul", ps[:, n * 512:(n + 1) * 512], lhsT,
                              h0c[:, k, n * 512:(n + 1) * 512],
                              start=(k == 0), stop=(k == DC - 1), w=wl,
                              inc="PE" if (k == DC - 1 and n == 3) else None)
            evac_val[bank] = pb.s("copy", out=chunks(dest, DC)[:, oc], in_=ps,
                                  w=[("PE", mm)], inc="ACT")
    last_qk_evac = max(evac_val[0], evac_val[4])
    vv_evac = {0: None, 2: None}
    for tc in range(TC):
        bank = 0 if tc % 2 == 0 else 2
        ps = pb.PS(bank, 2)
        mm = None
        for k in range(DC):
            lhsT = h0c[:, k, tc * 128:(tc + 1) * 128]
            for n in range(2):
                wl = []
                if k == 0 and n == 0:
                    if tc < 2:
                        wl = [("ACT", last_qk_evac), ("DVE", const_v)]
                    else:
                        wl = [("ACT", vv_evac[bank])]
                mm = pb.t("matmul", ps[:, n * 512:(n + 1) * 512], lhsT,
                          wqc[:, k, 2048 + n * 512:2048 + (n + 1) * 512],
                          start=(k == 0), stop=(k == DC - 1), w=wl,
                          inc="PE" if (k == DC - 1 and n == 1) else None)
        out_view = chunks(VEXT, TC)[:, tc].rearrange("p (h e) -> p h e", h=NH)[:, :, 0:64]
        vv_evac[bank] = pb.s("copy", out=out_view,
                             in_=ps.rearrange("p (h e) -> p h e", h=NH),
                             w=[("PE", mm)], inc="ACT")
    qkv_done_act = pb.cnt["ACT"]
    pe_after_qkv = pb.cnt["PE"]
    wo1 = pb.dma(chunks(WOUT, DC), io["w_out_a"].rearrange("(c p) f -> p c f", p=128), "LW",
                 cast=True, w=[("PE", pe_after_qkv)])

    ovl = {"LRW": LRW, "RHIB": RHIB, "RLOB": RLOB, "RBC": RBC, "OSCR": OSCR}
    o_free = emit_attention(pb, io, consts, cols, M4, NEGMH, QT, KT, VEXT, OT,
                            ATB, ovl, scale=0.125, causal=False,
                            entry_free=("ACT", qkv_done_act))

    xout_r = io["x_out"].rearrange("(tc p) d -> tc p d", p=128)
    ev_prev = {0: None, 2: None}
    xo_free = [None, None]
    otc = chunks(OT, DC)
    woc = chunks(WOUT, DC)
    for tc in range(TC):
        bank = 0 if tc % 2 == 0 else 2
        ps = pb.PS(bank, 2)
        mm = None
        for k in range(DC):
            for n in range(2):
                wl = []
                if k == 0 and n == 0:
                    wl = [o_free, wo1,
                          ("DVE", ev_prev[bank]) if ev_prev[bank] else None]
                mm = pb.t("matmul", ps[:, n * 512:(n + 1) * 512],
                          otc[:, k, tc * 128:(tc + 1) * 128], woc[:, k, n * 512:(n + 1) * 512],
                          start=(k == 0), stop=(k == DC - 1), w=wl,
                          inc="PE" if (k == DC - 1 and n == 1) else None)
        par = tc % 2
        xc = XC[par]
        ldw = [xo_free[par]] if xo_free[par] else [("ACT", qkv_done_act)]
        ldv = pb.dma(xc, xin_r[tc], "LD", w=ldw)
        addv = pb.v("tensor_tensor", out=xc, in0=xc, in1=ps, op=ALU.add,
                    w=[ldv, ("PE", mm)], inc="DVE")
        ev_prev[bank] = addv
        xo_free[par] = pb.dma(xout_r[tc], xc, "ST", w=[("DVE", addv)])
    pb.wait("sync", [("ST", pb.cnt["ST"])])



EPS_RMS = 1.1920929e-07


def emit_rep(pb, consts, row_io, width, dest, tmprow, pbank=6, w=(), prev=None):
    ld = pb.dma(tmprow[0:1, 0:width], row_io, "LD", w=list(w) + [prev])
    mm = None
    for n0 in range(0, width, 512):
        n1 = min(n0 + 512, width)
        wl = [ld, consts["_ld"], prev] if n0 == 0 else []
        mm = pb.t("matmul", pb.PS(pbank, 2)[:, n0:n1], consts["ones_row"][0:1, :],
                  tmprow[0:1, n0:n1], start=True, stop=True, w=wl,
                  inc="PE" if n1 == width else None)
    cp = pb.v("tensor_copy", out=dest, in_=pb.PS(pbank, 2)[:, 0:width],
              w=[("PE", mm)], inc="DVE")
    return ("DVE", cp)


def emit_up(pb, consts, w_hbm_T, in_ft, nin, nout, dst_ft, th0, tw, evac,
            wbufs_raw, wfree, extra_w=(), evac_w=None, entry=None,
            bias_cols=None, bias_w=None):
    evs = []
    bank_free = {0: entry, 2: entry}
    blk = 4 if nin <= 8 else 2
    wbufs = [b[:, 0:nin * blk * 128].rearrange("p (c f) -> p c f", c=nin)
             for b in wbufs_raw]
    for ob in range(nout // blk):
        par = ob % 2
        wld = pb.dma(wbufs[par], w_hbm_T[:, ob],
                     "LW", cast=True, w=[wfree[par]])
        wfree[par] = None
        mm_last = None
        for j in range(blk):
            oc = ob * blk + j
            bank = 0 if oc % 2 == 0 else 2
            ps = pb.PS(bank, 2)
            mm = None
            for k in range(nin):
                for n0 in range(0, tw, 512):
                    n1 = min(n0 + 512, tw)
                    wl = []
                    if k == 0 and n0 == 0:
                        wl = [wld if j == 0 else None, bank_free[bank]]
                        if ob < 2 and j == 0:
                            wl += list(extra_w)
                    mm = pb.t("matmul", ps[:, n0:n1],
                              wbufs[par][:, k, j * 128:(j + 1) * 128],
                              in_ft[:, k, th0 + n0:th0 + n1],
                              start=(k == 0), stop=(k == nin - 1), w=wl,
                              inc="PE" if (k == nin - 1 and n1 == tw) else None)
            dst = dst_ft[:, oc, 0:tw]
            ew = [("PE", mm)] + ([evac_w[oc]] if evac_w else [])
            if bias_cols is not None:
                ew.append(bias_w if oc < 2 else None)
                ev = pb.s("activation", out=dst, in_=ps[:, 0:tw], func=AF.Gelu,
                          bias=bias_cols[:, oc:oc + 1], w=ew, inc="ACT")
                evs.append(("ACT", ev))
            elif evac == "gelu":
                ev = pb.s("activation", out=dst, in_=ps[:, 0:tw], func=AF.Gelu, w=ew, inc="ACT")
                evs.append(("ACT", ev))
            elif evac == "silu":
                ev = pb.s("activation", out=dst, in_=ps[:, 0:tw], func=AF.Silu, w=ew, inc="ACT")
                evs.append(("ACT", ev))
            elif evac == "copy":
                ev = pb.s("copy", out=dst, in_=ps[:, 0:tw], w=ew, inc="ACT")
                evs.append(("ACT", ev))
            elif evac == "mult_into":
                ev = pb.v("tensor_tensor", out=dst, in0=ps[:, 0:tw], in1=dst,
                          op=ALU.mult, w=ew, inc="DVE")
                evs.append(("DVE", ev))
            bank_free[bank] = evs[-1]
            mm_last = mm
        wfree[par] = ("PE", mm_last)
    return evs


def emit_down(pb, consts, w_hbm_T, hsrc_ft, nhc, e_idx, COEF, XINT, th0, tw,
              wd_buf, wd_w=(), src_w=None, entry=None):
    wld = pb.dma(wd_buf, w_hbm_T.rearrange("(c p) f -> p c f", p=128)[:, :, 256:1024],
                 "LW", cast=True, w=wd_w)
    bank_free = {4: entry, 6: entry}
    last = None
    for tci in range(tw // 128):
        tc = th0 // 128 + tci
        bank = 4 if tci % 2 == 0 else 6
        ps = pb.PS(bank, 2)
        mm = None
        for hc in range(nhc):
            for (n0, n1) in ((0, 512), (512, 768)):
                wl = []
                if hc == 0 and n0 == 0:
                    wl = [wld, bank_free[bank]]
                    if src_w is not None:
                        wl.append(src_w)
                mm = pb.t("matmul", ps[:, n0:n1],
                          hsrc_ft[:, hc, tci * 128:(tci + 1) * 128],
                          wd_buf[:, hc, n0:n1],
                          start=(hc == 0), stop=(hc == nhc - 1), w=wl,
                          inc="PE" if (hc == nhc - 1 and n0 == 512) else None)
        coef = COEF[:, tc * 4 + e_idx:tc * 4 + e_idx + 1]
        xslice = chunks(XINT, TC)[:, tc][:, 256:1024]
        ev = pb.v("scalar_tensor_tensor", out=xslice, in0=ps[:, 0:768],
                  scalar=coef, in1=xslice, op0=ALU.mult, op1=ALU.add,
                  w=[("PE", mm)], inc="DVE")
        bank_free[bank] = ("DVE", ev)
        last = ("DVE", ev)
    return last


def emit_phase_b(pb, io, consts, al):
    cols = al.cols(["SQC", "SMC", "NEGMN", "EX2", "M2", "VARC", "SDC", "RSTD",
                    "MHS", "RMSV", "RMSD", "RMSR", "NEGL", "SUME", "RS",
                    "M1C", "M2C", "DEN", "RDEN", "W1C", "W2C"])
    NEGMH = al.r("NEGMH", 4)[:, 0:1]
    M4 = al.r("M4", 64)
    SQ = al.r("SQ", D)
    XC = [al.r("XC0", D), al.r("XC1", D)]
    HB = [al.r("HB0", 512, BF16), al.r("HB1", 512, BF16)]
    N1REP = al.r("N1REP", D)
    N2REP = al.r("N2REP", D)
    QNREP = al.r("QNREP", 64)
    KNREP = al.r("KNREP", 64)
    GATEB = al.r("GATEB", 16, BF16)
    COEF = al.r("COEF", 64)
    PRB = al.r("PRB", 4)
    PRN = al.r("PRN", 4)
    MK1 = al.r("MK1", 4)
    MK2 = al.r("MK2", 4)
    PRM = al.r("PRM", 4)
    T2S = al.r("T2S", 4)
    QSC = al.r("QSC", 16)
    QSD = al.r("QSD", 16)
    QRS = al.r("QRS", 16)
    SCR2 = al.r("SCR2", 4096)
    FT1 = al.r("FT1", 8192, BF16)
    QK = al.r("QK", 16384)
    VX = al.r("VX", 8320)
    POOL = al.map["VX"][0] + 8320
    pool_words = ARENA_WORDS - POOL
    assert pool_words >= 8300, pool_words
    s2 = al.map["SCR2"][0]

    QNT = pb.R(al.map["QK"][0], 8192, BF16)
    KNT = pb.R(al.map["QK"][0] + 8192, 8192, BF16)
    XINT = pb.R(al.map["QK"][0], 16384)
    VEXT = pb.R(al.map["VX"][0], 8320, BF16)
    XFT = pb.R(al.map["VX"][0], 8192, BF16)
    XIC = [pb.R(s2, D), pb.R(s2 + 1024, D)]
    QRAW = XIC
    QTOK = [pb.R(s2 + 2048, 512, BF16), pb.R(s2 + 2560, 512, BF16)]
    ROWSC = pb.R(s2 + 3072, D)
    WS1 = [pb.R(s2, 2048, BF16), pb.R(s2 + 2048, 2048, BF16)]
    WS2 = WS1
    WB2 = [pb.R(POOL, 4096, BF16), pb.R(POOL + 4096, 4096, BF16)]
    ATB = [pb.R(POOL, 1024, BF16), pb.R(POOL + 1024, 1024, BF16)]
    LRW = pb.R(POOL + 2048, 2048)
    RHIB = pb.R(POOL + 4096, 1024, BF16)
    RLOB = pb.R(POOL + 5120, 1024, BF16)
    OSCR = pb.R(POOL + 2048, 1024, BF16)
    RBC = pb.R(POOL + 3072, 2048)
    WOB = pb.R(POOL, 3072, BF16)
    WD = pb.R(POOL, 6144, BF16)
    D1BQ = pb.R(POOL + 6144, 2048, BF16)
    assert POOL + 6144 + 2048 <= ARENA_WORDS

    xin_r = io["x_in"].rearrange("(tc p) d -> tc p d", p=128)
    xout_r = io["x_out"].rearrange("(tc p) d -> tc p d", p=128)
    xint_r = io["xint_scr"].rearrange("(tc p) d -> tc p d", p=128)

    r1 = emit_rep(pb, consts, io["n1_w"], D, N1REP, ROWSC, pbank=4)
    r2 = emit_rep(pb, consts, io["n2_w"], D, N2REP, ROWSC, pbank=4, prev=r1)
    r3 = emit_rep(pb, consts, io["qn_w"], HD, QNREP, ROWSC, pbank=4, prev=r2)
    rep_done = emit_rep(pb, consts, io["kn_w"], HD, KNREP, ROWSC, pbank=4, prev=r3)
    gld = pb.dma(GATEB.rearrange("p (c f) -> p c f", c=DC),
                 io["gate_b"].rearrange("(c p) f -> p c f", p=128), "LW", cast=True)
    const_v = pb.v("memset",
                   VEXT.rearrange("p (tc h e) -> p tc h e", tc=TC, h=NH)[:, :, :, 64:65],
                   1.0, inc="DVE")
    wq_ld = pb.dma(chunks(WB2[0], DC), io["wq_b"].rearrange("(c p) f -> p c f", p=128), "LW", cast=True)
    wk_ld = pb.dma(chunks(WB2[1], DC), io["wk_b"].rearrange("(c p) f -> p c f", p=128), "LW", cast=True)

    evac_prev = [0, 0]
    xc_free = [None, None]
    xic_free = [None, None]
    hb_free = [0, 0]
    for tc in range(TC):
        par = tc % 2
        xc = XC[par]
        ldv = pb.dma(xc, xin_r[tc], "LD", w=[xc_free[par]])
        pb.v("scalar_tensor_tensor", out=SQ, in0=xc, scalar=1.0, in1=xc,
             op0=ALU.mult, op1=ALU.mult, accum_out=cols["SQC"], w=[ldv])
        pb.v("reduce_sum", out=cols["SMC"], in_=xc, axis=AX.X)
        pb.v("tensor_scalar_mul", out=cols["NEGMN"], in0=cols["SMC"], scalar1=-1.0 / D)
        pb.v("tensor_scalar_mul", out=cols["EX2"], in0=cols["SQC"], scalar1=1.0 / D)
        pb.v("tensor_tensor", out=cols["M2"], in0=cols["NEGMN"], in1=cols["NEGMN"], op=ALU.mult)
        pb.v("tensor_tensor", out=cols["VARC"], in0=cols["EX2"], in1=cols["M2"], op=ALU.subtract)
        vv = pb.v("tensor_scalar_add", out=cols["VARC"], in0=cols["VARC"], scalar1=1e-5, inc="DVE")
        av = pb.s("activation", out=cols["SDC"], in_=cols["VARC"], func=AF.Sqrt,
                  w=[("DVE", vv)], inc="ACT")
        pb.v("reciprocal", out=cols["RSTD"], in_=cols["SDC"], w=[("ACT", av)])
        xic = XIC[par]
        xv = pb.v("tensor_scalar", out=xic, in0=xc, scalar1=cols["NEGMN"],
                  scalar2=cols["RSTD"], op0=ALU.add, op1=ALU.mult,
                  w=[xic_free[par]], inc="DVE")
        xc_free[par] = ("DVE", xv)
        sp = pb.dma(xint_r[tc], xic, "SP", w=[("DVE", xv)])
        pb.v("scalar_tensor_tensor", out=SQ, in0=xic, scalar=1.0, in1=xic,
             op0=ALU.mult, op1=ALU.mult, accum_out=cols["RMSV"])
        rv = pb.v("tensor_scalar", out=cols["RMSV"], in0=cols["RMSV"],
                  scalar1=1.0 / D, scalar2=EPS_RMS, op0=ALU.mult, op1=ALU.add, inc="DVE")
        a2 = pb.s("activation", out=cols["RMSD"], in_=cols["RMSV"], func=AF.Sqrt,
                  w=[("DVE", rv)], inc="ACT")
        pb.v("reciprocal", out=cols["RMSR"], in_=cols["RMSD"], w=[("ACT", a2)])
        hb = HB[par]
        nv = pb.v("scalar_tensor_tensor", out=hb, in0=xic, scalar=cols["RMSR"],
                  in1=N1REP, op0=ALU.mult, op1=ALU.mult,
                  w=[("PE", hb_free[par])] + ([rep_done] if tc == 0 else []), inc="DVE")
        xic_free[par] = (("DVE", nv) if True else None)
        xic_free[par] = sp
        pe_last, ev = emit_tok_to_ft(pb, consts, hb, FT1, tc, 6 + par,
                                     first_wait=[("DVE", nv), ("ACT", evac_prev[par])],
                                     evac_prev=0)
        hb_free[par] = pe_last
        evac_prev[par] = ev
    hrt_done = ("ACT", pb.cnt["ACT"])
    b1_sp = ("SP", pb.cnt["SP"])

    ft1c = chunks(FT1, DC)
    evac_bank = {0: None, 2: None}
    wlds = [wq_ld, wk_ld, None]
    for which in range(3):
        if which == 2:
            wlds[2] = pb.dma(chunks(WB2[0], DC),
                             io["wv_b"].rearrange("(c p) f -> p c f", p=128),
                             "LW", cast=True, w=[("PE", pe_after_q)])
        wb, wld = WB2[which % 2], wlds[which]
        wbc = chunks(wb, DC)
        for tc in range(TC):
            par = tc % 2
            bank = 0 if par == 0 else 2
            ps = pb.PS(bank, 2)
            mm = None
            for k in range(DC):
                for n in range(2):
                    wl = []
                    if k == 0 and n == 0:
                        wl = [wld, hrt_done, evac_bank[bank]]
                    mm = pb.t("matmul", ps[:, n * 512:(n + 1) * 512],
                              ft1c[:, k, tc * 128:(tc + 1) * 128],
                              wbc[:, k, n * 512:(n + 1) * 512],
                              start=(k == 0), stop=(k == DC - 1), w=wl,
                              inc="PE" if (k == DC - 1 and n == 1) else None)
            if which < 2:
                qraw = QRAW[par]
                qv = pb.v("tensor_copy", out=qraw, in_=ps,
                          w=[("PE", mm)] + ([b1_sp] if tc < 2 else []), inc="DVE")
                evac_bank[bank] = ("DVE", qv)
                pb.v("scalar_tensor_tensor", out=SQ, in0=qraw, scalar=1.0, in1=qraw,
                     op0=ALU.mult, op1=ALU.mult)
                pb.v("tensor_reduce", out=QSC, in_=chunks(SQ, NH), axis=AX.X, op=ALU.add)
                qs = pb.v("tensor_scalar", out=QSC, in0=QSC, scalar1=1.0 / HD,
                          scalar2=EPS_RMS, op0=ALU.mult, op1=ALU.add, inc="DVE")
                qa = pb.s("activation", out=QSD, in_=QSC, func=AF.Sqrt,
                          w=[("DVE", qs)], inc="ACT")
                pb.v("reciprocal", out=QRS, in_=QSD, w=[("ACT", qa)])
                qtok = QTOK[par]
                rep = QNREP if which == 0 else KNREP
                nv = None
                for hh in range(NH):
                    nv = pb.v("scalar_tensor_tensor",
                              out=qtok[:, hh * 64:(hh + 1) * 64],
                              in0=qraw[:, hh * 64:(hh + 1) * 64],
                              scalar=QRS[:, hh:hh + 1], in1=rep,
                              op0=ALU.mult, op1=ALU.mult,
                              w=[("PE", hb_free[par])] if hh == 0 else [], inc="DVE")
                dest = QNT if which == 0 else KNT
                pe_last, ev = emit_tok_to_ft(pb, consts, qtok, dest, tc, 6 + par,
                                             first_wait=[("DVE", nv)],
                                             evac_prev=evac_prev[par])
                hb_free[par] = pe_last
                evac_prev[par] = ev
            else:
                out_view = chunks(VEXT, TC)[:, tc].rearrange("p (h e) -> p h e", h=NH)[:, :, 0:64]
                ev = pb.s("copy", out=out_view, in_=ps.rearrange("p (h e) -> p h e", h=NH),
                          w=[("PE", mm), ("DVE", const_v) if tc == 0 else None], inc="ACT")
                evac_bank[bank] = ("ACT", ev)
        if which == 0:
            pe_after_q = pb.cnt["PE"]
    qkv_done = ("ACT", pb.cnt["ACT"])

    ovl = {"LRW": LRW, "RHIB": RHIB, "RLOB": RLOB, "RBC": RBC, "OSCR": OSCR}
    OT = FT1
    o_free = emit_attention(pb, io, consts, cols, M4, NEGMH, QNT, KNT, VEXT, OT,
                            ATB, ovl, scale=0.125, causal=True, entry_free=qkv_done)
    attn_pe = ("PE", pb.cnt["PE"])
    attn_dve = ("DVE", pb.cnt["DVE"])
    attn_act = ("ACT", pb.cnt["ACT"])

    xl = pb.dma(chunks(XINT, TC),
                io["xint_scr"].rearrange("(tc p) d -> p tc d", p=128),
                "LD", w=[attn_pe, attn_dve])
    wo_ld = pb.dma(chunks(WOB, DC),
                   io["wo_b"].rearrange("(c p) f -> p c f", p=128)[:, :, 256:1024],
                   "LW", cast=True, w=[attn_pe, attn_act, o_free])
    otc = chunks(OT, DC)
    wobc = chunks(WOB, DC)
    bank_free = {0: None, 2: None}
    for tc in range(TC):
        bank = 0 if tc % 2 == 0 else 2
        ps = pb.PS(bank, 2)
        mm = None
        for k in range(DC):
            for (n0, n1) in ((0, 512), (512, 768)):
                wl = []
                if k == 0 and n0 == 0:
                    wl = [wo_ld, o_free, bank_free[bank]]
                mm = pb.t("matmul", ps[:, n0:n1], otc[:, k, tc * 128:(tc + 1) * 128],
                          wobc[:, k, n0:n1], start=(k == 0), stop=(k == DC - 1),
                          w=wl, inc="PE" if (k == DC - 1 and n0 == 512) else None)
        xslice = chunks(XINT, TC)[:, tc][:, 256:1024]
        ev = pb.v("tensor_tensor", out=xslice, in0=xslice, in1=ps[:, 0:768],
                  op=ALU.add, w=[("PE", mm), xl], inc="DVE")
        bank_free[bank] = ("DVE", ev)

    scratch = {"XC": XC, "SQ": SQ, "HB": HB}
    xft_done = emit_ln_to_ft(pb, cols, consts, XFT, scratch, kind="rms",
                             src_sbuf=XINT, gamma_rep=N2REP, eps=EPS_RMS,
                             pbank=6, extra_first_wait=attn_pe)
    xftc = chunks(XFT, DC)
    gate_free = None
    for tc in range(TC):
        ps = pb.PS(4, 1)
        mm = None
        for k in range(DC):
            wl = []
            if k == 0:
                wl = [gld, xft_done, gate_free if gate_free else attn_dve]
            mm = pb.t("matmul", ps[:, 0:4], xftc[:, k, tc * 128:(tc + 1) * 128],
                      GATEB.rearrange("p (c f) -> p c f", c=DC)[:, k, :],
                      start=(k == 0), stop=(k == DC - 1), w=wl,
                      inc="PE" if k == DC - 1 else None)
        nm = pb.v("tensor_reduce", out=cols["NEGL"], in_=ps[:, 0:4], axis=AX.X,
                  op=ALU.max, negate=True, w=[("PE", mm)], inc="DVE")
        ea = pb.s("activation", out=PRB, in_=ps[:, 0:4], func=AF.Exp,
                  bias=cols["NEGL"], accum_out=cols["SUME"],
                  w=[("DVE", nm)], inc="ACT")
        pb.v("reciprocal", out=cols["RS"], in_=cols["SUME"], w=[("ACT", ea)])
        pb.v("tensor_single_scalar", out=PRN, in_=PRB, scalar=cols["RS"], op=ALU.mult)
        pb.v("tensor_reduce", out=cols["M1C"], in_=PRN, axis=AX.X, op=ALU.max)
        pb.v("tensor_single_scalar", out=MK1, in_=PRN, scalar=cols["M1C"], op=ALU.is_equal)
        pb.v("scalar_tensor_tensor", out=PRM, in0=MK1, scalar=-1e30, in1=PRN,
             op0=ALU.mult, op1=ALU.add)
        pb.v("tensor_reduce", out=cols["M2C"], in_=PRM, axis=AX.X, op=ALU.max)
        pb.v("tensor_single_scalar", out=MK2, in_=PRM, scalar=cols["M2C"], op=ALU.is_equal)
        pb.v("tensor_tensor", out=cols["DEN"], in0=cols["M1C"], in1=cols["M2C"], op=ALU.add)
        pb.v("tensor_scalar_add", out=cols["DEN"], in0=cols["DEN"], scalar1=1e-8)
        pb.v("reciprocal", out=cols["RDEN"], in_=cols["DEN"])
        pb.v("tensor_tensor", out=cols["W1C"], in0=cols["M1C"], in1=cols["RDEN"], op=ALU.mult)
        pb.v("tensor_tensor", out=cols["W2C"], in0=cols["M2C"], in1=cols["RDEN"], op=ALU.mult)
        pb.v("tensor_single_scalar", out=T2S, in_=MK2, scalar=cols["W2C"], op=ALU.mult)
        gf = pb.v("scalar_tensor_tensor", out=COEF[:, tc * 4:(tc + 1) * 4],
                  in0=MK1, scalar=cols["W1C"], in1=T2S,
                  op0=ALU.mult, op1=ALU.add, inc="DVE")
        gate_free = ("DVE", gf)

    h1t_half = FT1.rearrange("p (c t) -> p c t", c=NH)
    h1t_q = pb.R(al.map["FT1"][0], 4096, BF16).rearrange("p (c t) -> p c t", c=NH)
    h2t_q = pb.R(al.map["FT1"][0] + 4096, 4096, BF16).rearrange("p (c t) -> p c t", c=NH)
    wdv = WD.rearrange("p (c f) -> p c f", c=NH)
    wfree = [attn_pe, attn_pe]
    tail = gate_free
    wd_w = [gate_free, attn_dve, attn_act]
    for half in range(2):
        th0 = half * 1024
        ups = emit_up(pb, consts, io["e0_up_t"], xftc, DC, NH, h1t_half, th0, 1024,
                      "gelu", WS1, wfree, extra_w=[tail, xft_done], entry=tail)
        tail = emit_down(pb, consts, io["e0_down_t"], h1t_half, NH, 0, COEF, XINT,
                         th0, 1024, wdv, wd_w=wd_w, src_w=ups[-1], entry=tail)
        wd_w = [tail]
    for half in range(2):
        th0 = half * 1024
        u1 = emit_up(pb, consts, io["e1_w1_t"], xftc, DC, NH, h1t_half, th0, 1024,
                     "silu", WS1, wfree, extra_w=[tail], entry=tail)
        u2 = emit_up(pb, consts, io["e1_w2_t"], xftc, DC, NH, h1t_half, th0, 1024,
                     "mult_into", WS2, wfree, evac_w=u1, entry=u1[-1])
        tail = emit_down(pb, consts, io["e1_down_t"], h1t_half, NH, 1, COEF, XINT,
                         th0, 1024, wdv, wd_w=wd_w, src_w=u2[-1], entry=tail)
        wd_w = [tail]
    for quar in range(4):
        th0 = quar * 512
        a1 = emit_up(pb, consts, io["e2_l1_t"], xftc, DC, NH, h1t_q, th0, 512,
                     "gelu", WS1, wfree, extra_w=[tail], entry=tail)
        a2 = emit_up(pb, consts, io["e2_l2_t"], h1t_q, NH, NH, h2t_q, 0, 512,
                     "gelu", WS2, wfree, extra_w=[a1[-1]], entry=a1[-1])
        a3 = emit_up(pb, consts, io["e2_l3_t"], h2t_q, NH, NH, h1t_q, 0, 512,
                     "gelu", WS1, wfree, extra_w=[a2[-1]], entry=a2[-1])
        tail = emit_down(pb, consts, io["e2_l4_t"], h1t_q, NH, 2, COEF, XINT,
                         th0, 512, wdv, wd_w=wd_w, src_w=a3[-1], entry=tail)
        wd_w = [tail]
    d1bq = D1BQ.rearrange("p (c t) -> p c t", c=DC)
    for quar in range(4):
        th0 = quar * 512
        b1 = emit_up(pb, consts, io["e3_down_t"], xftc, DC, DC, d1bq, th0, 512,
                     "gelu", WS1, wfree, extra_w=[tail], entry=tail)
        b2 = emit_up(pb, consts, io["e3_up_t"], d1bq, DC, NH, h1t_q, 0, 512,
                     "gelu", WS2, wfree, extra_w=[b1[-1]], entry=b1[-1])
        tail = emit_down(pb, consts, io["e3_out_t"], h1t_q, NH, 3, COEF, XINT,
                         th0, 512, wdv, wd_w=wd_w, src_w=b2[-1], entry=tail)
        wd_w = [tail]

    xo_free = [None, None]
    for tc in range(TC):
        par = tc % 2
        xc = XC[par]
        ldv = pb.dma(xc, xin_r[tc], "LD", w=[xo_free[par] or wd_w[0]])
        addv = pb.v("tensor_tensor", out=xc, in0=xc, in1=chunks(XINT, TC)[:, tc],
                    op=ALU.add, w=[ldv] + ([wd_w[0]] if tc == 0 else []), inc="DVE")
        xo_free[par] = pb.dma(xout_r[tc], xc, "ST", w=[("DVE", addv)])
    pb.wait("sync", [("ST", pb.cnt["ST"])])


def pm_tile(wt):
    nin = wt.shape[0] // 128
    blk = 4 if nin <= 8 else 2
    bw = blk * 128
    nob = wt.shape[1] // bw
    return np.ascontiguousarray(wt.reshape(nin, 128, nob, bw).transpose(1, 2, 0, 3))


def prep_phase_b(inputs):
    tr = lambda k: np.ascontiguousarray(np.asarray(inputs[k]).T)
    d = {
        "n1_w": np.asarray(inputs["L0_ffn_n1_w"]).reshape(1, D),
        "n2_w": np.asarray(inputs["L0_ffn_n2_w"]).reshape(1, D),
        "qn_w": np.asarray(inputs["L0_ffn_qn_w"]).reshape(1, HD),
        "kn_w": np.asarray(inputs["L0_ffn_kn_w"]).reshape(1, HD),
        "wq_b": tr("L0_ffn_wq"), "wk_b": tr("L0_ffn_wk"),
        "wv_b": tr("L0_ffn_wv"), "wo_b": tr("L0_ffn_wo"),
        "gate_b": tr("L0_ffn_gate_w"),
        "e0_up_t": pm_tile(tr("L0_ffn_e0_up")), "e0_down_t": tr("L0_ffn_e0_down"),
        "e1_w1_t": pm_tile(tr("L0_ffn_e1_w1")), "e1_w2_t": pm_tile(tr("L0_ffn_e1_w2")),
        "e1_down_t": tr("L0_ffn_e1_down"),
        "e2_l1_t": pm_tile(tr("L0_ffn_e2_l1")), "e2_l2_t": pm_tile(tr("L0_ffn_e2_l2")),
        "e2_l3_t": pm_tile(tr("L0_ffn_e2_l3")), "e2_l4_t": tr("L0_ffn_e2_l4"),
        "e3_down_t": pm_tile(tr("L0_ffn_e3_down")), "e3_up_t": pm_tile(tr("L0_ffn_e3_up")),
        "e3_out_t": tr("L0_ffn_e3_out"),
    }
    d.update(make_consts("B"))
    return d



def _c_finish(pb, XC, xin_r, xout_r, w=()):
    xo_free = [None, None]
    for tc in range(TC):
        par = tc % 2
        ldv = pb.dma(XC[par], xin_r[tc], "LD", w=[xo_free[par]] + list(w))
        xo_free[par] = pb.dma(xout_r[tc], XC[par], "ST", w=[ldv])
    pb.wait("sync", [("ST", pb.cnt["ST"])])


C_UPTO = 9
C3SUB = 3

def emit_phase_c(pb, io, consts, al):
    cols = al.cols(["SQC", "SMC", "NEGMN", "EX2", "M2", "VARC", "SDC", "RSTD",
                    "DCOL", "RCC"])
    XC = [al.r("XC0", D), al.r("XC1", D)]
    HB = [al.r("HB0", 512, BF16), al.r("HB1", 512, BF16)]
    SQ = al.r("SQ", D)
    TMP1 = SQ
    KVH = al.r("KVH", 512, BF16)
    KVL = al.r("KVL", 512, BF16)
    H2T = al.r("H2T", 8192, BF16)
    WQKVC = al.r("WQKVC", 12288, BF16)
    QFT = al.r("QFT", 8192, BF16)
    KF = al.r("KF", 8192, BF16)
    VEXT1 = al.r("VEXT1", 8320, BF16)
    assert al.off <= ARENA_WORDS, al.off
    OT1 = pb.R(al.map["H2T"][0], 8192, BF16)
    WOUTC = pb.R(al.map["WQKVC"][0], 4096, BF16)
    OTOK = pb.R(al.map["WQKVC"][0] + 4096, 8192, BF16)

    xin_r = io["x_in"].rearrange("(tc p) d -> tc p d", p=128)
    xout_r = io["x_out"].rearrange("(tc p) d -> tc p d", p=128)

    wq_ld = pb.dma(chunks(WQKVC, DC), io["w_qkv_c"].rearrange("(c p) f -> p c f", p=128),
                   "LW", cast=True)
    const_v = pb.v("memset",
                   VEXT1.rearrange("p (tc h e) -> p tc h e", tc=TC, h=NH)[:, :, :, 64:65],
                   1.0, inc="DVE")

    scratch = {"XC": XC, "SQ": SQ, "HB": HB}
    h2t_done = emit_ln_to_ft(pb, cols, consts, H2T, scratch, kind="ln",
                             xin_r=xin_r, pbank=6)

    if C_UPTO < 2:
        _c_finish(pb, XC, xin_r, xout_r)
        return
    h2c = chunks(H2T, DC)
    wqc = chunks(WQKVC, DC)
    QR = XC[0]
    RT = XC[1]
    cosv = chunks(consts["cos"], TC)
    sinv = chunks(consts["sin"], TC)
    evac_bank = {0: ("DVE", 1), 2: ("DVE", 1)}
    tr_evac = [0, 0]
    hb_free2 = [0, 0]
    ae_prev = None
    for which in range(3):
        for tc in range(TC):
            par = tc % 2
            bank = 0 if par == 0 else 2
            ps = pb.PS(bank, 2)
            mm = None
            for k in range(DC):
                for n in range(2):
                    wl = []
                    if k == 0 and n == 0:
                        wl = [wq_ld, h2t_done, evac_bank[bank]]
                    mm = pb.t("matmul", ps[:, n * 512:(n + 1) * 512],
                              h2c[:, k, tc * 128:(tc + 1) * 128],
                              wqc[:, k, which * 1024 + n * 512:which * 1024 + (n + 1) * 512],
                              start=(k == 0), stop=(k == DC - 1), w=wl,
                              inc="PE" if (k == DC - 1 and n == 1) else None)
            if which < 2:
                qv = pb.v("tensor_copy", out=QR, in_=ps,
                          w=[("PE", mm), ae_prev], inc="DVE")
                evac_bank[bank] = ("DVE", qv)
                qrv = QR.rearrange("p (h two x) -> p h two x", h=NH, two=2)
                rtv = RT.rearrange("p (h two x) -> p h two x", h=NH, two=2)
                pb.v("tensor_copy", out=rtv[:, :, 1], in_=qrv[:, :, 0])
                pb.v("tensor_scalar_mul", out=rtv[:, :, 0], in0=qrv[:, :, 1], scalar1=-1.0)
                cosb = cosv[:, tc][:, None, :].broadcast_to([128, NH, HD])
                sinb = sinv[:, tc][:, None, :].broadcast_to([128, NH, HD])
                pb.v("tensor_tensor", out=TMP1.rearrange("p (h x) -> p h x", h=NH),
                     in0=QR.rearrange("p (h x) -> p h x", h=NH),
                     in1=cosb, op=ALU.mult)
                pb.v("tensor_tensor", out=RT.rearrange("p (h x) -> p h x", h=NH),
                     in0=RT.rearrange("p (h x) -> p h x", h=NH),
                     in1=sinb, op=ALU.mult)
                rope_done = pb.v("tensor_tensor", out=QR, in0=TMP1, in1=RT,
                                 op=ALU.add, inc="DVE")
                sc = 0.125 if which == 0 else 1.0
                ae = pb.s("activation", out=TMP1, in_=QR, func=AF.Exp, scale=sc,
                          w=[("DVE", rope_done)], inc="ACT")
                ae_prev = ("ACT", ae)
                pb.v("tensor_scalar_min", out=TMP1, in0=TMP1, scalar1=1.0, w=[("ACT", ae)])
                pb.v("tensor_scalar", out=RT, in0=QR, scalar1=sc, scalar2=1.0,
                     op0=ALU.mult, op1=ALU.add)
                if which == 0:
                    fv = pb.v("tensor_tensor", out=HB[par], in0=RT, in1=TMP1,
                              op=ALU.max, w=[("PE", hb_free2[par])], inc="DVE")
                    pe_last, ev = emit_tok_to_ft(pb, consts, HB[par], QFT, tc, 6 + par,
                                                 first_wait=[("DVE", fv)],
                                                 evac_prev=tr_evac[par])
                    hb_free2[par] = pe_last
                    tr_evac[par] = ev
                else:
                    fv = pb.v("tensor_tensor", out=chunks(KF, TC)[:, tc], in0=RT,
                              in1=TMP1, op=ALU.max, inc="DVE")
            else:
                out_view = chunks(VEXT1, TC)[:, tc].rearrange("p (h e) -> p h e", h=NH)[:, :, 0:64]
                ev = pb.s("copy", out=out_view, in_=ps.rearrange("p (h e) -> p h e", h=NH),
                          w=[("PE", mm), ("DVE", const_v) if tc == 0 else None], inc="ACT")
                evac_bank[bank] = ("ACT", ev)
    qkv_done = ("ACT", pb.cnt["ACT"])
    qkv_dve = ("DVE", pb.cnt["DVE"])
    pe_after_qkv = ("PE", pb.cnt["PE"])
    wo_ld = pb.dma(chunks(WOUTC, DC), io["w_out_c"].rearrange("(c p) f -> p c f", p=128),
                   "LW", cast=True, w=[pe_after_qkv])

    if C_UPTO < 3:
        _c_finish(pb, XC, xin_r, xout_r, w=[qkv_done, qkv_dve])
        return
    kfv = chunks(KF, TC)
    vev = VEXT1.rearrange("p (tc h e) -> p tc h e", tc=TC, h=NH)
    kv_ps = pb.PS(0, 4)
    kvpsoff = lambda h: (h % 4) * 512 + (h // 4) * 66
    kvoff = lambda h: ((h // 2) // 7) * 512 + ((h // 2) % 7) * 65
    mm = None
    for h in range(NH):
        base = 64 * (h % 2)
        out = kv_ps[base:base + 64, kvpsoff(h):kvpsoff(h) + 65]
        for kt in range(TC):
            wl = []
            if h == 0 and kt == 0:
                wl = [qkv_done, qkv_dve]
            mm = pb.t("matmul", out, kfv[:, kt, h * 64:(h + 1) * 64],
                      vev[:, kt, h, :], start=(kt == 0), stop=(kt == TC - 1),
                      w=wl, inc="PE" if (h == NH - 1 and kt == TC - 1) else None)
    kh1 = kl1 = None
    for h in range(NH):
        base = 64 * (h % 2)
        src_ = kv_ps[base:base + 64, kvpsoff(h):kvpsoff(h) + 65]
        dsth = KVH[base:base + 64, kvoff(h):kvoff(h) + 65]
        dstl = KVL[base:base + 64, kvoff(h):kvoff(h) + 65]
        kh1 = pb.s("copy", out=dsth, in_=src_,
                   w=[("PE", mm)] if h == 0 else [], inc="ACT")
        kl1 = pb.v("tensor_tensor", out=dstl, in0=src_, in1=dsth,
                   op=ALU.subtract, w=[("ACT", kh1)], inc="DVE")

    if C_UPTO < 4:
        _c_finish(pb, XC, xin_r, xout_r, w=[("DVE", kl1)])
        return
    zoff = lambda h: (h % 4) * 512 + (h // 4) * 66
    z_free = {0: None}
    tr_evac2 = [0, 0]
    hb_free3 = [0, 0]
    for tc in range(TC):
        par = tc % 2
        zb = 0
        zp = pb.PS(zb, 4)
        mm = {}
        for h in range(NH):
            base = 64 * (h % 2)
            qsl = chunks(QFT, DC)[base:base + 64, h // 2, tc * 128:(tc + 1) * 128]
            out = zp[:, zoff(h):zoff(h) + 65]
            wl = []
            if h == 0:
                wl = [("DVE", kl1), z_free[zb]]
            pb.t("matmul", out, qsl, KVH[base:base + 64, kvoff(h):kvoff(h) + 65],
                 start=True, stop=False, w=wl)
            mm[h] = pb.t("matmul", out, qsl,
                         KVL[base:base + 64, kvoff(h):kvoff(h) + 65],
                         start=False, stop=True, inc="PE" if h == NH - 1 else None)
        last = None
        if C3SUB >= 3:
            for h in range(NH):
                dn = pb.v("tensor_scalar_max", out=cols["DCOL"],
                          in0=zp[:, zoff(h) + 64:zoff(h) + 65], scalar1=1e-6,
                          w=([("PE", mm[NH - 1])] if h == 0 else []) +
                            ([("ACT", last)] if last else []), inc="DVE")
                rc = pb.v("reciprocal", out=cols["RCC"], in_=cols["DCOL"],
                          w=[("DVE", dn)], inc="DVE")
                sc_ap = cols["RCC"] if C_UPTO >= 5 else 1.0
                last = pb.s("activation", out=chunks(OTOK, TC)[:, tc][:, h * 64:(h + 1) * 64],
                            in_=zp[:, zoff(h):zoff(h) + 64], func=AF.Copy,
                            scale=sc_ap, w=[("DVE", rc)] + ([wo_ld] if tc == 0 and h == 0 else []),
                            inc="ACT")
        else:
            for h in range(NH):
                last = pb.s("copy", out=chunks(OTOK, TC)[:, tc][:, h * 64:(h + 1) * 64],
                            in_=zp[:, zoff(h):zoff(h) + 64],
                            w=[("PE", mm[NH - 1])] if h == 0 else [], inc="ACT")
        z_free[zb] = ("ACT", last)
        if C3SUB >= 2:
            fv = pb.v("tensor_copy", out=HB[par], in_=chunks(OTOK, TC)[:, tc],
                      w=[("ACT", last), ("PE", hb_free3[par])], inc="DVE")
            pe_last, ev = emit_tok_to_ft(pb, consts, HB[par], OT1, tc, 6 + par,
                                         first_wait=[("DVE", fv)], evac_prev=tr_evac2[par])
            hb_free3[par] = pe_last
            tr_evac2[par] = ev
    ot_done = ("ACT", pb.cnt["ACT"])

    if C_UPTO < 5:
        _c_finish(pb, XC, xin_r, xout_r, w=[ot_done])
        return
    ev_prev = {0: None, 2: None}
    xo_free = [None, None]
    otc = chunks(OT1, DC)
    woc = chunks(WOUTC, DC)
    for tc in range(TC):
        bank = 0 if tc % 2 == 0 else 2
        ps = pb.PS(bank, 2)
        mm = None
        for k in range(DC):
            for n in range(2):
                wl = []
                if k == 0 and n == 0:
                    wl = [ot_done, wo_ld,
                          ("DVE", ev_prev[bank]) if ev_prev[bank] else None]
                mm = pb.t("matmul", ps[:, n * 512:(n + 1) * 512],
                          otc[:, k, tc * 128:(tc + 1) * 128], woc[:, k, n * 512:(n + 1) * 512],
                          start=(k == 0), stop=(k == DC - 1), w=wl,
                          inc="PE" if (k == DC - 1 and n == 1) else None)
        par = tc % 2
        xc = XC[par]
        ldw = [xo_free[par] or qkv_dve, ot_done]
        ldv = pb.dma(xc, xin_r[tc], "LD", w=ldw)
        addv = pb.v("tensor_tensor", out=xc, in0=xc, in1=ps, op=ALU.add,
                    w=[ldv, ("PE", mm)], inc="DVE")
        ev_prev[bank] = addv
        xo_free[par] = pb.dma(xout_r[tc], xc, "ST", w=[("DVE", addv)])
    pb.wait("sync", [("ST", pb.cnt["ST"])])


def prep_phase_c(inputs):
    d = {"w_qkv_c": np.ascontiguousarray(np.asarray(inputs["L1_attn_W_qkv"])),
         "w_out_c": np.ascontiguousarray(np.asarray(inputs["L1_attn_W_out"]))}
    d.update(make_consts("C"))
    return d



def emit_phase_d(pb, io, consts, al):
    cols = al.cols(["SQC", "SMC", "NEGMN", "EX2", "M2", "VARC", "SDC", "RSTD",
                    "LMX", "NEG", "SUME", "RS"])
    SQ = al.r("SQ", D)
    HB = [al.r("HB0", 512, BF16), al.r("HB1", 512, BF16)]
    MK = al.r("MK", 8)
    COEFD = al.r("COEFD", 128)
    B1S = al.r("B1S", 128)
    B1C = al.r("B1C", 32)
    B2ROW = al.r("B2ROW", 1024)
    B2HI = al.r("B2HI", 512, BF16)
    B2LO = al.r("B2LO", 512, BF16)
    XD = al.r("XD", 16384)
    XFT = al.r("XFT", 8192, BF16)
    H1T = al.r("H1T", 16384, BF16)
    WS1 = [al.r("WS1a", 2048, BF16), al.r("WS1b", 2048, BF16)]
    WS2 = [al.r("WS2a", 512, BF16), al.r("WS2b", 512, BF16)]
    assert al.off <= ARENA_WORDS, al.off

    xin_r = io["x_in"].rearrange("(tc p) d -> tc p d", p=128)
    xout_r = io["x_out"].rearrange("(tc p) d -> tc p d", p=128)
    GB = al.r("GB", 32, BF16)
    g_ld = pb.dma(GB.rearrange("p (c f) -> p c f", c=DC),
                  io["gate_d"].rearrange("(c p) f -> p c f", p=128), "LW", cast=True)

    xd_ld = pb.dma(chunks(XD, TC), io["x_in"].rearrange("(tc p) d -> p tc d", p=128), "LD")
    scratch = {"XC": None, "SQ": SQ, "HB": HB}
    xft_done = emit_ln_to_ft(pb, cols, consts, XFT, scratch, kind="ln",
                             src_sbuf=XD, pbank=6, extra_first_wait=xd_ld)
    xftc = chunks(XFT, DC)

    gate_free = None
    for tc in range(TC):
        ps = pb.PS(4, 1)
        mm = None
        for k in range(DC):
            wl = []
            if k == 0:
                wl = [g_ld, xft_done, gate_free]
            mm = pb.t("matmul", ps[:, 0:8], xftc[:, k, tc * 128:(tc + 1) * 128],
                      GB.rearrange("p (c f) -> p c f", c=DC)[:, k, :],
                      start=(k == 0), stop=(k == DC - 1), w=wl,
                      inc="PE" if k == DC - 1 else None)
        lm = pb.v("tensor_reduce", out=cols["LMX"], in_=ps[:, 0:8], axis=AX.X,
                  op=ALU.max, w=[("PE", mm)], inc="DVE")
        ng = pb.v("tensor_scalar_mul", out=cols["NEG"], in0=cols["LMX"], scalar1=-1.0,
                  inc="DVE")
        ea = pb.s("activation", out=MK, in_=ps[:, 0:8], func=AF.Exp,
                  bias=cols["NEG"], accum_out=cols["SUME"], w=[("DVE", ng)], inc="ACT")
        pb.v("reciprocal", out=cols["RS"], in_=cols["SUME"], w=[("ACT", ea)])
        pb.v("tensor_single_scalar", out=MK, in_=ps[:, 0:8], scalar=cols["LMX"],
             op=ALU.is_equal, w=[("ACT", ea)])
        gf = pb.v("tensor_single_scalar", out=COEFD[:, tc * 8:(tc + 1) * 8],
                  in_=MK, scalar=cols["RS"], op=ALU.mult, inc="DVE")
        gate_free = ("DVE", gf)

    h1t = H1T.rearrange("p (c t) -> p c t", c=32)
    tail = gate_free
    wfree = [gate_free, gate_free]
    for e in range(8):
        b1l = pb.dma(B1S.rearrange("p f -> p f")[0:32, :], io["b1_d"][e].rearrange("(c f) -> c f", f=128), "LD", w=[tail])
        bt = pb.t("transpose", pb.PS(6, 1)[0:128, 0:32], B1S[0:32, 0:128],
                  consts["ident_f"][0:32, 0:32], w=[b1l, tail], inc="PE")
        bc = pb.v("tensor_copy", out=B1C, in_=pb.PS(6, 1)[:, 0:32], w=[("PE", bt)], inc="DVE")
        b2l = pb.dma(B2ROW[0:1, :], io["b2_d"][e][None, :], "LD", w=[tail])
        bh = pb.v("tensor_copy", out=B2HI[0:1, 0:1024], in_=B2ROW[0:1, :], w=[b2l], inc="DVE")
        bl = pb.v("tensor_tensor", out=B2LO[0:1, 0:1024], in0=B2ROW[0:1, :],
                  in1=B2HI[0:1, 0:1024], op=ALU.subtract, w=[("DVE", bh)], inc="DVE")
        w1r = io["w1_d"][e]
        w2r = io["w2_d"][e].rearrange("(c p) f -> p c f", p=128)
        for half in range(2):
            th0 = half * 1024
            ups = emit_up(pb, consts, w1r, xftc, DC, 32, h1t, th0, 1024,
                          None, WS1, wfree, extra_w=[("DVE", bl), xft_done, tail],
                          entry=tail, bias_cols=B1C, bias_w=("DVE", bc))
            up_last = ups[-1]
            for q in range(2):
                qt0 = th0 + q * 512
                bank_of = lambda tci: 2 * tci
                mmq = {}
                for hc in range(32):
                    par = hc % 2
                    wld = pb.dma(WS2[par][:, 0:1024],
                                 w2r[:, hc, :], "LW", cast=True, w=[wfree[par]])
                    wfree[par] = None
                    for tci in range(4):
                        tc = qt0 // 128 + tci
                        ps = pb.PS(bank_of(tci), 2)
                        mm = None
                        for n in range(2):
                            wl = []
                            if n == 0:
                                wl = [wld]
                                if hc < 2:
                                    wl += [up_last, tail]
                            mm = pb.t("matmul", ps[:, n * 512:(n + 1) * 512],
                                      h1t[:, hc, (qt0 - th0) + tci * 128:(qt0 - th0) + (tci + 1) * 128],
                                      WS2[par][:, n * 512:(n + 1) * 512],
                                      start=(hc == 0), stop=False, w=wl,
                                      inc="PE" if n == 1 else None)
                        mmq[tci] = mm
                    wfree[par] = ("PE", mmq[3])
                last_ev = None
                for tci in range(4):
                    tc = qt0 // 128 + tci
                    ps = pb.PS(bank_of(tci), 2)
                    mm = None
                    for n in range(2):
                        pb.t("matmul", ps[:, n * 512:(n + 1) * 512],
                             consts["ones_bf"][0:1, 0:128],
                             B2HI[0:1, n * 512:(n + 1) * 512],
                             start=False, stop=False,
                             w=[("DVE", bl)] if (tci == 0 and n == 0) else [])
                        mm = pb.t("matmul", ps[:, n * 512:(n + 1) * 512],
                                  consts["ones_bf"][0:1, 0:128],
                                  B2LO[0:1, n * 512:(n + 1) * 512],
                                  start=False, stop=True,
                                  inc="PE" if n == 1 else None)
                    coef = COEFD[:, tc * 8 + e:tc * 8 + e + 1]
                    xslice = chunks(XD, TC)[:, tc]
                    ev = pb.v("scalar_tensor_tensor", out=xslice, in0=ps,
                              scalar=coef, in1=xslice, op0=ALU.mult, op1=ALU.add,
                              w=[("PE", mm)], inc="DVE")
                    last_ev = ("DVE", ev)
                tail = last_ev

    st = None
    for tc in range(TC):
        st = pb.dma(xout_r[tc], chunks(XD, TC)[:, tc], "ST", w=[tail])
    pb.wait("sync", [("ST", pb.cnt["ST"])])


def prep_phase_d(inputs):
    w1 = np.asarray(inputs["L1_ffn_w1"])
    d = {"gate_d": np.ascontiguousarray(np.asarray(inputs["L1_ffn_gate_w"])),
         "w1_d": np.stack([pm_tile(w1[e]) for e in range(8)]),
         "b1_d": np.ascontiguousarray(np.asarray(inputs["L1_ffn_b1"])),
         "w2_d": np.ascontiguousarray(np.asarray(inputs["L1_ffn_w2"])),
         "b2_d": np.ascontiguousarray(np.asarray(inputs["L1_ffn_b2"]))}
    d.update(make_consts("D"))
    return d



def make_consts(phase):
    c = {}
    c["ident_bf"] = np.eye(128).astype(ml_dtypes.bfloat16)
    c["ident_f"] = np.eye(128, dtype=np.float32)
    c["ones_row"] = np.ones((1, 128), np.float32)
    c["ones_bf"] = np.ones((128, 128), np.float32).astype(ml_dtypes.bfloat16)
    if phase == "B":
        i = np.arange(128)
        c["mask"] = np.where(i[:, None] >= i[None, :], 0.0, -1e30).astype(np.float32)
        c["maskT"] = np.where(i[None, :] >= i[:, None], 0.0, -1e30).astype(np.float32)
    if phase == "C":
        hd = HD
        inv_freq = 1.0 / (500000.0 ** (np.arange(0, hd, 2, dtype=np.float32) / hd))
        t = np.arange(T, dtype=np.float32)
        freqs = t[:, None] * inv_freq[None, :]
        emb = np.concatenate([freqs, freqs], axis=-1)
        c["cos"] = np.cos(emb).astype(np.float32)
        c["sin"] = np.sin(emb).astype(np.float32)
    return c


def prep_phase_a(inputs):
    W = np.asarray(inputs["L0_attn_W_qkv"])
    Wr = np.ascontiguousarray(W.reshape(1024, 16, 64, 3).transpose(0, 3, 1, 2).reshape(1024, 3072))
    d = {"w_qkv_a": Wr,
         "w_out_a": np.ascontiguousarray(np.asarray(inputs["L0_attn_W_out"]))}
    d.update(make_consts("A"))
    return d


_PROG_CACHE = {}
_ALLOC_MAP = {}


def build_phase(phase):
    if phase in _PROG_CACHE:
        return _PROG_CACHE[phase]
    pb = PB()
    for s in ("LD", "LWa", "LWb", "LWc", "LWd", "PE", "ACT", "DVE", "ST", "SP"):
        pb.sem(s)
    pb.arena_init()
    io = {}
    io["x_in"] = pb.dram("x_in", [T, D])
    io["x_out"] = pb.dram("x_out", [T, D], kind="ExternalOutput")
    io["ident_bf"] = pb.dram("ident_bf", [128, 128], BF16)
    io["ident_f"] = pb.dram("ident_f", [128, 128])
    io["ones_row"] = pb.dram("ones_row", [1, 128])
    io["ones_bf"] = pb.dram("ones_bf", [128, 128], BF16)
    al = Alloc(pb, 0)
    if phase == "A":
        io["w_qkv_a"] = pb.dram("w_qkv_a", [1024, 3072])
        io["w_out_a"] = pb.dram("w_out_a", [1024, 1024])
        consts = emit_consts(pb, io, al)
        emit_phase_a(pb, io, consts, al)
    elif phase == "B":
        io["mask"] = pb.dram("mask", [128, 128])
        io["maskT"] = pb.dram("maskT", [128, 128])
        io["n1_w"] = pb.dram("n1_w", [1, D])
        io["n2_w"] = pb.dram("n2_w", [1, D])
        io["qn_w"] = pb.dram("qn_w", [1, HD])
        io["kn_w"] = pb.dram("kn_w", [1, HD])
        for nm in ("wq_b", "wk_b", "wv_b", "wo_b"):
            io[nm] = pb.dram(nm, [1024, 1024])
        io["e3_down_t"] = pb.dram("e3_down_t", [128, 2, 8, 512])
        io["gate_b"] = pb.dram("gate_b", [1024, 4])
        for nm in ("e0_up_t", "e1_w1_t", "e1_w2_t", "e2_l1_t", "e3_up_t"):
            io[nm] = pb.dram(nm, [128, 4, 8, 512])
        for nm in ("e0_down_t", "e1_down_t", "e2_l4_t", "e3_out_t"):
            io[nm] = pb.dram(nm, [2048, 1024])
        for nm in ("e2_l2_t", "e2_l3_t"):
            io[nm] = pb.dram(nm, [128, 8, 16, 256])
        io["xint_scr"] = pb.dram("xint_scr", [T, D], kind="Internal")
        consts = emit_consts(pb, io, al, need=("mask",))
        emit_phase_b(pb, io, consts, al)
    elif phase == "C":
        io["w_qkv_c"] = pb.dram("w_qkv_c", [1024, 3072])
        io["w_out_c"] = pb.dram("w_out_c", [1024, 1024])
        io["cos"] = pb.dram("cos", [T, HD])
        io["sin"] = pb.dram("sin", [T, HD])
        consts = emit_consts(pb, io, al, need=("rope",))
        emit_phase_c(pb, io, consts, al)
    elif phase == "D":
        io["gate_d"] = pb.dram("gate_d", [1024, 8])
        io["w1_d"] = pb.dram("w1_d", [8, 128, 8, 8, 512])
        io["b1_d"] = pb.dram("b1_d", [8, 4096])
        io["w2_d"] = pb.dram("w2_d", [8, 4096, 1024])
        io["b2_d"] = pb.dram("b2_d", [8, 1024])
        consts = emit_consts(pb, io, al)
        emit_phase_d(pb, io, consts, al)
    else:
        raise NotImplementedError(phase)
    _ALLOC_MAP.clear()
    _ALLOC_MAP.update(al.map)
    nc = pb.finalize()
    _PROG_CACHE[phase] = nc
    return nc


def build_fused():
    if "FUSED" in _PROG_CACHE:
        return _PROG_CACHE["FUSED"]
    pb = PB()
    for s in ("LD", "LWa", "LWb", "LWc", "LWd", "PE", "ACT", "DVE", "ST", "SP"):
        pb.sem(s)
    pb.arena_init()
    io = {}
    io["ident_bf"] = pb.dram("ident_bf", [128, 128], BF16)
    io["ident_f"] = pb.dram("ident_f", [128, 128])
    io["ones_row"] = pb.dram("ones_row", [1, 128])
    io["ones_bf"] = pb.dram("ones_bf", [128, 128], BF16)
    io["mask"] = pb.dram("mask", [128, 128])
    io["maskT"] = pb.dram("maskT", [128, 128])
    io["cos"] = pb.dram("cos", [T, HD])
    io["sin"] = pb.dram("sin", [T, HD])
    io["w_qkv_a"] = pb.dram("w_qkv_a", [1024, 3072])
    io["w_out_a"] = pb.dram("w_out_a", [1024, 1024])
    io["n1_w"] = pb.dram("n1_w", [1, D])
    io["n2_w"] = pb.dram("n2_w", [1, D])
    io["qn_w"] = pb.dram("qn_w", [1, HD])
    io["kn_w"] = pb.dram("kn_w", [1, HD])
    for nm in ("wq_b", "wk_b", "wv_b", "wo_b"):
        io[nm] = pb.dram(nm, [1024, 1024])
    io["e3_down_t"] = pb.dram("e3_down_t", [128, 2, 8, 512])
    io["gate_b"] = pb.dram("gate_b", [1024, 4])
    for nm in ("e0_up_t", "e1_w1_t", "e1_w2_t", "e2_l1_t", "e3_up_t"):
        io[nm] = pb.dram(nm, [128, 4, 8, 512])
    for nm in ("e0_down_t", "e1_down_t", "e2_l4_t", "e3_out_t"):
        io[nm] = pb.dram(nm, [2048, 1024])
    for nm in ("e2_l2_t", "e2_l3_t"):
        io[nm] = pb.dram(nm, [128, 8, 16, 256])
    io["w_qkv_c"] = pb.dram("w_qkv_c", [1024, 3072])
    io["w_out_c"] = pb.dram("w_out_c", [1024, 1024])
    io["gate_d"] = pb.dram("gate_d", [1024, 8])
    io["w1_d"] = pb.dram("w1_d", [8, 128, 8, 8, 512])
    io["b1_d"] = pb.dram("b1_d", [8, 4096])
    io["w2_d"] = pb.dram("w2_d", [8, 4096, 1024])
    io["b2_d"] = pb.dram("b2_d", [8, 1024])
    x_in = pb.dram("x_in", [SEQ_PER_CORE, T, D])
    x_out = pb.dram("x_out", [SEQ_PER_CORE, T, D], kind="ExternalOutput")
    xa = pb.dram("xa", [SEQ_PER_CORE, T, D], kind="Internal")
    xb = pb.dram("xb", [SEQ_PER_CORE, T, D], kind="Internal")
    xc = pb.dram("xc", [SEQ_PER_CORE, T, D], kind="Internal")
    xint = pb.dram("xint4", [SEQ_PER_CORE, T, D], kind="Internal")
    chain = {"A": (x_in, xa), "B": (xa, xb), "C": (xb, xc), "D": (xc, x_out)}
    needs = {"A": (), "B": ("mask",), "C": ("rope",), "D": ()}
    emitters = {"A": emit_phase_a, "B": emit_phase_b, "C": emit_phase_c,
                "D": emit_phase_d}
    for phase in "ABCD":
        for s in range(SEQ_PER_CORE):
            al = Alloc(pb, 0)
            consts = emit_consts(pb, io, al, need=needs[phase])
            pio = dict(io)
            pio["x_in"] = chain[phase][0][s]
            pio["x_out"] = chain[phase][1][s]
            if phase == "B":
                pio["xint_scr"] = xint[s]
            emitters[phase](pb, pio, consts, al)
            pb.section_barrier()
    nc = pb.finalize()
    _PROG_CACHE["FUSED"] = nc
    return nc


FUSED = True


def kernel(**inputs):
    if FUSED:
        return kernel_fused(**inputs)
    return kernel_unfused(**inputs)


def kernel_fused(**inputs):
    X = np.asarray(inputs["L0_attn_X"])
    prep = {}
    for p in (prep_phase_a(inputs), prep_phase_b(inputs),
              prep_phase_c(inputs), prep_phase_d(inputs)):
        prep.update(p)
    prep.update(make_consts("B"))
    prep.update(make_consts("C"))
    nc = build_fused()
    Xs = X.reshape(NCORES, SEQ_PER_CORE, T, D)
    in_maps = [dict(prep, x_in=np.ascontiguousarray(Xs[c])) for c in range(NCORES)]
    res = run_bass_kernel_spmd(nc, in_maps, core_ids=list(range(NCORES)))
    out = np.stack([res.results[c]["x_out"] for c in range(NCORES)])
    return np.ascontiguousarray(out.reshape(32, T, D))


def kernel_unfused(**inputs):
    X = np.asarray(inputs["L0_attn_X"])
    preps = {"A": prep_phase_a(inputs), "B": prep_phase_b(inputs),
             "C": prep_phase_c(inputs), "D": prep_phase_d(inputs)}
    cur = X.reshape(NCORES, SEQ_PER_CORE, T, D)
    for phase in "ABCD":
        nc = build_phase(phase)
        outs = np.empty_like(cur)
        for s in range(SEQ_PER_CORE):
            in_maps = [dict(preps[phase], x_in=np.ascontiguousarray(cur[c, s]))
                       for c in range(NCORES)]
            res = run_bass_kernel_spmd(nc, in_maps, core_ids=list(range(NCORES)))
            for c in range(NCORES):
                outs[c, s] = res.results[c]["x_out"]
        cur = outs
    return np.ascontiguousarray(cur.reshape(32, T, D))


# version 05
# speedup vs baseline: 1.2985x; 1.0956x over previous
import numpy as np
import ml_dtypes
from contextlib import ExitStack

import concourse.bass as bass
import concourse.mybir as mybir
from concourse.bass_utils import run_bass_kernel_spmd

F32 = mybir.dt.float32
BF16 = mybir.dt.bfloat16
AF = mybir.ActivationFunctionType
ALU = mybir.AluOpType
AX = mybir.AxisListType

T = 2048
TC = 16
D = 1024
DC = 8
NH = 16
HD = 64
NCORES = 8
SEQ_PER_CORE = 4

ARENA_WORDS = 52992



class PB:
    ENGINES = ("sync", "gpsimd", "tensor", "scalar", "vector")

    def __init__(self):
        self.nc = bass.Bass("TRN2", target_bir_lowering=False)
        self.ctx = ExitStack()
        self.q = {e: [] for e in self.ENGINES}
        self.sems = {}
        self.cnt = {}
        self._wdb = {}
        self._rdb = {}
        self._watermark = {}
        self._wm_all = {}

    def sem(self, name):
        self.sems[name] = self.ctx.enter_context(self.nc.semaphore(name))
        self.cnt[name] = 0

    def dram(self, name, shape, dt=F32, kind="ExternalInput"):
        return self.nc.dram_tensor(name, list(shape), dt, kind=kind).ap()

    def arena_init(self):
        self._arena = self.ctx.enter_context(
            self.nc.sbuf_tensor("arena", [128, ARENA_WORDS], F32))
        self._psum = self.ctx.enter_context(
            self.nc.psum_tensor("ps", [128, 4096], F32))

    def R(self, off, words, dt=F32):
        assert off >= 0 and off + words <= ARENA_WORDS, (off, words)
        ap = self._arena.ap()[:, off:off + words]
        if dt == BF16:
            ap = ap.bitcast(BF16)
        return ap

    def PS(self, bank, nbanks=1):
        return self._psum.ap()[:, bank * 512:(bank + nbanks) * 512]

    _WRITE_KEYS = ("out", "dest", "accum_out")
    _READ_KEYS = ("in_", "in0", "in1", "scalar", "scalar1", "scalar2", "bias",
                  "scale", "data0", "data1", "mask", "on_true", "on_false",
                  "data", "initial")
    _AUTO_ENGS = ("vector", "scalar")

    @staticmethod
    def _interval(ap):
        import concourse.mybir as _mb
        esz = _mb.dt.size(ap.dtype)
        pairs = list(ap.ap)
        pstride = pairs[0][0]
        pcount = pairs[0][1]
        off = ap.offset
        plo = off // pstride if pstride else 0
        off_in = off - plo * pstride
        span = 1
        lo = 0
        hi = 0
        for step, count in pairs[1:]:
            if step >= 0:
                hi += step * (count - 1)
            else:
                lo += step * (count - 1)
        return (ap.tensor.name, (off_in + lo) * esz, (off_in + hi + 1) * esz,
                plo, plo + pcount)

    def _deps(self, eng, args, kwargs):
        from concourse.ap import AP as _AP
        reads, writes = [], []
        for k, v in kwargs.items():
            if isinstance(v, _AP):
                (writes if k in self._WRITE_KEYS else reads).append(v)
        for v in args:
            if isinstance(v, _AP):
                writes.append(v)
        need = 0
        rr = [self._interval(a) for a in reads]
        ww = [self._interval(a) for a in writes]

        def overlap(a, b):
            return (a[0] == b[0] and a[1] < b[2] and b[1] < a[2]
                    and a[3] < b[4] and b[3] < a[4])

        wdb = self._wdb.setdefault(eng, {})
        rdb = self._rdb.setdefault(eng, {})
        for iv in rr:
            for rec in wdb.get(iv[0], ()):
                if overlap(iv, rec[0]):
                    need = max(need, rec[1])
        for iv in ww:
            for rec in wdb.get(iv[0], ()):
                if overlap(iv, rec[0]):
                    need = max(need, rec[1])
            for rec in rdb.get(iv[0], ()):
                if overlap(iv, rec[0]):
                    need = max(need, rec[1])
        ordinal = self.cnt[self._eng_sem[eng]] + 1
        for iv in ww:
            lst = wdb.setdefault(iv[0], [])
            lst[:] = [r for r in lst if not (r[0] == iv)]
            lst.append((iv, ordinal))
            if len(lst) > 256:
                del lst[0]
        for iv in rr:
            lst = rdb.setdefault(iv[0], [])
            lst[:] = [r for r in lst if not (r[0] == iv)]
            lst.append((iv, ordinal))
            if len(lst) > 256:
                del lst[0]
        return need

    _eng_sem = {"vector": "DVE", "scalar": "ACT"}

    def op(self, eng, method, *args, w=(), inc=None, **kwargs):
        auto = eng in self._AUTO_ENGS
        wm_all = self._wm_all.setdefault(eng, {})
        wl = []
        best = {}
        for item in w:
            if item is None:
                continue
            sname, val = item
            if val and val > 0 and val > best.get(sname, 0):
                best[sname] = val
        for sname, val in best.items():
            if val > wm_all.get(sname, 0):
                wl.append((sname, val))
                wm_all[sname] = val
        if auto:
            sem = self._eng_sem[eng]
            need = self._deps(eng, args, kwargs)
            wm = max(self._watermark.get(eng, 0), wm_all.get(sem, 0))
            if need > wm:
                wl.append((sem, need))
                wm = need
                wm_all[sem] = need
            self._watermark[eng] = wm
            assert inc is None or inc == sem or inc == (sem, 1), inc
            self.cnt[sem] += 1
            ret = self.cnt[sem]
            inc = (sem, 1)
        elif inc is not None:
            if isinstance(inc, str):
                inc = (inc, 1)
            self.cnt[inc[0]] += inc[1]
            ret = self.cnt[inc[0]]
        else:
            ret = None
        self.q[eng].append((method, args, kwargs, tuple(wl), inc))
        return ret

    def t(self, m, *a, **k):
        return self.op("tensor", m, *a, **k)

    def v(self, m, *a, **k):
        return self.op("vector", m, *a, **k)

    def s(self, m, *a, **k):
        return self.op("scalar", m, *a, **k)

    def dma(self, out, in_, incsem, w=(), cast=False, **kw):
        eng = "gpsimd" if cast else "sync"
        if incsem == "LW":
            self._lw_rr = (getattr(self, "_lw_rr", -1) + 1) % 4
            incsem = "LW" + "abcd"[self._lw_rr]
        w = list(w) + [(incsem, self.cnt[incsem])]
        v = self.op(eng, "dma_start", w=w, inc=(incsem, 16), out=out, in_=in_, **kw)
        return (incsem, v)

    def wait(self, eng, w):
        wm_all = self._wm_all.setdefault(eng, {})
        wl = []
        for item in w:
            if not item or not item[1]:
                continue
            sname, val = item
            if val > wm_all.get(sname, 0):
                wl.append((sname, val))
                wm_all[sname] = val
        if wl:
            self.q[eng].append(("__wait__", (), {}, tuple(wl), None))

    def section_barrier(self):
        cur = [(s, self.cnt[s]) for s in self.cnt]
        for eng in self.ENGINES:
            self.wait(eng, cur)
        self._wdb.clear()
        self._rdb.clear()
        for eng in self._AUTO_ENGS:
            self._watermark[eng] = self.cnt[self._eng_sem[eng]]

    def finalize(self):
        block = self.ctx.enter_context(self.nc.Block())
        sems = self.sems

        def mk(ops):
            def body(eng):
                for method, args, kwargs, waits, inc in ops:
                    for sname, val in waits:
                        eng.wait_ge(sems[sname], val)
                    if method == "__wait__":
                        continue
                    r = getattr(eng, method)(*args, **kwargs)
                    if inc is not None:
                        r.then_inc(sems[inc[0]], inc[1])
            return body

        for eng in self.ENGINES:
            if self.q[eng]:
                getattr(block, eng)(mk(self.q[eng]))
        self.ctx.close()
        return self.nc


def chunks(ap, n):
    return ap.rearrange("p (c t) -> p c t", c=n)



class Alloc:
    def __init__(self, pb, base=0):
        self.pb = pb
        self.off = base
        self.map = {}

    def r(self, name, words, dt=F32):
        ap = self.pb.R(self.off, words, dt)
        self.map[name] = (self.off, words)
        self.off += words
        return ap

    def cols(self, names):
        return {n: self.r("col_" + n, 4)[:, 0:1] for n in names}


def emit_consts(pb, io, al, need=()):
    c = {}
    c["scr"] = al.r("scr", 16)
    c["ident_bf"] = al.r("ident_bf", 64, BF16)
    pb.dma(c["ident_bf"], io["ident_bf"], "LD")
    if "lite" not in need:
        c["ident_f"] = al.r("ident_f", 128)
        c["ones_row"] = al.r("ones_row", 128)
        c["ones_bf"] = al.r("ones_bf", 64, BF16)
        pb.dma(c["ident_f"], io["ident_f"], "LD")
        pb.dma(c["ones_row"][0:1, :], io["ones_row"], "LD")
        pb.dma(c["ones_bf"], io["ones_bf"], "LD")
    if "mask" in need:
        c["mask"] = al.r("mask", 128)
        c["maskT"] = al.r("maskT", 128)
        pb.dma(c["mask"], io["mask"], "LD")
        pb.dma(c["maskT"], io["maskT"], "LD")
    if "rope" in need:
        c["cos"] = al.r("cos", 1024)
        c["sin"] = al.r("sin", 1024)
        pb.dma(chunks(c["cos"], TC), io["cos"].rearrange("(tc p) d -> p tc d", p=128), "LD")
        pb.dma(chunks(c["sin"], TC), io["sin"].rearrange("(tc p) d -> p tc d", p=128), "LD")
    c["_ld"] = ("LD", pb.cnt["LD"])
    return c



def emit_ln_to_ft(pb, cols, consts, out_ft, scratch, kind="ln", xin_r=None,
                  src_sbuf=None, gamma_rep=None, gamma_wait=None, eps=1e-5,
                  pbank=6, extra_first_wait=None, statarr=None):
    SMA = statarr[:, 0:64]
    SQA = statarr[:, 64:128]
    M2A = statarr[:, 128:192]
    VARA = statarr[:, 192:256]
    SDA = statarr[:, 256:320]
    sv = lambda a: a.rearrange("p (n s) -> p n s", s=4)[:, :, 0]
    xc_free = [None, None]
    last_stat = None
    for tc in range(TC):
        par = tc % 2
        if src_sbuf is not None:
            xc = chunks(src_sbuf, TC)[:, tc]
            v_w = [extra_first_wait] if (tc == 0 and extra_first_wait) else []
        else:
            xc = scratch["XC"][par]
            ldv = pb.dma(xc, xin_r[tc], "LD", w=[xc_free[par]])
            v_w = [ldv]
            if tc == 0 and extra_first_wait:
                v_w.append(extra_first_wait)
        s1 = pb.v("scalar_tensor_tensor", out=scratch["SQ"], in0=xc, scalar=1.0,
                  in1=xc, op0=ALU.mult, op1=ALU.mult,
                  accum_out=SQA[:, 4 * tc:4 * tc + 1], w=v_w, inc="DVE")
        if kind == "ln":
            s1 = pb.v("reduce_sum", out=SMA[:, 4 * tc:4 * tc + 1], in_=xc,
                      axis=AX.X, inc="DVE")
        xc_free[par] = ("DVE", s1)
    if kind == "ln":
        pb.v("tensor_scalar_mul", out=sv(SMA), in0=sv(SMA), scalar1=-1.0 / D)
        pb.v("tensor_scalar_mul", out=sv(SQA), in0=sv(SQA), scalar1=1.0 / D)
        pb.v("tensor_tensor", out=sv(M2A), in0=sv(SMA), in1=sv(SMA), op=ALU.mult)
        pb.v("tensor_tensor", out=sv(VARA), in0=sv(SQA), in1=sv(M2A), op=ALU.subtract)
        vv = pb.v("tensor_scalar_add", out=sv(VARA), in0=sv(VARA),
                  scalar1=float(eps), inc="DVE")
    else:
        vv = pb.v("tensor_scalar", out=sv(VARA), in0=sv(SQA), scalar1=1.0 / D,
                  scalar2=float(eps), op0=ALU.mult, op1=ALU.add, inc="DVE")
    av = pb.s("activation", out=sv(SDA), in_=sv(VARA), func=AF.Sqrt,
              w=[("DVE", vv)], inc="ACT")
    rv = pb.v("reciprocal", out=sv(VARA), in_=sv(SDA), w=[("ACT", av)], inc="DVE")
    evac_prev = [0, 0]
    hb_free = [0, 0]
    last_evac = 0
    for tc in range(TC):
        par = tc % 2
        if src_sbuf is not None:
            xc = chunks(src_sbuf, TC)[:, tc]
            v_w = []
        else:
            xc = scratch["XC"][par]
            ldv = pb.dma(xc, xin_r[tc], "LD", w=[xc_free[par]])
            v_w = [ldv]
        hb = scratch["HB"][par]
        wl = v_w + [("DVE", rv), ("PE", hb_free[par])]
        if kind == "ln":
            nv = pb.v("tensor_scalar", out=hb, in0=xc,
                      scalar1=SMA[:, 4 * tc:4 * tc + 1],
                      scalar2=VARA[:, 4 * tc:4 * tc + 1],
                      op0=ALU.add, op1=ALU.mult, w=wl, inc="DVE")
        else:
            gw = [gamma_wait] if (tc == 0 and gamma_wait) else []
            nv = pb.v("scalar_tensor_tensor", out=hb, in0=xc,
                      scalar=VARA[:, 4 * tc:4 * tc + 1], in1=gamma_rep,
                      op0=ALU.mult, op1=ALU.mult, w=wl + gw, inc="DVE")
        xc_free[par] = ("DVE", nv)
        pbk = pbank if par == 0 else pbank + 1
        ps_bf = pb.PS(pbk, 1).bitcast(BF16)
        pe_last = None
        for ci in range(DC):
            wlist = []
            if ci == 0:
                wlist = [("DVE", nv), ("ACT", evac_prev[par]),
                         consts["_ld"] if tc == 0 else None]
            pe_last = pb.t("transpose", ps_bf[:, ci * 128:(ci + 1) * 128],
                           hb[:, ci * 128:(ci + 1) * 128], consts["ident_bf"],
                           w=wlist, inc="PE" if ci == DC - 1 else None)
        hb_free[par] = pe_last
        out_view = chunks(out_ft, DC)[:, :, tc * 128:(tc + 1) * 128]
        last_evac = pb.s("copy", out=out_view, in_=chunks(ps_bf, DC),
                         w=[("PE", pe_last)], inc="ACT")
        evac_prev[par] = last_evac
    return ("ACT", last_evac)


def emit_tok_to_ft(pb, consts, tok_bf, ft, tc, pbank, first_wait=(), evac_prev=0):
    ps_bf = pb.PS(pbank, 1).bitcast(BF16)
    pe_last = None
    for ci in range(DC):
        wl = list(first_wait) + [("ACT", evac_prev)] if ci == 0 else []
        pe_last = pb.t("transpose", ps_bf[:, ci * 128:(ci + 1) * 128],
                       tok_bf[:, ci * 128:(ci + 1) * 128], consts["ident_bf"],
                       w=wl, inc="PE" if ci == DC - 1 else None)
    ev = pb.s("copy", out=chunks(ft, DC)[:, :, tc * 128:(tc + 1) * 128],
              in_=chunks(ps_bf, DC), w=[("PE", pe_last)], inc="ACT")
    return pe_last, ev


def emit_attention(pb, io, consts, cols, M4, NEGMH, QT, KT, VEXT, OT, ATB, ovl,
                   scale, causal, entry_free):
    LRW, RHIB, RLOB, RBC, OSCR = (ovl[k] for k in ("LRW", "RHIB", "RLOB", "RBC", "OSCR"))

    def head_sl(ft, h):
        base = 64 * (h % 2)
        return chunks(ft, DC)[base:base + 64, h // 2]

    s_free = {0: entry_free, 4: entry_free}
    o_free = None
    for h in range(NH):
        qh = head_sl(QT, h)
        kh = head_sl(KT, h)
        for qt in range(TC):
            bank = 0 if qt % 2 == 0 else 4
            ps = pb.PS(bank, 4)
            kmax = (qt + 1) * 128 if causal else T
            nblk = (kmax + 511) // 512
            mm = None
            for n in range(nblk):
                w0 = n * 512
                w1 = min(w0 + 512, kmax)
                wl = [s_free[bank]] if n == 0 else []
                mm = pb.t("matmul", ps[:, w0:w1],
                          qh[:, qt * 128:(qt + 1) * 128], kh[:, w0:w1],
                          start=True, stop=True, w=wl,
                          inc="PE" if n == nblk - 1 else None)
            if causal:
                madd = pb.v("tensor_tensor", out=ps[:, qt * 128:(qt + 1) * 128],
                            in0=ps[:, qt * 128:(qt + 1) * 128], in1=consts["mask"],
                            op=ALU.add, w=[("PE", mm)], inc="DVE")
            mx = pb.v("tensor_reduce", out=M4[:, 4 * qt:4 * qt + 1],
                      in_=ps[:, 0:kmax].rearrange("p (n s) -> p n s", s=4)[:, :, 0],
                      axis=AX.X, op=ALU.max, w=[("PE", mm)], inc="DVE")
            s_free[bank] = ("DVE", mx)
        vv = pb.v("tensor_reduce", out=cols["VARC"],
                  in_=M4.rearrange("p (n s) -> p n s", s=4)[:, :, 0],
                  axis=AX.X, op=ALU.max, inc="DVE")
        pt = pb.t("transpose", pb.PS(0, 1)[0:1, 0:128], cols["VARC"], consts["ident_f"],
                  w=[("DVE", vv), s_free[0]], inc="PE")
        mv = pb.v("tensor_reduce", out=cols["MHS"][0:1, :], in_=pb.PS(0, 1)[0:1, 0:128],
                  axis=AX.X, op=ALU.max, w=[("PE", pt)], inc="DVE")
        bc = pb.t("matmul", pb.PS(1, 1)[:, 0:1], consts["ones_row"][0:1, :],
                  cols["MHS"][0:1, 0:1], start=True, stop=True, w=[("DVE", mv)], inc="PE")
        ng = pb.v("tensor_scalar_mul", out=NEGMH, in0=pb.PS(1, 1)[:, 0:1],
                  scalar1=-scale, w=[("PE", bc)], inc="DVE")
        s_free[0] = ("DVE", ng)
        av_last = None
        g_free = {0: s_free[0], 1024: s_free[0]}
        atb_free2 = {}
        for kt in range(TC):
            par = kt % 2
            ps = pb.PS(0, 4)
            q0 = kt * 128 if causal else 0
            po = pb.PS(4, 4)
            vsl = chunks(VEXT, TC)[:, kt][:, h * 65:(h + 1) * 65]
            groups = []
            for g0 in (0, 1024):
                ga = max(q0, g0)
                gb = g0 + 1024
                if ga >= gb:
                    continue
                blks = []
                pos = ga
                while pos < gb:
                    w1 = min((pos // 512 + 1) * 512, gb)
                    blks.append((pos, w1))
                    pos = w1
                groups.append((g0, ga, gb, blks))
            mmg = {}
            madds = {}
            for g0, ga, gb, blks in groups:
                mm = None
                for i, (b0, b1) in enumerate(blks):
                    wl = [g_free[g0]] if i == 0 else []
                    mm = pb.t("matmul", ps[:, b0:b1],
                              kh[:, kt * 128:(kt + 1) * 128], qh[:, b0:b1],
                              start=True, stop=True, w=wl,
                              inc="PE" if i == len(blks) - 1 else None)
                mmg[g0] = mm
                if causal and g0 <= kt * 128 < gb:
                    madds[g0] = pb.v("tensor_tensor",
                                     out=ps[:, kt * 128:(kt + 1) * 128],
                                     in0=ps[:, kt * 128:(kt + 1) * 128],
                                     in1=consts["maskT"], op=ALU.add,
                                     w=[("PE", mm)], inc="DVE")
            exg = {}
            for g0, ga, gb, blks in groups:
                exw = [("PE", mmg[g0]), atb_free2.get((par, g0))]
                if g0 in madds:
                    exw.append(("DVE", madds[g0]))
                exg[g0] = pb.s("activation", out=ATB[par][:, ga:gb], in_=ps[:, ga:gb],
                               func=AF.Exp, bias=NEGMH, scale=scale, w=exw, inc="ACT")
                g_free[g0] = ("ACT", exg[g0])
            for g0, ga, gb, blks in groups:
                for i, (b0, b1) in enumerate(blks):
                    wl = [("ACT", exg[g0])] if i == 0 else []
                    if kt == 0 and i == 0 and g0 == 0 and o_free:
                        wl.append(o_free)
                    bank = b0 // 512
                    stop = (kt == 4 * bank + 3) if causal else (kt == TC - 1)
                    av_last = pb.t("matmul", po[0:65, b0:b1], vsl, ATB[par][:, b0:b1],
                                   start=(kt == 0), stop=stop, w=wl,
                                   inc="PE" if (kt == TC - 1 and i == len(blks) - 1) else None)
                    atb_free2[(par, g0)] = ("PE", av_last)
        s_free[0] = g_free[0]
        pb.wait("tensor", [g_free[1024]])
        po = pb.PS(4, 4)
        r1 = pb.v("reciprocal", out=LRW[64:65, 0:T], in_=po[64:65, :],
                  w=[("PE", av_last)], inc="DVE")
        r2 = pb.v("tensor_copy", out=RHIB[64:65, 0:T], in_=LRW[64:65, 0:T],
                  w=[("DVE", r1)], inc="DVE")
        r3 = pb.v("tensor_tensor", out=RLOB[64:65, 0:T], in0=LRW[64:65, 0:T],
                  in1=RHIB[64:65, 0:T], op=ALU.subtract, w=[("DVE", r2)], inc="DVE")
        bmm = None
        for n in range(4):
            wl = [("DVE", r3), s_free[0], consts["_ld"] if h == 0 else None] if n == 0 else []
            pb.t("matmul", pb.PS(0, 4)[0:64, n * 512:(n + 1) * 512],
                 consts["ones_bf"][64:65, 0:64], RHIB[64:65, n * 512:(n + 1) * 512],
                 start=True, stop=False, w=wl)
            bmm = pb.t("matmul", pb.PS(0, 4)[0:64, n * 512:(n + 1) * 512],
                       consts["ones_bf"][64:65, 0:64], RLOB[64:65, n * 512:(n + 1) * 512],
                       start=False, stop=True, inc="PE" if n == 3 else None)
        rc = pb.s("copy", out=RBC[0:64, :], in_=pb.PS(0, 4)[0:64, :],
                  w=[("PE", bmm)], inc="ACT")
        s_free[0] = ("ACT", rc)
        if h % 2 == 0:
            dest = chunks(OT, DC)[0:64, h // 2]
            ov = pb.v("tensor_tensor", out=dest, in0=po[0:64, :], in1=RBC[0:64, :],
                      op=ALU.mult, w=[("ACT", rc)], inc="DVE")
            o_free = ("DVE", ov)
        else:
            ov = pb.v("tensor_tensor", out=OSCR[0:64, 0:T], in0=po[0:64, :],
                      in1=RBC[0:64, :], op=ALU.mult, w=[("ACT", rc)], inc="DVE")
            dst = chunks(OT, DC)[64:128, h // 2]
            o_free = pb.dma(dst, OSCR[0:64, 0:T], "LD", w=[("DVE", ov)])
        s_free[4] = o_free
    return o_free



def emit_phase_a(pb, io, consts, al):
    cols = al.cols(["SQC", "SMC", "NEGMN", "EX2", "M2", "VARC", "SDC", "RSTD", "MHS"])
    STATA = al.r("STATA", 320)
    NEGMH = al.r("NEGMH", 4)[:, 0:1]
    M4 = al.r("M4", 64)
    SQ = al.r("SQ", D)
    XC = [al.r("XC0", D), al.r("XC1", D)]
    HB = [al.r("HB0", 512, BF16), al.r("HB1", 512, BF16)]
    H0T = al.r("H0T", 8192, BF16)
    WQKV = al.r("WQKV", 12288, BF16)
    QT = al.r("QT", 8192, BF16)
    KT = al.r("KT", 8192, BF16)
    VEXT = al.r("VEXT", 8320, BF16)
    ATB = [al.r("ATB0", 1024, BF16), al.r("ATB1", 1024, BF16)]
    assert al.off <= ARENA_WORDS, al.off
    h0t_off = al.map["H0T"][0]
    LRW = pb.R(h0t_off, 2048)
    RHIB = pb.R(h0t_off + 2048, 1024, BF16)
    RLOB = pb.R(h0t_off + 3072, 1024, BF16)
    RBC = pb.R(h0t_off + 4096, 2048)
    OSCR = pb.R(h0t_off + 6144, 1024, BF16)
    wq_off = al.map["WQKV"][0]
    OT = pb.R(wq_off, 8192, BF16)
    WOUT = pb.R(wq_off + 8192, 4096, BF16)

    wq1 = pb.dma(chunks(WQKV, DC), io["w_qkv_a"].rearrange("(c p) f -> p c f", p=128), "LW", cast=True)
    const_v = pb.v("memset",
                   VEXT.rearrange("p (tc h e) -> p tc h e", tc=TC, h=NH)[:, :, :, 64:65],
                   1.0, inc="DVE")
    xin_r = io["x_in"].rearrange("(tc p) d -> tc p d", p=128)

    scratch = {"XC": XC, "SQ": SQ, "HB": HB}
    h0t_done = emit_ln_to_ft(pb, cols, consts, H0T, scratch, kind="ln",
                             xin_r=xin_r, pbank=6, statarr=STATA)

    wqc = chunks(WQKV, DC)
    h0c = chunks(H0T, DC)
    evac_val = {0: None, 4: None}
    last_pe_v = None
    for which, dest in ((0, QT), (1, KT)):
        for oc in range(DC):
            bank = 0 if oc % 2 == 0 else 4
            ps = pb.PS(bank, 4)
            mm = None
            for k in range(DC):
                lhsT = wqc[:, k, which * 1024 + oc * 128: which * 1024 + (oc + 1) * 128]
                for n in range(4):
                    wl = []
                    if k == 0 and n == 0:
                        wl = [wq1, h0t_done,
                              ("ACT", evac_val[bank]) if evac_val[bank] else None]
                    mm = pb.t("matmul", ps[:, n * 512:(n + 1) * 512], lhsT,
                              h0c[:, k, n * 512:(n + 1) * 512],
                              start=(k == 0), stop=(k == DC - 1), w=wl,
                              inc="PE" if (k == DC - 1 and n == 3) else None)
            evac_val[bank] = pb.s("copy", out=chunks(dest, DC)[:, oc], in_=ps,
                                  w=[("PE", mm)], inc="ACT")
    last_qk_evac = max(evac_val[0], evac_val[4])
    vv_evac = {0: None, 2: None}
    for tc in range(TC):
        bank = 0 if tc % 2 == 0 else 2
        ps = pb.PS(bank, 2)
        mm = None
        for k in range(DC):
            lhsT = h0c[:, k, tc * 128:(tc + 1) * 128]
            for n in range(2):
                wl = []
                if k == 0 and n == 0:
                    if tc < 2:
                        wl = [("ACT", last_qk_evac), ("DVE", const_v)]
                    else:
                        wl = [("ACT", vv_evac[bank])]
                mm = pb.t("matmul", ps[:, n * 512:(n + 1) * 512], lhsT,
                          wqc[:, k, 2048 + n * 512:2048 + (n + 1) * 512],
                          start=(k == 0), stop=(k == DC - 1), w=wl,
                          inc="PE" if (k == DC - 1 and n == 1) else None)
        out_view = chunks(VEXT, TC)[:, tc].rearrange("p (h e) -> p h e", h=NH)[:, :, 0:64]
        vv_evac[bank] = pb.s("copy", out=out_view,
                             in_=ps.rearrange("p (h e) -> p h e", h=NH),
                             w=[("PE", mm)], inc="ACT")
    qkv_done_act = pb.cnt["ACT"]
    pe_after_qkv = pb.cnt["PE"]
    wo1 = pb.dma(chunks(WOUT, DC), io["w_out_a"].rearrange("(c p) f -> p c f", p=128), "LW",
                 cast=True, w=[("PE", pe_after_qkv)])

    ovl = {"LRW": LRW, "RHIB": RHIB, "RLOB": RLOB, "RBC": RBC, "OSCR": OSCR}
    o_free = emit_attention(pb, io, consts, cols, M4, NEGMH, QT, KT, VEXT, OT,
                            ATB, ovl, scale=0.125, causal=False,
                            entry_free=("ACT", qkv_done_act))

    xout_r = io["x_out"].rearrange("(tc p) d -> tc p d", p=128)
    ev_prev = {0: None, 2: None}
    xo_free = [None, None]
    otc = chunks(OT, DC)
    woc = chunks(WOUT, DC)
    for tc in range(TC):
        bank = 0 if tc % 2 == 0 else 2
        ps = pb.PS(bank, 2)
        mm = None
        for k in range(DC):
            for n in range(2):
                wl = []
                if k == 0 and n == 0:
                    wl = [o_free, wo1,
                          ("DVE", ev_prev[bank]) if ev_prev[bank] else None]
                mm = pb.t("matmul", ps[:, n * 512:(n + 1) * 512],
                          otc[:, k, tc * 128:(tc + 1) * 128], woc[:, k, n * 512:(n + 1) * 512],
                          start=(k == 0), stop=(k == DC - 1), w=wl,
                          inc="PE" if (k == DC - 1 and n == 1) else None)
        par = tc % 2
        xc = XC[par]
        ldw = [xo_free[par]] if xo_free[par] else [("ACT", qkv_done_act)]
        ldv = pb.dma(xc, xin_r[tc], "LD", w=ldw)
        addv = pb.v("tensor_tensor", out=xc, in0=xc, in1=ps, op=ALU.add,
                    w=[ldv, ("PE", mm)], inc="DVE")
        ev_prev[bank] = addv
        xo_free[par] = pb.dma(xout_r[tc], xc, "ST", w=[("DVE", addv)])
    pb.wait("sync", [("ST", pb.cnt["ST"])])



EPS_RMS = 1.1920929e-07


def emit_rep(pb, consts, row_io, width, dest, tmprow, pbank=6, w=(), prev=None):
    ld = pb.dma(tmprow[0:1, 0:width], row_io, "LD", w=list(w) + [prev])
    mm = None
    for n0 in range(0, width, 512):
        n1 = min(n0 + 512, width)
        wl = [ld, consts["_ld"], prev] if n0 == 0 else []
        mm = pb.t("matmul", pb.PS(pbank, 2)[:, n0:n1], consts["ones_row"][0:1, :],
                  tmprow[0:1, n0:n1], start=True, stop=True, w=wl,
                  inc="PE" if n1 == width else None)
    cp = pb.v("tensor_copy", out=dest, in_=pb.PS(pbank, 2)[:, 0:width],
              w=[("PE", mm)], inc="DVE")
    return ("DVE", cp)


def emit_up(pb, consts, w_hbm_T, in_ft, nin, nout, dst_ft, th0, tw, evac,
            wbufs_raw, wfree, extra_w=(), evac_w=None, entry=None,
            bias_cols=None, bias_w=None):
    evs = []
    bank_free = {0: entry, 2: entry}
    blk = 4 if nin <= 8 else 2
    wbufs = [b[:, 0:nin * blk * 128].rearrange("p (c f) -> p c f", c=nin)
             for b in wbufs_raw]
    for ob in range(nout // blk):
        par = ob % 2
        wld = pb.dma(wbufs[par], w_hbm_T[:, ob],
                     "LW", cast=True, w=[wfree[par]])
        wfree[par] = None
        mm_last = None
        for j in range(blk):
            oc = ob * blk + j
            bank = 0 if oc % 2 == 0 else 2
            ps = pb.PS(bank, 2)
            mm = None
            for k in range(nin):
                for n0 in range(0, tw, 512):
                    n1 = min(n0 + 512, tw)
                    wl = []
                    if k == 0 and n0 == 0:
                        wl = [wld if j == 0 else None, bank_free[bank]]
                        if ob < 2 and j == 0:
                            wl += list(extra_w)
                    mm = pb.t("matmul", ps[:, n0:n1],
                              wbufs[par][:, k, j * 128:(j + 1) * 128],
                              in_ft[:, k, th0 + n0:th0 + n1],
                              start=(k == 0), stop=(k == nin - 1), w=wl,
                              inc="PE" if (k == nin - 1 and n1 == tw) else None)
            dst = dst_ft[:, oc, 0:tw]
            ew = [("PE", mm)] + ([evac_w[oc]] if evac_w else [])
            if bias_cols is not None:
                ew.append(bias_w if oc < 2 else None)
                ev = pb.s("activation", out=dst, in_=ps[:, 0:tw], func=AF.Gelu,
                          bias=bias_cols[:, oc:oc + 1], w=ew, inc="ACT")
                evs.append(("ACT", ev))
            elif evac == "gelu":
                ev = pb.s("activation", out=dst, in_=ps[:, 0:tw], func=AF.Gelu, w=ew, inc="ACT")
                evs.append(("ACT", ev))
            elif evac == "silu":
                ev = pb.s("activation", out=dst, in_=ps[:, 0:tw], func=AF.Silu, w=ew, inc="ACT")
                evs.append(("ACT", ev))
            elif evac == "copy":
                ev = pb.s("copy", out=dst, in_=ps[:, 0:tw], w=ew, inc="ACT")
                evs.append(("ACT", ev))
            elif evac == "mult_into":
                ev = pb.v("tensor_tensor", out=dst, in0=ps[:, 0:tw], in1=dst,
                          op=ALU.mult, w=ew, inc="DVE")
                evs.append(("DVE", ev))
            bank_free[bank] = evs[-1]
            mm_last = mm
        wfree[par] = ("PE", mm_last)
    return evs


def emit_down(pb, consts, w_hbm_T, hsrc_ft, nhc, e_idx, COEF, XINT, th0, tw,
              wd_buf, wd_w=(), src_w=None, entry=None):
    wld = pb.dma(wd_buf, w_hbm_T.rearrange("(c p) f -> p c f", p=128)[:, :, 256:1024],
                 "LW", cast=True, w=wd_w)
    bank_free = {4: entry, 6: entry}
    last = None
    for tci in range(tw // 128):
        tc = th0 // 128 + tci
        bank = 4 if tci % 2 == 0 else 6
        ps = pb.PS(bank, 2)
        mm = None
        for hc in range(nhc):
            for (n0, n1) in ((0, 512), (512, 768)):
                wl = []
                if hc == 0 and n0 == 0:
                    wl = [wld, bank_free[bank]]
                    if src_w is not None:
                        wl.append(src_w)
                mm = pb.t("matmul", ps[:, n0:n1],
                          hsrc_ft[:, hc, tci * 128:(tci + 1) * 128],
                          wd_buf[:, hc, n0:n1],
                          start=(hc == 0), stop=(hc == nhc - 1), w=wl,
                          inc="PE" if (hc == nhc - 1 and n0 == 512) else None)
        coef = COEF[:, tc * 4 + e_idx:tc * 4 + e_idx + 1]
        xslice = chunks(XINT, TC)[:, tc][:, 256:1024]
        ev = pb.v("scalar_tensor_tensor", out=xslice, in0=ps[:, 0:768],
                  scalar=coef, in1=xslice, op0=ALU.mult, op1=ALU.add,
                  w=[("PE", mm)], inc="DVE")
        bank_free[bank] = ("DVE", ev)
        last = ("DVE", ev)
    return last


def emit_phase_b(pb, io, consts, al):
    cols = al.cols(["SQC", "SMC", "NEGMN", "EX2", "M2", "VARC", "SDC", "RSTD",
                    "MHS", "RMSV", "RMSD", "RMSR", "NEGL", "SUME", "RS",
                    "M1C", "M2C", "DEN", "RDEN", "W1C", "W2C"])
    NEGMH = al.r("NEGMH", 4)[:, 0:1]
    M4 = al.r("M4", 64)
    STATA = al.r("STATA", 320)
    SQ = al.r("SQ", D)
    XC = [al.r("XC0", D), al.r("XC1", D)]
    HB = [al.r("HB0", 512, BF16), al.r("HB1", 512, BF16)]
    N1REP = al.r("N1REP", D)
    N2REP = al.r("N2REP", D)
    QNREP = al.r("QNREP", 64)
    KNREP = al.r("KNREP", 64)
    GATEB = al.r("GATEB", 16, BF16)
    COEF = al.r("COEF", 64)
    PRB = al.r("PRB", 4)
    PRN = al.r("PRN", 4)
    MK1 = al.r("MK1", 4)
    MK2 = al.r("MK2", 4)
    PRM = al.r("PRM", 4)
    T2S = al.r("T2S", 4)
    QSC = al.r("QSC", 16)
    QSD = al.r("QSD", 16)
    QRS = al.r("QRS", 16)
    SCR2 = al.r("SCR2", 4096)
    FT1 = al.r("FT1", 8192, BF16)
    QK = al.r("QK", 16384)
    VX = al.r("VX", 8320)
    POOL = al.map["VX"][0] + 8320
    pool_words = ARENA_WORDS - POOL
    assert pool_words >= 8300, pool_words
    s2 = al.map["SCR2"][0]

    QNT = pb.R(al.map["QK"][0], 8192, BF16)
    KNT = pb.R(al.map["QK"][0] + 8192, 8192, BF16)
    XINT = pb.R(al.map["QK"][0], 16384)
    VEXT = pb.R(al.map["VX"][0], 8320, BF16)
    XFT = pb.R(al.map["VX"][0], 8192, BF16)
    XIC = [pb.R(s2, D), pb.R(s2 + 1024, D)]
    QRAW = XIC
    QTOK = [pb.R(s2 + 2048, 512, BF16), pb.R(s2 + 2560, 512, BF16)]
    ROWSC = pb.R(s2 + 3072, D)
    WS1 = [pb.R(s2, 2048, BF16), pb.R(s2 + 2048, 2048, BF16)]
    WS2 = WS1
    WB2 = [pb.R(POOL, 4096, BF16), pb.R(POOL + 4096, 4096, BF16)]
    ATB = [pb.R(POOL, 1024, BF16), pb.R(POOL + 1024, 1024, BF16)]
    LRW = pb.R(POOL + 2048, 2048)
    RHIB = pb.R(POOL + 4096, 1024, BF16)
    RLOB = pb.R(POOL + 5120, 1024, BF16)
    OSCR = pb.R(POOL + 2048, 1024, BF16)
    RBC = pb.R(POOL + 3072, 2048)
    WOB = pb.R(POOL, 3072, BF16)
    WD = pb.R(POOL, 6144, BF16)
    D1BQ = pb.R(POOL + 6144, 2048, BF16)
    assert POOL + 6144 + 2048 <= ARENA_WORDS

    xin_r = io["x_in"].rearrange("(tc p) d -> tc p d", p=128)
    xout_r = io["x_out"].rearrange("(tc p) d -> tc p d", p=128)
    xint_r = io["xint_scr"].rearrange("(tc p) d -> tc p d", p=128)

    r1 = emit_rep(pb, consts, io["n1_w"], D, N1REP, ROWSC, pbank=4)
    r2 = emit_rep(pb, consts, io["n2_w"], D, N2REP, ROWSC, pbank=4, prev=r1)
    r3 = emit_rep(pb, consts, io["qn_w"], HD, QNREP, ROWSC, pbank=4, prev=r2)
    rep_done = emit_rep(pb, consts, io["kn_w"], HD, KNREP, ROWSC, pbank=4, prev=r3)
    gld = pb.dma(GATEB.rearrange("p (c f) -> p c f", c=DC),
                 io["gate_b"].rearrange("(c p) f -> p c f", p=128), "LW", cast=True)
    const_v = pb.v("memset",
                   VEXT.rearrange("p (tc h e) -> p tc h e", tc=TC, h=NH)[:, :, :, 64:65],
                   1.0, inc="DVE")
    wq_ld = pb.dma(chunks(WB2[0], DC), io["wq_b"].rearrange("(c p) f -> p c f", p=128), "LW", cast=True)
    wk_ld = pb.dma(chunks(WB2[1], DC), io["wk_b"].rearrange("(c p) f -> p c f", p=128), "LW", cast=True)

    evac_prev = [0, 0]
    xc_free = [None, None]
    xic_free = [None, None]
    hb_free = [0, 0]
    for tc in range(TC):
        par = tc % 2
        xc = XC[par]
        ldv = pb.dma(xc, xin_r[tc], "LD", w=[xc_free[par]])
        pb.v("scalar_tensor_tensor", out=SQ, in0=xc, scalar=1.0, in1=xc,
             op0=ALU.mult, op1=ALU.mult, accum_out=cols["SQC"], w=[ldv])
        pb.v("reduce_sum", out=cols["SMC"], in_=xc, axis=AX.X)
        pb.v("tensor_scalar_mul", out=cols["NEGMN"], in0=cols["SMC"], scalar1=-1.0 / D)
        pb.v("tensor_scalar_mul", out=cols["EX2"], in0=cols["SQC"], scalar1=1.0 / D)
        pb.v("tensor_tensor", out=cols["M2"], in0=cols["NEGMN"], in1=cols["NEGMN"], op=ALU.mult)
        pb.v("tensor_tensor", out=cols["VARC"], in0=cols["EX2"], in1=cols["M2"], op=ALU.subtract)
        vv = pb.v("tensor_scalar_add", out=cols["VARC"], in0=cols["VARC"], scalar1=1e-5, inc="DVE")
        av = pb.s("activation", out=cols["SDC"], in_=cols["VARC"], func=AF.Sqrt,
                  w=[("DVE", vv)], inc="ACT")
        pb.v("reciprocal", out=cols["RSTD"], in_=cols["SDC"], w=[("ACT", av)])
        xic = XIC[par]
        xv = pb.v("tensor_scalar", out=xic, in0=xc, scalar1=cols["NEGMN"],
                  scalar2=cols["RSTD"], op0=ALU.add, op1=ALU.mult,
                  w=[xic_free[par]], inc="DVE")
        xc_free[par] = ("DVE", xv)
        sp = pb.dma(xint_r[tc], xic, "SP", w=[("DVE", xv)])
        pb.v("scalar_tensor_tensor", out=SQ, in0=xic, scalar=1.0, in1=xic,
             op0=ALU.mult, op1=ALU.mult, accum_out=cols["RMSV"])
        rv = pb.v("tensor_scalar", out=cols["RMSV"], in0=cols["RMSV"],
                  scalar1=1.0 / D, scalar2=EPS_RMS, op0=ALU.mult, op1=ALU.add, inc="DVE")
        a2 = pb.s("activation", out=cols["RMSD"], in_=cols["RMSV"], func=AF.Sqrt,
                  w=[("DVE", rv)], inc="ACT")
        pb.v("reciprocal", out=cols["RMSR"], in_=cols["RMSD"], w=[("ACT", a2)])
        hb = HB[par]
        nv = pb.v("scalar_tensor_tensor", out=hb, in0=xic, scalar=cols["RMSR"],
                  in1=N1REP, op0=ALU.mult, op1=ALU.mult,
                  w=[("PE", hb_free[par])] + ([rep_done] if tc == 0 else []), inc="DVE")
        xic_free[par] = (("DVE", nv) if True else None)
        xic_free[par] = sp
        pe_last, ev = emit_tok_to_ft(pb, consts, hb, FT1, tc, 6 + par,
                                     first_wait=[("DVE", nv), ("ACT", evac_prev[par])],
                                     evac_prev=0)
        hb_free[par] = pe_last
        evac_prev[par] = ev
    hrt_done = ("ACT", pb.cnt["ACT"])
    b1_sp = ("SP", pb.cnt["SP"])

    ft1c = chunks(FT1, DC)
    evac_bank = {0: None, 2: None}
    wlds = [wq_ld, wk_ld, None]
    for which in range(3):
        if which == 2:
            wlds[2] = pb.dma(chunks(WB2[0], DC),
                             io["wv_b"].rearrange("(c p) f -> p c f", p=128),
                             "LW", cast=True, w=[("PE", pe_after_q)])
        wb, wld = WB2[which % 2], wlds[which]
        wbc = chunks(wb, DC)
        for tc in range(TC):
            par = tc % 2
            bank = 0 if par == 0 else 2
            ps = pb.PS(bank, 2)
            mm = None
            for k in range(DC):
                for n in range(2):
                    wl = []
                    if k == 0 and n == 0:
                        wl = [wld, hrt_done, evac_bank[bank]]
                    mm = pb.t("matmul", ps[:, n * 512:(n + 1) * 512],
                              ft1c[:, k, tc * 128:(tc + 1) * 128],
                              wbc[:, k, n * 512:(n + 1) * 512],
                              start=(k == 0), stop=(k == DC - 1), w=wl,
                              inc="PE" if (k == DC - 1 and n == 1) else None)
            if which < 2:
                qraw = QRAW[par]
                qv = pb.v("tensor_copy", out=qraw, in_=ps,
                          w=[("PE", mm)] + ([b1_sp] if tc < 2 else []), inc="DVE")
                evac_bank[bank] = ("DVE", qv)
                pb.v("scalar_tensor_tensor", out=SQ, in0=qraw, scalar=1.0, in1=qraw,
                     op0=ALU.mult, op1=ALU.mult)
                pb.v("tensor_reduce", out=QSC, in_=chunks(SQ, NH), axis=AX.X, op=ALU.add)
                qs = pb.v("tensor_scalar", out=QSC, in0=QSC, scalar1=1.0 / HD,
                          scalar2=EPS_RMS, op0=ALU.mult, op1=ALU.add, inc="DVE")
                qa = pb.s("activation", out=QSD, in_=QSC, func=AF.Sqrt,
                          w=[("DVE", qs)], inc="ACT")
                pb.v("reciprocal", out=QRS, in_=QSD, w=[("ACT", qa)])
                qtok = QTOK[par]
                rep = QNREP if which == 0 else KNREP
                nv = None
                for hh in range(NH):
                    nv = pb.v("scalar_tensor_tensor",
                              out=qtok[:, hh * 64:(hh + 1) * 64],
                              in0=qraw[:, hh * 64:(hh + 1) * 64],
                              scalar=QRS[:, hh:hh + 1], in1=rep,
                              op0=ALU.mult, op1=ALU.mult,
                              w=[("PE", hb_free[par])] if hh == 0 else [], inc="DVE")
                dest = QNT if which == 0 else KNT
                pe_last, ev = emit_tok_to_ft(pb, consts, qtok, dest, tc, 6 + par,
                                             first_wait=[("DVE", nv)],
                                             evac_prev=evac_prev[par])
                hb_free[par] = pe_last
                evac_prev[par] = ev
            else:
                out_view = chunks(VEXT, TC)[:, tc].rearrange("p (h e) -> p h e", h=NH)[:, :, 0:64]
                ev = pb.s("copy", out=out_view, in_=ps.rearrange("p (h e) -> p h e", h=NH),
                          w=[("PE", mm), ("DVE", const_v) if tc == 0 else None], inc="ACT")
                evac_bank[bank] = ("ACT", ev)
        if which == 0:
            pe_after_q = pb.cnt["PE"]
    qkv_done = ("ACT", pb.cnt["ACT"])

    ovl = {"LRW": LRW, "RHIB": RHIB, "RLOB": RLOB, "RBC": RBC, "OSCR": OSCR}
    OT = FT1
    o_free = emit_attention(pb, io, consts, cols, M4, NEGMH, QNT, KNT, VEXT, OT,
                            ATB, ovl, scale=0.125, causal=True, entry_free=qkv_done)
    attn_pe = ("PE", pb.cnt["PE"])
    attn_dve = ("DVE", pb.cnt["DVE"])
    attn_act = ("ACT", pb.cnt["ACT"])

    xl = pb.dma(chunks(XINT, TC),
                io["xint_scr"].rearrange("(tc p) d -> p tc d", p=128),
                "LD", w=[attn_pe, attn_dve])
    wo_ld = pb.dma(chunks(WOB, DC),
                   io["wo_b"].rearrange("(c p) f -> p c f", p=128)[:, :, 256:1024],
                   "LW", cast=True, w=[attn_pe, attn_act, o_free])
    otc = chunks(OT, DC)
    wobc = chunks(WOB, DC)
    bank_free = {0: None, 2: None}
    for tc in range(TC):
        bank = 0 if tc % 2 == 0 else 2
        ps = pb.PS(bank, 2)
        mm = None
        for k in range(DC):
            for (n0, n1) in ((0, 512), (512, 768)):
                wl = []
                if k == 0 and n0 == 0:
                    wl = [wo_ld, o_free, bank_free[bank]]
                mm = pb.t("matmul", ps[:, n0:n1], otc[:, k, tc * 128:(tc + 1) * 128],
                          wobc[:, k, n0:n1], start=(k == 0), stop=(k == DC - 1),
                          w=wl, inc="PE" if (k == DC - 1 and n0 == 512) else None)
        xslice = chunks(XINT, TC)[:, tc][:, 256:1024]
        ev = pb.v("tensor_tensor", out=xslice, in0=xslice, in1=ps[:, 0:768],
                  op=ALU.add, w=[("PE", mm), xl], inc="DVE")
        bank_free[bank] = ("DVE", ev)

    scratch = {"XC": XC, "SQ": SQ, "HB": HB}
    xft_done = emit_ln_to_ft(pb, cols, consts, XFT, scratch, kind="rms",
                             src_sbuf=XINT, gamma_rep=N2REP, eps=EPS_RMS,
                             pbank=6, extra_first_wait=attn_pe, statarr=STATA)
    xftc = chunks(XFT, DC)
    gate_free = None
    for tc in range(TC):
        ps = pb.PS(4, 1)
        mm = None
        for k in range(DC):
            wl = []
            if k == 0:
                wl = [gld, xft_done, gate_free if gate_free else attn_dve]
            mm = pb.t("matmul", ps[:, 0:4], xftc[:, k, tc * 128:(tc + 1) * 128],
                      GATEB.rearrange("p (c f) -> p c f", c=DC)[:, k, :],
                      start=(k == 0), stop=(k == DC - 1), w=wl,
                      inc="PE" if k == DC - 1 else None)
        nm = pb.v("tensor_reduce", out=cols["NEGL"], in_=ps[:, 0:4], axis=AX.X,
                  op=ALU.max, negate=True, w=[("PE", mm)], inc="DVE")
        ea = pb.s("activation", out=PRB, in_=ps[:, 0:4], func=AF.Exp,
                  bias=cols["NEGL"], accum_out=cols["SUME"],
                  w=[("DVE", nm)], inc="ACT")
        pb.v("reciprocal", out=cols["RS"], in_=cols["SUME"], w=[("ACT", ea)])
        pb.v("tensor_single_scalar", out=PRN, in_=PRB, scalar=cols["RS"], op=ALU.mult)
        pb.v("tensor_reduce", out=cols["M1C"], in_=PRN, axis=AX.X, op=ALU.max)
        pb.v("tensor_single_scalar", out=MK1, in_=PRN, scalar=cols["M1C"], op=ALU.is_equal)
        pb.v("scalar_tensor_tensor", out=PRM, in0=MK1, scalar=-1e30, in1=PRN,
             op0=ALU.mult, op1=ALU.add)
        pb.v("tensor_reduce", out=cols["M2C"], in_=PRM, axis=AX.X, op=ALU.max)
        pb.v("tensor_single_scalar", out=MK2, in_=PRM, scalar=cols["M2C"], op=ALU.is_equal)
        pb.v("tensor_tensor", out=cols["DEN"], in0=cols["M1C"], in1=cols["M2C"], op=ALU.add)
        pb.v("tensor_scalar_add", out=cols["DEN"], in0=cols["DEN"], scalar1=1e-8)
        pb.v("reciprocal", out=cols["RDEN"], in_=cols["DEN"])
        pb.v("tensor_tensor", out=cols["W1C"], in0=cols["M1C"], in1=cols["RDEN"], op=ALU.mult)
        pb.v("tensor_tensor", out=cols["W2C"], in0=cols["M2C"], in1=cols["RDEN"], op=ALU.mult)
        pb.v("tensor_single_scalar", out=T2S, in_=MK2, scalar=cols["W2C"], op=ALU.mult)
        gf = pb.v("scalar_tensor_tensor", out=COEF[:, tc * 4:(tc + 1) * 4],
                  in0=MK1, scalar=cols["W1C"], in1=T2S,
                  op0=ALU.mult, op1=ALU.add, inc="DVE")
        gate_free = ("DVE", gf)

    h1t_half = FT1.rearrange("p (c t) -> p c t", c=NH)
    h1t_q = pb.R(al.map["FT1"][0], 4096, BF16).rearrange("p (c t) -> p c t", c=NH)
    h2t_q = pb.R(al.map["FT1"][0] + 4096, 4096, BF16).rearrange("p (c t) -> p c t", c=NH)
    wdv = WD.rearrange("p (c f) -> p c f", c=NH)
    wfree = [attn_pe, attn_pe]
    tail = gate_free
    wd_w = [gate_free, attn_dve, attn_act]
    for half in range(2):
        th0 = half * 1024
        ups = emit_up(pb, consts, io["e0_up_t"], xftc, DC, NH, h1t_half, th0, 1024,
                      "gelu", WS1, wfree, extra_w=[tail, xft_done], entry=tail)
        tail = emit_down(pb, consts, io["e0_down_t"], h1t_half, NH, 0, COEF, XINT,
                         th0, 1024, wdv, wd_w=wd_w, src_w=ups[-1], entry=tail)
        wd_w = [tail]
    for half in range(2):
        th0 = half * 1024
        u1 = emit_up(pb, consts, io["e1_w1_t"], xftc, DC, NH, h1t_half, th0, 1024,
                     "silu", WS1, wfree, extra_w=[tail], entry=tail)
        u2 = emit_up(pb, consts, io["e1_w2_t"], xftc, DC, NH, h1t_half, th0, 1024,
                     "mult_into", WS2, wfree, evac_w=u1, entry=u1[-1])
        tail = emit_down(pb, consts, io["e1_down_t"], h1t_half, NH, 1, COEF, XINT,
                         th0, 1024, wdv, wd_w=wd_w, src_w=u2[-1], entry=tail)
        wd_w = [tail]
    for quar in range(4):
        th0 = quar * 512
        a1 = emit_up(pb, consts, io["e2_l1_t"], xftc, DC, NH, h1t_q, th0, 512,
                     "gelu", WS1, wfree, extra_w=[tail], entry=tail)
        a2 = emit_up(pb, consts, io["e2_l2_t"], h1t_q, NH, NH, h2t_q, 0, 512,
                     "gelu", WS2, wfree, extra_w=[a1[-1]], entry=a1[-1])
        a3 = emit_up(pb, consts, io["e2_l3_t"], h2t_q, NH, NH, h1t_q, 0, 512,
                     "gelu", WS1, wfree, extra_w=[a2[-1]], entry=a2[-1])
        tail = emit_down(pb, consts, io["e2_l4_t"], h1t_q, NH, 2, COEF, XINT,
                         th0, 512, wdv, wd_w=wd_w, src_w=a3[-1], entry=tail)
        wd_w = [tail]
    d1bq = D1BQ.rearrange("p (c t) -> p c t", c=DC)
    for quar in range(4):
        th0 = quar * 512
        b1 = emit_up(pb, consts, io["e3_down_t"], xftc, DC, DC, d1bq, th0, 512,
                     "gelu", WS1, wfree, extra_w=[tail], entry=tail)
        b2 = emit_up(pb, consts, io["e3_up_t"], d1bq, DC, NH, h1t_q, 0, 512,
                     "gelu", WS2, wfree, extra_w=[b1[-1]], entry=b1[-1])
        tail = emit_down(pb, consts, io["e3_out_t"], h1t_q, NH, 3, COEF, XINT,
                         th0, 512, wdv, wd_w=wd_w, src_w=b2[-1], entry=tail)
        wd_w = [tail]

    xo_free = [None, None]
    for tc in range(TC):
        par = tc % 2
        xc = XC[par]
        ldv = pb.dma(xc, xin_r[tc], "LD", w=[xo_free[par] or wd_w[0]])
        addv = pb.v("tensor_tensor", out=xc, in0=xc, in1=chunks(XINT, TC)[:, tc],
                    op=ALU.add, w=[ldv] + ([wd_w[0]] if tc == 0 else []), inc="DVE")
        xo_free[par] = pb.dma(xout_r[tc], xc, "ST", w=[("DVE", addv)])
    pb.wait("sync", [("ST", pb.cnt["ST"])])


def pm_tile(wt):
    nin = wt.shape[0] // 128
    blk = 4 if nin <= 8 else 2
    bw = blk * 128
    nob = wt.shape[1] // bw
    return np.ascontiguousarray(wt.reshape(nin, 128, nob, bw).transpose(1, 2, 0, 3))


def prep_phase_b(inputs):
    tr = lambda k: np.ascontiguousarray(np.asarray(inputs[k]).T)
    d = {
        "n1_w": np.asarray(inputs["L0_ffn_n1_w"]).reshape(1, D),
        "n2_w": np.asarray(inputs["L0_ffn_n2_w"]).reshape(1, D),
        "qn_w": np.asarray(inputs["L0_ffn_qn_w"]).reshape(1, HD),
        "kn_w": np.asarray(inputs["L0_ffn_kn_w"]).reshape(1, HD),
        "wq_b": tr("L0_ffn_wq"), "wk_b": tr("L0_ffn_wk"),
        "wv_b": tr("L0_ffn_wv"), "wo_b": tr("L0_ffn_wo"),
        "gate_b": tr("L0_ffn_gate_w"),
        "e0_up_t": pm_tile(tr("L0_ffn_e0_up")), "e0_down_t": tr("L0_ffn_e0_down"),
        "e1_w1_t": pm_tile(tr("L0_ffn_e1_w1")), "e1_w2_t": pm_tile(tr("L0_ffn_e1_w2")),
        "e1_down_t": tr("L0_ffn_e1_down"),
        "e2_l1_t": pm_tile(tr("L0_ffn_e2_l1")), "e2_l2_t": pm_tile(tr("L0_ffn_e2_l2")),
        "e2_l3_t": pm_tile(tr("L0_ffn_e2_l3")), "e2_l4_t": tr("L0_ffn_e2_l4"),
        "e3_down_t": pm_tile(tr("L0_ffn_e3_down")), "e3_up_t": pm_tile(tr("L0_ffn_e3_up")),
        "e3_out_t": tr("L0_ffn_e3_out"),
    }
    d.update(make_consts("B"))
    return d



def _c_finish(pb, XC, xin_r, xout_r, w=()):
    xo_free = [None, None]
    for tc in range(TC):
        par = tc % 2
        ldv = pb.dma(XC[par], xin_r[tc], "LD", w=[xo_free[par]] + list(w))
        xo_free[par] = pb.dma(xout_r[tc], XC[par], "ST", w=[ldv])
    pb.wait("sync", [("ST", pb.cnt["ST"])])


C_UPTO = 9
C3SUB = 3

def emit_phase_c(pb, io, consts, al):
    cols = al.cols(["SQC", "SMC", "NEGMN", "EX2", "M2", "VARC", "SDC", "RSTD",
                    "DCOL", "RCC"])
    XC = [al.r("XC0", D), al.r("XC1", D)]
    HB = [al.r("HB0", 512, BF16), al.r("HB1", 512, BF16)]
    STATA = al.r("STATA", 320)
    SQ = al.r("SQ", D)
    TMP1 = SQ
    KVH = al.r("KVH", 512, BF16)
    KVL = al.r("KVL", 512, BF16)
    H2T = al.r("H2T", 8192, BF16)
    WQKVC = al.r("WQKVC", 12288, BF16)
    QFT = al.r("QFT", 8192, BF16)
    KF = al.r("KF", 8192, BF16)
    VEXT1 = al.r("VEXT1", 8320, BF16)
    assert al.off <= ARENA_WORDS, al.off
    OT1 = pb.R(al.map["H2T"][0], 8192, BF16)
    WOUTC = pb.R(al.map["WQKVC"][0], 4096, BF16)
    OTOK = pb.R(al.map["WQKVC"][0] + 4096, 8192, BF16)

    xin_r = io["x_in"].rearrange("(tc p) d -> tc p d", p=128)
    xout_r = io["x_out"].rearrange("(tc p) d -> tc p d", p=128)

    wq_ld = pb.dma(chunks(WQKVC, DC), io["w_qkv_c"].rearrange("(c p) f -> p c f", p=128),
                   "LW", cast=True)
    const_v = pb.v("memset",
                   VEXT1.rearrange("p (tc h e) -> p tc h e", tc=TC, h=NH)[:, :, :, 64:65],
                   1.0, inc="DVE")

    scratch = {"XC": XC, "SQ": SQ, "HB": HB}
    h2t_done = emit_ln_to_ft(pb, cols, consts, H2T, scratch, kind="ln",
                             xin_r=xin_r, pbank=6, statarr=STATA)

    if C_UPTO < 2:
        _c_finish(pb, XC, xin_r, xout_r)
        return
    h2c = chunks(H2T, DC)
    wqc = chunks(WQKVC, DC)
    QR = XC[0]
    RT = XC[1]
    cosv = chunks(consts["cos"], TC)
    sinv = chunks(consts["sin"], TC)
    evac_bank = {0: ("DVE", 1), 2: ("DVE", 1)}
    tr_evac = [0, 0]
    hb_free2 = [0, 0]
    ae_prev = None
    for which in range(3):
        for tc in range(TC):
            par = tc % 2
            bank = 0 if par == 0 else 2
            ps = pb.PS(bank, 2)
            mm = None
            for k in range(DC):
                for n in range(2):
                    wl = []
                    if k == 0 and n == 0:
                        wl = [wq_ld, h2t_done, evac_bank[bank]]
                    mm = pb.t("matmul", ps[:, n * 512:(n + 1) * 512],
                              h2c[:, k, tc * 128:(tc + 1) * 128],
                              wqc[:, k, which * 1024 + n * 512:which * 1024 + (n + 1) * 512],
                              start=(k == 0), stop=(k == DC - 1), w=wl,
                              inc="PE" if (k == DC - 1 and n == 1) else None)
            if which < 2:
                qv = pb.v("tensor_copy", out=QR, in_=ps,
                          w=[("PE", mm), ae_prev], inc="DVE")
                evac_bank[bank] = ("DVE", qv)
                qrv = QR.rearrange("p (h two x) -> p h two x", h=NH, two=2)
                rtv = RT.rearrange("p (h two x) -> p h two x", h=NH, two=2)
                pb.v("tensor_copy", out=rtv[:, :, 1], in_=qrv[:, :, 0])
                pb.v("tensor_scalar_mul", out=rtv[:, :, 0], in0=qrv[:, :, 1], scalar1=-1.0)
                cosb = cosv[:, tc][:, None, :].broadcast_to([128, NH, HD])
                sinb = sinv[:, tc][:, None, :].broadcast_to([128, NH, HD])
                pb.v("tensor_tensor", out=TMP1.rearrange("p (h x) -> p h x", h=NH),
                     in0=QR.rearrange("p (h x) -> p h x", h=NH),
                     in1=cosb, op=ALU.mult)
                pb.v("tensor_tensor", out=RT.rearrange("p (h x) -> p h x", h=NH),
                     in0=RT.rearrange("p (h x) -> p h x", h=NH),
                     in1=sinb, op=ALU.mult)
                rope_done = pb.v("tensor_tensor", out=QR, in0=TMP1, in1=RT,
                                 op=ALU.add, inc="DVE")
                sc = 0.125 if which == 0 else 1.0
                ae = pb.s("activation", out=TMP1, in_=QR, func=AF.Exp, scale=sc,
                          w=[("DVE", rope_done)], inc="ACT")
                ae_prev = ("ACT", ae)
                pb.v("tensor_scalar_min", out=TMP1, in0=TMP1, scalar1=1.0, w=[("ACT", ae)])
                pb.v("tensor_scalar", out=RT, in0=QR, scalar1=sc, scalar2=1.0,
                     op0=ALU.mult, op1=ALU.add)
                if which == 0:
                    fv = pb.v("tensor_tensor", out=HB[par], in0=RT, in1=TMP1,
                              op=ALU.max, w=[("PE", hb_free2[par])], inc="DVE")
                    pe_last, ev = emit_tok_to_ft(pb, consts, HB[par], QFT, tc, 6 + par,
                                                 first_wait=[("DVE", fv)],
                                                 evac_prev=tr_evac[par])
                    hb_free2[par] = pe_last
                    tr_evac[par] = ev
                else:
                    fv = pb.v("tensor_tensor", out=chunks(KF, TC)[:, tc], in0=RT,
                              in1=TMP1, op=ALU.max, inc="DVE")
            else:
                out_view = chunks(VEXT1, TC)[:, tc].rearrange("p (h e) -> p h e", h=NH)[:, :, 0:64]
                ev = pb.s("copy", out=out_view, in_=ps.rearrange("p (h e) -> p h e", h=NH),
                          w=[("PE", mm), ("DVE", const_v) if tc == 0 else None], inc="ACT")
                evac_bank[bank] = ("ACT", ev)
    qkv_done = ("ACT", pb.cnt["ACT"])
    qkv_dve = ("DVE", pb.cnt["DVE"])
    pe_after_qkv = ("PE", pb.cnt["PE"])
    wo_ld = pb.dma(chunks(WOUTC, DC), io["w_out_c"].rearrange("(c p) f -> p c f", p=128),
                   "LW", cast=True, w=[pe_after_qkv])

    if C_UPTO < 3:
        _c_finish(pb, XC, xin_r, xout_r, w=[qkv_done, qkv_dve])
        return
    kfv = chunks(KF, TC)
    vev = VEXT1.rearrange("p (tc h e) -> p tc h e", tc=TC, h=NH)
    kv_ps = pb.PS(0, 4)
    kvpsoff = lambda h: (h % 4) * 512 + (h // 4) * 66
    kvoff = lambda h: ((h // 2) // 7) * 512 + ((h // 2) % 7) * 65
    mm = None
    for h in range(NH):
        base = 64 * (h % 2)
        out = kv_ps[base:base + 64, kvpsoff(h):kvpsoff(h) + 65]
        for kt in range(TC):
            wl = []
            if h == 0 and kt == 0:
                wl = [qkv_done, qkv_dve]
            mm = pb.t("matmul", out, kfv[:, kt, h * 64:(h + 1) * 64],
                      vev[:, kt, h, :], start=(kt == 0), stop=(kt == TC - 1),
                      w=wl, inc="PE" if (h == NH - 1 and kt == TC - 1) else None)
    kh1 = kl1 = None
    for h in range(NH):
        base = 64 * (h % 2)
        src_ = kv_ps[base:base + 64, kvpsoff(h):kvpsoff(h) + 65]
        dsth = KVH[base:base + 64, kvoff(h):kvoff(h) + 65]
        dstl = KVL[base:base + 64, kvoff(h):kvoff(h) + 65]
        kh1 = pb.s("copy", out=dsth, in_=src_,
                   w=[("PE", mm)] if h == 0 else [], inc="ACT")
        kl1 = pb.v("tensor_tensor", out=dstl, in0=src_, in1=dsth,
                   op=ALU.subtract, w=[("ACT", kh1)], inc="DVE")

    if C_UPTO < 4:
        _c_finish(pb, XC, xin_r, xout_r, w=[("DVE", kl1)])
        return
    zoff = lambda h: (h % 4) * 512 + (h // 4) * 66
    z_free = {0: None}
    tr_evac2 = [0, 0]
    hb_free3 = [0, 0]
    for tc in range(TC):
        par = tc % 2
        zb = 0
        zp = pb.PS(zb, 4)
        mm = {}
        for h in range(NH):
            base = 64 * (h % 2)
            qsl = chunks(QFT, DC)[base:base + 64, h // 2, tc * 128:(tc + 1) * 128]
            out = zp[:, zoff(h):zoff(h) + 65]
            wl = []
            if h == 0:
                wl = [("DVE", kl1), z_free[zb]]
            pb.t("matmul", out, qsl, KVH[base:base + 64, kvoff(h):kvoff(h) + 65],
                 start=True, stop=False, w=wl)
            mm[h] = pb.t("matmul", out, qsl,
                         KVL[base:base + 64, kvoff(h):kvoff(h) + 65],
                         start=False, stop=True, inc="PE" if h == NH - 1 else None)
        last = None
        if C3SUB >= 3:
            for h in range(NH):
                dn = pb.v("tensor_scalar_max", out=cols["DCOL"],
                          in0=zp[:, zoff(h) + 64:zoff(h) + 65], scalar1=1e-6,
                          w=([("PE", mm[NH - 1])] if h == 0 else []) +
                            ([("ACT", last)] if last else []), inc="DVE")
                rc = pb.v("reciprocal", out=cols["RCC"], in_=cols["DCOL"],
                          w=[("DVE", dn)], inc="DVE")
                sc_ap = cols["RCC"] if C_UPTO >= 5 else 1.0
                last = pb.s("activation", out=chunks(OTOK, TC)[:, tc][:, h * 64:(h + 1) * 64],
                            in_=zp[:, zoff(h):zoff(h) + 64], func=AF.Copy,
                            scale=sc_ap, w=[("DVE", rc)] + ([wo_ld] if tc == 0 and h == 0 else []),
                            inc="ACT")
        else:
            for h in range(NH):
                last = pb.s("copy", out=chunks(OTOK, TC)[:, tc][:, h * 64:(h + 1) * 64],
                            in_=zp[:, zoff(h):zoff(h) + 64],
                            w=[("PE", mm[NH - 1])] if h == 0 else [], inc="ACT")
        z_free[zb] = ("ACT", last)
        if C3SUB >= 2:
            fv = pb.v("tensor_copy", out=HB[par], in_=chunks(OTOK, TC)[:, tc],
                      w=[("ACT", last), ("PE", hb_free3[par])], inc="DVE")
            pe_last, ev = emit_tok_to_ft(pb, consts, HB[par], OT1, tc, 6 + par,
                                         first_wait=[("DVE", fv)], evac_prev=tr_evac2[par])
            hb_free3[par] = pe_last
            tr_evac2[par] = ev
    ot_done = ("ACT", pb.cnt["ACT"])

    if C_UPTO < 5:
        _c_finish(pb, XC, xin_r, xout_r, w=[ot_done])
        return
    ev_prev = {0: None, 2: None}
    xo_free = [None, None]
    otc = chunks(OT1, DC)
    woc = chunks(WOUTC, DC)
    for tc in range(TC):
        bank = 0 if tc % 2 == 0 else 2
        ps = pb.PS(bank, 2)
        mm = None
        for k in range(DC):
            for n in range(2):
                wl = []
                if k == 0 and n == 0:
                    wl = [ot_done, wo_ld,
                          ("DVE", ev_prev[bank]) if ev_prev[bank] else None]
                mm = pb.t("matmul", ps[:, n * 512:(n + 1) * 512],
                          otc[:, k, tc * 128:(tc + 1) * 128], woc[:, k, n * 512:(n + 1) * 512],
                          start=(k == 0), stop=(k == DC - 1), w=wl,
                          inc="PE" if (k == DC - 1 and n == 1) else None)
        par = tc % 2
        xc = XC[par]
        ldw = [xo_free[par] or qkv_dve, ot_done]
        ldv = pb.dma(xc, xin_r[tc], "LD", w=ldw)
        addv = pb.v("tensor_tensor", out=xc, in0=xc, in1=ps, op=ALU.add,
                    w=[ldv, ("PE", mm)], inc="DVE")
        ev_prev[bank] = addv
        xo_free[par] = pb.dma(xout_r[tc], xc, "ST", w=[("DVE", addv)])
    pb.wait("sync", [("ST", pb.cnt["ST"])])


def prep_phase_c(inputs):
    d = {"w_qkv_c": np.ascontiguousarray(np.asarray(inputs["L1_attn_W_qkv"])),
         "w_out_c": np.ascontiguousarray(np.asarray(inputs["L1_attn_W_out"]))}
    d.update(make_consts("C"))
    return d



def emit_phase_d(pb, io, consts, al):
    cols = al.cols(["SQC", "SMC", "NEGMN", "EX2", "M2", "VARC", "SDC", "RSTD",
                    "LMX", "NEG", "SUME", "RS"])
    SQ = al.r("SQ", D)
    HB = [al.r("HB0", 512, BF16), al.r("HB1", 512, BF16)]
    STATA = al.r("STATA", 320)
    MK = al.r("MK", 8)
    COEFD = al.r("COEFD", 128)
    B1S = al.r("B1S", 128)
    B1C = al.r("B1C", 32)
    B2ROW = al.r("B2ROW", 1024)
    B2HI = al.r("B2HI", 512, BF16)
    B2LO = al.r("B2LO", 512, BF16)
    XD = al.r("XD", 16384)
    XFT = al.r("XFT", 8192, BF16)
    H1T = al.r("H1T", 16384, BF16)
    WS1 = [al.r("WS1a", 2048, BF16), al.r("WS1b", 2048, BF16)]
    WS2 = [al.r("WS2a", 512, BF16), al.r("WS2b", 512, BF16)]
    assert al.off <= ARENA_WORDS, al.off

    xin_r = io["x_in"].rearrange("(tc p) d -> tc p d", p=128)
    xout_r = io["x_out"].rearrange("(tc p) d -> tc p d", p=128)
    GB = al.r("GB", 32, BF16)
    g_ld = pb.dma(GB.rearrange("p (c f) -> p c f", c=DC),
                  io["gate_d"].rearrange("(c p) f -> p c f", p=128), "LW", cast=True)

    xd_ld = pb.dma(chunks(XD, TC), io["x_in"].rearrange("(tc p) d -> p tc d", p=128), "LD")
    scratch = {"XC": None, "SQ": SQ, "HB": HB}
    xft_done = emit_ln_to_ft(pb, cols, consts, XFT, scratch, kind="ln",
                             src_sbuf=XD, pbank=6, extra_first_wait=xd_ld,
                             statarr=STATA)
    xftc = chunks(XFT, DC)

    gate_free = None
    for tc in range(TC):
        ps = pb.PS(4, 1)
        mm = None
        for k in range(DC):
            wl = []
            if k == 0:
                wl = [g_ld, xft_done, gate_free]
            mm = pb.t("matmul", ps[:, 0:8], xftc[:, k, tc * 128:(tc + 1) * 128],
                      GB.rearrange("p (c f) -> p c f", c=DC)[:, k, :],
                      start=(k == 0), stop=(k == DC - 1), w=wl,
                      inc="PE" if k == DC - 1 else None)
        lm = pb.v("tensor_reduce", out=cols["LMX"], in_=ps[:, 0:8], axis=AX.X,
                  op=ALU.max, w=[("PE", mm)], inc="DVE")
        ng = pb.v("tensor_scalar_mul", out=cols["NEG"], in0=cols["LMX"], scalar1=-1.0,
                  inc="DVE")
        ea = pb.s("activation", out=MK, in_=ps[:, 0:8], func=AF.Exp,
                  bias=cols["NEG"], accum_out=cols["SUME"], w=[("DVE", ng)], inc="ACT")
        pb.v("reciprocal", out=cols["RS"], in_=cols["SUME"], w=[("ACT", ea)])
        pb.v("tensor_single_scalar", out=MK, in_=ps[:, 0:8], scalar=cols["LMX"],
             op=ALU.is_equal, w=[("ACT", ea)])
        gf = pb.v("tensor_single_scalar", out=COEFD[:, tc * 8:(tc + 1) * 8],
                  in_=MK, scalar=cols["RS"], op=ALU.mult, inc="DVE")
        gate_free = ("DVE", gf)

    h1t = H1T.rearrange("p (c t) -> p c t", c=32)
    tail = gate_free
    wfree = [gate_free, gate_free]
    for e in range(8):
        b1l = pb.dma(B1S.rearrange("p f -> p f")[0:32, :], io["b1_d"][e].rearrange("(c f) -> c f", f=128), "LD", w=[tail])
        bt = pb.t("transpose", pb.PS(6, 1)[0:128, 0:32], B1S[0:32, 0:128],
                  consts["ident_f"][0:32, 0:32], w=[b1l, tail], inc="PE")
        bc = pb.v("tensor_copy", out=B1C, in_=pb.PS(6, 1)[:, 0:32], w=[("PE", bt)], inc="DVE")
        b2l = pb.dma(B2ROW[0:1, :], io["b2_d"][e][None, :], "LD", w=[tail])
        bh = pb.v("tensor_copy", out=B2HI[0:1, 0:1024], in_=B2ROW[0:1, :], w=[b2l], inc="DVE")
        bl = pb.v("tensor_tensor", out=B2LO[0:1, 0:1024], in0=B2ROW[0:1, :],
                  in1=B2HI[0:1, 0:1024], op=ALU.subtract, w=[("DVE", bh)], inc="DVE")
        w1r = io["w1_d"][e]
        w2r = io["w2_d"][e].rearrange("(c p) f -> p c f", p=128)
        for half in range(2):
            th0 = half * 1024
            ups = emit_up(pb, consts, w1r, xftc, DC, 32, h1t, th0, 1024,
                          None, WS1, wfree, extra_w=[("DVE", bl), xft_done, tail],
                          entry=tail, bias_cols=B1C, bias_w=("DVE", bc))
            up_last = ups[-1]
            for yh in (0, 512):
                mmq = {}
                for hc in range(32):
                    par = hc % 2
                    wld = pb.dma(WS2[par][:, 0:512], w2r[:, hc, yh:yh + 512],
                                 "LW", cast=True, w=[wfree[par]])
                    wfree[par] = None
                    for tci in range(8):
                        tc = th0 // 128 + tci
                        ps = pb.PS(tci, 1)
                        wl = []
                        if hc < 2 and tci == 0:
                            wl = [wld, up_last, tail]
                        elif tci == 0:
                            wl = [wld]
                        mm = pb.t("matmul", ps, h1t[:, hc, tci * 128:(tci + 1) * 128],
                                  WS2[par][:, 0:512], start=(hc == 0), stop=False,
                                  w=wl, inc="PE" if tci == 7 else None)
                        mmq[tci] = mm
                    wfree[par] = ("PE", mmq[7])
                last_ev = None
                for tci in range(8):
                    tc = th0 // 128 + tci
                    ps = pb.PS(tci, 1)
                    pb.t("matmul", ps, consts["ones_bf"][0:1, 0:128],
                         B2HI[0:1, yh:yh + 512], start=False, stop=False,
                         w=[("DVE", bl)] if (tci == 0 and yh == 0) else [])
                    mm = pb.t("matmul", ps, consts["ones_bf"][0:1, 0:128],
                              B2LO[0:1, yh:yh + 512], start=False, stop=True,
                              inc="PE")
                    coef = COEFD[:, tc * 8 + e:tc * 8 + e + 1]
                    xslice = chunks(XD, TC)[:, tc][:, yh:yh + 512]
                    ev = pb.v("scalar_tensor_tensor", out=xslice, in0=ps,
                              scalar=coef, in1=xslice, op0=ALU.mult, op1=ALU.add,
                              w=[("PE", mm)], inc="DVE")
                    last_ev = ("DVE", ev)
                tail = last_ev

    st = None
    for tc in range(TC):
        st = pb.dma(xout_r[tc], chunks(XD, TC)[:, tc], "ST", w=[tail])
    pb.wait("sync", [("ST", pb.cnt["ST"])])


def prep_phase_d(inputs):
    w1 = np.asarray(inputs["L1_ffn_w1"])
    d = {"gate_d": np.ascontiguousarray(np.asarray(inputs["L1_ffn_gate_w"])),
         "w1_d": np.stack([pm_tile(w1[e]) for e in range(8)]),
         "b1_d": np.ascontiguousarray(np.asarray(inputs["L1_ffn_b1"])),
         "w2_d": np.ascontiguousarray(np.asarray(inputs["L1_ffn_w2"])),
         "b2_d": np.ascontiguousarray(np.asarray(inputs["L1_ffn_b2"]))}
    d.update(make_consts("D"))
    return d



def make_consts(phase):
    c = {}
    c["ident_bf"] = np.eye(128).astype(ml_dtypes.bfloat16)
    c["ident_f"] = np.eye(128, dtype=np.float32)
    c["ones_row"] = np.ones((1, 128), np.float32)
    c["ones_bf"] = np.ones((128, 128), np.float32).astype(ml_dtypes.bfloat16)
    if phase == "B":
        i = np.arange(128)
        c["mask"] = np.where(i[:, None] >= i[None, :], 0.0, -1e30).astype(np.float32)
        c["maskT"] = np.where(i[None, :] >= i[:, None], 0.0, -1e30).astype(np.float32)
    if phase == "C":
        hd = HD
        inv_freq = 1.0 / (500000.0 ** (np.arange(0, hd, 2, dtype=np.float32) / hd))
        t = np.arange(T, dtype=np.float32)
        freqs = t[:, None] * inv_freq[None, :]
        emb = np.concatenate([freqs, freqs], axis=-1)
        c["cos"] = np.cos(emb).astype(np.float32)
        c["sin"] = np.sin(emb).astype(np.float32)
    return c


def prep_phase_a(inputs):
    W = np.asarray(inputs["L0_attn_W_qkv"])
    Wr = np.ascontiguousarray(W.reshape(1024, 16, 64, 3).transpose(0, 3, 1, 2).reshape(1024, 3072))
    d = {"w_qkv_a": Wr,
         "w_out_a": np.ascontiguousarray(np.asarray(inputs["L0_attn_W_out"]))}
    d.update(make_consts("A"))
    return d


_PROG_CACHE = {}
_ALLOC_MAP = {}


def build_phase(phase):
    if phase in _PROG_CACHE:
        return _PROG_CACHE[phase]
    pb = PB()
    for s in ("LD", "LWa", "LWb", "LWc", "LWd", "PE", "ACT", "DVE", "ST", "SP"):
        pb.sem(s)
    pb.arena_init()
    io = {}
    io["x_in"] = pb.dram("x_in", [T, D])
    io["x_out"] = pb.dram("x_out", [T, D], kind="ExternalOutput")
    io["ident_bf"] = pb.dram("ident_bf", [128, 128], BF16)
    io["ident_f"] = pb.dram("ident_f", [128, 128])
    io["ones_row"] = pb.dram("ones_row", [1, 128])
    io["ones_bf"] = pb.dram("ones_bf", [128, 128], BF16)
    al = Alloc(pb, 0)
    if phase == "A":
        io["w_qkv_a"] = pb.dram("w_qkv_a", [1024, 3072])
        io["w_out_a"] = pb.dram("w_out_a", [1024, 1024])
        consts = emit_consts(pb, io, al)
        emit_phase_a(pb, io, consts, al)
    elif phase == "B":
        io["mask"] = pb.dram("mask", [128, 128])
        io["maskT"] = pb.dram("maskT", [128, 128])
        io["n1_w"] = pb.dram("n1_w", [1, D])
        io["n2_w"] = pb.dram("n2_w", [1, D])
        io["qn_w"] = pb.dram("qn_w", [1, HD])
        io["kn_w"] = pb.dram("kn_w", [1, HD])
        for nm in ("wq_b", "wk_b", "wv_b", "wo_b"):
            io[nm] = pb.dram(nm, [1024, 1024])
        io["e3_down_t"] = pb.dram("e3_down_t", [128, 2, 8, 512])
        io["gate_b"] = pb.dram("gate_b", [1024, 4])
        for nm in ("e0_up_t", "e1_w1_t", "e1_w2_t", "e2_l1_t", "e3_up_t"):
            io[nm] = pb.dram(nm, [128, 4, 8, 512])
        for nm in ("e0_down_t", "e1_down_t", "e2_l4_t", "e3_out_t"):
            io[nm] = pb.dram(nm, [2048, 1024])
        for nm in ("e2_l2_t", "e2_l3_t"):
            io[nm] = pb.dram(nm, [128, 8, 16, 256])
        io["xint_scr"] = pb.dram("xint_scr", [T, D], kind="Internal")
        consts = emit_consts(pb, io, al, need=("mask",))
        emit_phase_b(pb, io, consts, al)
    elif phase == "C":
        io["w_qkv_c"] = pb.dram("w_qkv_c", [1024, 3072])
        io["w_out_c"] = pb.dram("w_out_c", [1024, 1024])
        io["cos"] = pb.dram("cos", [T, HD])
        io["sin"] = pb.dram("sin", [T, HD])
        consts = emit_consts(pb, io, al, need=("rope", "lite"))
        emit_phase_c(pb, io, consts, al)
    elif phase == "D":
        io["gate_d"] = pb.dram("gate_d", [1024, 8])
        io["w1_d"] = pb.dram("w1_d", [8, 128, 8, 8, 512])
        io["b1_d"] = pb.dram("b1_d", [8, 4096])
        io["w2_d"] = pb.dram("w2_d", [8, 4096, 1024])
        io["b2_d"] = pb.dram("b2_d", [8, 1024])
        consts = emit_consts(pb, io, al)
        emit_phase_d(pb, io, consts, al)
    else:
        raise NotImplementedError(phase)
    _ALLOC_MAP.clear()
    _ALLOC_MAP.update(al.map)
    nc = pb.finalize()
    _PROG_CACHE[phase] = nc
    return nc


def build_fused():
    if "FUSED" in _PROG_CACHE:
        return _PROG_CACHE["FUSED"]
    pb = PB()
    for s in ("LD", "LWa", "LWb", "LWc", "LWd", "PE", "ACT", "DVE", "ST", "SP"):
        pb.sem(s)
    pb.arena_init()
    io = {}
    io["ident_bf"] = pb.dram("ident_bf", [128, 128], BF16)
    io["ident_f"] = pb.dram("ident_f", [128, 128])
    io["ones_row"] = pb.dram("ones_row", [1, 128])
    io["ones_bf"] = pb.dram("ones_bf", [128, 128], BF16)
    io["mask"] = pb.dram("mask", [128, 128])
    io["maskT"] = pb.dram("maskT", [128, 128])
    io["cos"] = pb.dram("cos", [T, HD])
    io["sin"] = pb.dram("sin", [T, HD])
    io["w_qkv_a"] = pb.dram("w_qkv_a", [1024, 3072])
    io["w_out_a"] = pb.dram("w_out_a", [1024, 1024])
    io["n1_w"] = pb.dram("n1_w", [1, D])
    io["n2_w"] = pb.dram("n2_w", [1, D])
    io["qn_w"] = pb.dram("qn_w", [1, HD])
    io["kn_w"] = pb.dram("kn_w", [1, HD])
    for nm in ("wq_b", "wk_b", "wv_b", "wo_b"):
        io[nm] = pb.dram(nm, [1024, 1024])
    io["e3_down_t"] = pb.dram("e3_down_t", [128, 2, 8, 512])
    io["gate_b"] = pb.dram("gate_b", [1024, 4])
    for nm in ("e0_up_t", "e1_w1_t", "e1_w2_t", "e2_l1_t", "e3_up_t"):
        io[nm] = pb.dram(nm, [128, 4, 8, 512])
    for nm in ("e0_down_t", "e1_down_t", "e2_l4_t", "e3_out_t"):
        io[nm] = pb.dram(nm, [2048, 1024])
    for nm in ("e2_l2_t", "e2_l3_t"):
        io[nm] = pb.dram(nm, [128, 8, 16, 256])
    io["w_qkv_c"] = pb.dram("w_qkv_c", [1024, 3072])
    io["w_out_c"] = pb.dram("w_out_c", [1024, 1024])
    io["gate_d"] = pb.dram("gate_d", [1024, 8])
    io["w1_d"] = pb.dram("w1_d", [8, 128, 8, 8, 512])
    io["b1_d"] = pb.dram("b1_d", [8, 4096])
    io["w2_d"] = pb.dram("w2_d", [8, 4096, 1024])
    io["b2_d"] = pb.dram("b2_d", [8, 1024])
    x_in = pb.dram("x_in", [SEQ_PER_CORE, T, D])
    x_out = pb.dram("x_out", [SEQ_PER_CORE, T, D], kind="ExternalOutput")
    xa = pb.dram("xa", [SEQ_PER_CORE, T, D], kind="Internal")
    xb = pb.dram("xb", [SEQ_PER_CORE, T, D], kind="Internal")
    xc = pb.dram("xc", [SEQ_PER_CORE, T, D], kind="Internal")
    xint = pb.dram("xint4", [SEQ_PER_CORE, T, D], kind="Internal")
    chain = {"A": (x_in, xa), "B": (xa, xb), "C": (xb, xc), "D": (xc, x_out)}
    needs = {"A": (), "B": ("mask",), "C": ("rope", "lite"), "D": ()}
    emitters = {"A": emit_phase_a, "B": emit_phase_b, "C": emit_phase_c,
                "D": emit_phase_d}
    for phase in "ABCD":
        for s in range(SEQ_PER_CORE):
            al = Alloc(pb, 0)
            consts = emit_consts(pb, io, al, need=needs[phase])
            pio = dict(io)
            pio["x_in"] = chain[phase][0][s]
            pio["x_out"] = chain[phase][1][s]
            if phase == "B":
                pio["xint_scr"] = xint[s]
            emitters[phase](pb, pio, consts, al)
            pb.section_barrier()
    nc = pb.finalize()
    _PROG_CACHE["FUSED"] = nc
    return nc


FUSED = True


def kernel(**inputs):
    if FUSED:
        return kernel_fused(**inputs)
    return kernel_unfused(**inputs)


def kernel_fused(**inputs):
    X = np.asarray(inputs["L0_attn_X"])
    prep = {}
    for p in (prep_phase_a(inputs), prep_phase_b(inputs),
              prep_phase_c(inputs), prep_phase_d(inputs)):
        prep.update(p)
    prep.update(make_consts("B"))
    prep.update(make_consts("C"))
    nc = build_fused()
    Xs = X.reshape(NCORES, SEQ_PER_CORE, T, D)
    in_maps = [dict(prep, x_in=np.ascontiguousarray(Xs[c])) for c in range(NCORES)]
    res = run_bass_kernel_spmd(nc, in_maps, core_ids=list(range(NCORES)))
    out = np.stack([res.results[c]["x_out"] for c in range(NCORES)])
    return np.ascontiguousarray(out.reshape(32, T, D))


def kernel_unfused(**inputs):
    X = np.asarray(inputs["L0_attn_X"])
    preps = {"A": prep_phase_a(inputs), "B": prep_phase_b(inputs),
             "C": prep_phase_c(inputs), "D": prep_phase_d(inputs)}
    cur = X.reshape(NCORES, SEQ_PER_CORE, T, D)
    for phase in "ABCD":
        nc = build_phase(phase)
        outs = np.empty_like(cur)
        for s in range(SEQ_PER_CORE):
            in_maps = [dict(preps[phase], x_in=np.ascontiguousarray(cur[c, s]))
                       for c in range(NCORES)]
            res = run_bass_kernel_spmd(nc, in_maps, core_ids=list(range(NCORES)))
            for c in range(NCORES):
                outs[c, s] = res.results[c]["x_out"]
        cur = outs
    return np.ascontiguousarray(cur.reshape(32, T, D))


# version 06
# speedup vs baseline: 1.3622x; 1.0491x over previous
import numpy as np
import ml_dtypes
from contextlib import ExitStack

import concourse.bass as bass
import concourse.mybir as mybir
from concourse.bass_utils import run_bass_kernel_spmd

F32 = mybir.dt.float32
BF16 = mybir.dt.bfloat16
AF = mybir.ActivationFunctionType
ALU = mybir.AluOpType
AX = mybir.AxisListType

T = 2048
TC = 16
D = 1024
DC = 8
NH = 16
HD = 64
NCORES = 8
SEQ_PER_CORE = 4

ARENA_WORDS = 52992



class PB:
    ENGINES = ("sync", "gpsimd", "tensor", "scalar", "vector")

    def __init__(self):
        self.nc = bass.Bass("TRN2", target_bir_lowering=False)
        self.ctx = ExitStack()
        self.q = {e: [] for e in self.ENGINES}
        self.sems = {}
        self.cnt = {}
        self._wdb = {}
        self._rdb = {}
        self._watermark = {}
        self._wm_all = {}

    def sem(self, name):
        self.sems[name] = self.ctx.enter_context(self.nc.semaphore(name))
        self.cnt[name] = 0

    def dram(self, name, shape, dt=F32, kind="ExternalInput"):
        return self.nc.dram_tensor(name, list(shape), dt, kind=kind).ap()

    def arena_init(self):
        self._arena = self.ctx.enter_context(
            self.nc.sbuf_tensor("arena", [128, ARENA_WORDS], F32))
        self._psum = self.ctx.enter_context(
            self.nc.psum_tensor("ps", [128, 4096], F32))

    def R(self, off, words, dt=F32):
        assert off >= 0 and off + words <= ARENA_WORDS, (off, words)
        ap = self._arena.ap()[:, off:off + words]
        if dt == BF16:
            ap = ap.bitcast(BF16)
        return ap

    def PS(self, bank, nbanks=1):
        return self._psum.ap()[:, bank * 512:(bank + nbanks) * 512]

    _WRITE_KEYS = ("out", "dest", "accum_out")
    _READ_KEYS = ("in_", "in0", "in1", "scalar", "scalar1", "scalar2", "bias",
                  "scale", "data0", "data1", "mask", "on_true", "on_false",
                  "data", "initial")
    _AUTO_ENGS = ("vector", "scalar")

    @staticmethod
    def _interval(ap):
        import concourse.mybir as _mb
        esz = _mb.dt.size(ap.dtype)
        pairs = list(ap.ap)
        pstride = pairs[0][0]
        pcount = pairs[0][1]
        off = ap.offset
        plo = off // pstride if pstride else 0
        off_in = off - plo * pstride
        span = 1
        lo = 0
        hi = 0
        for step, count in pairs[1:]:
            if step >= 0:
                hi += step * (count - 1)
            else:
                lo += step * (count - 1)
        return (ap.tensor.name, (off_in + lo) * esz, (off_in + hi + 1) * esz,
                plo, plo + pcount)

    def _deps(self, eng, args, kwargs):
        from concourse.ap import AP as _AP
        reads, writes = [], []
        for k, v in kwargs.items():
            if isinstance(v, _AP):
                (writes if k in self._WRITE_KEYS else reads).append(v)
        for v in args:
            if isinstance(v, _AP):
                writes.append(v)
        need = 0
        rr = [self._interval(a) for a in reads]
        ww = [self._interval(a) for a in writes]

        def overlap(a, b):
            return (a[0] == b[0] and a[1] < b[2] and b[1] < a[2]
                    and a[3] < b[4] and b[3] < a[4])

        wdb = self._wdb.setdefault(eng, {})
        rdb = self._rdb.setdefault(eng, {})
        for iv in rr:
            for rec in wdb.get(iv[0], ()):
                if overlap(iv, rec[0]):
                    need = max(need, rec[1])
        for iv in ww:
            for rec in wdb.get(iv[0], ()):
                if overlap(iv, rec[0]):
                    need = max(need, rec[1])
            for rec in rdb.get(iv[0], ()):
                if overlap(iv, rec[0]):
                    need = max(need, rec[1])
        ordinal = self.cnt[self._eng_sem[eng]] + 1
        for iv in ww:
            lst = wdb.setdefault(iv[0], [])
            lst[:] = [r for r in lst if not (r[0] == iv)]
            lst.append((iv, ordinal))
            if len(lst) > 256:
                del lst[0]
        for iv in rr:
            lst = rdb.setdefault(iv[0], [])
            lst[:] = [r for r in lst if not (r[0] == iv)]
            lst.append((iv, ordinal))
            if len(lst) > 256:
                del lst[0]
        return need

    _eng_sem = {"vector": "DVE", "scalar": "ACT"}

    def op(self, eng, method, *args, w=(), inc=None, **kwargs):
        auto = eng in self._AUTO_ENGS
        wm_all = self._wm_all.setdefault(eng, {})
        wl = []
        best = {}
        for item in w:
            if item is None:
                continue
            sname, val = item
            if val and val > 0 and val > best.get(sname, 0):
                best[sname] = val
        for sname, val in best.items():
            if val > wm_all.get(sname, 0):
                wl.append((sname, val))
                wm_all[sname] = val
        if auto:
            sem = self._eng_sem[eng]
            need = self._deps(eng, args, kwargs)
            wm = max(self._watermark.get(eng, 0), wm_all.get(sem, 0))
            if need > wm:
                wl.append((sem, need))
                wm = need
                wm_all[sem] = need
            self._watermark[eng] = wm
            assert inc is None or inc == sem or inc == (sem, 1), inc
            self.cnt[sem] += 1
            ret = self.cnt[sem]
            inc = (sem, 1)
        elif inc is not None:
            if isinstance(inc, str):
                inc = (inc, 1)
            self.cnt[inc[0]] += inc[1]
            ret = self.cnt[inc[0]]
        else:
            ret = None
        self.q[eng].append((method, args, kwargs, tuple(wl), inc))
        return ret

    def t(self, m, *a, **k):
        return self.op("tensor", m, *a, **k)

    def v(self, m, *a, **k):
        return self.op("vector", m, *a, **k)

    def s(self, m, *a, **k):
        return self.op("scalar", m, *a, **k)

    def dma(self, out, in_, incsem, w=(), cast=False, **kw):
        eng = "gpsimd" if cast else "sync"
        if incsem == "LW":
            self._lw_rr = (getattr(self, "_lw_rr", -1) + 1) % 4
            incsem = "LW" + "abcd"[self._lw_rr]
        w = list(w) + [(incsem, self.cnt[incsem])]
        v = self.op(eng, "dma_start", w=w, inc=(incsem, 16), out=out, in_=in_, **kw)
        return (incsem, v)

    def wait(self, eng, w):
        wm_all = self._wm_all.setdefault(eng, {})
        wl = []
        for item in w:
            if not item or not item[1]:
                continue
            sname, val = item
            if val > wm_all.get(sname, 0):
                wl.append((sname, val))
                wm_all[sname] = val
        if wl:
            self.q[eng].append(("__wait__", (), {}, tuple(wl), None))

    def section_barrier(self):
        cur = [(s, self.cnt[s]) for s in self.cnt]
        for eng in self.ENGINES:
            self.wait(eng, cur)
        self._wdb.clear()
        self._rdb.clear()
        for eng in self._AUTO_ENGS:
            self._watermark[eng] = self.cnt[self._eng_sem[eng]]

    def finalize(self):
        block = self.ctx.enter_context(self.nc.Block())
        sems = self.sems

        def mk(ops):
            def body(eng):
                for method, args, kwargs, waits, inc in ops:
                    for sname, val in waits:
                        eng.wait_ge(sems[sname], val)
                    if method == "__wait__":
                        continue
                    r = getattr(eng, method)(*args, **kwargs)
                    if inc is not None:
                        r.then_inc(sems[inc[0]], inc[1])
            return body

        for eng in self.ENGINES:
            if self.q[eng]:
                getattr(block, eng)(mk(self.q[eng]))
        self.ctx.close()
        return self.nc


def chunks(ap, n):
    return ap.rearrange("p (c t) -> p c t", c=n)



class Alloc:
    def __init__(self, pb, base=0):
        self.pb = pb
        self.off = base
        self.map = {}

    def r(self, name, words, dt=F32):
        ap = self.pb.R(self.off, words, dt)
        self.map[name] = (self.off, words)
        self.off += words
        return ap

    def cols(self, names):
        return {n: self.r("col_" + n, 4)[:, 0:1] for n in names}


def emit_consts(pb, io, al, need=()):
    c = {}
    c["scr"] = al.r("scr", 16)
    c["ident_bf"] = al.r("ident_bf", 64, BF16)
    pb.dma(c["ident_bf"], io["ident_bf"], "LD")
    if "lite" not in need:
        c["ident_f"] = al.r("ident_f", 128)
        c["ones_row"] = al.r("ones_row", 128)
        c["ones_bf"] = al.r("ones_bf", 64, BF16)
        pb.dma(c["ident_f"], io["ident_f"], "LD")
        pb.dma(c["ones_row"][0:1, :], io["ones_row"], "LD")
        pb.dma(c["ones_bf"], io["ones_bf"], "LD")
    if "mask" in need:
        c["mask"] = al.r("mask", 128)
        c["maskT"] = al.r("maskT", 128)
        pb.dma(c["mask"], io["mask"], "LD")
        pb.dma(c["maskT"], io["maskT"], "LD")
    if "rope" in need:
        c["cos"] = al.r("cos", 1024)
        c["sin"] = al.r("sin", 1024)
        pb.dma(chunks(c["cos"], TC), io["cos"].rearrange("(tc p) d -> p tc d", p=128), "LD")
        pb.dma(chunks(c["sin"], TC), io["sin"].rearrange("(tc p) d -> p tc d", p=128), "LD")
    c["_ld"] = ("LD", pb.cnt["LD"])
    return c



def emit_ln_to_ft(pb, cols, consts, out_ft, scratch, kind="ln", xin_r=None,
                  src_sbuf=None, gamma_rep=None, gamma_wait=None, eps=1e-5,
                  pbank=6, extra_first_wait=None, statarr=None):
    SMA = statarr[:, 0:64]
    SQA = statarr[:, 64:128]
    M2A = statarr[:, 128:192]
    VARA = statarr[:, 192:256]
    SDA = statarr[:, 256:320]
    sv = lambda a: a.rearrange("p (n s) -> p n s", s=4)[:, :, 0]
    xc_free = [None, None]
    last_stat = None
    for tc in range(TC):
        par = tc % 2
        if src_sbuf is not None:
            xc = chunks(src_sbuf, TC)[:, tc]
            v_w = [extra_first_wait] if (tc == 0 and extra_first_wait) else []
        else:
            xc = scratch["XC"][par]
            ldv = pb.dma(xc, xin_r[tc], "LD", w=[xc_free[par]])
            v_w = [ldv]
            if tc == 0 and extra_first_wait:
                v_w.append(extra_first_wait)
        s1 = pb.v("scalar_tensor_tensor", out=scratch["SQ"], in0=xc, scalar=1.0,
                  in1=xc, op0=ALU.mult, op1=ALU.mult,
                  accum_out=SQA[:, 4 * tc:4 * tc + 1], w=v_w, inc="DVE")
        if kind == "ln":
            s1 = pb.v("reduce_sum", out=SMA[:, 4 * tc:4 * tc + 1], in_=xc,
                      axis=AX.X, inc="DVE")
        xc_free[par] = ("DVE", s1)
    if kind == "ln":
        pb.v("tensor_scalar_mul", out=sv(SMA), in0=sv(SMA), scalar1=-1.0 / D)
        pb.v("tensor_scalar_mul", out=sv(SQA), in0=sv(SQA), scalar1=1.0 / D)
        pb.v("tensor_tensor", out=sv(M2A), in0=sv(SMA), in1=sv(SMA), op=ALU.mult)
        pb.v("tensor_tensor", out=sv(VARA), in0=sv(SQA), in1=sv(M2A), op=ALU.subtract)
        vv = pb.v("tensor_scalar_add", out=sv(VARA), in0=sv(VARA),
                  scalar1=float(eps), inc="DVE")
    else:
        vv = pb.v("tensor_scalar", out=sv(VARA), in0=sv(SQA), scalar1=1.0 / D,
                  scalar2=float(eps), op0=ALU.mult, op1=ALU.add, inc="DVE")
    av = pb.s("activation", out=sv(SDA), in_=sv(VARA), func=AF.Sqrt,
              w=[("DVE", vv)], inc="ACT")
    rv = pb.v("reciprocal", out=sv(VARA), in_=sv(SDA), w=[("ACT", av)], inc="DVE")
    evac_prev = [0, 0]
    hb_free = [0, 0]
    last_evac = 0
    for tc in range(TC):
        par = tc % 2
        if src_sbuf is not None:
            xc = chunks(src_sbuf, TC)[:, tc]
            v_w = []
        else:
            xc = scratch["XC"][par]
            ldv = pb.dma(xc, xin_r[tc], "LD", w=[xc_free[par]])
            v_w = [ldv]
        hb = scratch["HB"][par]
        wl = v_w + [("DVE", rv), ("PE", hb_free[par])]
        if kind == "ln":
            nv = pb.v("tensor_scalar", out=hb, in0=xc,
                      scalar1=SMA[:, 4 * tc:4 * tc + 1],
                      scalar2=VARA[:, 4 * tc:4 * tc + 1],
                      op0=ALU.add, op1=ALU.mult, w=wl, inc="DVE")
        else:
            gw = [gamma_wait] if (tc == 0 and gamma_wait) else []
            nv = pb.v("scalar_tensor_tensor", out=hb, in0=xc,
                      scalar=VARA[:, 4 * tc:4 * tc + 1], in1=gamma_rep,
                      op0=ALU.mult, op1=ALU.mult, w=wl + gw, inc="DVE")
        xc_free[par] = ("DVE", nv)
        pbk = pbank if par == 0 else pbank + 1
        ps_bf = pb.PS(pbk, 1).bitcast(BF16)
        pe_last = None
        for ci in range(DC):
            wlist = []
            if ci == 0:
                wlist = [("DVE", nv), ("ACT", evac_prev[par]),
                         consts["_ld"] if tc == 0 else None]
            pe_last = pb.t("transpose", ps_bf[:, ci * 128:(ci + 1) * 128],
                           hb[:, ci * 128:(ci + 1) * 128], consts["ident_bf"],
                           w=wlist, inc="PE" if ci == DC - 1 else None)
        hb_free[par] = pe_last
        out_view = chunks(out_ft, DC)[:, :, tc * 128:(tc + 1) * 128]
        last_evac = pb.s("copy", out=out_view, in_=chunks(ps_bf, DC),
                         w=[("PE", pe_last)], inc="ACT")
        evac_prev[par] = last_evac
    return ("ACT", last_evac)


def emit_tok_to_ft(pb, consts, tok_bf, ft, tc, pbank, first_wait=(), evac_prev=0):
    ps_bf = pb.PS(pbank, 1).bitcast(BF16)
    pe_last = None
    for ci in range(DC):
        wl = list(first_wait) + [("ACT", evac_prev)] if ci == 0 else []
        pe_last = pb.t("transpose", ps_bf[:, ci * 128:(ci + 1) * 128],
                       tok_bf[:, ci * 128:(ci + 1) * 128], consts["ident_bf"],
                       w=wl, inc="PE" if ci == DC - 1 else None)
    ev = pb.s("copy", out=chunks(ft, DC)[:, :, tc * 128:(tc + 1) * 128],
              in_=chunks(ps_bf, DC), w=[("PE", pe_last)], inc="ACT")
    return pe_last, ev


def emit_attention(pb, io, consts, cols, M4, NEGMH, QT, KT, VEXT, OT, ATB, ovl,
                   scale, causal, entry_free):
    LRW, RHIB, RLOB, RBC, OSCR = (ovl[k] for k in ("LRW", "RHIB", "RLOB", "RBC", "OSCR"))

    def head_sl(ft, h):
        base = 64 * (h % 2)
        return chunks(ft, DC)[base:base + 64, h // 2]

    s_free = {0: entry_free, 4: entry_free}
    o_free = None
    for h in range(NH):
        qh = head_sl(QT, h)
        kh = head_sl(KT, h)
        qts = [3, 7, 11, 15] if causal else [0, 5, 10, 15]
        for j, qt in enumerate(qts):
            bank = 0 if j % 2 == 0 else 4
            ps = pb.PS(bank, 4)
            kmax = (qt + 1) * 128 if causal else T
            nblk = (kmax + 511) // 512
            mm = None
            for n in range(nblk):
                w0 = n * 512
                w1 = min(w0 + 512, kmax)
                wl = [s_free[bank]] if n == 0 else []
                mm = pb.t("matmul", ps[:, w0:w1],
                          qh[:, qt * 128:(qt + 1) * 128], kh[:, w0:w1],
                          start=True, stop=True, w=wl,
                          inc="PE" if n == nblk - 1 else None)
            if causal:
                madd = pb.v("tensor_tensor", out=ps[:, qt * 128:(qt + 1) * 128],
                            in0=ps[:, qt * 128:(qt + 1) * 128], in1=consts["mask"],
                            op=ALU.add, w=[("PE", mm)], inc="DVE")
            mx = pb.v("tensor_reduce", out=M4[:, 4 * j:4 * j + 1],
                      in_=ps[:, 0:kmax].rearrange("p (n s) -> p n s", s=4)[:, :, 0],
                      axis=AX.X, op=ALU.max, w=[("PE", mm)], inc="DVE")
            s_free[bank] = ("DVE", mx)
        vv = pb.v("tensor_reduce", out=cols["VARC"],
                  in_=M4.rearrange("p (n s) -> p n s", s=4)[:, 0:len(qts), 0],
                  axis=AX.X, op=ALU.max, inc="DVE")
        pt = pb.t("transpose", pb.PS(0, 1)[0:1, 0:128], cols["VARC"], consts["ident_f"],
                  w=[("DVE", vv), s_free[0]], inc="PE")
        mv = pb.v("tensor_reduce", out=cols["MHS"][0:1, :], in_=pb.PS(0, 1)[0:1, 0:128],
                  axis=AX.X, op=ALU.max, w=[("PE", pt)], inc="DVE")
        bc = pb.t("matmul", pb.PS(1, 1)[:, 0:1], consts["ones_row"][0:1, :],
                  cols["MHS"][0:1, 0:1], start=True, stop=True, w=[("DVE", mv)], inc="PE")
        ng = pb.v("tensor_scalar_mul", out=NEGMH, in0=pb.PS(1, 1)[:, 0:1],
                  scalar1=-scale, w=[("PE", bc)], inc="DVE")
        s_free[0] = ("DVE", ng)
        av_last = None
        g_free = {0: s_free[0], 1024: s_free[0]}
        atb_free2 = {}
        for kt in range(TC):
            par = kt % 2
            ps = pb.PS(0, 4)
            q0 = kt * 128 if causal else 0
            po = pb.PS(4, 4)
            vsl = chunks(VEXT, TC)[:, kt][:, h * 65:(h + 1) * 65]
            groups = []
            for g0 in (0, 1024):
                ga = max(q0, g0)
                gb = g0 + 1024
                if ga >= gb:
                    continue
                blks = []
                pos = ga
                while pos < gb:
                    w1 = min((pos // 512 + 1) * 512, gb)
                    blks.append((pos, w1))
                    pos = w1
                groups.append((g0, ga, gb, blks))
            mmg = {}
            madds = {}
            for g0, ga, gb, blks in groups:
                mm = None
                for i, (b0, b1) in enumerate(blks):
                    wl = [g_free[g0]] if i == 0 else []
                    mm = pb.t("matmul", ps[:, b0:b1],
                              kh[:, kt * 128:(kt + 1) * 128], qh[:, b0:b1],
                              start=True, stop=True, w=wl,
                              inc="PE" if i == len(blks) - 1 else None)
                mmg[g0] = mm
                if causal and g0 <= kt * 128 < gb:
                    madds[g0] = pb.v("tensor_tensor",
                                     out=ps[:, kt * 128:(kt + 1) * 128],
                                     in0=ps[:, kt * 128:(kt + 1) * 128],
                                     in1=consts["maskT"], op=ALU.add,
                                     w=[("PE", mm)], inc="DVE")
            exg = {}
            for g0, ga, gb, blks in groups:
                exw = [("PE", mmg[g0]), atb_free2.get((par, g0))]
                if g0 in madds:
                    exw.append(("DVE", madds[g0]))
                exg[g0] = pb.s("activation", out=ATB[par][:, ga:gb], in_=ps[:, ga:gb],
                               func=AF.Exp, bias=NEGMH, scale=scale, w=exw, inc="ACT")
                g_free[g0] = ("ACT", exg[g0])
            for g0, ga, gb, blks in groups:
                for i, (b0, b1) in enumerate(blks):
                    wl = [("ACT", exg[g0])] if i == 0 else []
                    if kt == 0 and i == 0 and g0 == 0 and o_free:
                        wl.append(o_free)
                    bank = b0 // 512
                    stop = (kt == 4 * bank + 3) if causal else (kt == TC - 1)
                    av_last = pb.t("matmul", po[0:65, b0:b1], vsl, ATB[par][:, b0:b1],
                                   start=(kt == 0), stop=stop, w=wl,
                                   inc="PE" if (kt == TC - 1 and i == len(blks) - 1) else None)
                    atb_free2[(par, g0)] = ("PE", av_last)
        s_free[0] = g_free[0]
        pb.wait("tensor", [g_free[1024]])
        po = pb.PS(4, 4)
        r1 = pb.v("reciprocal", out=LRW[64:65, 0:T], in_=po[64:65, :],
                  w=[("PE", av_last)], inc="DVE")
        r2 = pb.v("tensor_copy", out=RHIB[64:65, 0:T], in_=LRW[64:65, 0:T],
                  w=[("DVE", r1)], inc="DVE")
        r3 = pb.v("tensor_tensor", out=RLOB[64:65, 0:T], in0=LRW[64:65, 0:T],
                  in1=RHIB[64:65, 0:T], op=ALU.subtract, w=[("DVE", r2)], inc="DVE")
        bmm = None
        for n in range(4):
            wl = [("DVE", r3), s_free[0], consts["_ld"] if h == 0 else None] if n == 0 else []
            pb.t("matmul", pb.PS(0, 4)[0:64, n * 512:(n + 1) * 512],
                 consts["ones_bf"][64:65, 0:64], RHIB[64:65, n * 512:(n + 1) * 512],
                 start=True, stop=False, w=wl)
            bmm = pb.t("matmul", pb.PS(0, 4)[0:64, n * 512:(n + 1) * 512],
                       consts["ones_bf"][64:65, 0:64], RLOB[64:65, n * 512:(n + 1) * 512],
                       start=False, stop=True, inc="PE" if n == 3 else None)
        rc = pb.s("copy", out=RBC[0:64, :], in_=pb.PS(0, 4)[0:64, :],
                  w=[("PE", bmm)], inc="ACT")
        s_free[0] = ("ACT", rc)
        if h % 2 == 0:
            dest = chunks(OT, DC)[0:64, h // 2]
            ov = pb.v("tensor_tensor", out=dest, in0=po[0:64, :], in1=RBC[0:64, :],
                      op=ALU.mult, w=[("ACT", rc)], inc="DVE")
            o_free = ("DVE", ov)
        else:
            ov = pb.v("tensor_tensor", out=OSCR[0:64, 0:T], in0=po[0:64, :],
                      in1=RBC[0:64, :], op=ALU.mult, w=[("ACT", rc)], inc="DVE")
            dst = chunks(OT, DC)[64:128, h // 2]
            o_free = pb.dma(dst, OSCR[0:64, 0:T], "LD", w=[("DVE", ov)])
        s_free[4] = o_free
    return o_free



def emit_phase_a(pb, io, consts, al):
    cols = al.cols(["SQC", "SMC", "NEGMN", "EX2", "M2", "VARC", "SDC", "RSTD", "MHS"])
    STATA = al.r("STATA", 320)
    NEGMH = al.r("NEGMH", 4)[:, 0:1]
    M4 = al.r("M4", 64)
    SQ = al.r("SQ", D)
    XC = [al.r("XC0", D), al.r("XC1", D)]
    HB = [al.r("HB0", 512, BF16), al.r("HB1", 512, BF16)]
    H0T = al.r("H0T", 8192, BF16)
    WQKV = al.r("WQKV", 12288, BF16)
    QT = al.r("QT", 8192, BF16)
    KT = al.r("KT", 8192, BF16)
    VEXT = al.r("VEXT", 8320, BF16)
    ATB = [al.r("ATB0", 1024, BF16), al.r("ATB1", 1024, BF16)]
    assert al.off <= ARENA_WORDS, al.off
    h0t_off = al.map["H0T"][0]
    LRW = pb.R(h0t_off, 2048)
    RHIB = pb.R(h0t_off + 2048, 1024, BF16)
    RLOB = pb.R(h0t_off + 3072, 1024, BF16)
    RBC = pb.R(h0t_off + 4096, 2048)
    OSCR = pb.R(h0t_off + 6144, 1024, BF16)
    wq_off = al.map["WQKV"][0]
    OT = pb.R(wq_off, 8192, BF16)
    WOUT = pb.R(wq_off + 8192, 4096, BF16)

    wq1 = pb.dma(chunks(WQKV, DC), io["w_qkv_a"].rearrange("(c p) f -> p c f", p=128), "LW", cast=True)
    const_v = pb.v("memset",
                   VEXT.rearrange("p (tc h e) -> p tc h e", tc=TC, h=NH)[:, :, :, 64:65],
                   1.0, inc="DVE")
    xin_r = io["x_in"].rearrange("(tc p) d -> tc p d", p=128)

    scratch = {"XC": XC, "SQ": SQ, "HB": HB}
    h0t_done = emit_ln_to_ft(pb, cols, consts, H0T, scratch, kind="ln",
                             xin_r=xin_r, pbank=6, statarr=STATA)

    wqc = chunks(WQKV, DC)
    h0c = chunks(H0T, DC)
    evac_val = {0: None, 4: None}
    last_pe_v = None
    for which, dest in ((0, QT), (1, KT)):
        for oc in range(DC):
            bank = 0 if oc % 2 == 0 else 4
            ps = pb.PS(bank, 4)
            mm = None
            for k in range(DC):
                lhsT = wqc[:, k, which * 1024 + oc * 128: which * 1024 + (oc + 1) * 128]
                for n in range(4):
                    wl = []
                    if k == 0 and n == 0:
                        wl = [wq1, h0t_done,
                              ("ACT", evac_val[bank]) if evac_val[bank] else None]
                    mm = pb.t("matmul", ps[:, n * 512:(n + 1) * 512], lhsT,
                              h0c[:, k, n * 512:(n + 1) * 512],
                              start=(k == 0), stop=(k == DC - 1), w=wl,
                              inc="PE" if (k == DC - 1 and n == 3) else None)
            evac_val[bank] = pb.s("copy", out=chunks(dest, DC)[:, oc], in_=ps,
                                  w=[("PE", mm)], inc="ACT")
    last_qk_evac = max(evac_val[0], evac_val[4])
    vv_evac = {0: None, 2: None}
    for tc in range(TC):
        bank = 0 if tc % 2 == 0 else 2
        ps = pb.PS(bank, 2)
        mm = None
        for k in range(DC):
            lhsT = h0c[:, k, tc * 128:(tc + 1) * 128]
            for n in range(2):
                wl = []
                if k == 0 and n == 0:
                    if tc < 2:
                        wl = [("ACT", last_qk_evac), ("DVE", const_v)]
                    else:
                        wl = [("ACT", vv_evac[bank])]
                mm = pb.t("matmul", ps[:, n * 512:(n + 1) * 512], lhsT,
                          wqc[:, k, 2048 + n * 512:2048 + (n + 1) * 512],
                          start=(k == 0), stop=(k == DC - 1), w=wl,
                          inc="PE" if (k == DC - 1 and n == 1) else None)
        out_view = chunks(VEXT, TC)[:, tc].rearrange("p (h e) -> p h e", h=NH)[:, :, 0:64]
        vv_evac[bank] = pb.s("copy", out=out_view,
                             in_=ps.rearrange("p (h e) -> p h e", h=NH),
                             w=[("PE", mm)], inc="ACT")
    qkv_done_act = pb.cnt["ACT"]
    pe_after_qkv = pb.cnt["PE"]
    wo1 = pb.dma(chunks(WOUT, DC), io["w_out_a"].rearrange("(c p) f -> p c f", p=128), "LW",
                 cast=True, w=[("PE", pe_after_qkv)])

    ovl = {"LRW": LRW, "RHIB": RHIB, "RLOB": RLOB, "RBC": RBC, "OSCR": OSCR}
    o_free = emit_attention(pb, io, consts, cols, M4, NEGMH, QT, KT, VEXT, OT,
                            ATB, ovl, scale=0.125, causal=False,
                            entry_free=("ACT", qkv_done_act))

    xout_r = io["x_out"].rearrange("(tc p) d -> tc p d", p=128)
    ev_prev = {0: None, 2: None}
    xo_free = [None, None]
    otc = chunks(OT, DC)
    woc = chunks(WOUT, DC)
    for tc in range(TC):
        bank = 0 if tc % 2 == 0 else 2
        ps = pb.PS(bank, 2)
        mm = None
        for k in range(DC):
            for n in range(2):
                wl = []
                if k == 0 and n == 0:
                    wl = [o_free, wo1,
                          ("DVE", ev_prev[bank]) if ev_prev[bank] else None]
                mm = pb.t("matmul", ps[:, n * 512:(n + 1) * 512],
                          otc[:, k, tc * 128:(tc + 1) * 128], woc[:, k, n * 512:(n + 1) * 512],
                          start=(k == 0), stop=(k == DC - 1), w=wl,
                          inc="PE" if (k == DC - 1 and n == 1) else None)
        par = tc % 2
        xc = XC[par]
        ldw = [xo_free[par]] if xo_free[par] else [("ACT", qkv_done_act)]
        ldv = pb.dma(xc, xin_r[tc], "LD", w=ldw)
        addv = pb.v("tensor_tensor", out=xc, in0=xc, in1=ps, op=ALU.add,
                    w=[ldv, ("PE", mm)], inc="DVE")
        ev_prev[bank] = addv
        xo_free[par] = pb.dma(xout_r[tc], xc, "ST", w=[("DVE", addv)])
    pb.wait("sync", [("ST", pb.cnt["ST"])])



EPS_RMS = 1.1920929e-07


def emit_rep(pb, consts, row_io, width, dest, tmprow, pbank=6, w=(), prev=None):
    ld = pb.dma(tmprow[0:1, 0:width], row_io, "LD", w=list(w) + [prev])
    mm = None
    for n0 in range(0, width, 512):
        n1 = min(n0 + 512, width)
        wl = [ld, consts["_ld"], prev] if n0 == 0 else []
        mm = pb.t("matmul", pb.PS(pbank, 2)[:, n0:n1], consts["ones_row"][0:1, :],
                  tmprow[0:1, n0:n1], start=True, stop=True, w=wl,
                  inc="PE" if n1 == width else None)
    cp = pb.v("tensor_copy", out=dest, in_=pb.PS(pbank, 2)[:, 0:width],
              w=[("PE", mm)], inc="DVE")
    return ("DVE", cp)


def emit_up(pb, consts, w_hbm_T, in_ft, nin, nout, dst_ft, th0, tw, evac,
            wbufs_raw, wfree, extra_w=(), evac_w=None, entry=None,
            bias_cols=None, bias_w=None):
    evs = []
    bank_free = {0: entry, 2: entry}
    blk = 4 if nin <= 8 else 2
    wbufs = [b[:, 0:nin * blk * 128].rearrange("p (c f) -> p c f", c=nin)
             for b in wbufs_raw]
    for ob in range(nout // blk):
        par = ob % 2
        wld = pb.dma(wbufs[par], w_hbm_T[:, ob],
                     "LW", cast=True, w=[wfree[par]])
        wfree[par] = None
        mm_last = None
        for j in range(blk):
            oc = ob * blk + j
            bank = 0 if oc % 2 == 0 else 2
            ps = pb.PS(bank, 2)
            mm = None
            for k in range(nin):
                for n0 in range(0, tw, 512):
                    n1 = min(n0 + 512, tw)
                    wl = []
                    if k == 0 and n0 == 0:
                        wl = [wld if j == 0 else None, bank_free[bank]]
                        if ob < 2 and j == 0:
                            wl += list(extra_w)
                    mm = pb.t("matmul", ps[:, n0:n1],
                              wbufs[par][:, k, j * 128:(j + 1) * 128],
                              in_ft[:, k, th0 + n0:th0 + n1],
                              start=(k == 0), stop=(k == nin - 1), w=wl,
                              inc="PE" if (k == nin - 1 and n1 == tw) else None)
            dst = dst_ft[:, oc, 0:tw]
            ew = [("PE", mm)] + ([evac_w[oc]] if evac_w else [])
            if bias_cols is not None:
                ew.append(bias_w if oc < 2 else None)
                ev = pb.s("activation", out=dst, in_=ps[:, 0:tw], func=AF.Gelu,
                          bias=bias_cols[:, oc:oc + 1], w=ew, inc="ACT")
                evs.append(("ACT", ev))
            elif evac == "gelu":
                ev = pb.s("activation", out=dst, in_=ps[:, 0:tw], func=AF.Gelu, w=ew, inc="ACT")
                evs.append(("ACT", ev))
            elif evac == "silu":
                ev = pb.s("activation", out=dst, in_=ps[:, 0:tw], func=AF.Silu, w=ew, inc="ACT")
                evs.append(("ACT", ev))
            elif evac == "copy":
                ev = pb.s("copy", out=dst, in_=ps[:, 0:tw], w=ew, inc="ACT")
                evs.append(("ACT", ev))
            elif evac == "mult_into":
                ev = pb.v("tensor_tensor", out=dst, in0=ps[:, 0:tw], in1=dst,
                          op=ALU.mult, w=ew, inc="DVE")
                evs.append(("DVE", ev))
            bank_free[bank] = evs[-1]
            mm_last = mm
        wfree[par] = ("PE", mm_last)
    return evs


def emit_down(pb, consts, w_hbm_T, hsrc_ft, nhc, e_idx, COEF, XINT, th0, tw,
              wd_buf, wd_w=(), src_w=None, entry=None):
    wld = pb.dma(wd_buf, w_hbm_T.rearrange("(c p) f -> p c f", p=128)[:, :, 256:1024],
                 "LW", cast=True, w=wd_w)
    bank_free = {4: entry, 6: entry}
    last = None
    for tci in range(tw // 128):
        tc = th0 // 128 + tci
        bank = 4 if tci % 2 == 0 else 6
        ps = pb.PS(bank, 2)
        mm = None
        for hc in range(nhc):
            for (n0, n1) in ((0, 512), (512, 768)):
                wl = []
                if hc == 0 and n0 == 0:
                    wl = [wld, bank_free[bank]]
                    if src_w is not None:
                        wl.append(src_w)
                mm = pb.t("matmul", ps[:, n0:n1],
                          hsrc_ft[:, hc, tci * 128:(tci + 1) * 128],
                          wd_buf[:, hc, n0:n1],
                          start=(hc == 0), stop=(hc == nhc - 1), w=wl,
                          inc="PE" if (hc == nhc - 1 and n0 == 512) else None)
        coef = COEF[:, tc * 4 + e_idx:tc * 4 + e_idx + 1]
        xslice = chunks(XINT, TC)[:, tc][:, 256:1024]
        ev = pb.v("scalar_tensor_tensor", out=xslice, in0=ps[:, 0:768],
                  scalar=coef, in1=xslice, op0=ALU.mult, op1=ALU.add,
                  w=[("PE", mm)], inc="DVE")
        bank_free[bank] = ("DVE", ev)
        last = ("DVE", ev)
    return last


def emit_phase_b(pb, io, consts, al):
    cols = al.cols(["SQC", "SMC", "NEGMN", "EX2", "M2", "VARC", "SDC", "RSTD",
                    "MHS", "RMSV", "RMSD", "RMSR", "NEGL", "SUME", "RS",
                    "M1C", "M2C", "DEN", "RDEN", "W1C", "W2C"])
    NEGMH = al.r("NEGMH", 4)[:, 0:1]
    M4 = al.r("M4", 64)
    STATA = al.r("STATA", 320)
    SQ = al.r("SQ", D)
    XC = [al.r("XC0", D), al.r("XC1", D)]
    HB = [al.r("HB0", 512, BF16), al.r("HB1", 512, BF16)]
    N1REP = al.r("N1REP", D)
    N2REP = al.r("N2REP", D)
    QNREP = al.r("QNREP", 64)
    KNREP = al.r("KNREP", 64)
    GATEB = al.r("GATEB", 16, BF16)
    COEF = al.r("COEF", 64)
    PRB = al.r("PRB", 4)
    PRN = al.r("PRN", 4)
    MK1 = al.r("MK1", 4)
    MK2 = al.r("MK2", 4)
    PRM = al.r("PRM", 4)
    T2S = al.r("T2S", 4)
    QSC = al.r("QSC", 16)
    QSD = al.r("QSD", 16)
    QRS = al.r("QRS", 16)
    SCR2 = al.r("SCR2", 4096)
    FT1 = al.r("FT1", 8192, BF16)
    QK = al.r("QK", 16384)
    VX = al.r("VX", 8320)
    POOL = al.map["VX"][0] + 8320
    pool_words = ARENA_WORDS - POOL
    assert pool_words >= 8300, pool_words
    s2 = al.map["SCR2"][0]

    QNT = pb.R(al.map["QK"][0], 8192, BF16)
    KNT = pb.R(al.map["QK"][0] + 8192, 8192, BF16)
    XINT = pb.R(al.map["QK"][0], 16384)
    VEXT = pb.R(al.map["VX"][0], 8320, BF16)
    XFT = pb.R(al.map["VX"][0], 8192, BF16)
    XIC = [pb.R(s2, D), pb.R(s2 + 1024, D)]
    QRAW = XIC
    QTOK = [pb.R(s2 + 2048, 512, BF16), pb.R(s2 + 2560, 512, BF16)]
    ROWSC = pb.R(s2 + 3072, D)
    WS1 = [pb.R(s2, 2048, BF16), pb.R(s2 + 2048, 2048, BF16)]
    WS2 = WS1
    WB2 = [pb.R(POOL, 4096, BF16), pb.R(POOL + 4096, 4096, BF16)]
    ATB = [pb.R(POOL, 1024, BF16), pb.R(POOL + 1024, 1024, BF16)]
    LRW = pb.R(POOL + 2048, 2048)
    RHIB = pb.R(POOL + 4096, 1024, BF16)
    RLOB = pb.R(POOL + 5120, 1024, BF16)
    OSCR = pb.R(POOL + 2048, 1024, BF16)
    RBC = pb.R(POOL + 3072, 2048)
    WOB = pb.R(POOL, 3072, BF16)
    WD = pb.R(POOL, 6144, BF16)
    D1BQ = pb.R(POOL + 6144, 2048, BF16)
    assert POOL + 6144 + 2048 <= ARENA_WORDS

    xin_r = io["x_in"].rearrange("(tc p) d -> tc p d", p=128)
    xout_r = io["x_out"].rearrange("(tc p) d -> tc p d", p=128)
    xint_r = io["xint_scr"].rearrange("(tc p) d -> tc p d", p=128)

    r1 = emit_rep(pb, consts, io["n1_w"], D, N1REP, ROWSC, pbank=4)
    r2 = emit_rep(pb, consts, io["n2_w"], D, N2REP, ROWSC, pbank=4, prev=r1)
    r3 = emit_rep(pb, consts, io["qn_w"], HD, QNREP, ROWSC, pbank=4, prev=r2)
    rep_done = emit_rep(pb, consts, io["kn_w"], HD, KNREP, ROWSC, pbank=4, prev=r3)
    gld = pb.dma(GATEB.rearrange("p (c f) -> p c f", c=DC),
                 io["gate_b"].rearrange("(c p) f -> p c f", p=128), "LW", cast=True)
    const_v = pb.v("memset",
                   VEXT.rearrange("p (tc h e) -> p tc h e", tc=TC, h=NH)[:, :, :, 64:65],
                   1.0, inc="DVE")
    wq_ld = pb.dma(chunks(WB2[0], DC), io["wq_b"].rearrange("(c p) f -> p c f", p=128), "LW", cast=True)
    wk_ld = pb.dma(chunks(WB2[1], DC), io["wk_b"].rearrange("(c p) f -> p c f", p=128), "LW", cast=True)

    evac_prev = [0, 0]
    xc_free = [None, None]
    xic_free = [None, None]
    hb_free = [0, 0]
    for tc in range(TC):
        par = tc % 2
        xc = XC[par]
        ldv = pb.dma(xc, xin_r[tc], "LD", w=[xc_free[par]])
        pb.v("scalar_tensor_tensor", out=SQ, in0=xc, scalar=1.0, in1=xc,
             op0=ALU.mult, op1=ALU.mult, accum_out=cols["SQC"], w=[ldv])
        pb.v("reduce_sum", out=cols["SMC"], in_=xc, axis=AX.X)
        pb.v("tensor_scalar_mul", out=cols["NEGMN"], in0=cols["SMC"], scalar1=-1.0 / D)
        pb.v("tensor_scalar_mul", out=cols["EX2"], in0=cols["SQC"], scalar1=1.0 / D)
        pb.v("tensor_tensor", out=cols["M2"], in0=cols["NEGMN"], in1=cols["NEGMN"], op=ALU.mult)
        pb.v("tensor_tensor", out=cols["VARC"], in0=cols["EX2"], in1=cols["M2"], op=ALU.subtract)
        vv = pb.v("tensor_scalar_add", out=cols["VARC"], in0=cols["VARC"], scalar1=1e-5, inc="DVE")
        av = pb.s("activation", out=cols["SDC"], in_=cols["VARC"], func=AF.Sqrt,
                  w=[("DVE", vv)], inc="ACT")
        pb.v("reciprocal", out=cols["RSTD"], in_=cols["SDC"], w=[("ACT", av)])
        xic = XIC[par]
        xv = pb.v("tensor_scalar", out=xic, in0=xc, scalar1=cols["NEGMN"],
                  scalar2=cols["RSTD"], op0=ALU.add, op1=ALU.mult,
                  w=[xic_free[par]], inc="DVE")
        xc_free[par] = ("DVE", xv)
        sp = pb.dma(xint_r[tc], xic, "SP", w=[("DVE", xv)])
        pb.v("scalar_tensor_tensor", out=SQ, in0=xic, scalar=1.0, in1=xic,
             op0=ALU.mult, op1=ALU.mult, accum_out=cols["RMSV"])
        rv = pb.v("tensor_scalar", out=cols["RMSV"], in0=cols["RMSV"],
                  scalar1=1.0 / D, scalar2=EPS_RMS, op0=ALU.mult, op1=ALU.add, inc="DVE")
        a2 = pb.s("activation", out=cols["RMSD"], in_=cols["RMSV"], func=AF.Sqrt,
                  w=[("DVE", rv)], inc="ACT")
        pb.v("reciprocal", out=cols["RMSR"], in_=cols["RMSD"], w=[("ACT", a2)])
        hb = HB[par]
        nv = pb.v("scalar_tensor_tensor", out=hb, in0=xic, scalar=cols["RMSR"],
                  in1=N1REP, op0=ALU.mult, op1=ALU.mult,
                  w=[("PE", hb_free[par])] + ([rep_done] if tc == 0 else []), inc="DVE")
        xic_free[par] = (("DVE", nv) if True else None)
        xic_free[par] = sp
        pe_last, ev = emit_tok_to_ft(pb, consts, hb, FT1, tc, 6 + par,
                                     first_wait=[("DVE", nv), ("ACT", evac_prev[par])],
                                     evac_prev=0)
        hb_free[par] = pe_last
        evac_prev[par] = ev
    hrt_done = ("ACT", pb.cnt["ACT"])
    b1_sp = ("SP", pb.cnt["SP"])

    ft1c = chunks(FT1, DC)
    evac_bank = {0: None, 2: None}
    wlds = [wq_ld, wk_ld, None]
    for which in range(3):
        if which == 2:
            wlds[2] = pb.dma(chunks(WB2[0], DC),
                             io["wv_b"].rearrange("(c p) f -> p c f", p=128),
                             "LW", cast=True, w=[("PE", pe_after_q)])
        wb, wld = WB2[which % 2], wlds[which]
        wbc = chunks(wb, DC)
        for tc in range(TC):
            par = tc % 2
            bank = 0 if par == 0 else 2
            ps = pb.PS(bank, 2)
            mm = None
            for k in range(DC):
                for n in range(2):
                    wl = []
                    if k == 0 and n == 0:
                        wl = [wld, hrt_done, evac_bank[bank]]
                    mm = pb.t("matmul", ps[:, n * 512:(n + 1) * 512],
                              ft1c[:, k, tc * 128:(tc + 1) * 128],
                              wbc[:, k, n * 512:(n + 1) * 512],
                              start=(k == 0), stop=(k == DC - 1), w=wl,
                              inc="PE" if (k == DC - 1 and n == 1) else None)
            if which < 2:
                qraw = QRAW[par]
                qv = pb.v("tensor_copy", out=qraw, in_=ps,
                          w=[("PE", mm)] + ([b1_sp] if tc < 2 else []), inc="DVE")
                evac_bank[bank] = ("DVE", qv)
                pb.v("scalar_tensor_tensor", out=SQ, in0=qraw, scalar=1.0, in1=qraw,
                     op0=ALU.mult, op1=ALU.mult)
                pb.v("tensor_reduce", out=QSC, in_=chunks(SQ, NH), axis=AX.X, op=ALU.add)
                qs = pb.v("tensor_scalar", out=QSC, in0=QSC, scalar1=1.0 / HD,
                          scalar2=EPS_RMS, op0=ALU.mult, op1=ALU.add, inc="DVE")
                qa = pb.s("activation", out=QSD, in_=QSC, func=AF.Sqrt,
                          w=[("DVE", qs)], inc="ACT")
                pb.v("reciprocal", out=QRS, in_=QSD, w=[("ACT", qa)])
                qtok = QTOK[par]
                rep = QNREP if which == 0 else KNREP
                nv = None
                for hh in range(NH):
                    nv = pb.v("scalar_tensor_tensor",
                              out=qtok[:, hh * 64:(hh + 1) * 64],
                              in0=qraw[:, hh * 64:(hh + 1) * 64],
                              scalar=QRS[:, hh:hh + 1], in1=rep,
                              op0=ALU.mult, op1=ALU.mult,
                              w=[("PE", hb_free[par])] if hh == 0 else [], inc="DVE")
                dest = QNT if which == 0 else KNT
                pe_last, ev = emit_tok_to_ft(pb, consts, qtok, dest, tc, 6 + par,
                                             first_wait=[("DVE", nv)],
                                             evac_prev=evac_prev[par])
                hb_free[par] = pe_last
                evac_prev[par] = ev
            else:
                out_view = chunks(VEXT, TC)[:, tc].rearrange("p (h e) -> p h e", h=NH)[:, :, 0:64]
                ev = pb.s("copy", out=out_view, in_=ps.rearrange("p (h e) -> p h e", h=NH),
                          w=[("PE", mm), ("DVE", const_v) if tc == 0 else None], inc="ACT")
                evac_bank[bank] = ("ACT", ev)
        if which == 0:
            pe_after_q = pb.cnt["PE"]
    qkv_done = ("ACT", pb.cnt["ACT"])

    ovl = {"LRW": LRW, "RHIB": RHIB, "RLOB": RLOB, "RBC": RBC, "OSCR": OSCR}
    OT = FT1
    o_free = emit_attention(pb, io, consts, cols, M4, NEGMH, QNT, KNT, VEXT, OT,
                            ATB, ovl, scale=0.125, causal=True, entry_free=qkv_done)
    attn_pe = ("PE", pb.cnt["PE"])
    attn_dve = ("DVE", pb.cnt["DVE"])
    attn_act = ("ACT", pb.cnt["ACT"])

    xl = pb.dma(chunks(XINT, TC),
                io["xint_scr"].rearrange("(tc p) d -> p tc d", p=128),
                "LD", w=[attn_pe, attn_dve])
    wo_ld = pb.dma(chunks(WOB, DC),
                   io["wo_b"].rearrange("(c p) f -> p c f", p=128)[:, :, 256:1024],
                   "LW", cast=True, w=[attn_pe, attn_act, o_free])
    otc = chunks(OT, DC)
    wobc = chunks(WOB, DC)
    bank_free = {0: None, 2: None}
    for tc in range(TC):
        bank = 0 if tc % 2 == 0 else 2
        ps = pb.PS(bank, 2)
        mm = None
        for k in range(DC):
            for (n0, n1) in ((0, 512), (512, 768)):
                wl = []
                if k == 0 and n0 == 0:
                    wl = [wo_ld, o_free, bank_free[bank]]
                mm = pb.t("matmul", ps[:, n0:n1], otc[:, k, tc * 128:(tc + 1) * 128],
                          wobc[:, k, n0:n1], start=(k == 0), stop=(k == DC - 1),
                          w=wl, inc="PE" if (k == DC - 1 and n0 == 512) else None)
        xslice = chunks(XINT, TC)[:, tc][:, 256:1024]
        ev = pb.v("tensor_tensor", out=xslice, in0=xslice, in1=ps[:, 0:768],
                  op=ALU.add, w=[("PE", mm), xl], inc="DVE")
        bank_free[bank] = ("DVE", ev)

    scratch = {"XC": XC, "SQ": SQ, "HB": HB}
    xft_done = emit_ln_to_ft(pb, cols, consts, XFT, scratch, kind="rms",
                             src_sbuf=XINT, gamma_rep=N2REP, eps=EPS_RMS,
                             pbank=6, extra_first_wait=attn_pe, statarr=STATA)
    xftc = chunks(XFT, DC)
    gate_free = None
    for tc in range(TC):
        ps = pb.PS(4, 1)
        mm = None
        for k in range(DC):
            wl = []
            if k == 0:
                wl = [gld, xft_done, gate_free if gate_free else attn_dve]
            mm = pb.t("matmul", ps[:, 0:4], xftc[:, k, tc * 128:(tc + 1) * 128],
                      GATEB.rearrange("p (c f) -> p c f", c=DC)[:, k, :],
                      start=(k == 0), stop=(k == DC - 1), w=wl,
                      inc="PE" if k == DC - 1 else None)
        nm = pb.v("tensor_reduce", out=cols["NEGL"], in_=ps[:, 0:4], axis=AX.X,
                  op=ALU.max, negate=True, w=[("PE", mm)], inc="DVE")
        ea = pb.s("activation", out=PRB, in_=ps[:, 0:4], func=AF.Exp,
                  bias=cols["NEGL"], accum_out=cols["SUME"],
                  w=[("DVE", nm)], inc="ACT")
        pb.v("reciprocal", out=cols["RS"], in_=cols["SUME"], w=[("ACT", ea)])
        pb.v("tensor_single_scalar", out=PRN, in_=PRB, scalar=cols["RS"], op=ALU.mult)
        pb.v("tensor_reduce", out=cols["M1C"], in_=PRN, axis=AX.X, op=ALU.max)
        pb.v("tensor_single_scalar", out=MK1, in_=PRN, scalar=cols["M1C"], op=ALU.is_equal)
        pb.v("scalar_tensor_tensor", out=PRM, in0=MK1, scalar=-1e30, in1=PRN,
             op0=ALU.mult, op1=ALU.add)
        pb.v("tensor_reduce", out=cols["M2C"], in_=PRM, axis=AX.X, op=ALU.max)
        pb.v("tensor_single_scalar", out=MK2, in_=PRM, scalar=cols["M2C"], op=ALU.is_equal)
        pb.v("tensor_tensor", out=cols["DEN"], in0=cols["M1C"], in1=cols["M2C"], op=ALU.add)
        pb.v("tensor_scalar_add", out=cols["DEN"], in0=cols["DEN"], scalar1=1e-8)
        pb.v("reciprocal", out=cols["RDEN"], in_=cols["DEN"])
        pb.v("tensor_tensor", out=cols["W1C"], in0=cols["M1C"], in1=cols["RDEN"], op=ALU.mult)
        pb.v("tensor_tensor", out=cols["W2C"], in0=cols["M2C"], in1=cols["RDEN"], op=ALU.mult)
        pb.v("tensor_single_scalar", out=T2S, in_=MK2, scalar=cols["W2C"], op=ALU.mult)
        gf = pb.v("scalar_tensor_tensor", out=COEF[:, tc * 4:(tc + 1) * 4],
                  in0=MK1, scalar=cols["W1C"], in1=T2S,
                  op0=ALU.mult, op1=ALU.add, inc="DVE")
        gate_free = ("DVE", gf)

    h1t_half = FT1.rearrange("p (c t) -> p c t", c=NH)
    h1t_q = pb.R(al.map["FT1"][0], 4096, BF16).rearrange("p (c t) -> p c t", c=NH)
    h2t_q = pb.R(al.map["FT1"][0] + 4096, 4096, BF16).rearrange("p (c t) -> p c t", c=NH)
    wdv = WD.rearrange("p (c f) -> p c f", c=NH)
    wfree = [attn_pe, attn_pe]
    tail = gate_free
    wd_w = [gate_free, attn_dve, attn_act]
    for half in range(2):
        th0 = half * 1024
        ups = emit_up(pb, consts, io["e0_up_t"], xftc, DC, NH, h1t_half, th0, 1024,
                      "gelu", WS1, wfree, extra_w=[tail, xft_done], entry=tail)
        tail = emit_down(pb, consts, io["e0_down_t"], h1t_half, NH, 0, COEF, XINT,
                         th0, 1024, wdv, wd_w=wd_w, src_w=ups[-1], entry=tail)
        wd_w = [tail]
    for half in range(2):
        th0 = half * 1024
        u1 = emit_up(pb, consts, io["e1_w1_t"], xftc, DC, NH, h1t_half, th0, 1024,
                     "silu", WS1, wfree, extra_w=[tail], entry=tail)
        u2 = emit_up(pb, consts, io["e1_w2_t"], xftc, DC, NH, h1t_half, th0, 1024,
                     "mult_into", WS2, wfree, evac_w=u1, entry=u1[-1])
        tail = emit_down(pb, consts, io["e1_down_t"], h1t_half, NH, 1, COEF, XINT,
                         th0, 1024, wdv, wd_w=wd_w, src_w=u2[-1], entry=tail)
        wd_w = [tail]
    for quar in range(4):
        th0 = quar * 512
        a1 = emit_up(pb, consts, io["e2_l1_t"], xftc, DC, NH, h1t_q, th0, 512,
                     "gelu", WS1, wfree, extra_w=[tail], entry=tail)
        a2 = emit_up(pb, consts, io["e2_l2_t"], h1t_q, NH, NH, h2t_q, 0, 512,
                     "gelu", WS2, wfree, extra_w=[a1[-1]], entry=a1[-1])
        a3 = emit_up(pb, consts, io["e2_l3_t"], h2t_q, NH, NH, h1t_q, 0, 512,
                     "gelu", WS1, wfree, extra_w=[a2[-1]], entry=a2[-1])
        tail = emit_down(pb, consts, io["e2_l4_t"], h1t_q, NH, 2, COEF, XINT,
                         th0, 512, wdv, wd_w=wd_w, src_w=a3[-1], entry=tail)
        wd_w = [tail]
    d1bq = D1BQ.rearrange("p (c t) -> p c t", c=DC)
    for quar in range(4):
        th0 = quar * 512
        b1 = emit_up(pb, consts, io["e3_down_t"], xftc, DC, DC, d1bq, th0, 512,
                     "gelu", WS1, wfree, extra_w=[tail], entry=tail)
        b2 = emit_up(pb, consts, io["e3_up_t"], d1bq, DC, NH, h1t_q, 0, 512,
                     "gelu", WS2, wfree, extra_w=[b1[-1]], entry=b1[-1])
        tail = emit_down(pb, consts, io["e3_out_t"], h1t_q, NH, 3, COEF, XINT,
                         th0, 512, wdv, wd_w=wd_w, src_w=b2[-1], entry=tail)
        wd_w = [tail]

    xo_free = [None, None]
    for tc in range(TC):
        par = tc % 2
        xc = XC[par]
        ldv = pb.dma(xc, xin_r[tc], "LD", w=[xo_free[par] or wd_w[0]])
        addv = pb.v("tensor_tensor", out=xc, in0=xc, in1=chunks(XINT, TC)[:, tc],
                    op=ALU.add, w=[ldv] + ([wd_w[0]] if tc == 0 else []), inc="DVE")
        xo_free[par] = pb.dma(xout_r[tc], xc, "ST", w=[("DVE", addv)])
    pb.wait("sync", [("ST", pb.cnt["ST"])])


def pm_tile(wt):
    nin = wt.shape[0] // 128
    blk = 4 if nin <= 8 else 2
    bw = blk * 128
    nob = wt.shape[1] // bw
    return np.ascontiguousarray(wt.reshape(nin, 128, nob, bw).transpose(1, 2, 0, 3))


def prep_phase_b(inputs):
    tr = lambda k: np.ascontiguousarray(np.asarray(inputs[k]).T)
    d = {
        "n1_w": np.asarray(inputs["L0_ffn_n1_w"]).reshape(1, D),
        "n2_w": np.asarray(inputs["L0_ffn_n2_w"]).reshape(1, D),
        "qn_w": np.asarray(inputs["L0_ffn_qn_w"]).reshape(1, HD),
        "kn_w": np.asarray(inputs["L0_ffn_kn_w"]).reshape(1, HD),
        "wq_b": tr("L0_ffn_wq"), "wk_b": tr("L0_ffn_wk"),
        "wv_b": tr("L0_ffn_wv"), "wo_b": tr("L0_ffn_wo"),
        "gate_b": tr("L0_ffn_gate_w"),
        "e0_up_t": pm_tile(tr("L0_ffn_e0_up")), "e0_down_t": tr("L0_ffn_e0_down"),
        "e1_w1_t": pm_tile(tr("L0_ffn_e1_w1")), "e1_w2_t": pm_tile(tr("L0_ffn_e1_w2")),
        "e1_down_t": tr("L0_ffn_e1_down"),
        "e2_l1_t": pm_tile(tr("L0_ffn_e2_l1")), "e2_l2_t": pm_tile(tr("L0_ffn_e2_l2")),
        "e2_l3_t": pm_tile(tr("L0_ffn_e2_l3")), "e2_l4_t": tr("L0_ffn_e2_l4"),
        "e3_down_t": pm_tile(tr("L0_ffn_e3_down")), "e3_up_t": pm_tile(tr("L0_ffn_e3_up")),
        "e3_out_t": tr("L0_ffn_e3_out"),
    }
    d.update(make_consts("B"))
    return d



def _c_finish(pb, XC, xin_r, xout_r, w=()):
    xo_free = [None, None]
    for tc in range(TC):
        par = tc % 2
        ldv = pb.dma(XC[par], xin_r[tc], "LD", w=[xo_free[par]] + list(w))
        xo_free[par] = pb.dma(xout_r[tc], XC[par], "ST", w=[ldv])
    pb.wait("sync", [("ST", pb.cnt["ST"])])


C_UPTO = 9
C3SUB = 3

def emit_phase_c(pb, io, consts, al):
    cols = al.cols(["SQC", "SMC", "NEGMN", "EX2", "M2", "VARC", "SDC", "RSTD",
                    "DCOL", "RCC"])
    XC = [al.r("XC0", D), al.r("XC1", D)]
    HB = [al.r("HB0", 512, BF16), al.r("HB1", 512, BF16)]
    STATA = al.r("STATA", 320)
    SQ = al.r("SQ", D)
    TMP1 = SQ
    KVH = al.r("KVH", 512, BF16)
    KVL = al.r("KVL", 512, BF16)
    H2T = al.r("H2T", 8192, BF16)
    WQKVC = al.r("WQKVC", 12288, BF16)
    QFT = al.r("QFT", 8192, BF16)
    KF = al.r("KF", 8192, BF16)
    VEXT1 = al.r("VEXT1", 8320, BF16)
    assert al.off <= ARENA_WORDS, al.off
    OT1 = pb.R(al.map["H2T"][0], 8192, BF16)
    WOUTC = pb.R(al.map["WQKVC"][0], 4096, BF16)
    OTOK = pb.R(al.map["WQKVC"][0] + 4096, 8192, BF16)

    xin_r = io["x_in"].rearrange("(tc p) d -> tc p d", p=128)
    xout_r = io["x_out"].rearrange("(tc p) d -> tc p d", p=128)

    wq_ld = pb.dma(chunks(WQKVC, DC), io["w_qkv_c"].rearrange("(c p) f -> p c f", p=128),
                   "LW", cast=True)
    const_v = pb.v("memset",
                   VEXT1.rearrange("p (tc h e) -> p tc h e", tc=TC, h=NH)[:, :, :, 64:65],
                   1.0, inc="DVE")

    scratch = {"XC": XC, "SQ": SQ, "HB": HB}
    h2t_done = emit_ln_to_ft(pb, cols, consts, H2T, scratch, kind="ln",
                             xin_r=xin_r, pbank=6, statarr=STATA)

    if C_UPTO < 2:
        _c_finish(pb, XC, xin_r, xout_r)
        return
    h2c = chunks(H2T, DC)
    wqc = chunks(WQKVC, DC)
    QR = XC[0]
    RT = XC[1]
    cosv = chunks(consts["cos"], TC)
    sinv = chunks(consts["sin"], TC)
    evac_bank = {0: ("DVE", 1), 2: ("DVE", 1)}
    tr_evac = [0, 0]
    hb_free2 = [0, 0]
    ae_prev = None
    for which in range(3):
        for tc in range(TC):
            par = tc % 2
            bank = 0 if par == 0 else 2
            ps = pb.PS(bank, 2)
            mm = None
            for k in range(DC):
                for n in range(2):
                    wl = []
                    if k == 0 and n == 0:
                        wl = [wq_ld, h2t_done, evac_bank[bank]]
                    mm = pb.t("matmul", ps[:, n * 512:(n + 1) * 512],
                              h2c[:, k, tc * 128:(tc + 1) * 128],
                              wqc[:, k, which * 1024 + n * 512:which * 1024 + (n + 1) * 512],
                              start=(k == 0), stop=(k == DC - 1), w=wl,
                              inc="PE" if (k == DC - 1 and n == 1) else None)
            if which < 2:
                qv = pb.v("tensor_copy", out=QR, in_=ps,
                          w=[("PE", mm), ae_prev], inc="DVE")
                evac_bank[bank] = ("DVE", qv)
                qrv = QR.rearrange("p (h two x) -> p h two x", h=NH, two=2)
                rtv = RT.rearrange("p (h two x) -> p h two x", h=NH, two=2)
                pb.v("tensor_copy", out=rtv[:, :, 1], in_=qrv[:, :, 0])
                pb.v("tensor_scalar_mul", out=rtv[:, :, 0], in0=qrv[:, :, 1], scalar1=-1.0)
                cosb = cosv[:, tc][:, None, :].broadcast_to([128, NH, HD])
                sinb = sinv[:, tc][:, None, :].broadcast_to([128, NH, HD])
                pb.v("tensor_tensor", out=TMP1.rearrange("p (h x) -> p h x", h=NH),
                     in0=QR.rearrange("p (h x) -> p h x", h=NH),
                     in1=cosb, op=ALU.mult)
                pb.v("tensor_tensor", out=RT.rearrange("p (h x) -> p h x", h=NH),
                     in0=RT.rearrange("p (h x) -> p h x", h=NH),
                     in1=sinb, op=ALU.mult)
                rope_done = pb.v("tensor_tensor", out=QR, in0=TMP1, in1=RT,
                                 op=ALU.add, inc="DVE")
                sc = 0.125 if which == 0 else 1.0
                ae = pb.s("activation", out=TMP1, in_=QR, func=AF.Exp, scale=sc,
                          w=[("DVE", rope_done)], inc="ACT")
                ae_prev = ("ACT", ae)
                pb.v("tensor_scalar_min", out=TMP1, in0=TMP1, scalar1=1.0, w=[("ACT", ae)])
                pb.v("tensor_scalar", out=RT, in0=QR, scalar1=sc, scalar2=1.0,
                     op0=ALU.mult, op1=ALU.add)
                if which == 0:
                    fv = pb.v("tensor_tensor", out=HB[par], in0=RT, in1=TMP1,
                              op=ALU.max, w=[("PE", hb_free2[par])], inc="DVE")
                    pe_last, ev = emit_tok_to_ft(pb, consts, HB[par], QFT, tc, 6 + par,
                                                 first_wait=[("DVE", fv)],
                                                 evac_prev=tr_evac[par])
                    hb_free2[par] = pe_last
                    tr_evac[par] = ev
                else:
                    fv = pb.v("tensor_tensor", out=chunks(KF, TC)[:, tc], in0=RT,
                              in1=TMP1, op=ALU.max, inc="DVE")
            else:
                out_view = chunks(VEXT1, TC)[:, tc].rearrange("p (h e) -> p h e", h=NH)[:, :, 0:64]
                ev = pb.s("copy", out=out_view, in_=ps.rearrange("p (h e) -> p h e", h=NH),
                          w=[("PE", mm), ("DVE", const_v) if tc == 0 else None], inc="ACT")
                evac_bank[bank] = ("ACT", ev)
    qkv_done = ("ACT", pb.cnt["ACT"])
    qkv_dve = ("DVE", pb.cnt["DVE"])
    pe_after_qkv = ("PE", pb.cnt["PE"])
    wo_ld = pb.dma(chunks(WOUTC, DC), io["w_out_c"].rearrange("(c p) f -> p c f", p=128),
                   "LW", cast=True, w=[pe_after_qkv])

    if C_UPTO < 3:
        _c_finish(pb, XC, xin_r, xout_r, w=[qkv_done, qkv_dve])
        return
    kfv = chunks(KF, TC)
    vev = VEXT1.rearrange("p (tc h e) -> p tc h e", tc=TC, h=NH)
    kv_ps = pb.PS(0, 4)
    kvpsoff = lambda h: (h % 4) * 512 + (h // 4) * 66
    kvoff = lambda h: ((h // 2) // 7) * 512 + ((h // 2) % 7) * 65
    mm = None
    for h in range(NH):
        base = 64 * (h % 2)
        out = kv_ps[base:base + 64, kvpsoff(h):kvpsoff(h) + 65]
        for kt in range(TC):
            wl = []
            if h == 0 and kt == 0:
                wl = [qkv_done, qkv_dve]
            mm = pb.t("matmul", out, kfv[:, kt, h * 64:(h + 1) * 64],
                      vev[:, kt, h, :], start=(kt == 0), stop=(kt == TC - 1),
                      w=wl, inc="PE" if (h == NH - 1 and kt == TC - 1) else None)
    kh1 = kl1 = None
    for h in range(NH):
        base = 64 * (h % 2)
        src_ = kv_ps[base:base + 64, kvpsoff(h):kvpsoff(h) + 65]
        dsth = KVH[base:base + 64, kvoff(h):kvoff(h) + 65]
        dstl = KVL[base:base + 64, kvoff(h):kvoff(h) + 65]
        kh1 = pb.s("copy", out=dsth, in_=src_,
                   w=[("PE", mm)] if h == 0 else [], inc="ACT")
        kl1 = pb.v("tensor_tensor", out=dstl, in0=src_, in1=dsth,
                   op=ALU.subtract, w=[("ACT", kh1)], inc="DVE")

    if C_UPTO < 4:
        _c_finish(pb, XC, xin_r, xout_r, w=[("DVE", kl1)])
        return
    zoff = lambda h: (h % 4) * 512 + (h // 4) * 66
    z_free = {0: None}
    tr_evac2 = [0, 0]
    hb_free3 = [0, 0]
    for tc in range(TC):
        par = tc % 2
        zb = 0
        zp = pb.PS(zb, 4)
        mm = {}
        for h in range(NH):
            base = 64 * (h % 2)
            qsl = chunks(QFT, DC)[base:base + 64, h // 2, tc * 128:(tc + 1) * 128]
            out = zp[:, zoff(h):zoff(h) + 65]
            wl = []
            if h == 0:
                wl = [("DVE", kl1), z_free[zb]]
            pb.t("matmul", out, qsl, KVH[base:base + 64, kvoff(h):kvoff(h) + 65],
                 start=True, stop=False, w=wl)
            mm[h] = pb.t("matmul", out, qsl,
                         KVL[base:base + 64, kvoff(h):kvoff(h) + 65],
                         start=False, stop=True, inc="PE" if h == NH - 1 else None)
        last = None
        if C3SUB >= 3:
            for h in range(NH):
                dn = pb.v("tensor_scalar_max", out=cols["DCOL"],
                          in0=zp[:, zoff(h) + 64:zoff(h) + 65], scalar1=1e-6,
                          w=([("PE", mm[NH - 1])] if h == 0 else []) +
                            ([("ACT", last)] if last else []), inc="DVE")
                rc = pb.v("reciprocal", out=cols["RCC"], in_=cols["DCOL"],
                          w=[("DVE", dn)], inc="DVE")
                sc_ap = cols["RCC"] if C_UPTO >= 5 else 1.0
                last = pb.s("activation", out=chunks(OTOK, TC)[:, tc][:, h * 64:(h + 1) * 64],
                            in_=zp[:, zoff(h):zoff(h) + 64], func=AF.Copy,
                            scale=sc_ap, w=[("DVE", rc)] + ([wo_ld] if tc == 0 and h == 0 else []),
                            inc="ACT")
        else:
            for h in range(NH):
                last = pb.s("copy", out=chunks(OTOK, TC)[:, tc][:, h * 64:(h + 1) * 64],
                            in_=zp[:, zoff(h):zoff(h) + 64],
                            w=[("PE", mm[NH - 1])] if h == 0 else [], inc="ACT")
        z_free[zb] = ("ACT", last)
        if C3SUB >= 2:
            fv = pb.v("tensor_copy", out=HB[par], in_=chunks(OTOK, TC)[:, tc],
                      w=[("ACT", last), ("PE", hb_free3[par])], inc="DVE")
            pe_last, ev = emit_tok_to_ft(pb, consts, HB[par], OT1, tc, 6 + par,
                                         first_wait=[("DVE", fv)], evac_prev=tr_evac2[par])
            hb_free3[par] = pe_last
            tr_evac2[par] = ev
    ot_done = ("ACT", pb.cnt["ACT"])

    if C_UPTO < 5:
        _c_finish(pb, XC, xin_r, xout_r, w=[ot_done])
        return
    ev_prev = {0: None, 2: None}
    xo_free = [None, None]
    otc = chunks(OT1, DC)
    woc = chunks(WOUTC, DC)
    for tc in range(TC):
        bank = 0 if tc % 2 == 0 else 2
        ps = pb.PS(bank, 2)
        mm = None
        for k in range(DC):
            for n in range(2):
                wl = []
                if k == 0 and n == 0:
                    wl = [ot_done, wo_ld,
                          ("DVE", ev_prev[bank]) if ev_prev[bank] else None]
                mm = pb.t("matmul", ps[:, n * 512:(n + 1) * 512],
                          otc[:, k, tc * 128:(tc + 1) * 128], woc[:, k, n * 512:(n + 1) * 512],
                          start=(k == 0), stop=(k == DC - 1), w=wl,
                          inc="PE" if (k == DC - 1 and n == 1) else None)
        par = tc % 2
        xc = XC[par]
        ldw = [xo_free[par] or qkv_dve, ot_done]
        ldv = pb.dma(xc, xin_r[tc], "LD", w=ldw)
        addv = pb.v("tensor_tensor", out=xc, in0=xc, in1=ps, op=ALU.add,
                    w=[ldv, ("PE", mm)], inc="DVE")
        ev_prev[bank] = addv
        xo_free[par] = pb.dma(xout_r[tc], xc, "ST", w=[("DVE", addv)])
    pb.wait("sync", [("ST", pb.cnt["ST"])])


def prep_phase_c(inputs):
    d = {"w_qkv_c": np.ascontiguousarray(np.asarray(inputs["L1_attn_W_qkv"])),
         "w_out_c": np.ascontiguousarray(np.asarray(inputs["L1_attn_W_out"]))}
    d.update(make_consts("C"))
    return d



def emit_phase_d(pb, io, consts, al):
    cols = al.cols(["SQC", "SMC", "NEGMN", "EX2", "M2", "VARC", "SDC", "RSTD",
                    "LMX", "NEG", "SUME", "RS"])
    SQ = al.r("SQ", D)
    HB = [al.r("HB0", 512, BF16), al.r("HB1", 512, BF16)]
    STATA = al.r("STATA", 320)
    MK = al.r("MK", 8)
    COEFD = al.r("COEFD", 128)
    B1S = al.r("B1S", 128)
    B1C = al.r("B1C", 32)
    B2ROW = al.r("B2ROW", 1024)
    B2HI = al.r("B2HI", 512, BF16)
    B2LO = al.r("B2LO", 512, BF16)
    XD = al.r("XD", 16384)
    XFT = al.r("XFT", 8192, BF16)
    H1T = al.r("H1T", 16384, BF16)
    WS1 = [al.r("WS1a", 2048, BF16), al.r("WS1b", 2048, BF16)]
    WS2 = [al.r("WS2a", 512, BF16), al.r("WS2b", 512, BF16)]
    assert al.off <= ARENA_WORDS, al.off

    xin_r = io["x_in"].rearrange("(tc p) d -> tc p d", p=128)
    xout_r = io["x_out"].rearrange("(tc p) d -> tc p d", p=128)
    GB = al.r("GB", 32, BF16)
    g_ld = pb.dma(GB.rearrange("p (c f) -> p c f", c=DC),
                  io["gate_d"].rearrange("(c p) f -> p c f", p=128), "LW", cast=True)

    xd_ld = pb.dma(chunks(XD, TC), io["x_in"].rearrange("(tc p) d -> p tc d", p=128), "LD")
    scratch = {"XC": None, "SQ": SQ, "HB": HB}
    xft_done = emit_ln_to_ft(pb, cols, consts, XFT, scratch, kind="ln",
                             src_sbuf=XD, pbank=6, extra_first_wait=xd_ld,
                             statarr=STATA)
    xftc = chunks(XFT, DC)

    gate_free = None
    for tc in range(TC):
        ps = pb.PS(4, 1)
        mm = None
        for k in range(DC):
            wl = []
            if k == 0:
                wl = [g_ld, xft_done, gate_free]
            mm = pb.t("matmul", ps[:, 0:8], xftc[:, k, tc * 128:(tc + 1) * 128],
                      GB.rearrange("p (c f) -> p c f", c=DC)[:, k, :],
                      start=(k == 0), stop=(k == DC - 1), w=wl,
                      inc="PE" if k == DC - 1 else None)
        lm = pb.v("tensor_reduce", out=cols["LMX"], in_=ps[:, 0:8], axis=AX.X,
                  op=ALU.max, w=[("PE", mm)], inc="DVE")
        ng = pb.v("tensor_scalar_mul", out=cols["NEG"], in0=cols["LMX"], scalar1=-1.0,
                  inc="DVE")
        ea = pb.s("activation", out=MK, in_=ps[:, 0:8], func=AF.Exp,
                  bias=cols["NEG"], accum_out=cols["SUME"], w=[("DVE", ng)], inc="ACT")
        pb.v("reciprocal", out=cols["RS"], in_=cols["SUME"], w=[("ACT", ea)])
        pb.v("tensor_single_scalar", out=MK, in_=ps[:, 0:8], scalar=cols["LMX"],
             op=ALU.is_equal, w=[("ACT", ea)])
        gf = pb.v("tensor_single_scalar", out=COEFD[:, tc * 8:(tc + 1) * 8],
                  in_=MK, scalar=cols["RS"], op=ALU.mult, inc="DVE")
        gate_free = ("DVE", gf)

    h1t = H1T.rearrange("p (c t) -> p c t", c=32)
    tail = gate_free
    wfree = [gate_free, gate_free]
    for e in range(8):
        b1l = pb.dma(B1S.rearrange("p f -> p f")[0:32, :], io["b1_d"][e].rearrange("(c f) -> c f", f=128), "LD", w=[tail])
        bt = pb.t("transpose", pb.PS(6, 1)[0:128, 0:32], B1S[0:32, 0:128],
                  consts["ident_f"][0:32, 0:32], w=[b1l, tail], inc="PE")
        bc = pb.v("tensor_copy", out=B1C, in_=pb.PS(6, 1)[:, 0:32], w=[("PE", bt)], inc="DVE")
        b2l = pb.dma(B2ROW[0:1, :], io["b2_d"][e][None, :], "LD", w=[tail])
        bh = pb.v("tensor_copy", out=B2HI[0:1, 0:1024], in_=B2ROW[0:1, :], w=[b2l], inc="DVE")
        bl = pb.v("tensor_tensor", out=B2LO[0:1, 0:1024], in0=B2ROW[0:1, :],
                  in1=B2HI[0:1, 0:1024], op=ALU.subtract, w=[("DVE", bh)], inc="DVE")
        w1r = io["w1_d"][e]
        w2r = io["w2_d"][e].rearrange("(c p) f -> p c f", p=128)
        for half in range(2):
            th0 = half * 1024
            ups = emit_up(pb, consts, w1r, xftc, DC, 32, h1t, th0, 1024,
                          None, WS1, wfree, extra_w=[("DVE", bl), xft_done, tail],
                          entry=tail, bias_cols=B1C, bias_w=("DVE", bc))
            up_last = ups[-1]
            for yh in (0, 512):
                mmq = {}
                for hc in range(32):
                    par = hc % 2
                    wld = pb.dma(WS2[par][:, 0:512], w2r[:, hc, yh:yh + 512],
                                 "LW", cast=True, w=[wfree[par]])
                    wfree[par] = None
                    for tci in range(8):
                        tc = th0 // 128 + tci
                        ps = pb.PS(tci, 1)
                        wl = []
                        if hc < 2 and tci == 0:
                            wl = [wld, up_last, tail]
                        elif tci == 0:
                            wl = [wld]
                        mm = pb.t("matmul", ps, h1t[:, hc, tci * 128:(tci + 1) * 128],
                                  WS2[par][:, 0:512], start=(hc == 0), stop=False,
                                  w=wl, inc="PE" if tci == 7 else None)
                        mmq[tci] = mm
                    wfree[par] = ("PE", mmq[7])
                last_ev = None
                for tci in range(8):
                    tc = th0 // 128 + tci
                    ps = pb.PS(tci, 1)
                    pb.t("matmul", ps, consts["ones_bf"][0:1, 0:128],
                         B2HI[0:1, yh:yh + 512], start=False, stop=False,
                         w=[("DVE", bl)] if (tci == 0 and yh == 0) else [])
                    mm = pb.t("matmul", ps, consts["ones_bf"][0:1, 0:128],
                              B2LO[0:1, yh:yh + 512], start=False, stop=True,
                              inc="PE")
                    coef = COEFD[:, tc * 8 + e:tc * 8 + e + 1]
                    xslice = chunks(XD, TC)[:, tc][:, yh:yh + 512]
                    ev = pb.v("scalar_tensor_tensor", out=xslice, in0=ps,
                              scalar=coef, in1=xslice, op0=ALU.mult, op1=ALU.add,
                              w=[("PE", mm)], inc="DVE")
                    last_ev = ("DVE", ev)
                tail = last_ev

    st = None
    for tc in range(TC):
        st = pb.dma(xout_r[tc], chunks(XD, TC)[:, tc], "ST", w=[tail])
    pb.wait("sync", [("ST", pb.cnt["ST"])])


def prep_phase_d(inputs):
    w1 = np.asarray(inputs["L1_ffn_w1"])
    d = {"gate_d": np.ascontiguousarray(np.asarray(inputs["L1_ffn_gate_w"])),
         "w1_d": np.stack([pm_tile(w1[e]) for e in range(8)]),
         "b1_d": np.ascontiguousarray(np.asarray(inputs["L1_ffn_b1"])),
         "w2_d": np.ascontiguousarray(np.asarray(inputs["L1_ffn_w2"])),
         "b2_d": np.ascontiguousarray(np.asarray(inputs["L1_ffn_b2"]))}
    d.update(make_consts("D"))
    return d



def make_consts(phase):
    c = {}
    c["ident_bf"] = np.eye(128).astype(ml_dtypes.bfloat16)
    c["ident_f"] = np.eye(128, dtype=np.float32)
    c["ones_row"] = np.ones((1, 128), np.float32)
    c["ones_bf"] = np.ones((128, 128), np.float32).astype(ml_dtypes.bfloat16)
    if phase == "B":
        i = np.arange(128)
        c["mask"] = np.where(i[:, None] >= i[None, :], 0.0, -1e30).astype(np.float32)
        c["maskT"] = np.where(i[None, :] >= i[:, None], 0.0, -1e30).astype(np.float32)
    if phase == "C":
        hd = HD
        inv_freq = 1.0 / (500000.0 ** (np.arange(0, hd, 2, dtype=np.float32) / hd))
        t = np.arange(T, dtype=np.float32)
        freqs = t[:, None] * inv_freq[None, :]
        emb = np.concatenate([freqs, freqs], axis=-1)
        c["cos"] = np.cos(emb).astype(np.float32)
        c["sin"] = np.sin(emb).astype(np.float32)
    return c


def prep_phase_a(inputs):
    W = np.asarray(inputs["L0_attn_W_qkv"])
    Wr = np.ascontiguousarray(W.reshape(1024, 16, 64, 3).transpose(0, 3, 1, 2).reshape(1024, 3072))
    d = {"w_qkv_a": Wr,
         "w_out_a": np.ascontiguousarray(np.asarray(inputs["L0_attn_W_out"]))}
    d.update(make_consts("A"))
    return d


_PROG_CACHE = {}
_ALLOC_MAP = {}


def build_phase(phase):
    if phase in _PROG_CACHE:
        return _PROG_CACHE[phase]
    pb = PB()
    for s in ("LD", "LWa", "LWb", "LWc", "LWd", "PE", "ACT", "DVE", "ST", "SP"):
        pb.sem(s)
    pb.arena_init()
    io = {}
    io["x_in"] = pb.dram("x_in", [T, D])
    io["x_out"] = pb.dram("x_out", [T, D], kind="ExternalOutput")
    io["ident_bf"] = pb.dram("ident_bf", [128, 128], BF16)
    io["ident_f"] = pb.dram("ident_f", [128, 128])
    io["ones_row"] = pb.dram("ones_row", [1, 128])
    io["ones_bf"] = pb.dram("ones_bf", [128, 128], BF16)
    al = Alloc(pb, 0)
    if phase == "A":
        io["w_qkv_a"] = pb.dram("w_qkv_a", [1024, 3072])
        io["w_out_a"] = pb.dram("w_out_a", [1024, 1024])
        consts = emit_consts(pb, io, al)
        emit_phase_a(pb, io, consts, al)
    elif phase == "B":
        io["mask"] = pb.dram("mask", [128, 128])
        io["maskT"] = pb.dram("maskT", [128, 128])
        io["n1_w"] = pb.dram("n1_w", [1, D])
        io["n2_w"] = pb.dram("n2_w", [1, D])
        io["qn_w"] = pb.dram("qn_w", [1, HD])
        io["kn_w"] = pb.dram("kn_w", [1, HD])
        for nm in ("wq_b", "wk_b", "wv_b", "wo_b"):
            io[nm] = pb.dram(nm, [1024, 1024])
        io["e3_down_t"] = pb.dram("e3_down_t", [128, 2, 8, 512])
        io["gate_b"] = pb.dram("gate_b", [1024, 4])
        for nm in ("e0_up_t", "e1_w1_t", "e1_w2_t", "e2_l1_t", "e3_up_t"):
            io[nm] = pb.dram(nm, [128, 4, 8, 512])
        for nm in ("e0_down_t", "e1_down_t", "e2_l4_t", "e3_out_t"):
            io[nm] = pb.dram(nm, [2048, 1024])
        for nm in ("e2_l2_t", "e2_l3_t"):
            io[nm] = pb.dram(nm, [128, 8, 16, 256])
        io["xint_scr"] = pb.dram("xint_scr", [T, D], kind="Internal")
        consts = emit_consts(pb, io, al, need=("mask",))
        emit_phase_b(pb, io, consts, al)
    elif phase == "C":
        io["w_qkv_c"] = pb.dram("w_qkv_c", [1024, 3072])
        io["w_out_c"] = pb.dram("w_out_c", [1024, 1024])
        io["cos"] = pb.dram("cos", [T, HD])
        io["sin"] = pb.dram("sin", [T, HD])
        consts = emit_consts(pb, io, al, need=("rope", "lite"))
        emit_phase_c(pb, io, consts, al)
    elif phase == "D":
        io["gate_d"] = pb.dram("gate_d", [1024, 8])
        io["w1_d"] = pb.dram("w1_d", [8, 128, 8, 8, 512])
        io["b1_d"] = pb.dram("b1_d", [8, 4096])
        io["w2_d"] = pb.dram("w2_d", [8, 4096, 1024])
        io["b2_d"] = pb.dram("b2_d", [8, 1024])
        consts = emit_consts(pb, io, al)
        emit_phase_d(pb, io, consts, al)
    else:
        raise NotImplementedError(phase)
    _ALLOC_MAP.clear()
    _ALLOC_MAP.update(al.map)
    nc = pb.finalize()
    _PROG_CACHE[phase] = nc
    return nc


def build_fused():
    if "FUSED" in _PROG_CACHE:
        return _PROG_CACHE["FUSED"]
    pb = PB()
    for s in ("LD", "LWa", "LWb", "LWc", "LWd", "PE", "ACT", "DVE", "ST", "SP"):
        pb.sem(s)
    pb.arena_init()
    io = {}
    io["ident_bf"] = pb.dram("ident_bf", [128, 128], BF16)
    io["ident_f"] = pb.dram("ident_f", [128, 128])
    io["ones_row"] = pb.dram("ones_row", [1, 128])
    io["ones_bf"] = pb.dram("ones_bf", [128, 128], BF16)
    io["mask"] = pb.dram("mask", [128, 128])
    io["maskT"] = pb.dram("maskT", [128, 128])
    io["cos"] = pb.dram("cos", [T, HD])
    io["sin"] = pb.dram("sin", [T, HD])
    io["w_qkv_a"] = pb.dram("w_qkv_a", [1024, 3072])
    io["w_out_a"] = pb.dram("w_out_a", [1024, 1024])
    io["n1_w"] = pb.dram("n1_w", [1, D])
    io["n2_w"] = pb.dram("n2_w", [1, D])
    io["qn_w"] = pb.dram("qn_w", [1, HD])
    io["kn_w"] = pb.dram("kn_w", [1, HD])
    for nm in ("wq_b", "wk_b", "wv_b", "wo_b"):
        io[nm] = pb.dram(nm, [1024, 1024])
    io["e3_down_t"] = pb.dram("e3_down_t", [128, 2, 8, 512])
    io["gate_b"] = pb.dram("gate_b", [1024, 4])
    for nm in ("e0_up_t", "e1_w1_t", "e1_w2_t", "e2_l1_t", "e3_up_t"):
        io[nm] = pb.dram(nm, [128, 4, 8, 512])
    for nm in ("e0_down_t", "e1_down_t", "e2_l4_t", "e3_out_t"):
        io[nm] = pb.dram(nm, [2048, 1024])
    for nm in ("e2_l2_t", "e2_l3_t"):
        io[nm] = pb.dram(nm, [128, 8, 16, 256])
    io["w_qkv_c"] = pb.dram("w_qkv_c", [1024, 3072])
    io["w_out_c"] = pb.dram("w_out_c", [1024, 1024])
    io["gate_d"] = pb.dram("gate_d", [1024, 8])
    io["w1_d"] = pb.dram("w1_d", [8, 128, 8, 8, 512])
    io["b1_d"] = pb.dram("b1_d", [8, 4096])
    io["w2_d"] = pb.dram("w2_d", [8, 4096, 1024])
    io["b2_d"] = pb.dram("b2_d", [8, 1024])
    x_in = pb.dram("x_in", [SEQ_PER_CORE, T, D])
    x_out = pb.dram("x_out", [SEQ_PER_CORE, T, D], kind="ExternalOutput")
    xa = pb.dram("xa", [SEQ_PER_CORE, T, D], kind="Internal")
    xb = pb.dram("xb", [SEQ_PER_CORE, T, D], kind="Internal")
    xc = pb.dram("xc", [SEQ_PER_CORE, T, D], kind="Internal")
    xint = pb.dram("xint4", [SEQ_PER_CORE, T, D], kind="Internal")
    chain = {"A": (x_in, xa), "B": (xa, xb), "C": (xb, xc), "D": (xc, x_out)}
    needs = {"A": (), "B": ("mask",), "C": ("rope", "lite"), "D": ()}
    emitters = {"A": emit_phase_a, "B": emit_phase_b, "C": emit_phase_c,
                "D": emit_phase_d}
    for phase in "ABCD":
        for s in range(SEQ_PER_CORE):
            al = Alloc(pb, 0)
            consts = emit_consts(pb, io, al, need=needs[phase])
            pio = dict(io)
            pio["x_in"] = chain[phase][0][s]
            pio["x_out"] = chain[phase][1][s]
            if phase == "B":
                pio["xint_scr"] = xint[s]
            emitters[phase](pb, pio, consts, al)
            pb.section_barrier()
    nc = pb.finalize()
    _PROG_CACHE["FUSED"] = nc
    return nc


FUSED = True


def kernel(**inputs):
    if FUSED:
        return kernel_fused(**inputs)
    return kernel_unfused(**inputs)


def kernel_fused(**inputs):
    X = np.asarray(inputs["L0_attn_X"])
    prep = {}
    for p in (prep_phase_a(inputs), prep_phase_b(inputs),
              prep_phase_c(inputs), prep_phase_d(inputs)):
        prep.update(p)
    prep.update(make_consts("B"))
    prep.update(make_consts("C"))
    nc = build_fused()
    Xs = X.reshape(NCORES, SEQ_PER_CORE, T, D)
    in_maps = [dict(prep, x_in=np.ascontiguousarray(Xs[c])) for c in range(NCORES)]
    res = run_bass_kernel_spmd(nc, in_maps, core_ids=list(range(NCORES)))
    out = np.stack([res.results[c]["x_out"] for c in range(NCORES)])
    return np.ascontiguousarray(out.reshape(32, T, D))


def kernel_unfused(**inputs):
    X = np.asarray(inputs["L0_attn_X"])
    preps = {"A": prep_phase_a(inputs), "B": prep_phase_b(inputs),
             "C": prep_phase_c(inputs), "D": prep_phase_d(inputs)}
    cur = X.reshape(NCORES, SEQ_PER_CORE, T, D)
    for phase in "ABCD":
        nc = build_phase(phase)
        outs = np.empty_like(cur)
        for s in range(SEQ_PER_CORE):
            in_maps = [dict(preps[phase], x_in=np.ascontiguousarray(cur[c, s]))
                       for c in range(NCORES)]
            res = run_bass_kernel_spmd(nc, in_maps, core_ids=list(range(NCORES)))
            for c in range(NCORES):
                outs[c, s] = res.results[c]["x_out"]
        cur = outs
    return np.ascontiguousarray(cur.reshape(32, T, D))


# version 07 (final)
# speedup vs baseline: 1.4858x; 1.0907x over previous
import numpy as np
import ml_dtypes
from contextlib import ExitStack

import concourse.bass as bass
import concourse.mybir as mybir
from concourse.bass_utils import run_bass_kernel_spmd

F32 = mybir.dt.float32
BF16 = mybir.dt.bfloat16
AF = mybir.ActivationFunctionType
ALU = mybir.AluOpType
AX = mybir.AxisListType

T = 2048
TC = 16
D = 1024
DC = 8
NH = 16
HD = 64
NCORES = 8
SEQ_PER_CORE = 4

ARENA_WORDS = 52992



class PB:
    ENGINES = ("sync", "gpsimd", "tensor", "scalar", "vector")

    def __init__(self):
        self.nc = bass.Bass("TRN2", target_bir_lowering=False)
        self.ctx = ExitStack()
        self.q = {e: [] for e in self.ENGINES}
        self.sems = {}
        self.cnt = {}
        self._wdb = {}
        self._rdb = {}
        self._watermark = {}
        self._wm_all = {}

    def sem(self, name):
        self.sems[name] = self.ctx.enter_context(self.nc.semaphore(name))
        self.cnt[name] = 0

    def dram(self, name, shape, dt=F32, kind="ExternalInput"):
        return self.nc.dram_tensor(name, list(shape), dt, kind=kind).ap()

    def arena_init(self):
        self._arena = self.ctx.enter_context(
            self.nc.sbuf_tensor("arena", [128, ARENA_WORDS], F32))
        self._psum = self.ctx.enter_context(
            self.nc.psum_tensor("ps", [128, 4096], F32))

    def R(self, off, words, dt=F32):
        assert off >= 0 and off + words <= ARENA_WORDS, (off, words)
        ap = self._arena.ap()[:, off:off + words]
        if dt == BF16:
            ap = ap.bitcast(BF16)
        return ap

    def PS(self, bank, nbanks=1):
        return self._psum.ap()[:, bank * 512:(bank + nbanks) * 512]

    _WRITE_KEYS = ("out", "dest", "accum_out")
    _READ_KEYS = ("in_", "in0", "in1", "scalar", "scalar1", "scalar2", "bias",
                  "scale", "data0", "data1", "mask", "on_true", "on_false",
                  "data", "initial")
    _AUTO_ENGS = ("vector", "scalar")

    @staticmethod
    def _interval(ap):
        import concourse.mybir as _mb
        esz = _mb.dt.size(ap.dtype)
        pairs = list(ap.ap)
        pstride = pairs[0][0]
        pcount = pairs[0][1]
        off = ap.offset
        plo = off // pstride if pstride else 0
        off_in = off - plo * pstride
        span = 1
        lo = 0
        hi = 0
        for step, count in pairs[1:]:
            if step >= 0:
                hi += step * (count - 1)
            else:
                lo += step * (count - 1)
        return (ap.tensor.name, (off_in + lo) * esz, (off_in + hi + 1) * esz,
                plo, plo + pcount)

    def _deps(self, eng, args, kwargs):
        from concourse.ap import AP as _AP
        reads, writes = [], []
        for k, v in kwargs.items():
            if isinstance(v, _AP):
                (writes if k in self._WRITE_KEYS else reads).append(v)
        for v in args:
            if isinstance(v, _AP):
                writes.append(v)
        need = 0
        rr = [self._interval(a) for a in reads]
        ww = [self._interval(a) for a in writes]

        def overlap(a, b):
            return (a[0] == b[0] and a[1] < b[2] and b[1] < a[2]
                    and a[3] < b[4] and b[3] < a[4])

        wdb = self._wdb.setdefault(eng, {})
        rdb = self._rdb.setdefault(eng, {})
        for iv in rr:
            for rec in wdb.get(iv[0], ()):
                if overlap(iv, rec[0]):
                    need = max(need, rec[1])
        for iv in ww:
            for rec in wdb.get(iv[0], ()):
                if overlap(iv, rec[0]):
                    need = max(need, rec[1])
            for rec in rdb.get(iv[0], ()):
                if overlap(iv, rec[0]):
                    need = max(need, rec[1])
        ordinal = self.cnt[self._eng_sem[eng]] + 1
        for iv in ww:
            lst = wdb.setdefault(iv[0], [])
            lst[:] = [r for r in lst if not (r[0] == iv)]
            lst.append((iv, ordinal))
            if len(lst) > 256:
                del lst[0]
        for iv in rr:
            lst = rdb.setdefault(iv[0], [])
            lst[:] = [r for r in lst if not (r[0] == iv)]
            lst.append((iv, ordinal))
            if len(lst) > 256:
                del lst[0]
        return need

    _eng_sem = {"vector": "DVE", "scalar": "ACT"}

    def op(self, eng, method, *args, w=(), inc=None, **kwargs):
        auto = eng in self._AUTO_ENGS
        wm_all = self._wm_all.setdefault(eng, {})
        wl = []
        best = {}
        for item in w:
            if item is None:
                continue
            sname, val = item
            if val and val > 0 and val > best.get(sname, 0):
                best[sname] = val
        for sname, val in best.items():
            if val > wm_all.get(sname, 0):
                wl.append((sname, val))
                wm_all[sname] = val
        if auto:
            sem = self._eng_sem[eng]
            need = self._deps(eng, args, kwargs)
            wm = max(self._watermark.get(eng, 0), wm_all.get(sem, 0))
            if need > wm:
                wl.append((sem, need))
                wm = need
                wm_all[sem] = need
            self._watermark[eng] = wm
            assert inc is None or inc == sem or inc == (sem, 1), inc
            self.cnt[sem] += 1
            ret = self.cnt[sem]
            inc = (sem, 1)
        elif inc is not None:
            if isinstance(inc, str):
                inc = (inc, 1)
            self.cnt[inc[0]] += inc[1]
            ret = self.cnt[inc[0]]
        else:
            ret = None
        self.q[eng].append((method, args, kwargs, tuple(wl), inc))
        return ret

    def t(self, m, *a, **k):
        return self.op("tensor", m, *a, **k)

    def v(self, m, *a, **k):
        return self.op("vector", m, *a, **k)

    def s(self, m, *a, **k):
        return self.op("scalar", m, *a, **k)

    def dma(self, out, in_, incsem, w=(), cast=False, **kw):
        eng = "gpsimd" if cast else "sync"
        if incsem == "LW":
            self._lw_rr = (getattr(self, "_lw_rr", -1) + 1) % 4
            incsem = "LW" + "abcd"[self._lw_rr]
        w = list(w) + [(incsem, self.cnt[incsem])]
        v = self.op(eng, "dma_start", w=w, inc=(incsem, 16), out=out, in_=in_, **kw)
        return (incsem, v)

    def wait(self, eng, w):
        wm_all = self._wm_all.setdefault(eng, {})
        wl = []
        for item in w:
            if not item or not item[1]:
                continue
            sname, val = item
            if val > wm_all.get(sname, 0):
                wl.append((sname, val))
                wm_all[sname] = val
        if wl:
            self.q[eng].append(("__wait__", (), {}, tuple(wl), None))

    def section_barrier(self):
        cur = [(s, self.cnt[s]) for s in self.cnt]
        for eng in self.ENGINES:
            self.wait(eng, cur)
        self._wdb.clear()
        self._rdb.clear()
        for eng in self._AUTO_ENGS:
            self._watermark[eng] = self.cnt[self._eng_sem[eng]]

    def finalize(self):
        block = self.ctx.enter_context(self.nc.Block())
        sems = self.sems

        def mk(ops):
            def body(eng):
                for method, args, kwargs, waits, inc in ops:
                    for sname, val in waits:
                        eng.wait_ge(sems[sname], val)
                    if method == "__wait__":
                        continue
                    r = getattr(eng, method)(*args, **kwargs)
                    if inc is not None:
                        r.then_inc(sems[inc[0]], inc[1])
            return body

        for eng in self.ENGINES:
            if self.q[eng]:
                getattr(block, eng)(mk(self.q[eng]))
        self.ctx.close()
        return self.nc


def chunks(ap, n):
    return ap.rearrange("p (c t) -> p c t", c=n)



class Alloc:
    def __init__(self, pb, base=0):
        self.pb = pb
        self.off = base
        self.map = {}

    def r(self, name, words, dt=F32):
        ap = self.pb.R(self.off, words, dt)
        self.map[name] = (self.off, words)
        self.off += words
        return ap

    def cols(self, names):
        return {n: self.r("col_" + n, 4)[:, 0:1] for n in names}


def emit_consts(pb, io, al, need=()):
    c = {}
    c["scr"] = al.r("scr", 16)
    c["ident_bf"] = al.r("ident_bf", 64, BF16)
    pb.dma(c["ident_bf"], io["ident_bf"], "LD")
    if "lite" not in need:
        c["ident_f"] = al.r("ident_f", 128)
        c["ones_row"] = al.r("ones_row", 128)
        c["ones_bf"] = al.r("ones_bf", 64, BF16)
        pb.dma(c["ident_f"], io["ident_f"], "LD")
        pb.dma(c["ones_row"][0:1, :], io["ones_row"], "LD")
        pb.dma(c["ones_bf"], io["ones_bf"], "LD")
    if "mask" in need:
        c["mask"] = al.r("mask", 128)
        c["maskT"] = al.r("maskT", 128)
        pb.dma(c["mask"], io["mask"], "LD")
        pb.dma(c["maskT"], io["maskT"], "LD")
    if "rope" in need:
        c["cos"] = al.r("cos", 1024)
        c["sin"] = al.r("sin", 1024)
        pb.dma(chunks(c["cos"], TC), io["cos"].rearrange("(tc p) d -> p tc d", p=128), "LD")
        pb.dma(chunks(c["sin"], TC), io["sin"].rearrange("(tc p) d -> p tc d", p=128), "LD")
    c["_ld"] = ("LD", pb.cnt["LD"])
    return c



def emit_ln_to_ft(pb, cols, consts, out_ft, scratch, kind="ln", xin_r=None,
                  src_sbuf=None, gamma_rep=None, gamma_wait=None, eps=1e-5,
                  pbank=6, extra_first_wait=None, statarr=None):
    SMA = statarr[:, 0:64]
    SQA = statarr[:, 64:128]
    M2A = statarr[:, 128:192]
    VARA = statarr[:, 192:256]
    SDA = statarr[:, 256:320]
    sv = lambda a: a.rearrange("p (n s) -> p n s", s=4)[:, :, 0]
    xc_free = [None, None]
    last_stat = None
    for tc in range(TC):
        par = tc % 2
        if src_sbuf is not None:
            xc = chunks(src_sbuf, TC)[:, tc]
            v_w = [extra_first_wait] if (tc == 0 and extra_first_wait) else []
        else:
            xc = scratch["XC"][par]
            ldv = pb.dma(xc, xin_r[tc], "LD", w=[xc_free[par]])
            v_w = [ldv]
            if tc == 0 and extra_first_wait:
                v_w.append(extra_first_wait)
        s1 = pb.v("scalar_tensor_tensor", out=scratch["SQ"], in0=xc, scalar=1.0,
                  in1=xc, op0=ALU.mult, op1=ALU.mult,
                  accum_out=SQA[:, 4 * tc:4 * tc + 1], w=v_w, inc="DVE")
        if kind == "ln":
            s1 = pb.v("reduce_sum", out=SMA[:, 4 * tc:4 * tc + 1], in_=xc,
                      axis=AX.X, inc="DVE")
        xc_free[par] = ("DVE", s1)
    if kind == "ln":
        pb.v("tensor_scalar_mul", out=sv(SMA), in0=sv(SMA), scalar1=-1.0 / D)
        pb.v("tensor_scalar_mul", out=sv(SQA), in0=sv(SQA), scalar1=1.0 / D)
        pb.v("tensor_tensor", out=sv(M2A), in0=sv(SMA), in1=sv(SMA), op=ALU.mult)
        pb.v("tensor_tensor", out=sv(VARA), in0=sv(SQA), in1=sv(M2A), op=ALU.subtract)
        vv = pb.v("tensor_scalar_add", out=sv(VARA), in0=sv(VARA),
                  scalar1=float(eps), inc="DVE")
    else:
        vv = pb.v("tensor_scalar", out=sv(VARA), in0=sv(SQA), scalar1=1.0 / D,
                  scalar2=float(eps), op0=ALU.mult, op1=ALU.add, inc="DVE")
    av = pb.s("activation", out=sv(SDA), in_=sv(VARA), func=AF.Sqrt,
              w=[("DVE", vv)], inc="ACT")
    rv = pb.v("reciprocal", out=sv(VARA), in_=sv(SDA), w=[("ACT", av)], inc="DVE")
    evac_prev = [0, 0]
    hb_free = [0, 0]
    last_evac = 0
    for tc in range(TC):
        par = tc % 2
        if src_sbuf is not None:
            xc = chunks(src_sbuf, TC)[:, tc]
            v_w = []
        else:
            xc = scratch["XC"][par]
            ldv = pb.dma(xc, xin_r[tc], "LD", w=[xc_free[par]])
            v_w = [ldv]
        hb = scratch["HB"][par]
        wl = v_w + [("DVE", rv), ("PE", hb_free[par])]
        if kind == "ln":
            nv = pb.v("tensor_scalar", out=hb, in0=xc,
                      scalar1=SMA[:, 4 * tc:4 * tc + 1],
                      scalar2=VARA[:, 4 * tc:4 * tc + 1],
                      op0=ALU.add, op1=ALU.mult, w=wl, inc="DVE")
        else:
            gw = [gamma_wait] if (tc == 0 and gamma_wait) else []
            nv = pb.v("scalar_tensor_tensor", out=hb, in0=xc,
                      scalar=VARA[:, 4 * tc:4 * tc + 1], in1=gamma_rep,
                      op0=ALU.mult, op1=ALU.mult, w=wl + gw, inc="DVE")
        xc_free[par] = ("DVE", nv)
        pbk = pbank if par == 0 else pbank + 1
        ps_bf = pb.PS(pbk, 1).bitcast(BF16)
        pe_last = None
        for ci in range(DC):
            wlist = []
            if ci == 0:
                wlist = [("DVE", nv), ("ACT", evac_prev[par]),
                         consts["_ld"] if tc == 0 else None]
            pe_last = pb.t("transpose", ps_bf[:, ci * 128:(ci + 1) * 128],
                           hb[:, ci * 128:(ci + 1) * 128], consts["ident_bf"],
                           w=wlist, inc="PE" if ci == DC - 1 else None)
        hb_free[par] = pe_last
        out_view = chunks(out_ft, DC)[:, :, tc * 128:(tc + 1) * 128]
        last_evac = pb.s("copy", out=out_view, in_=chunks(ps_bf, DC),
                         w=[("PE", pe_last)], inc="ACT")
        evac_prev[par] = last_evac
    return ("ACT", last_evac)


def emit_tok_to_ft(pb, consts, tok_bf, ft, tc, pbank, first_wait=(), evac_prev=0):
    ps_bf = pb.PS(pbank, 1).bitcast(BF16)
    pe_last = None
    for ci in range(DC):
        wl = list(first_wait) + [("ACT", evac_prev)] if ci == 0 else []
        pe_last = pb.t("transpose", ps_bf[:, ci * 128:(ci + 1) * 128],
                       tok_bf[:, ci * 128:(ci + 1) * 128], consts["ident_bf"],
                       w=wl, inc="PE" if ci == DC - 1 else None)
    ev = pb.s("copy", out=chunks(ft, DC)[:, :, tc * 128:(tc + 1) * 128],
              in_=chunks(ps_bf, DC), w=[("PE", pe_last)], inc="ACT")
    return pe_last, ev


def emit_attention(pb, io, consts, cols, M4, NEGMHA, QT, KT, VEXT, OT, ATB, ovl,
                   scale, causal, entry_free):
    LRW, RBC, OSCR = ovl["LRW"], ovl["RBC"], ovl["OSCR"]

    def head_sl(ft, h):
        base = 64 * (h % 2)
        return chunks(ft, DC)[base:base + 64, h // 2]

    s_free = {0: entry_free, 4: entry_free}
    qts = [3, 7, 11, 15] if causal else [0, 5, 10, 15]
    for h in range(NH):
        qh = head_sl(QT, h)
        kh = head_sl(KT, h)
        for j, qt in enumerate(qts):
            bank = 0 if j % 2 == 0 else 4
            ps = pb.PS(bank, 4)
            kmax = (qt + 1) * 128 if causal else T
            nblk = (kmax + 511) // 512
            mm = None
            for n in range(nblk):
                w0 = n * 512
                w1 = min(w0 + 512, kmax)
                wl = [s_free[bank]] if n == 0 else []
                mm = pb.t("matmul", ps[:, w0:w1],
                          qh[:, qt * 128:(qt + 1) * 128], kh[:, w0:w1],
                          start=True, stop=True, w=wl,
                          inc="PE" if n == nblk - 1 else None)
            if causal:
                pb.v("tensor_tensor", out=ps[:, qt * 128:(qt + 1) * 128],
                     in0=ps[:, qt * 128:(qt + 1) * 128], in1=consts["mask"],
                     op=ALU.add, w=[("PE", mm)])
            mx = pb.v("tensor_reduce", out=M4[:, 4 * j:4 * j + 1],
                      in_=ps[:, 0:kmax].rearrange("p (n s) -> p n s", s=4)[:, :, 0],
                      axis=AX.X, op=ALU.max, w=[("PE", mm)], inc="DVE")
            s_free[bank] = ("DVE", mx)
        vv = pb.v("tensor_reduce", out=cols["VARC"],
                  in_=M4.rearrange("p (n s) -> p n s", s=4)[:, 0:len(qts), 0],
                  axis=AX.X, op=ALU.max, inc="DVE")
        pt = pb.t("transpose", pb.PS(0, 1)[0:1, 0:128], cols["VARC"], consts["ident_f"],
                  w=[("DVE", vv), s_free[0]], inc="PE")
        mv = pb.v("tensor_reduce", out=cols["MHS"][0:1, :], in_=pb.PS(0, 1)[0:1, 0:128],
                  axis=AX.X, op=ALU.max, w=[("PE", pt)], inc="DVE")
        bc = pb.t("matmul", pb.PS(1, 1)[:, 0:1], consts["ones_row"][0:1, :],
                  cols["MHS"][0:1, 0:1], start=True, stop=True, w=[("DVE", mv)], inc="PE")
        ng = pb.v("tensor_scalar_mul", out=NEGMHA[:, 4 * h:4 * h + 1],
                  in0=pb.PS(1, 1)[:, 0:1], scalar1=-scale, w=[("PE", bc)], inc="DVE")
        s_free[0] = ("DVE", ng)
    pass1_done = ("DVE", pb.cnt["DVE"])

    o_free = None
    g_free = {0: pass1_done, 1024: pass1_done}
    atb_free2 = {}
    for h in range(NH):
        qh = head_sl(QT, h)
        kh = head_sl(KT, h)
        negmh = NEGMHA[:, 4 * h:4 * h + 1]
        av_last = None
        for kt in range(TC):
            par = kt % 2
            ps = pb.PS(0, 4)
            q0 = kt * 128 if causal else 0
            po = pb.PS(4, 4)
            vsl = chunks(VEXT, TC)[:, kt][:, h * 65:(h + 1) * 65]
            groups = []
            for g0 in (0, 1024):
                ga = max(q0, g0)
                gb = g0 + 1024
                if ga >= gb:
                    continue
                blks = []
                pos = ga
                while pos < gb:
                    w1 = min((pos // 512 + 1) * 512, gb)
                    blks.append((pos, w1))
                    pos = w1
                groups.append((g0, ga, gb, blks))
            mmg = {}
            madds = {}
            for g0, ga, gb, blks in groups:
                mm = None
                for i, (b0, b1) in enumerate(blks):
                    wl = [g_free[g0]] if i == 0 else []
                    if h == 0 and kt < 2:
                        wl.append(pass1_done)
                    mm = pb.t("matmul", ps[:, b0:b1],
                              kh[:, kt * 128:(kt + 1) * 128], qh[:, b0:b1],
                              start=True, stop=True, w=wl,
                              inc="PE" if i == len(blks) - 1 else None)
                mmg[g0] = mm
                if causal and g0 <= kt * 128 < gb:
                    madds[g0] = pb.v("tensor_tensor",
                                     out=ps[:, kt * 128:(kt + 1) * 128],
                                     in0=ps[:, kt * 128:(kt + 1) * 128],
                                     in1=consts["maskT"], op=ALU.add,
                                     w=[("PE", mm)], inc="DVE")
            exg = {}
            for g0, ga, gb, blks in groups:
                exw = [("PE", mmg[g0]), atb_free2.get((par, g0))]
                if g0 in madds:
                    exw.append(("DVE", madds[g0]))
                exg[g0] = pb.s("activation", out=ATB[par][:, ga:gb], in_=ps[:, ga:gb],
                               func=AF.Exp, bias=negmh, scale=scale, w=exw, inc="ACT")
                g_free[g0] = ("ACT", exg[g0])
            for g0, ga, gb, blks in groups:
                for i, (b0, b1) in enumerate(blks):
                    wl = [("ACT", exg[g0])] if i == 0 else []
                    if kt == 0 and i == 0 and g0 == 0 and o_free:
                        wl.append(o_free)
                    bank = b0 // 512
                    stop = (kt == 4 * bank + 3) if causal else (kt == TC - 1)
                    av_last = pb.t("matmul", po[0:65, b0:b1], vsl, ATB[par][:, b0:b1],
                                   start=(kt == 0), stop=stop, w=wl,
                                   inc="PE" if (kt == TC - 1 and i == len(blks) - 1) else None)
                    atb_free2[(par, g0)] = ("PE", av_last)
        po = pb.PS(4, 4)
        r1 = pb.v("tensor_copy", out=LRW[64:65, 0:T], in_=po[64:65, :],
                  w=[("PE", av_last)], inc="DVE")
        b1 = pb.dma(ovl["LSCR"], LRW[64:65, 0:T], "LD", w=[("DVE", r1)])
        bd = pb.dma(RBC[0:64, :], ovl["LSCR"].broadcast_to([64, T]), "LD", w=[b1])
        rec = pb.v("reciprocal", out=RBC[0:64, :], in_=RBC[0:64, :], w=[bd], inc="DVE")
        if h % 2 == 0:
            dest = chunks(OT, DC)[0:64, h // 2]
            ov = pb.v("tensor_tensor", out=dest, in0=po[0:64, :], in1=RBC[0:64, :],
                      op=ALU.mult, w=[("DVE", rec)], inc="DVE")
            o_free = ("DVE", ov)
        else:
            ov = pb.v("tensor_tensor", out=OSCR[0:64, 0:T], in0=po[0:64, :],
                      in1=RBC[0:64, :], op=ALU.mult, w=[("DVE", rec)], inc="DVE")
            dst = chunks(OT, DC)[64:128, h // 2]
            o_free = pb.dma(dst, OSCR[0:64, 0:T], "LD", w=[("DVE", ov)])
    return o_free



def emit_phase_a(pb, io, consts, al):
    cols = al.cols(["SQC", "SMC", "NEGMN", "EX2", "M2", "VARC", "SDC", "RSTD", "MHS"])
    STATA = al.r("STATA", 320)
    NEGMHA = al.r("NEGMHA", 64)
    M4 = al.r("M4", 64)
    SQ = al.r("SQ", D)
    XC = [al.r("XC0", D), al.r("XC1", D)]
    HB = [al.r("HB0", 512, BF16), al.r("HB1", 512, BF16)]
    H0T = al.r("H0T", 8192, BF16)
    WQKV = al.r("WQKV", 12288, BF16)
    QT = al.r("QT", 8192, BF16)
    KT = al.r("KT", 8192, BF16)
    VEXT = al.r("VEXT", 8320, BF16)
    ATB = [al.r("ATB0", 1024, BF16), al.r("ATB1", 1024, BF16)]
    assert al.off <= ARENA_WORDS, al.off
    h0t_off = al.map["H0T"][0]
    LRW = pb.R(h0t_off, 2048)
    RHIB = pb.R(h0t_off + 2048, 1024, BF16)
    RLOB = pb.R(h0t_off + 3072, 1024, BF16)
    RBC = pb.R(h0t_off + 4096, 2048)
    OSCR = pb.R(h0t_off + 6144, 1024, BF16)
    wq_off = al.map["WQKV"][0]
    OT = pb.R(wq_off, 8192, BF16)
    WOUT = pb.R(wq_off + 8192, 4096, BF16)

    wq1 = pb.dma(chunks(WQKV, DC), io["w_qkv_a"].rearrange("(c p) f -> p c f", p=128), "LW", cast=True)
    const_v = pb.v("memset",
                   VEXT.rearrange("p (tc h e) -> p tc h e", tc=TC, h=NH)[:, :, :, 64:65],
                   1.0, inc="DVE")
    xin_r = io["x_in"].rearrange("(tc p) d -> tc p d", p=128)

    scratch = {"XC": XC, "SQ": SQ, "HB": HB}
    h0t_done = emit_ln_to_ft(pb, cols, consts, H0T, scratch, kind="ln",
                             xin_r=xin_r, pbank=6, statarr=STATA)

    wqc = chunks(WQKV, DC)
    h0c = chunks(H0T, DC)
    evac_val = {0: None, 4: None}
    last_pe_v = None
    for which, dest in ((0, QT), (1, KT)):
        for oc in range(DC):
            bank = 0 if oc % 2 == 0 else 4
            ps = pb.PS(bank, 4)
            mm = None
            for k in range(DC):
                lhsT = wqc[:, k, which * 1024 + oc * 128: which * 1024 + (oc + 1) * 128]
                for n in range(4):
                    wl = []
                    if k == 0 and n == 0:
                        wl = [wq1, h0t_done,
                              ("ACT", evac_val[bank]) if evac_val[bank] else None]
                    mm = pb.t("matmul", ps[:, n * 512:(n + 1) * 512], lhsT,
                              h0c[:, k, n * 512:(n + 1) * 512],
                              start=(k == 0), stop=(k == DC - 1), w=wl,
                              inc="PE" if (k == DC - 1 and n == 3) else None)
            evac_val[bank] = pb.s("copy", out=chunks(dest, DC)[:, oc], in_=ps,
                                  w=[("PE", mm)], inc="ACT")
    last_qk_evac = max(evac_val[0], evac_val[4])
    vv_evac = {0: None, 2: None}
    for tc in range(TC):
        bank = 0 if tc % 2 == 0 else 2
        ps = pb.PS(bank, 2)
        mm = None
        for k in range(DC):
            lhsT = h0c[:, k, tc * 128:(tc + 1) * 128]
            for n in range(2):
                wl = []
                if k == 0 and n == 0:
                    if tc < 2:
                        wl = [("ACT", last_qk_evac), ("DVE", const_v)]
                    else:
                        wl = [("ACT", vv_evac[bank])]
                mm = pb.t("matmul", ps[:, n * 512:(n + 1) * 512], lhsT,
                          wqc[:, k, 2048 + n * 512:2048 + (n + 1) * 512],
                          start=(k == 0), stop=(k == DC - 1), w=wl,
                          inc="PE" if (k == DC - 1 and n == 1) else None)
        out_view = chunks(VEXT, TC)[:, tc].rearrange("p (h e) -> p h e", h=NH)[:, :, 0:64]
        vv_evac[bank] = pb.s("copy", out=out_view,
                             in_=ps.rearrange("p (h e) -> p h e", h=NH),
                             w=[("PE", mm)], inc="ACT")
    qkv_done_act = pb.cnt["ACT"]
    pe_after_qkv = pb.cnt["PE"]
    wo1 = pb.dma(chunks(WOUT, DC), io["w_out_a"].rearrange("(c p) f -> p c f", p=128), "LW",
                 cast=True, w=[("PE", pe_after_qkv)])

    ovl = {"LRW": LRW, "RBC": RBC, "OSCR": OSCR, "LSCR": io["lrow_scr"]}
    o_free = emit_attention(pb, io, consts, cols, M4, NEGMHA, QT, KT, VEXT, OT,
                            ATB, ovl, scale=0.125, causal=False,
                            entry_free=("ACT", qkv_done_act))

    xout_r = io["x_out"].rearrange("(tc p) d -> tc p d", p=128)
    ev_prev = {0: None, 2: None}
    xo_free = [None, None]
    otc = chunks(OT, DC)
    woc = chunks(WOUT, DC)
    for tc in range(TC):
        bank = 0 if tc % 2 == 0 else 2
        ps = pb.PS(bank, 2)
        mm = None
        for k in range(DC):
            for n in range(2):
                wl = []
                if k == 0 and n == 0:
                    wl = [o_free, wo1,
                          ("DVE", ev_prev[bank]) if ev_prev[bank] else None]
                mm = pb.t("matmul", ps[:, n * 512:(n + 1) * 512],
                          otc[:, k, tc * 128:(tc + 1) * 128], woc[:, k, n * 512:(n + 1) * 512],
                          start=(k == 0), stop=(k == DC - 1), w=wl,
                          inc="PE" if (k == DC - 1 and n == 1) else None)
        par = tc % 2
        xc = XC[par]
        ldw = [xo_free[par]] if xo_free[par] else [("ACT", qkv_done_act)]
        ldv = pb.dma(xc, xin_r[tc], "LD", w=ldw)
        addv = pb.v("tensor_tensor", out=xc, in0=xc, in1=ps, op=ALU.add,
                    w=[ldv, ("PE", mm)], inc="DVE")
        ev_prev[bank] = addv
        xo_free[par] = pb.dma(xout_r[tc], xc, "ST", w=[("DVE", addv)])
    pb.wait("sync", [("ST", pb.cnt["ST"])])



EPS_RMS = 1.1920929e-07


def emit_rep(pb, consts, row_io, width, dest, tmprow, pbank=6, w=(), prev=None):
    ld = pb.dma(tmprow[0:1, 0:width], row_io, "LD", w=list(w) + [prev])
    mm = None
    for n0 in range(0, width, 512):
        n1 = min(n0 + 512, width)
        wl = [ld, consts["_ld"], prev] if n0 == 0 else []
        mm = pb.t("matmul", pb.PS(pbank, 2)[:, n0:n1], consts["ones_row"][0:1, :],
                  tmprow[0:1, n0:n1], start=True, stop=True, w=wl,
                  inc="PE" if n1 == width else None)
    cp = pb.v("tensor_copy", out=dest, in_=pb.PS(pbank, 2)[:, 0:width],
              w=[("PE", mm)], inc="DVE")
    return ("DVE", cp)


def emit_up(pb, consts, w_hbm_T, in_ft, nin, nout, dst_ft, th0, tw, evac,
            wbufs_raw, wfree, extra_w=(), evac_w=None, entry=None,
            bias_cols=None, bias_w=None):
    evs = []
    bank_free = {0: entry, 2: entry}
    blk = 4 if nin <= 8 else 2
    wbufs = [b[:, 0:nin * blk * 128].rearrange("p (c f) -> p c f", c=nin)
             for b in wbufs_raw]
    for ob in range(nout // blk):
        par = ob % 2
        wld = pb.dma(wbufs[par], w_hbm_T[:, ob],
                     "LW", cast=True, w=[wfree[par]])
        wfree[par] = None
        mm_last = None
        for j in range(blk):
            oc = ob * blk + j
            bank = 0 if oc % 2 == 0 else 2
            ps = pb.PS(bank, 2)
            mm = None
            for k in range(nin):
                for n0 in range(0, tw, 512):
                    n1 = min(n0 + 512, tw)
                    wl = []
                    if k == 0 and n0 == 0:
                        wl = [wld if j == 0 else None, bank_free[bank]]
                        if ob < 2 and j == 0:
                            wl += list(extra_w)
                    mm = pb.t("matmul", ps[:, n0:n1],
                              wbufs[par][:, k, j * 128:(j + 1) * 128],
                              in_ft[:, k, th0 + n0:th0 + n1],
                              start=(k == 0), stop=(k == nin - 1), w=wl,
                              inc="PE" if (k == nin - 1 and n1 == tw) else None)
            dst = dst_ft[:, oc, 0:tw]
            ew = [("PE", mm)] + ([evac_w[oc]] if evac_w else [])
            if bias_cols is not None:
                ew.append(bias_w if oc < 2 else None)
                ev = pb.s("activation", out=dst, in_=ps[:, 0:tw], func=AF.Gelu,
                          bias=bias_cols[:, oc:oc + 1], w=ew, inc="ACT")
                evs.append(("ACT", ev))
            elif evac == "gelu":
                ev = pb.s("activation", out=dst, in_=ps[:, 0:tw], func=AF.Gelu, w=ew, inc="ACT")
                evs.append(("ACT", ev))
            elif evac == "silu":
                ev = pb.s("activation", out=dst, in_=ps[:, 0:tw], func=AF.Silu, w=ew, inc="ACT")
                evs.append(("ACT", ev))
            elif evac == "copy":
                ev = pb.s("copy", out=dst, in_=ps[:, 0:tw], w=ew, inc="ACT")
                evs.append(("ACT", ev))
            elif evac == "mult_into":
                ev = pb.v("tensor_tensor", out=dst, in0=ps[:, 0:tw], in1=dst,
                          op=ALU.mult, w=ew, inc="DVE")
                evs.append(("DVE", ev))
            bank_free[bank] = evs[-1]
            mm_last = mm
        wfree[par] = ("PE", mm_last)
    return evs


def emit_down(pb, consts, w_hbm_T, hsrc_ft, nhc, e_idx, COEF, XINT, th0, tw,
              wd_buf, wd_w=(), src_w=None, entry=None):
    wld = pb.dma(wd_buf, w_hbm_T.rearrange("(c p) f -> p c f", p=128)[:, :, 256:1024],
                 "LW", cast=True, w=wd_w)
    bank_free = {4: entry, 6: entry}
    last = None
    for tci in range(tw // 128):
        tc = th0 // 128 + tci
        bank = 4 if tci % 2 == 0 else 6
        ps = pb.PS(bank, 2)
        mm = None
        for hc in range(nhc):
            for (n0, n1) in ((0, 512), (512, 768)):
                wl = []
                if hc == 0 and n0 == 0:
                    wl = [wld, bank_free[bank]]
                    if src_w is not None:
                        wl.append(src_w)
                mm = pb.t("matmul", ps[:, n0:n1],
                          hsrc_ft[:, hc, tci * 128:(tci + 1) * 128],
                          wd_buf[:, hc, n0:n1],
                          start=(hc == 0), stop=(hc == nhc - 1), w=wl,
                          inc="PE" if (hc == nhc - 1 and n0 == 512) else None)
        coef = COEF[:, tc * 4 + e_idx:tc * 4 + e_idx + 1]
        xslice = chunks(XINT, TC)[:, tc][:, 256:1024]
        ev = pb.v("scalar_tensor_tensor", out=xslice, in0=ps[:, 0:768],
                  scalar=coef, in1=xslice, op0=ALU.mult, op1=ALU.add,
                  w=[("PE", mm)], inc="DVE")
        bank_free[bank] = ("DVE", ev)
        last = ("DVE", ev)
    return last


def emit_phase_b(pb, io, consts, al):
    cols = al.cols(["SQC", "SMC", "NEGMN", "EX2", "M2", "VARC", "SDC", "RSTD",
                    "MHS", "RMSV", "RMSD", "RMSR", "NEGL", "SUME", "RS",
                    "M1C", "M2C", "DEN", "RDEN", "W1C", "W2C"])
    NEGMHA = al.r("NEGMHA", 64)
    M4 = al.r("M4", 64)
    STATA = al.r("STATA", 320)
    SQ = al.r("SQ", D)
    XC = [al.r("XC0", D), al.r("XC1", D)]
    HB = [al.r("HB0", 512, BF16), al.r("HB1", 512, BF16)]
    N1REP = al.r("N1REP", D)
    N2REP = al.r("N2REP", D)
    QNREP = al.r("QNREP", 64)
    KNREP = al.r("KNREP", 64)
    GATEB = al.r("GATEB", 16, BF16)
    COEF = al.r("COEF", 64)
    PRB = al.r("PRB", 4)
    PRN = al.r("PRN", 4)
    MK1 = al.r("MK1", 4)
    MK2 = al.r("MK2", 4)
    PRM = al.r("PRM", 4)
    T2S = al.r("T2S", 4)
    QSC = al.r("QSC", 16)
    QSD = al.r("QSD", 16)
    QRS = al.r("QRS", 16)
    SCR2 = al.r("SCR2", 4096)
    FT1 = al.r("FT1", 8192, BF16)
    QK = al.r("QK", 16384)
    VX = al.r("VX", 8320)
    POOL = al.map["VX"][0] + 8320
    pool_words = ARENA_WORDS - POOL
    assert pool_words >= 8300, pool_words
    s2 = al.map["SCR2"][0]

    QNT = pb.R(al.map["QK"][0], 8192, BF16)
    KNT = pb.R(al.map["QK"][0] + 8192, 8192, BF16)
    XINT = pb.R(al.map["QK"][0], 16384)
    VEXT = pb.R(al.map["VX"][0], 8320, BF16)
    XFT = pb.R(al.map["VX"][0], 8192, BF16)
    XIC = [pb.R(s2, D), pb.R(s2 + 1024, D)]
    QRAW = XIC
    QTOK = [pb.R(s2 + 2048, 512, BF16), pb.R(s2 + 2560, 512, BF16)]
    ROWSC = pb.R(s2 + 3072, D)
    WS1 = [pb.R(s2, 2048, BF16), pb.R(s2 + 2048, 2048, BF16)]
    WS2 = WS1
    WB2 = [pb.R(POOL, 4096, BF16), pb.R(POOL + 4096, 4096, BF16)]
    ATB = [pb.R(POOL, 1024, BF16), pb.R(POOL + 1024, 1024, BF16)]
    LRW = pb.R(POOL + 2048, 2048)
    RHIB = pb.R(POOL + 4096, 1024, BF16)
    RLOB = pb.R(POOL + 5120, 1024, BF16)
    OSCR = pb.R(POOL + 2048, 1024, BF16)
    RBC = pb.R(POOL + 3072, 2048)
    WOB = pb.R(POOL, 3072, BF16)
    WD = pb.R(POOL, 6144, BF16)
    D1BQ = pb.R(POOL + 6144, 2048, BF16)
    assert POOL + 6144 + 2048 <= ARENA_WORDS

    xin_r = io["x_in"].rearrange("(tc p) d -> tc p d", p=128)
    xout_r = io["x_out"].rearrange("(tc p) d -> tc p d", p=128)
    xint_r = io["xint_scr"].rearrange("(tc p) d -> tc p d", p=128)

    r1 = emit_rep(pb, consts, io["n1_w"], D, N1REP, ROWSC, pbank=4)
    r2 = emit_rep(pb, consts, io["n2_w"], D, N2REP, ROWSC, pbank=4, prev=r1)
    r3 = emit_rep(pb, consts, io["qn_w"], HD, QNREP, ROWSC, pbank=4, prev=r2)
    rep_done = emit_rep(pb, consts, io["kn_w"], HD, KNREP, ROWSC, pbank=4, prev=r3)
    gld = pb.dma(GATEB.rearrange("p (c f) -> p c f", c=DC),
                 io["gate_b"].rearrange("(c p) f -> p c f", p=128), "LW", cast=True)
    const_v = pb.v("memset",
                   VEXT.rearrange("p (tc h e) -> p tc h e", tc=TC, h=NH)[:, :, :, 64:65],
                   1.0, inc="DVE")
    wq_ld = pb.dma(chunks(WB2[0], DC), io["wq_b"].rearrange("(c p) f -> p c f", p=128), "LW", cast=True)
    wk_ld = pb.dma(chunks(WB2[1], DC), io["wk_b"].rearrange("(c p) f -> p c f", p=128), "LW", cast=True)

    evac_prev = [0, 0]
    xc_free = [None, None]
    xic_free = [None, None]
    hb_free = [0, 0]
    for tc in range(TC):
        par = tc % 2
        xc = XC[par]
        ldv = pb.dma(xc, xin_r[tc], "LD", w=[xc_free[par]])
        pb.v("scalar_tensor_tensor", out=SQ, in0=xc, scalar=1.0, in1=xc,
             op0=ALU.mult, op1=ALU.mult, accum_out=cols["SQC"], w=[ldv])
        pb.v("reduce_sum", out=cols["SMC"], in_=xc, axis=AX.X)
        pb.v("tensor_scalar_mul", out=cols["NEGMN"], in0=cols["SMC"], scalar1=-1.0 / D)
        pb.v("tensor_scalar_mul", out=cols["EX2"], in0=cols["SQC"], scalar1=1.0 / D)
        pb.v("tensor_tensor", out=cols["M2"], in0=cols["NEGMN"], in1=cols["NEGMN"], op=ALU.mult)
        pb.v("tensor_tensor", out=cols["VARC"], in0=cols["EX2"], in1=cols["M2"], op=ALU.subtract)
        vv = pb.v("tensor_scalar_add", out=cols["VARC"], in0=cols["VARC"], scalar1=1e-5, inc="DVE")
        av = pb.s("activation", out=cols["SDC"], in_=cols["VARC"], func=AF.Sqrt,
                  w=[("DVE", vv)], inc="ACT")
        pb.v("reciprocal", out=cols["RSTD"], in_=cols["SDC"], w=[("ACT", av)])
        xic = XIC[par]
        xv = pb.v("tensor_scalar", out=xic, in0=xc, scalar1=cols["NEGMN"],
                  scalar2=cols["RSTD"], op0=ALU.add, op1=ALU.mult,
                  w=[xic_free[par]], inc="DVE")
        xc_free[par] = ("DVE", xv)
        sp = pb.dma(xint_r[tc], xic, "SP", w=[("DVE", xv)])
        pb.v("scalar_tensor_tensor", out=SQ, in0=xic, scalar=1.0, in1=xic,
             op0=ALU.mult, op1=ALU.mult, accum_out=cols["RMSV"])
        rv = pb.v("tensor_scalar", out=cols["RMSV"], in0=cols["RMSV"],
                  scalar1=1.0 / D, scalar2=EPS_RMS, op0=ALU.mult, op1=ALU.add, inc="DVE")
        a2 = pb.s("activation", out=cols["RMSD"], in_=cols["RMSV"], func=AF.Sqrt,
                  w=[("DVE", rv)], inc="ACT")
        pb.v("reciprocal", out=cols["RMSR"], in_=cols["RMSD"], w=[("ACT", a2)])
        hb = HB[par]
        nv = pb.v("scalar_tensor_tensor", out=hb, in0=xic, scalar=cols["RMSR"],
                  in1=N1REP, op0=ALU.mult, op1=ALU.mult,
                  w=[("PE", hb_free[par])] + ([rep_done] if tc == 0 else []), inc="DVE")
        xic_free[par] = (("DVE", nv) if True else None)
        xic_free[par] = sp
        pe_last, ev = emit_tok_to_ft(pb, consts, hb, FT1, tc, 6 + par,
                                     first_wait=[("DVE", nv), ("ACT", evac_prev[par])],
                                     evac_prev=0)
        hb_free[par] = pe_last
        evac_prev[par] = ev
    hrt_done = ("ACT", pb.cnt["ACT"])
    b1_sp = ("SP", pb.cnt["SP"])

    ft1c = chunks(FT1, DC)
    evac_bank = {0: None, 2: None}
    wlds = [wq_ld, wk_ld, None]
    for which in range(3):
        if which == 2:
            wlds[2] = pb.dma(chunks(WB2[0], DC),
                             io["wv_b"].rearrange("(c p) f -> p c f", p=128),
                             "LW", cast=True, w=[("PE", pe_after_q)])
        wb, wld = WB2[which % 2], wlds[which]
        wbc = chunks(wb, DC)
        for tc in range(TC):
            par = tc % 2
            bank = 0 if par == 0 else 2
            ps = pb.PS(bank, 2)
            mm = None
            for k in range(DC):
                for n in range(2):
                    wl = []
                    if k == 0 and n == 0:
                        wl = [wld, hrt_done, evac_bank[bank]]
                    mm = pb.t("matmul", ps[:, n * 512:(n + 1) * 512],
                              ft1c[:, k, tc * 128:(tc + 1) * 128],
                              wbc[:, k, n * 512:(n + 1) * 512],
                              start=(k == 0), stop=(k == DC - 1), w=wl,
                              inc="PE" if (k == DC - 1 and n == 1) else None)
            if which < 2:
                qraw = QRAW[par]
                qv = pb.v("tensor_copy", out=qraw, in_=ps,
                          w=[("PE", mm)] + ([b1_sp] if tc < 2 else []), inc="DVE")
                evac_bank[bank] = ("DVE", qv)
                pb.v("scalar_tensor_tensor", out=SQ, in0=qraw, scalar=1.0, in1=qraw,
                     op0=ALU.mult, op1=ALU.mult)
                pb.v("tensor_reduce", out=QSC, in_=chunks(SQ, NH), axis=AX.X, op=ALU.add)
                qs = pb.v("tensor_scalar", out=QSC, in0=QSC, scalar1=1.0 / HD,
                          scalar2=EPS_RMS, op0=ALU.mult, op1=ALU.add, inc="DVE")
                qa = pb.s("activation", out=QSD, in_=QSC, func=AF.Sqrt,
                          w=[("DVE", qs)], inc="ACT")
                pb.v("reciprocal", out=QRS, in_=QSD, w=[("ACT", qa)])
                qtok = QTOK[par]
                rep = QNREP if which == 0 else KNREP
                nv = None
                for hh in range(NH):
                    nv = pb.v("scalar_tensor_tensor",
                              out=qtok[:, hh * 64:(hh + 1) * 64],
                              in0=qraw[:, hh * 64:(hh + 1) * 64],
                              scalar=QRS[:, hh:hh + 1], in1=rep,
                              op0=ALU.mult, op1=ALU.mult,
                              w=[("PE", hb_free[par])] if hh == 0 else [], inc="DVE")
                dest = QNT if which == 0 else KNT
                pe_last, ev = emit_tok_to_ft(pb, consts, qtok, dest, tc, 6 + par,
                                             first_wait=[("DVE", nv)],
                                             evac_prev=evac_prev[par])
                hb_free[par] = pe_last
                evac_prev[par] = ev
            else:
                out_view = chunks(VEXT, TC)[:, tc].rearrange("p (h e) -> p h e", h=NH)[:, :, 0:64]
                ev = pb.s("copy", out=out_view, in_=ps.rearrange("p (h e) -> p h e", h=NH),
                          w=[("PE", mm), ("DVE", const_v) if tc == 0 else None], inc="ACT")
                evac_bank[bank] = ("ACT", ev)
        if which == 0:
            pe_after_q = pb.cnt["PE"]
    qkv_done = ("ACT", pb.cnt["ACT"])

    ovl = {"LRW": LRW, "RBC": RBC, "OSCR": OSCR, "LSCR": io["lrow_scr"]}
    OT = FT1
    o_free = emit_attention(pb, io, consts, cols, M4, NEGMHA, QNT, KNT, VEXT, OT,
                            ATB, ovl, scale=0.125, causal=True, entry_free=qkv_done)
    attn_pe = ("PE", pb.cnt["PE"])
    attn_dve = ("DVE", pb.cnt["DVE"])
    attn_act = ("ACT", pb.cnt["ACT"])

    xl = pb.dma(chunks(XINT, TC),
                io["xint_scr"].rearrange("(tc p) d -> p tc d", p=128),
                "LD", w=[attn_pe, attn_dve])
    wo_ld = pb.dma(chunks(WOB, DC),
                   io["wo_b"].rearrange("(c p) f -> p c f", p=128)[:, :, 256:1024],
                   "LW", cast=True, w=[attn_pe, attn_act, o_free])
    otc = chunks(OT, DC)
    wobc = chunks(WOB, DC)
    bank_free = {0: None, 2: None}
    for tc in range(TC):
        bank = 0 if tc % 2 == 0 else 2
        ps = pb.PS(bank, 2)
        mm = None
        for k in range(DC):
            for (n0, n1) in ((0, 512), (512, 768)):
                wl = []
                if k == 0 and n0 == 0:
                    wl = [wo_ld, o_free, bank_free[bank]]
                mm = pb.t("matmul", ps[:, n0:n1], otc[:, k, tc * 128:(tc + 1) * 128],
                          wobc[:, k, n0:n1], start=(k == 0), stop=(k == DC - 1),
                          w=wl, inc="PE" if (k == DC - 1 and n0 == 512) else None)
        xslice = chunks(XINT, TC)[:, tc][:, 256:1024]
        ev = pb.v("tensor_tensor", out=xslice, in0=xslice, in1=ps[:, 0:768],
                  op=ALU.add, w=[("PE", mm), xl], inc="DVE")
        bank_free[bank] = ("DVE", ev)

    scratch = {"XC": XC, "SQ": SQ, "HB": HB}
    xft_done = emit_ln_to_ft(pb, cols, consts, XFT, scratch, kind="rms",
                             src_sbuf=XINT, gamma_rep=N2REP, eps=EPS_RMS,
                             pbank=6, extra_first_wait=attn_pe, statarr=STATA)
    xftc = chunks(XFT, DC)
    gate_free = None
    for tc in range(TC):
        ps = pb.PS(4, 1)
        mm = None
        for k in range(DC):
            wl = []
            if k == 0:
                wl = [gld, xft_done, gate_free if gate_free else attn_dve]
            mm = pb.t("matmul", ps[:, 0:4], xftc[:, k, tc * 128:(tc + 1) * 128],
                      GATEB.rearrange("p (c f) -> p c f", c=DC)[:, k, :],
                      start=(k == 0), stop=(k == DC - 1), w=wl,
                      inc="PE" if k == DC - 1 else None)
        nm = pb.v("tensor_reduce", out=cols["NEGL"], in_=ps[:, 0:4], axis=AX.X,
                  op=ALU.max, negate=True, w=[("PE", mm)], inc="DVE")
        ea = pb.s("activation", out=PRB, in_=ps[:, 0:4], func=AF.Exp,
                  bias=cols["NEGL"], accum_out=cols["SUME"],
                  w=[("DVE", nm)], inc="ACT")
        pb.v("reciprocal", out=cols["RS"], in_=cols["SUME"], w=[("ACT", ea)])
        pb.v("tensor_single_scalar", out=PRN, in_=PRB, scalar=cols["RS"], op=ALU.mult)
        pb.v("tensor_reduce", out=cols["M1C"], in_=PRN, axis=AX.X, op=ALU.max)
        pb.v("tensor_single_scalar", out=MK1, in_=PRN, scalar=cols["M1C"], op=ALU.is_equal)
        pb.v("scalar_tensor_tensor", out=PRM, in0=MK1, scalar=-1e30, in1=PRN,
             op0=ALU.mult, op1=ALU.add)
        pb.v("tensor_reduce", out=cols["M2C"], in_=PRM, axis=AX.X, op=ALU.max)
        pb.v("tensor_single_scalar", out=MK2, in_=PRM, scalar=cols["M2C"], op=ALU.is_equal)
        pb.v("tensor_tensor", out=cols["DEN"], in0=cols["M1C"], in1=cols["M2C"], op=ALU.add)
        pb.v("tensor_scalar_add", out=cols["DEN"], in0=cols["DEN"], scalar1=1e-8)
        pb.v("reciprocal", out=cols["RDEN"], in_=cols["DEN"])
        pb.v("tensor_tensor", out=cols["W1C"], in0=cols["M1C"], in1=cols["RDEN"], op=ALU.mult)
        pb.v("tensor_tensor", out=cols["W2C"], in0=cols["M2C"], in1=cols["RDEN"], op=ALU.mult)
        pb.v("tensor_single_scalar", out=T2S, in_=MK2, scalar=cols["W2C"], op=ALU.mult)
        gf = pb.v("scalar_tensor_tensor", out=COEF[:, tc * 4:(tc + 1) * 4],
                  in0=MK1, scalar=cols["W1C"], in1=T2S,
                  op0=ALU.mult, op1=ALU.add, inc="DVE")
        gate_free = ("DVE", gf)

    h1t_half = FT1.rearrange("p (c t) -> p c t", c=NH)
    h1t_q = pb.R(al.map["FT1"][0], 4096, BF16).rearrange("p (c t) -> p c t", c=NH)
    h2t_q = pb.R(al.map["FT1"][0] + 4096, 4096, BF16).rearrange("p (c t) -> p c t", c=NH)
    wdv = WD.rearrange("p (c f) -> p c f", c=NH)
    wfree = [attn_pe, attn_pe]
    tail = gate_free
    wd_w = [gate_free, attn_dve, attn_act]
    for half in range(2):
        th0 = half * 1024
        ups = emit_up(pb, consts, io["e0_up_t"], xftc, DC, NH, h1t_half, th0, 1024,
                      "gelu", WS1, wfree, extra_w=[tail, xft_done], entry=tail)
        tail = emit_down(pb, consts, io["e0_down_t"], h1t_half, NH, 0, COEF, XINT,
                         th0, 1024, wdv, wd_w=wd_w, src_w=ups[-1], entry=tail)
        wd_w = [tail]
    for half in range(2):
        th0 = half * 1024
        u1 = emit_up(pb, consts, io["e1_w1_t"], xftc, DC, NH, h1t_half, th0, 1024,
                     "silu", WS1, wfree, extra_w=[tail], entry=tail)
        u2 = emit_up(pb, consts, io["e1_w2_t"], xftc, DC, NH, h1t_half, th0, 1024,
                     "mult_into", WS2, wfree, evac_w=u1, entry=u1[-1])
        tail = emit_down(pb, consts, io["e1_down_t"], h1t_half, NH, 1, COEF, XINT,
                         th0, 1024, wdv, wd_w=wd_w, src_w=u2[-1], entry=tail)
        wd_w = [tail]
    for quar in range(4):
        th0 = quar * 512
        a1 = emit_up(pb, consts, io["e2_l1_t"], xftc, DC, NH, h1t_q, th0, 512,
                     "gelu", WS1, wfree, extra_w=[tail], entry=tail)
        a2 = emit_up(pb, consts, io["e2_l2_t"], h1t_q, NH, NH, h2t_q, 0, 512,
                     "gelu", WS2, wfree, extra_w=[a1[-1]], entry=a1[-1])
        a3 = emit_up(pb, consts, io["e2_l3_t"], h2t_q, NH, NH, h1t_q, 0, 512,
                     "gelu", WS1, wfree, extra_w=[a2[-1]], entry=a2[-1])
        tail = emit_down(pb, consts, io["e2_l4_t"], h1t_q, NH, 2, COEF, XINT,
                         th0, 512, wdv, wd_w=wd_w, src_w=a3[-1], entry=tail)
        wd_w = [tail]
    d1bq = D1BQ.rearrange("p (c t) -> p c t", c=DC)
    for quar in range(4):
        th0 = quar * 512
        b1 = emit_up(pb, consts, io["e3_down_t"], xftc, DC, DC, d1bq, th0, 512,
                     "gelu", WS1, wfree, extra_w=[tail], entry=tail)
        b2 = emit_up(pb, consts, io["e3_up_t"], d1bq, DC, NH, h1t_q, 0, 512,
                     "gelu", WS2, wfree, extra_w=[b1[-1]], entry=b1[-1])
        tail = emit_down(pb, consts, io["e3_out_t"], h1t_q, NH, 3, COEF, XINT,
                         th0, 512, wdv, wd_w=wd_w, src_w=b2[-1], entry=tail)
        wd_w = [tail]

    xo_free = [None, None]
    for tc in range(TC):
        par = tc % 2
        xc = XC[par]
        ldv = pb.dma(xc, xin_r[tc], "LD", w=[xo_free[par] or wd_w[0]])
        addv = pb.v("tensor_tensor", out=xc, in0=xc, in1=chunks(XINT, TC)[:, tc],
                    op=ALU.add, w=[ldv] + ([wd_w[0]] if tc == 0 else []), inc="DVE")
        xo_free[par] = pb.dma(xout_r[tc], xc, "ST", w=[("DVE", addv)])
    pb.wait("sync", [("ST", pb.cnt["ST"])])


def pm_tile(wt):
    nin = wt.shape[0] // 128
    blk = 4 if nin <= 8 else 2
    bw = blk * 128
    nob = wt.shape[1] // bw
    return np.ascontiguousarray(wt.reshape(nin, 128, nob, bw).transpose(1, 2, 0, 3))


def prep_phase_b(inputs):
    tr = lambda k: np.ascontiguousarray(np.asarray(inputs[k]).T)
    d = {
        "n1_w": np.asarray(inputs["L0_ffn_n1_w"]).reshape(1, D),
        "n2_w": np.asarray(inputs["L0_ffn_n2_w"]).reshape(1, D),
        "qn_w": np.asarray(inputs["L0_ffn_qn_w"]).reshape(1, HD),
        "kn_w": np.asarray(inputs["L0_ffn_kn_w"]).reshape(1, HD),
        "wq_b": tr("L0_ffn_wq"), "wk_b": tr("L0_ffn_wk"),
        "wv_b": tr("L0_ffn_wv"), "wo_b": tr("L0_ffn_wo"),
        "gate_b": tr("L0_ffn_gate_w"),
        "e0_up_t": pm_tile(tr("L0_ffn_e0_up")), "e0_down_t": tr("L0_ffn_e0_down"),
        "e1_w1_t": pm_tile(tr("L0_ffn_e1_w1")), "e1_w2_t": pm_tile(tr("L0_ffn_e1_w2")),
        "e1_down_t": tr("L0_ffn_e1_down"),
        "e2_l1_t": pm_tile(tr("L0_ffn_e2_l1")), "e2_l2_t": pm_tile(tr("L0_ffn_e2_l2")),
        "e2_l3_t": pm_tile(tr("L0_ffn_e2_l3")), "e2_l4_t": tr("L0_ffn_e2_l4"),
        "e3_down_t": pm_tile(tr("L0_ffn_e3_down")), "e3_up_t": pm_tile(tr("L0_ffn_e3_up")),
        "e3_out_t": tr("L0_ffn_e3_out"),
    }
    d.update(make_consts("B"))
    return d



def _c_finish(pb, XC, xin_r, xout_r, w=()):
    xo_free = [None, None]
    for tc in range(TC):
        par = tc % 2
        ldv = pb.dma(XC[par], xin_r[tc], "LD", w=[xo_free[par]] + list(w))
        xo_free[par] = pb.dma(xout_r[tc], XC[par], "ST", w=[ldv])
    pb.wait("sync", [("ST", pb.cnt["ST"])])


C_UPTO = 9
C3SUB = 3

def emit_phase_c(pb, io, consts, al):
    cols = al.cols(["SQC", "SMC", "NEGMN", "EX2", "M2", "VARC", "SDC", "RSTD",
                    "DCOL", "RCC"])
    XC = [al.r("XC0", D), al.r("XC1", D)]
    HB = [al.r("HB0", 512, BF16), al.r("HB1", 512, BF16)]
    STATA = al.r("STATA", 320)
    SQ = al.r("SQ", D)
    TMP1 = SQ
    KVH = al.r("KVH", 512, BF16)
    KVL = al.r("KVL", 512, BF16)
    H2T = al.r("H2T", 8192, BF16)
    WQKVC = al.r("WQKVC", 12288, BF16)
    QFT = al.r("QFT", 8192, BF16)
    KF = al.r("KF", 8192, BF16)
    VEXT1 = al.r("VEXT1", 8320, BF16)
    assert al.off <= ARENA_WORDS, al.off
    OT1 = pb.R(al.map["H2T"][0], 8192, BF16)
    WOUTC = pb.R(al.map["WQKVC"][0], 4096, BF16)
    OTOK = pb.R(al.map["WQKVC"][0] + 4096, 8192, BF16)

    xin_r = io["x_in"].rearrange("(tc p) d -> tc p d", p=128)
    xout_r = io["x_out"].rearrange("(tc p) d -> tc p d", p=128)

    wq_ld = pb.dma(chunks(WQKVC, DC), io["w_qkv_c"].rearrange("(c p) f -> p c f", p=128),
                   "LW", cast=True)
    const_v = pb.v("memset",
                   VEXT1.rearrange("p (tc h e) -> p tc h e", tc=TC, h=NH)[:, :, :, 64:65],
                   1.0, inc="DVE")

    scratch = {"XC": XC, "SQ": SQ, "HB": HB}
    h2t_done = emit_ln_to_ft(pb, cols, consts, H2T, scratch, kind="ln",
                             xin_r=xin_r, pbank=6, statarr=STATA)

    if C_UPTO < 2:
        _c_finish(pb, XC, xin_r, xout_r)
        return
    h2c = chunks(H2T, DC)
    wqc = chunks(WQKVC, DC)
    QR = XC[0]
    RT = XC[1]
    cosv = chunks(consts["cos"], TC)
    sinv = chunks(consts["sin"], TC)
    evac_bank = {0: ("DVE", 1), 2: ("DVE", 1)}
    tr_evac = [0, 0]
    hb_free2 = [0, 0]
    ae_prev = None
    for which in range(3):
        for tc in range(TC):
            par = tc % 2
            bank = 0 if par == 0 else 2
            ps = pb.PS(bank, 2)
            mm = None
            for k in range(DC):
                for n in range(2):
                    wl = []
                    if k == 0 and n == 0:
                        wl = [wq_ld, h2t_done, evac_bank[bank]]
                    mm = pb.t("matmul", ps[:, n * 512:(n + 1) * 512],
                              h2c[:, k, tc * 128:(tc + 1) * 128],
                              wqc[:, k, which * 1024 + n * 512:which * 1024 + (n + 1) * 512],
                              start=(k == 0), stop=(k == DC - 1), w=wl,
                              inc="PE" if (k == DC - 1 and n == 1) else None)
            if which < 2:
                qv = pb.v("tensor_copy", out=QR, in_=ps,
                          w=[("PE", mm), ae_prev], inc="DVE")
                evac_bank[bank] = ("DVE", qv)
                qrv = QR.rearrange("p (h two x) -> p h two x", h=NH, two=2)
                rtv = RT.rearrange("p (h two x) -> p h two x", h=NH, two=2)
                pb.v("tensor_copy", out=rtv[:, :, 1], in_=qrv[:, :, 0])
                pb.v("tensor_scalar_mul", out=rtv[:, :, 0], in0=qrv[:, :, 1], scalar1=-1.0)
                cosb = cosv[:, tc][:, None, :].broadcast_to([128, NH, HD])
                sinb = sinv[:, tc][:, None, :].broadcast_to([128, NH, HD])
                pb.v("tensor_tensor", out=TMP1.rearrange("p (h x) -> p h x", h=NH),
                     in0=QR.rearrange("p (h x) -> p h x", h=NH),
                     in1=cosb, op=ALU.mult)
                pb.v("tensor_tensor", out=RT.rearrange("p (h x) -> p h x", h=NH),
                     in0=RT.rearrange("p (h x) -> p h x", h=NH),
                     in1=sinb, op=ALU.mult)
                rope_done = pb.v("tensor_tensor", out=QR, in0=TMP1, in1=RT,
                                 op=ALU.add, inc="DVE")
                sc = 0.125 if which == 0 else 1.0
                ae = pb.s("activation", out=TMP1, in_=QR, func=AF.Exp, scale=sc,
                          w=[("DVE", rope_done)], inc="ACT")
                ae_prev = ("ACT", ae)
                pb.v("tensor_scalar_min", out=TMP1, in0=TMP1, scalar1=1.0, w=[("ACT", ae)])
                pb.v("tensor_scalar", out=RT, in0=QR, scalar1=sc, scalar2=1.0,
                     op0=ALU.mult, op1=ALU.add)
                if which == 0:
                    fv = pb.v("tensor_tensor", out=HB[par], in0=RT, in1=TMP1,
                              op=ALU.max, w=[("PE", hb_free2[par])], inc="DVE")
                    pe_last, ev = emit_tok_to_ft(pb, consts, HB[par], QFT, tc, 6 + par,
                                                 first_wait=[("DVE", fv)],
                                                 evac_prev=tr_evac[par])
                    hb_free2[par] = pe_last
                    tr_evac[par] = ev
                else:
                    fv = pb.v("tensor_tensor", out=chunks(KF, TC)[:, tc], in0=RT,
                              in1=TMP1, op=ALU.max, inc="DVE")
            else:
                out_view = chunks(VEXT1, TC)[:, tc].rearrange("p (h e) -> p h e", h=NH)[:, :, 0:64]
                ev = pb.s("copy", out=out_view, in_=ps.rearrange("p (h e) -> p h e", h=NH),
                          w=[("PE", mm), ("DVE", const_v) if tc == 0 else None], inc="ACT")
                evac_bank[bank] = ("ACT", ev)
    qkv_done = ("ACT", pb.cnt["ACT"])
    qkv_dve = ("DVE", pb.cnt["DVE"])
    pe_after_qkv = ("PE", pb.cnt["PE"])
    wo_ld = pb.dma(chunks(WOUTC, DC), io["w_out_c"].rearrange("(c p) f -> p c f", p=128),
                   "LW", cast=True, w=[pe_after_qkv])

    if C_UPTO < 3:
        _c_finish(pb, XC, xin_r, xout_r, w=[qkv_done, qkv_dve])
        return
    kfv = chunks(KF, TC)
    vev = VEXT1.rearrange("p (tc h e) -> p tc h e", tc=TC, h=NH)
    kv_ps = pb.PS(0, 4)
    kvpsoff = lambda h: (h % 4) * 512 + (h // 4) * 66
    kvoff = lambda h: ((h // 2) // 7) * 512 + ((h // 2) % 7) * 65
    mm = None
    for h in range(NH):
        base = 64 * (h % 2)
        out = kv_ps[base:base + 64, kvpsoff(h):kvpsoff(h) + 65]
        for kt in range(TC):
            wl = []
            if h == 0 and kt == 0:
                wl = [qkv_done, qkv_dve]
            mm = pb.t("matmul", out, kfv[:, kt, h * 64:(h + 1) * 64],
                      vev[:, kt, h, :], start=(kt == 0), stop=(kt == TC - 1),
                      w=wl, inc="PE" if (h == NH - 1 and kt == TC - 1) else None)
    kh1 = kl1 = None
    for h in range(NH):
        base = 64 * (h % 2)
        src_ = kv_ps[base:base + 64, kvpsoff(h):kvpsoff(h) + 65]
        dsth = KVH[base:base + 64, kvoff(h):kvoff(h) + 65]
        dstl = KVL[base:base + 64, kvoff(h):kvoff(h) + 65]
        kh1 = pb.s("copy", out=dsth, in_=src_,
                   w=[("PE", mm)] if h == 0 else [], inc="ACT")
        kl1 = pb.v("tensor_tensor", out=dstl, in0=src_, in1=dsth,
                   op=ALU.subtract, w=[("ACT", kh1)], inc="DVE")

    if C_UPTO < 4:
        _c_finish(pb, XC, xin_r, xout_r, w=[("DVE", kl1)])
        return
    zoff = lambda h: (h % 4) * 512 + (h // 4) * 66
    z_free = {0: None}
    tr_evac2 = [0, 0]
    hb_free3 = [0, 0]
    for tc in range(TC):
        par = tc % 2
        zb = 0
        zp = pb.PS(zb, 4)
        mm = {}
        for h in range(NH):
            base = 64 * (h % 2)
            qsl = chunks(QFT, DC)[base:base + 64, h // 2, tc * 128:(tc + 1) * 128]
            out = zp[:, zoff(h):zoff(h) + 65]
            wl = []
            if h == 0:
                wl = [("DVE", kl1), z_free[zb]]
            pb.t("matmul", out, qsl, KVH[base:base + 64, kvoff(h):kvoff(h) + 65],
                 start=True, stop=False, w=wl)
            mm[h] = pb.t("matmul", out, qsl,
                         KVL[base:base + 64, kvoff(h):kvoff(h) + 65],
                         start=False, stop=True, inc="PE" if h == NH - 1 else None)
        last = None
        if C3SUB >= 3:
            for h in range(NH):
                dn = pb.v("tensor_scalar_max", out=cols["DCOL"],
                          in0=zp[:, zoff(h) + 64:zoff(h) + 65], scalar1=1e-6,
                          w=([("PE", mm[NH - 1])] if h == 0 else []) +
                            ([("ACT", last)] if last else []), inc="DVE")
                rc = pb.v("reciprocal", out=cols["RCC"], in_=cols["DCOL"],
                          w=[("DVE", dn)], inc="DVE")
                sc_ap = cols["RCC"] if C_UPTO >= 5 else 1.0
                last = pb.s("activation", out=chunks(OTOK, TC)[:, tc][:, h * 64:(h + 1) * 64],
                            in_=zp[:, zoff(h):zoff(h) + 64], func=AF.Copy,
                            scale=sc_ap, w=[("DVE", rc)] + ([wo_ld] if tc == 0 and h == 0 else []),
                            inc="ACT")
        else:
            for h in range(NH):
                last = pb.s("copy", out=chunks(OTOK, TC)[:, tc][:, h * 64:(h + 1) * 64],
                            in_=zp[:, zoff(h):zoff(h) + 64],
                            w=[("PE", mm[NH - 1])] if h == 0 else [], inc="ACT")
        z_free[zb] = ("ACT", last)
        if C3SUB >= 2:
            fv = pb.v("tensor_copy", out=HB[par], in_=chunks(OTOK, TC)[:, tc],
                      w=[("ACT", last), ("PE", hb_free3[par])], inc="DVE")
            pe_last, ev = emit_tok_to_ft(pb, consts, HB[par], OT1, tc, 6 + par,
                                         first_wait=[("DVE", fv)], evac_prev=tr_evac2[par])
            hb_free3[par] = pe_last
            tr_evac2[par] = ev
    ot_done = ("ACT", pb.cnt["ACT"])

    if C_UPTO < 5:
        _c_finish(pb, XC, xin_r, xout_r, w=[ot_done])
        return
    ev_prev = {0: None, 2: None}
    xo_free = [None, None]
    otc = chunks(OT1, DC)
    woc = chunks(WOUTC, DC)
    for tc in range(TC):
        bank = 0 if tc % 2 == 0 else 2
        ps = pb.PS(bank, 2)
        mm = None
        for k in range(DC):
            for n in range(2):
                wl = []
                if k == 0 and n == 0:
                    wl = [ot_done, wo_ld,
                          ("DVE", ev_prev[bank]) if ev_prev[bank] else None]
                mm = pb.t("matmul", ps[:, n * 512:(n + 1) * 512],
                          otc[:, k, tc * 128:(tc + 1) * 128], woc[:, k, n * 512:(n + 1) * 512],
                          start=(k == 0), stop=(k == DC - 1), w=wl,
                          inc="PE" if (k == DC - 1 and n == 1) else None)
        par = tc % 2
        xc = XC[par]
        ldw = [xo_free[par] or qkv_dve, ot_done]
        ldv = pb.dma(xc, xin_r[tc], "LD", w=ldw)
        addv = pb.v("tensor_tensor", out=xc, in0=xc, in1=ps, op=ALU.add,
                    w=[ldv, ("PE", mm)], inc="DVE")
        ev_prev[bank] = addv
        xo_free[par] = pb.dma(xout_r[tc], xc, "ST", w=[("DVE", addv)])
    pb.wait("sync", [("ST", pb.cnt["ST"])])


def prep_phase_c(inputs):
    d = {"w_qkv_c": np.ascontiguousarray(np.asarray(inputs["L1_attn_W_qkv"])),
         "w_out_c": np.ascontiguousarray(np.asarray(inputs["L1_attn_W_out"]))}
    d.update(make_consts("C"))
    return d



def emit_phase_d(pb, io, consts, al):
    cols = al.cols(["SQC", "SMC", "NEGMN", "EX2", "M2", "VARC", "SDC", "RSTD",
                    "LMX", "NEG", "SUME", "RS"])
    SQ = al.r("SQ", D)
    HB = [al.r("HB0", 512, BF16), al.r("HB1", 512, BF16)]
    STATA = al.r("STATA", 320)
    MK = al.r("MK", 8)
    COEFD = al.r("COEFD", 128)
    B1S = al.r("B1S", 128)
    B1C = al.r("B1C", 32)
    B2ROW = al.r("B2ROW", 1024)
    B2HI = al.r("B2HI", 512, BF16)
    B2LO = al.r("B2LO", 512, BF16)
    XD = al.r("XD", 16384)
    XFT = al.r("XFT", 8192, BF16)
    H1T = al.r("H1T", 16384, BF16)
    WS1 = [al.r("WS1a", 2048, BF16), al.r("WS1b", 2048, BF16)]
    WS2 = [al.r("WS2a", 512, BF16), al.r("WS2b", 512, BF16),
           al.r("WS2c", 512, BF16), al.r("WS2d", 512, BF16)]
    assert al.off <= ARENA_WORDS, al.off

    xin_r = io["x_in"].rearrange("(tc p) d -> tc p d", p=128)
    xout_r = io["x_out"].rearrange("(tc p) d -> tc p d", p=128)
    GB = al.r("GB", 32, BF16)
    g_ld = pb.dma(GB.rearrange("p (c f) -> p c f", c=DC),
                  io["gate_d"].rearrange("(c p) f -> p c f", p=128), "LW", cast=True)

    xd_ld = pb.dma(chunks(XD, TC), io["x_in"].rearrange("(tc p) d -> p tc d", p=128), "LD")
    scratch = {"XC": None, "SQ": SQ, "HB": HB}
    xft_done = emit_ln_to_ft(pb, cols, consts, XFT, scratch, kind="ln",
                             src_sbuf=XD, pbank=6, extra_first_wait=xd_ld,
                             statarr=STATA)
    xftc = chunks(XFT, DC)

    gate_free = None
    for tc in range(TC):
        ps = pb.PS(4, 1)
        mm = None
        for k in range(DC):
            wl = []
            if k == 0:
                wl = [g_ld, xft_done, gate_free]
            mm = pb.t("matmul", ps[:, 0:8], xftc[:, k, tc * 128:(tc + 1) * 128],
                      GB.rearrange("p (c f) -> p c f", c=DC)[:, k, :],
                      start=(k == 0), stop=(k == DC - 1), w=wl,
                      inc="PE" if k == DC - 1 else None)
        lm = pb.v("tensor_reduce", out=cols["LMX"], in_=ps[:, 0:8], axis=AX.X,
                  op=ALU.max, w=[("PE", mm)], inc="DVE")
        ng = pb.v("tensor_scalar_mul", out=cols["NEG"], in0=cols["LMX"], scalar1=-1.0,
                  inc="DVE")
        ea = pb.s("activation", out=MK, in_=ps[:, 0:8], func=AF.Exp,
                  bias=cols["NEG"], accum_out=cols["SUME"], w=[("DVE", ng)], inc="ACT")
        pb.v("reciprocal", out=cols["RS"], in_=cols["SUME"], w=[("ACT", ea)])
        pb.v("tensor_single_scalar", out=MK, in_=ps[:, 0:8], scalar=cols["LMX"],
             op=ALU.is_equal, w=[("ACT", ea)])
        gf = pb.v("tensor_single_scalar", out=COEFD[:, tc * 8:(tc + 1) * 8],
                  in_=MK, scalar=cols["RS"], op=ALU.mult, inc="DVE")
        gate_free = ("DVE", gf)

    h1t = H1T.rearrange("p (c t) -> p c t", c=32)
    tail = gate_free
    wfree = {i: gate_free for i in range(8)}
    for e in range(8):
        b1l = pb.dma(B1S.rearrange("p f -> p f")[0:32, :], io["b1_d"][e].rearrange("(c f) -> c f", f=128), "LD", w=[tail])
        bt = pb.t("transpose", pb.PS(6, 1)[0:128, 0:32], B1S[0:32, 0:128],
                  consts["ident_f"][0:32, 0:32], w=[b1l, tail], inc="PE")
        bc = pb.v("tensor_copy", out=B1C, in_=pb.PS(6, 1)[:, 0:32], w=[("PE", bt)], inc="DVE")
        b2l = pb.dma(B2ROW[0:1, :], io["b2_d"][e][None, :], "LD", w=[tail])
        bh = pb.v("tensor_copy", out=B2HI[0:1, 0:1024], in_=B2ROW[0:1, :], w=[b2l], inc="DVE")
        bl = pb.v("tensor_tensor", out=B2LO[0:1, 0:1024], in0=B2ROW[0:1, :],
                  in1=B2HI[0:1, 0:1024], op=ALU.subtract, w=[("DVE", bh)], inc="DVE")
        w1r = io["w1_d"][e]
        w2r = io["w2_d"][e].rearrange("(c p) f -> p c f", p=128)
        for half in range(2):
            th0 = half * 1024
            ups = emit_up(pb, consts, w1r, xftc, DC, 32, h1t, th0, 1024,
                          None, WS1, wfree, extra_w=[("DVE", bl), xft_done, tail],
                          entry=tail, bias_cols=B1C, bias_w=("DVE", bc))
            up_last = ups[-1]
            for yh in (0, 512):
                mmq = {}
                for hc in range(32):
                    par = (hc % 4) + 4
                    wld = pb.dma(WS2[(hc % 4)][:, 0:512], w2r[:, hc, yh:yh + 512],
                                 "LW", cast=True, w=[wfree[par]])
                    wfree[par] = None
                    for tci in range(8):
                        tc = th0 // 128 + tci
                        ps = pb.PS(tci, 1)
                        wl = []
                        if hc < 4 and tci == 0:
                            wl = [wld, up_last, tail]
                        elif tci == 0:
                            wl = [wld]
                        mm = pb.t("matmul", ps, h1t[:, hc, tci * 128:(tci + 1) * 128],
                                  WS2[(hc % 4)][:, 0:512], start=(hc == 0), stop=False,
                                  w=wl, inc="PE" if tci == 7 else None)
                        mmq[tci] = mm
                    wfree[par] = ("PE", mmq[7])
                last_ev = None
                for tci in range(8):
                    tc = th0 // 128 + tci
                    ps = pb.PS(tci, 1)
                    pb.t("matmul", ps, consts["ones_bf"][0:1, 0:128],
                         B2HI[0:1, yh:yh + 512], start=False, stop=False,
                         w=[("DVE", bl)] if (tci == 0 and yh == 0) else [])
                    mm = pb.t("matmul", ps, consts["ones_bf"][0:1, 0:128],
                              B2LO[0:1, yh:yh + 512], start=False, stop=True,
                              inc="PE")
                    coef = COEFD[:, tc * 8 + e:tc * 8 + e + 1]
                    xslice = chunks(XD, TC)[:, tc][:, yh:yh + 512]
                    ev = pb.v("scalar_tensor_tensor", out=xslice, in0=ps,
                              scalar=coef, in1=xslice, op0=ALU.mult, op1=ALU.add,
                              w=[("PE", mm)], inc="DVE")
                    last_ev = ("DVE", ev)
                tail = last_ev

    st = None
    for tc in range(TC):
        st = pb.dma(xout_r[tc], chunks(XD, TC)[:, tc], "ST", w=[tail])
    pb.wait("sync", [("ST", pb.cnt["ST"])])


def prep_phase_d(inputs):
    w1 = np.asarray(inputs["L1_ffn_w1"])
    d = {"gate_d": np.ascontiguousarray(np.asarray(inputs["L1_ffn_gate_w"])),
         "w1_d": np.stack([pm_tile(w1[e]) for e in range(8)]),
         "b1_d": np.ascontiguousarray(np.asarray(inputs["L1_ffn_b1"])),
         "w2_d": np.ascontiguousarray(np.asarray(inputs["L1_ffn_w2"])),
         "b2_d": np.ascontiguousarray(np.asarray(inputs["L1_ffn_b2"]))}
    d.update(make_consts("D"))
    return d



def make_consts(phase):
    c = {}
    c["ident_bf"] = np.eye(128).astype(ml_dtypes.bfloat16)
    c["ident_f"] = np.eye(128, dtype=np.float32)
    c["ones_row"] = np.ones((1, 128), np.float32)
    c["ones_bf"] = np.ones((128, 128), np.float32).astype(ml_dtypes.bfloat16)
    if phase == "B":
        i = np.arange(128)
        c["mask"] = np.where(i[:, None] >= i[None, :], 0.0, -1e30).astype(np.float32)
        c["maskT"] = np.where(i[None, :] >= i[:, None], 0.0, -1e30).astype(np.float32)
    if phase == "C":
        hd = HD
        inv_freq = 1.0 / (500000.0 ** (np.arange(0, hd, 2, dtype=np.float32) / hd))
        t = np.arange(T, dtype=np.float32)
        freqs = t[:, None] * inv_freq[None, :]
        emb = np.concatenate([freqs, freqs], axis=-1)
        c["cos"] = np.cos(emb).astype(np.float32)
        c["sin"] = np.sin(emb).astype(np.float32)
    return c


def prep_phase_a(inputs):
    W = np.asarray(inputs["L0_attn_W_qkv"])
    Wr = np.ascontiguousarray(W.reshape(1024, 16, 64, 3).transpose(0, 3, 1, 2).reshape(1024, 3072))
    d = {"w_qkv_a": Wr,
         "w_out_a": np.ascontiguousarray(np.asarray(inputs["L0_attn_W_out"]))}
    d.update(make_consts("A"))
    return d


_PROG_CACHE = {}
_ALLOC_MAP = {}


def build_phase(phase):
    if phase in _PROG_CACHE:
        return _PROG_CACHE[phase]
    pb = PB()
    for s in ("LD", "LWa", "LWb", "LWc", "LWd", "PE", "ACT", "DVE", "ST", "SP"):
        pb.sem(s)
    pb.arena_init()
    io = {}
    io["x_in"] = pb.dram("x_in", [T, D])
    io["x_out"] = pb.dram("x_out", [T, D], kind="ExternalOutput")
    io["ident_bf"] = pb.dram("ident_bf", [128, 128], BF16)
    io["ident_f"] = pb.dram("ident_f", [128, 128])
    io["ones_row"] = pb.dram("ones_row", [1, 128])
    io["ones_bf"] = pb.dram("ones_bf", [128, 128], BF16)
    io["lrow_scr"] = pb.dram("lrow_scr", [1, T], kind="Internal")
    al = Alloc(pb, 0)
    if phase == "A":
        io["w_qkv_a"] = pb.dram("w_qkv_a", [1024, 3072])
        io["w_out_a"] = pb.dram("w_out_a", [1024, 1024])
        consts = emit_consts(pb, io, al)
        emit_phase_a(pb, io, consts, al)
    elif phase == "B":
        io["mask"] = pb.dram("mask", [128, 128])
        io["maskT"] = pb.dram("maskT", [128, 128])
        io["n1_w"] = pb.dram("n1_w", [1, D])
        io["n2_w"] = pb.dram("n2_w", [1, D])
        io["qn_w"] = pb.dram("qn_w", [1, HD])
        io["kn_w"] = pb.dram("kn_w", [1, HD])
        for nm in ("wq_b", "wk_b", "wv_b", "wo_b"):
            io[nm] = pb.dram(nm, [1024, 1024])
        io["e3_down_t"] = pb.dram("e3_down_t", [128, 2, 8, 512])
        io["gate_b"] = pb.dram("gate_b", [1024, 4])
        for nm in ("e0_up_t", "e1_w1_t", "e1_w2_t", "e2_l1_t", "e3_up_t"):
            io[nm] = pb.dram(nm, [128, 4, 8, 512])
        for nm in ("e0_down_t", "e1_down_t", "e2_l4_t", "e3_out_t"):
            io[nm] = pb.dram(nm, [2048, 1024])
        for nm in ("e2_l2_t", "e2_l3_t"):
            io[nm] = pb.dram(nm, [128, 8, 16, 256])
        io["xint_scr"] = pb.dram("xint_scr", [T, D], kind="Internal")
        consts = emit_consts(pb, io, al, need=("mask",))
        emit_phase_b(pb, io, consts, al)
    elif phase == "C":
        io["w_qkv_c"] = pb.dram("w_qkv_c", [1024, 3072])
        io["w_out_c"] = pb.dram("w_out_c", [1024, 1024])
        io["cos"] = pb.dram("cos", [T, HD])
        io["sin"] = pb.dram("sin", [T, HD])
        consts = emit_consts(pb, io, al, need=("rope", "lite"))
        emit_phase_c(pb, io, consts, al)
    elif phase == "D":
        io["gate_d"] = pb.dram("gate_d", [1024, 8])
        io["w1_d"] = pb.dram("w1_d", [8, 128, 8, 8, 512])
        io["b1_d"] = pb.dram("b1_d", [8, 4096])
        io["w2_d"] = pb.dram("w2_d", [8, 4096, 1024])
        io["b2_d"] = pb.dram("b2_d", [8, 1024])
        consts = emit_consts(pb, io, al)
        emit_phase_d(pb, io, consts, al)
    else:
        raise NotImplementedError(phase)
    _ALLOC_MAP.clear()
    _ALLOC_MAP.update(al.map)
    nc = pb.finalize()
    _PROG_CACHE[phase] = nc
    return nc


def build_fused():
    if "FUSED" in _PROG_CACHE:
        return _PROG_CACHE["FUSED"]
    pb = PB()
    for s in ("LD", "LWa", "LWb", "LWc", "LWd", "PE", "ACT", "DVE", "ST", "SP"):
        pb.sem(s)
    pb.arena_init()
    io = {}
    io["ident_bf"] = pb.dram("ident_bf", [128, 128], BF16)
    io["ident_f"] = pb.dram("ident_f", [128, 128])
    io["ones_row"] = pb.dram("ones_row", [1, 128])
    io["ones_bf"] = pb.dram("ones_bf", [128, 128], BF16)
    io["mask"] = pb.dram("mask", [128, 128])
    io["maskT"] = pb.dram("maskT", [128, 128])
    io["cos"] = pb.dram("cos", [T, HD])
    io["sin"] = pb.dram("sin", [T, HD])
    io["w_qkv_a"] = pb.dram("w_qkv_a", [1024, 3072])
    io["w_out_a"] = pb.dram("w_out_a", [1024, 1024])
    io["n1_w"] = pb.dram("n1_w", [1, D])
    io["n2_w"] = pb.dram("n2_w", [1, D])
    io["qn_w"] = pb.dram("qn_w", [1, HD])
    io["kn_w"] = pb.dram("kn_w", [1, HD])
    for nm in ("wq_b", "wk_b", "wv_b", "wo_b"):
        io[nm] = pb.dram(nm, [1024, 1024])
    io["e3_down_t"] = pb.dram("e3_down_t", [128, 2, 8, 512])
    io["gate_b"] = pb.dram("gate_b", [1024, 4])
    for nm in ("e0_up_t", "e1_w1_t", "e1_w2_t", "e2_l1_t", "e3_up_t"):
        io[nm] = pb.dram(nm, [128, 4, 8, 512])
    for nm in ("e0_down_t", "e1_down_t", "e2_l4_t", "e3_out_t"):
        io[nm] = pb.dram(nm, [2048, 1024])
    for nm in ("e2_l2_t", "e2_l3_t"):
        io[nm] = pb.dram(nm, [128, 8, 16, 256])
    io["w_qkv_c"] = pb.dram("w_qkv_c", [1024, 3072])
    io["w_out_c"] = pb.dram("w_out_c", [1024, 1024])
    io["gate_d"] = pb.dram("gate_d", [1024, 8])
    io["w1_d"] = pb.dram("w1_d", [8, 128, 8, 8, 512])
    io["b1_d"] = pb.dram("b1_d", [8, 4096])
    io["w2_d"] = pb.dram("w2_d", [8, 4096, 1024])
    io["b2_d"] = pb.dram("b2_d", [8, 1024])
    io["lrow_scr"] = pb.dram("lrow_scr", [1, T], kind="Internal")
    x_in = pb.dram("x_in", [SEQ_PER_CORE, T, D])
    x_out = pb.dram("x_out", [SEQ_PER_CORE, T, D], kind="ExternalOutput")
    xa = pb.dram("xa", [SEQ_PER_CORE, T, D], kind="Internal")
    xb = pb.dram("xb", [SEQ_PER_CORE, T, D], kind="Internal")
    xc = pb.dram("xc", [SEQ_PER_CORE, T, D], kind="Internal")
    xint = pb.dram("xint4", [SEQ_PER_CORE, T, D], kind="Internal")
    chain = {"A": (x_in, xa), "B": (xa, xb), "C": (xb, xc), "D": (xc, x_out)}
    needs = {"A": (), "B": ("mask",), "C": ("rope", "lite"), "D": ()}
    emitters = {"A": emit_phase_a, "B": emit_phase_b, "C": emit_phase_c,
                "D": emit_phase_d}
    for phase in "ABCD":
        for s in range(SEQ_PER_CORE):
            al = Alloc(pb, 0)
            consts = emit_consts(pb, io, al, need=needs[phase])
            pio = dict(io)
            pio["x_in"] = chain[phase][0][s]
            pio["x_out"] = chain[phase][1][s]
            if phase == "B":
                pio["xint_scr"] = xint[s]
            emitters[phase](pb, pio, consts, al)
            pb.section_barrier()
    nc = pb.finalize()
    _PROG_CACHE["FUSED"] = nc
    return nc


FUSED = True


def kernel(**inputs):
    if FUSED:
        return kernel_fused(**inputs)
    return kernel_unfused(**inputs)


def kernel_fused(**inputs):
    X = np.asarray(inputs["L0_attn_X"])
    prep = {}
    for p in (prep_phase_a(inputs), prep_phase_b(inputs),
              prep_phase_c(inputs), prep_phase_d(inputs)):
        prep.update(p)
    prep.update(make_consts("B"))
    prep.update(make_consts("C"))
    nc = build_fused()
    Xs = X.reshape(NCORES, SEQ_PER_CORE, T, D)
    in_maps = [dict(prep, x_in=np.ascontiguousarray(Xs[c])) for c in range(NCORES)]
    res = run_bass_kernel_spmd(nc, in_maps, core_ids=list(range(NCORES)))
    out = np.stack([res.results[c]["x_out"] for c in range(NCORES)])
    return np.ascontiguousarray(out.reshape(32, T, D))


def kernel_unfused(**inputs):
    X = np.asarray(inputs["L0_attn_X"])
    preps = {"A": prep_phase_a(inputs), "B": prep_phase_b(inputs),
             "C": prep_phase_c(inputs), "D": prep_phase_d(inputs)}
    cur = X.reshape(NCORES, SEQ_PER_CORE, T, D)
    for phase in "ABCD":
        nc = build_phase(phase)
        outs = np.empty_like(cur)
        for s in range(SEQ_PER_CORE):
            in_maps = [dict(preps[phase], x_in=np.ascontiguousarray(cur[c, s]))
                       for c in range(NCORES)]
            res = run_bass_kernel_spmd(nc, in_maps, core_ids=list(range(NCORES)))
            for c in range(NCORES):
                outs[c, s] = res.results[c]["x_out"]
        cur = outs
    return np.ascontiguousarray(cur.reshape(32, T, D))
